# version 12
# baseline (speedup 1.0000x reference)
import contextlib
import numpy as np
import concourse.bass as bass
import concourse.mybir as mybir
from concourse.bass_utils import run_bass_kernel_spmd

F32 = mybir.dt.float32
BF16 = mybir.dt.bfloat16
AF = mybir.ActivationFunctionType
ALU = mybir.AluOpType
AX = mybir.AxisListType

S, D = 4096, 1024
NT = 32
KC = 8
EWMAX = 1664


def _qgeom(c):
    lo = max(0, 4 * c - 4)
    hi = min(31, 4 * c + 7)
    Wb = 128 * (hi - lo + 1)
    EW = Wb + 128
    tiles = []
    for j in range(4):
        qt = 4 * c + j
        i0 = 128 * qt
        t_min = 128 * lo - i0 + 385
        Lp = max(0, -t_min)
        t_lo = max(0, t_min)
        mid = min(1024 - t_lo, EW - Lp)
        Rp = EW - Lp - mid
        tiles.append((qt, i0, Lp, t_lo, mid, Rp))
    return lo, hi, Wb, EW, tiles


QG = [_qgeom(c) for c in range(8)]


def build(upto=4, dbg=False, nhp=None, prod_only=False, pstage=9):
    nc = bass.Bass("TRN2", target_bir_lowering=False)
    dt = nc.dram_tensor

    x0 = dt("x0", [S, D], F32, kind="ExternalInput").ap()
    Wq = dt("Wq", [D, D], F32, kind="ExternalInput").ap()
    Wk = dt("Wk", [D, D], F32, kind="ExternalInput").ap()
    Wv = dt("Wv", [D, D], F32, kind="ExternalInput").ap()
    Wo = dt("Wo", [D, D], F32, kind="ExternalInput").ap()
    bq_pp = dt("bq_pp", [128, 8], F32, kind="ExternalInput").ap()
    bk_pp = dt("bk_pp", [128, 8], F32, kind="ExternalInput").ap()
    bv_row = dt("bv_row", [1, D], F32, kind="ExternalInput").ap()
    bo_row = dt("bo_row", [1, D], F32, kind="ExternalInput").ap()
    relT2 = dt("relT2", [128, 1025], F32, kind="ExternalInput").ap()
    w_q2 = dt("w_q2", [D, D], F32, kind="ExternalInput").ap()
    w_k2 = dt("w_k2", [D, D], F32, kind="ExternalInput").ap()
    w_v2 = dt("w_v2", [D, D], F32, kind="ExternalInput").ap()
    basewT = dt("basewT", [D, D], F32, kind="ExternalInput").ap()
    baseb_row = dt("baseb_row", [1, D], F32, kind="ExternalInput").ap()
    gatewT = dt("gatewT", [D, 10], F32, kind="ExternalInput").ap()
    AcatT = dt("AcatT", [D, 128], F32, kind="ExternalInput").ap()
    BcatT = dt("BcatT", [128, D], F32, kind="ExternalInput").ap()
    ident_in = dt("ident_in", [128, 128], F32, kind="ExternalInput").ap()
    out = dt("out", [S, D], F32, kind="ExternalOutput").ap()
    if dbg:
        dbg_qa = dt("dbg_qa", [128, S], BF16, kind="ExternalOutput").ap()
        dbg_ka = dt("dbg_ka", [128, S], BF16, kind="ExternalOutput").ap()
        dbg_va = dt("dbg_va", [128, 2080], BF16, kind="ExternalOutput").ap()
        dbg_e = dt("dbg_e", [128, EWMAX], BF16, kind="ExternalOutput").ap()
        dbg_gb = dt("dbg_gb", [128, 6144], BF16, kind="ExternalOutput").ap()
        dbg_a = dt("dbg_a", [128, 512], BF16, kind="ExternalOutput").ap()
        dbg_ct = dt("dbg_ct", [128, S], BF16, kind="ExternalOutput").ap()
        dbg_sml = dt("dbg_sml", [128, 64], F32, kind="ExternalOutput").ap()

    e_dram = dt("e_dram", [8 * 128 * EWMAX], BF16, kind="Internal")
    ct_dram = dt("ct_dram", [8 * 128 * S], BF16, kind="Internal")
    x1_dram = dt("x1_dram", [S, D], F32, kind="Internal").ap()
    x2_dram = dt("x2_dram", [S, D], F32, kind="Internal").ap()
    x3_dram = dt("x3_dram", [S, D], F32, kind="Internal").ap()
    h2t_dram = dt("h2t_dram", [S, D], BF16, kind="Internal").ap()

    base_st = contextlib.ExitStack()
    mk = base_st.enter_context

    id_bf = mk(nc.sbuf_tensor("id_bf", [128, 128], BF16))
    ones_bf = mk(nc.sbuf_tensor("ones_bf", [128, 512], BF16))
    ones32 = mk(nc.sbuf_tensor("ones32", [128, 16], F32))
    sml = mk(nc.sbuf_tensor("sml", [128, 64], F32))
    bor = mk(nc.sbuf_tensor("bor", [1, D], BF16))
    xbuf = [mk(nc.sbuf_tensor(f"xbuf{i}", [128, D], F32)) for i in range(2)]
    hbuf = [mk(nc.sbuf_tensor(f"hbuf{i}", [128, D], BF16)) for i in range(2)]
    scrb = mk(nc.sbuf_tensor("scrb", [128, D], BF16))

    MISCP = mk(nc.psum_tensor("MISCP", [128, 512], F32))

    sem_names = [
        "phz", "dma_w", "dma_x", "dma_s", "dma_st", "dma_e", "dma_g",
        "dma_ct", "dma_q", "dve_n",
        "pe_m", "pe_p", "pe_f", "pe_av", "pe_x",
        "act_m", "act_e", "act_l",
        "dve_m", "dve_e", "dve_c", "dve_x",
    ]
    sems = {n: mk(nc.semaphore(n)) for n in sem_names}
    cnt = {n: 0 for n in sem_names}
    nphase = [0]

    def inc(instr, name, k=1):
        instr.then_inc(sems[name], k)
        cnt[name] += k
        return instr

    def chk(name, v):
        assert cnt[name] == v, (name, cnt[name], v)

    def wge(eng, name, v):
        if v > 0:
            eng.wait_ge(sems[name], v)

    def snap():
        return dict(cnt)

    def selfedge(eng, instr, name="dve_x"):
        inc(instr, name)
        eng.wait_ge(sems[name], cnt[name])
        return instr

    def pdone(eng, sem=None):
        if sem is not None:
            eng.wait_ge(sems[sem], cnt[sem])
        eng.sem_inc(sems["phz"], 1)

    def end_phase():
        nphase[0] += 1
        cnt["phz"] += 5

    def bt():
        return 5 * nphase[0]

    def ln_dve_stats(v, xa, o=0):
        v.tensor_scalar(scrb[:, :], xa, 1.0, 0.0, ALU.mult, ALU.add,
                        accum_out=sml[:, o : o + 1])
        selfedge(v, v.scalar_tensor_tensor(scrb[:, :], xa, 1.0, xa, ALU.mult,
                                           ALU.mult,
                                           accum_out=sml[:, o + 1 : o + 2]))
        v.tensor_scalar(sml[:, o + 2 : o + 3], sml[:, o : o + 1], -1.0 / D,
                        None, ALU.mult)
        selfedge(v, v.tensor_scalar(sml[:, o + 3 : o + 4],
                                    sml[:, o + 1 : o + 2], 1.0 / D, None,
                                    ALU.mult))
        selfedge(v, v.tensor_mul(sml[:, o + 5 : o + 6], sml[:, o + 2 : o + 3],
                                 sml[:, o + 2 : o + 3]))
        return v.tensor_sub(sml[:, o + 3 : o + 4], sml[:, o + 3 : o + 4],
                            sml[:, o + 5 : o + 6])

    def ln_act_rstd(sc, o=0):
        sc.activation(sml[:, o + 4 : o + 5], sml[:, o + 3 : o + 4], AF.Ln,
                      bias=sml[:, 6:7])
        return sc.activation(sml[:, o + 4 : o + 5], sml[:, o + 4 : o + 5],
                             AF.Exp, scale=-0.5)

    def ln_dve_apply(v, xa, h_out, o=0):
        return v.tensor_scalar(h_out, xa, sml[:, o + 2 : o + 3],
                               sml[:, o + 4 : o + 5], ALU.add, ALU.mult)

    def tpose(t, out_ap, in_ap, ident):
        return t.matmul(out_ap, in_ap, ident, start=True, stop=True,
                        is_transpose=True, skip_group_check=True)

    s1 = contextlib.ExitStack()
    m1 = s1.enter_context
    id_f32 = m1(nc.sbuf_tensor("id_f32", [128, 128], F32))
    TT2 = m1(nc.sbuf_tensor("TT2", [128, 1025], BF16))
    bqc = m1(nc.sbuf_tensor("bqc", [128, 8], F32))
    bkc = m1(nc.sbuf_tensor("bkc", [128, 8], F32))
    bvr = m1(nc.sbuf_tensor("bvr", [1, D], BF16))
    maxT = m1(nc.sbuf_tensor("maxT", [1, 1], F32))
    TdT0 = m1(nc.sbuf_tensor("TdT0", [64, 2], BF16))
    tsq = m1(nc.sbuf_tensor("tsq", [64, 1025], BF16))

    wqh = [m1(nc.sbuf_tensor(f"wqh{i}", [128, 3072], BF16)) for i in range(2)]
    WQH_Q, WQH_K, WQH_V = 0, 1024, 2048
    h1T = m1(nc.sbuf_tensor("h1T", [128, KC * S], BF16))
    Qa = [m1(nc.sbuf_tensor(f"Qa{i}", [128, S], BF16)) for i in range(2)]
    Ka = [m1(nc.sbuf_tensor(f"Ka{i}", [128, S], BF16)) for i in range(2)]
    Va = [m1(nc.sbuf_tensor(f"Va{i}", [128, 32 * 65], BF16)) for i in range(2)]
    QKstg = [m1(nc.sbuf_tensor(f"QKstg{i}", [128, 512], BF16)) for i in range(8)]
    rowstg = m1(nc.sbuf_tensor("rowstg", [32, 128], BF16))
    nmall = [m1(nc.sbuf_tensor(f"nmall{i}", [128, 32], F32)) for i in range(2)]
    lnqq = m1(nc.sbuf_tensor("lnqq", [128, 32], F32))
    mallb = m1(nc.sbuf_tensor("mallb", [128, 32], F32))
    lncbf = m1(nc.sbuf_tensor("lncbf", [1, 2], BF16))
    Asb = [m1(nc.sbuf_tensor(f"Asb{i}", [128, 512], BF16)) for i in range(3)]
    lrow = [m1(nc.sbuf_tensor(f"lrow{i}", [1, 512], BF16)) for i in range(2)]
    Esb = [m1(nc.sbuf_tensor(f"Esb{i}", [128, EWMAX], BF16)) for i in range(2)]
    GBs = [m1(nc.sbuf_tensor(f"GBs{i}", [128, 4 * 1536], BF16)) for i in range(2)]
    pads = m1(nc.sbuf_tensor("pads", [128, 2], F32))
    rlt = m1(nc.sbuf_tensor("rlt", [1, 512], F32))
    rl_sb = m1(nc.sbuf_tensor("rl_sb", [1, 512], BF16))
    rlb_sb = m1(nc.sbuf_tensor("rlb_sb", [128, 512], BF16))
    CTs = m1(nc.sbuf_tensor("CTs", [128, S], BF16))

    with nc.Block() as block:
        B = snap()

        @block.gpsimd
        def _(g):
            inc(g.dma_start(id_bf[:, :], ident_in), "dma_w", 16)
            inc(g.dma_start(TT2[:, :], relT2), "dma_w", 16)
            inc(g.dma_start(bvr[:, :], bv_row), "dma_w", 16)
            inc(g.dma_start(bor[:, :], bo_row), "dma_w", 16)
            g.wait_ge(sems["dma_w"], B["dma_w"] + 64)
            g.wait_ge(sems["dma_s"], B["dma_s"] + 48)
            pdone(g)

        @block.sync
        def _(s):
            inc(s.dma_start(id_f32[:, :], ident_in), "dma_s", 16)
            inc(s.dma_start(bqc[:, :], bq_pp), "dma_s", 16)
            inc(s.dma_start(bkc[:, :], bk_pp), "dma_s", 16)
            s.wait_ge(sems["dma_w"], B["dma_w"] + 64)
            s.wait_ge(sems["dma_s"], B["dma_s"] + 48)
            pdone(s)

        @block.vector
        def _(v):
            v.memset(ones_bf[:, :], 1.0)
            v.memset(ones32[:, :], 1.0)
            v.memset(sml[:, :], 0.0)
            v.memset(sml[:, 6:7], 1e-5)
            v.memset(sml[:, 7:8], 0.6931471805599453)
            v.memset(Ka[0][64:128, :], 0.0)
            v.memset(Ka[1][64:128, :], 0.0)
            v.memset(Qa[0][64:128, :], 0.0)
            v.memset(Qa[1][64:128, :], 0.0)
            selfedge(v, v.memset(Ka[0][64:66, :], 1.0))
            v.memset(Ka[1][64:66, :], 1.0)
            v.memset(Va[0][:, :], 1.0)
            v.memset(Va[1][:, :], 1.0)
            v.wait_ge(sems["dma_w"], B["dma_w"] + 64)
            v.wait_ge(sems["dma_s"], B["dma_s"] + 48)
            v.tensor_copy(TdT0[0:64, 0:1], TT2[0:64, 0:1])
            ins = inc(v.tensor_sub(TdT0[0:64, 1:2], TT2[0:64, 1024:1025],
                                   TT2[0:64, 0:1]), "dve_m")
            pdone(v, "dve_m")

        @block.tensor
        def _(t):
            pdone(t)

        @block.scalar
        def _(sc):
            pdone(sc)

    end_phase()

    with nc.Block() as block:
        B = snap()

        @block.gpsimd
        def _(g):
            g.wait_ge(sems["phz"], bt())
            for w0, wsrc in ((WQH_Q, Wq), (WQH_K, Wk), (WQH_V, Wv)):
                for kcc in range(KC):
                    inc(g.dma_start(
                        wqh[0][:, w0 + kcc * 128 : w0 + (kcc + 1) * 128],
                        wsrc[kcc * 128 : (kcc + 1) * 128, 0:128]), "dma_w", 16)
            g.wait_ge(sems["dma_w"], B["dma_w"] + 24 * 16)
            pdone(g)

        @block.vector
        def _(v):
            v.wait_ge(sems["phz"], bt())
            inc(v.tensor_mul(tsq[0:64, 0:1025], TT2[0:64, :], TT2[0:64, :]),
                "dve_m")
            wge(v, "pe_m", B["pe_m"] + 1)
            inc(v.tensor_reduce(sml[:, 8:9], MISCP[:, 0:8], axis=AX.X,
                                op=ALU.max), "dve_m")
            wge(v, "pe_m", B["pe_m"] + 2)
            selfedge(v, v.tensor_reduce(sml[0:1, 9:10], MISCP[0:1, 256:384],
                                        axis=AX.X, op=ALU.max))
            inc(v.tensor_tensor(sml[0:1, 9:10], sml[0:1, 9:10],
                                MISCP[0:1, 8:9], ALU.max), "dve_m")
            pdone(v, "dve_m")

        @block.tensor
        def _(t):
            t.wait_ge(sems["phz"], bt())
            wge(t, "dve_m", B["dve_m"] + 1)
            ins = None
            for t9 in range(9):
                w = min(128, 1025 - t9 * 128)
                ins = t.matmul(MISCP[0:w, t9 : t9 + 1],
                               tsq[0:64, t9 * 128 : t9 * 128 + w],
                               ones_bf[0:64, 0:1], start=True, stop=True)
            inc(ins, "pe_m")
            wge(t, "dve_m", B["dve_m"] + 2)
            ins = inc(tpose(t, MISCP[0:1, 256:384], sml[:, 8:9],
                                  id_f32[:, :]), "pe_m")
            pdone(t, "pe_m")

        @block.scalar
        def _(sc):
            sc.wait_ge(sems["phz"], bt())
            wge(sc, "dve_m", B["dve_m"] + 3)
            sc.activation(sml[0:1, 10:11], sml[0:1, 9:10], AF.Ln)
            ins = inc(sc.activation(maxT[:, :], sml[0:1, 10:11], AF.Exp,
                                    scale=0.5), "act_m")
            pdone(sc, "act_m")

        @block.sync
        def _(s):
            s.wait_ge(sems["phz"], bt())
            pdone(s)

    end_phase()

    s1a = contextlib.ExitStack()
    stgbf = [s1a.enter_context(nc.psum_tensor(f"stgbf{i}", [128, 512], BF16))
             for i in range(2)]
    with nc.Block() as block:
        B = snap()

        @block.gpsimd
        def _(g):
            g.wait_ge(sems["phz"], bt())
            ins = None
            for tt in range(NT):
                if tt >= 2:
                    wge(g, "dve_m", B["dve_m"] + 3 * (tt - 2) + 2)
                ins = inc(g.dma_start(xbuf[tt % 2][:, :],
                                      x0[tt * 128 : (tt + 1) * 128, :]),
                          "dma_x", 16)
            pdone(g)

        @block.vector
        def _(v):
            v.wait_ge(sems["phz"], bt())
            ins = None
            for tt in range(NT):
                wge(v, "dma_x", B["dma_x"] + 16 * (tt + 1))
                inc(ln_dve_stats(v, xbuf[tt % 2][:, :]), "dve_m")
                wge(v, "act_m", B["act_m"] + tt + 1)
                if tt >= 2:
                    wge(v, "pe_m", B["pe_m"] + tt - 1)
                inc(ln_dve_apply(v, xbuf[tt % 2][:, :], hbuf[tt % 2][:, :]),
                    "dve_m")
                wge(v, "pe_m", B["pe_m"] + tt + 1)
                for k8 in range(KC):
                    ins = v.tensor_copy(
                        h1T[:, k8 * S + tt * 128 : k8 * S + (tt + 1) * 128],
                        stgbf[k8 // 4][:, (k8 % 4) * 128 : (k8 % 4 + 1) * 128])
                inc(ins, "dve_m")
            pdone(v, "dve_m")

        @block.scalar
        def _(sc):
            sc.wait_ge(sems["phz"], bt())
            ins = None
            for tt in range(NT):
                wge(sc, "dve_m", B["dve_m"] + 3 * tt + 1)
                ins = inc(ln_act_rstd(sc), "act_m")
            pdone(sc, "act_m")

        @block.tensor
        def _(t):
            t.wait_ge(sems["phz"], bt())
            ins = None
            for tt in range(NT):
                wge(t, "dve_m", B["dve_m"] + 3 * tt + 2)
                if tt >= 1:
                    wge(t, "dve_m", B["dve_m"] + 3 * (tt - 1) + 3)
                for k8 in range(KC):
                    ins = tpose(t, 
                        stgbf[k8 // 4][:, (k8 % 4) * 128 : (k8 % 4 + 1) * 128],
                        hbuf[tt % 2][:, k8 * 128 : (k8 + 1) * 128], id_bf[:, :])
                inc(ins, "pe_m")
            pdone(t, "pe_m")

        @block.sync
        def _(s):
            s.wait_ge(sems["phz"], bt())
            pdone(s)

    end_phase()
    s1a.close()

    s1p = contextlib.ExitStack()
    mp = s1p.enter_context
    PA = mp(nc.psum_tensor("PA", [128, 512], F32))
    PB = mp(nc.psum_tensor("PB", [128, 512], F32))
    CTX = mp(nc.psum_tensor("CTX", [128, 512], F32))
    Pp = mp(nc.psum_tensor("Pp", [128, 1536], F32))
    PC = mp(nc.psum_tensor("PC", [128, 512], F32))

    def pe_ptile(t, h_, c_, j, dve_e_wait):
        lo, hi, Wb, EW, tiles = QG[c_]
        qt, i0, Lp, t_lo, mid, Rp = tiles[j]
        qa = Qa[h_]
        wge(t, "dve_e", dve_e_wait)
        t.matmul(Pp[:, 0:512], qa[0:64, i0 : i0 + 128], TT2[0:64, 0:512],
                 start=True, stop=True)
        t.matmul(Pp[:, 512:1024], qa[0:64, i0 : i0 + 128], TT2[0:64, 512:1024],
                 start=True, stop=True)
        ins = t.matmul(Pp[:, 1024:1025], qa[0:64, i0 : i0 + 128],
                       TT2[0:64, 1024:1025], start=True, stop=True)
        inc(ins, "pe_p")

    def dve_etile(v, h_, c_, j, pe_p_wait, dma_e_wait, slot):
        lo, hi, Wb, EW, tiles = QG[c_]
        qt, i0, Lp, t_lo, mid, Rp = tiles[j]
        Eb = Esb[slot]
        wge(v, "pe_p", pe_p_wait)
        wge(v, "dma_e", dma_e_wait)
        selfedge(v, v.tensor_scalar(pads[:, 0:1], Pp[:, 0:1],
                                    nmall[h_][:, qt : qt + 1], None, ALU.add))
        selfedge(v, v.tensor_scalar(pads[:, 1:2], Pp[:, 1024:1025],
                                    nmall[h_][:, qt : qt + 1], None, ALU.add))
        ins = v.tensor_scalar(Eb[:, Lp : Lp + mid], Pp[:, t_lo : t_lo + mid],
                              nmall[h_][:, qt : qt + 1], None, ALU.add)
        off = 0
        while off < Lp:
            w = min(512, Lp - off)
            ins = v.tensor_scalar(Eb[:, off : off + w], ones_bf[:, 0:w],
                                  pads[:, 0:1], None, ALU.mult)
            off += w
        off = Lp + mid
        while off < EW:
            w = min(512, EW - off)
            ins = v.tensor_scalar(Eb[:, off : off + w], ones_bf[:, 0:w],
                                  pads[:, 1:2], None, ALU.mult)
            off += w
        inc(ins, "dve_e")

    def sync_etile(s, c_, j, sp, dve_e_wait, dmag_guard, dmae_wait, pef_guard,
                   slot):
        lo, hi, Wb, EW, tiles = QG[c_]
        slot_off = (sp * 4 + j) * 128 * EWMAX
        wge(s, "dve_e", dve_e_wait)
        wge(s, "dma_g", dmag_guard)
        inc(s.dma_start(bass.AP(e_dram, slot_off, [[EWMAX, 128], [1, EW]]),
                        Esb[slot][:, 0:EW]), "dma_e", 16)
        wge(s, "dma_e", dmae_wait)
        wge(s, "pe_f", pef_guard)
        inc(s.dma_start(GBs[sp][:, j * 1536 : j * 1536 + Wb],
                        bass.AP(e_dram, slot_off + 127,
                                [[EWMAX - 1, 128], [1, Wb]])), "dma_g", 16)

    for hp in range(8 if nhp is None else nhp):
        with nc.Block() as block:
            B = snap()
            wsl = hp % 2
            dma_w_rdy = cnt["dma_w"]

            @block.tensor
            def _(t):
                t.wait_ge(sems["phz"], bt())
                wge(t, "dma_w", dma_w_rdy)
                for p in range(16):
                    c = p % 8
                    w0 = WQH_K if p < 8 else WQH_Q
                    if p >= 2:
                        wge(t, "dve_m", B["dve_m"] + 1 + p)
                    bank = (PA if p % 2 == 0 else PB)[:, 0:512]
                    ins = None
                    for kcc in range(KC):
                        ins = t.matmul(
                            bank,
                            wqh[wsl][:, w0 + kcc * 128 : w0 + (kcc + 1) * 128],
                            h1T[:, kcc * S + c * 512 : kcc * S + (c + 1) * 512],
                            start=(kcc == 0), stop=(kcc == KC - 1))
                    inc(ins, "pe_m")
                for h in range(2 if pstage >= 2 else 0):
                    for ch in range(KC):
                        wge(t, "dve_m", B["dve_m"] + 19 + 10 * h + ch)
                        ins = None
                        for i4 in range(4):
                            qt = 4 * ch + i4
                            ins = t.matmul(
                                MISCP[:, 32 * h + qt : 32 * h + qt + 1],
                                Asb[ch % 2][0:64, i4 * 128 : (i4 + 1) * 128],
                                ones_bf[0:64, 0:1], start=True, stop=True,
                                skip_group_check=True)
                        inc(ins, "pe_m")
                    wge(t, "dve_m", B["dve_m"] + 27 + 10 * h)
                    inc(tpose(t, MISCP[0:1, 256:384], sml[:, 8 + h : 9 + h],
                                    id_f32[:, :]), "pe_m")
                if pstage >= 2:
                    chk("pe_m", B["pe_m"] + 34)
                for h in range(2 if pstage >= 2 else 0):
                    wge(t, "act_m", B["act_m"] + 3 + h)
                    inc(t.matmul(MISCP[:, 100 + h : 101 + h],
                                 ones_bf[0:1, 0:128], lncbf[0:1, h : h + 1],
                                 start=True, stop=True,
                                 skip_group_check=True), "pe_m")
                for h in range(2 if pstage >= 3 else 0):
                    for ch in range(KC):
                        wge(t, "dve_m", B["dve_m"] + 43 + 8 * h + ch)
                        if ch == 0:
                            wge(t, "dve_m", B["dve_m"] + 27 + 10 * h)
                        ins = None
                        for i4 in range(4):
                            qt = 4 * ch + i4
                            ins = t.matmul(
                                MISCP[:, 32 * h + qt : 32 * h + qt + 1],
                                Asb[ch % 2][0:64, i4 * 128 : (i4 + 1) * 128],
                                ones_bf[0:64, 0:1], start=True, stop=True,
                                skip_group_check=True)
                        inc(ins, "pe_m")
                if pstage >= 3:
                    chk("pe_m", B["pe_m"] + 52)
                for h in range(2 if pstage >= 3 else 0):
                    wge(t, "dve_m", B["dve_m"] + 60 + 3 * h)
                    wge(t, "act_m", B["act_m"] + 6)
                    inc(tpose(t, MISCP[0:32, 256:384], nmall[h][:, 0:32],
                                    id_f32[:, :]), "pe_m")
                for h in range(2 if pstage >= 4 else 0):
                    for c in range(8):
                        pidx = 8 * h + c
                        if pidx == 0:
                            wge(t, "act_m", B["act_m"] + 6)
                            wge(t, "dve_m", B["dve_m"] + 64)
                        else:
                            wge(t, "dve_m", B["dve_m"] + 64 + pidx)
                        inc(t.matmul(MISCP[0:2, 0:512], TdT0[0:64, 0:2],
                                     Qa[h][0:64, c * 512 : (c + 1) * 512],
                                     start=True, stop=True,
                                     skip_group_check=True), "pe_m")
                if pstage >= 4:
                    chk("pe_m", B["pe_m"] + 70)
                for kt in range(32 if pstage >= 5 else 0):
                    if kt >= 2:
                        wge(t, "dve_m", B["dve_m"] + 81 + kt - 2)
                    sub = (PA if kt % 2 == 0 else PB)[:, 0:128]
                    for kcc in range(KC):
                        t.matmul(sub,
                                 h1T[:, kcc * S + kt * 128 : kcc * S + (kt + 1) * 128],
                                 wqh[wsl][:, WQH_V + kcc * 128 : WQH_V + (kcc + 1) * 128],
                                 start=(kcc == 0), stop=False,
                                 skip_group_check=True)
                    ins = t.matmul(sub, ones_bf[0:1, 0:128],
                                   bvr[0:1, hp * 128 : (hp + 1) * 128],
                                   start=False, stop=True,
                                   skip_group_check=True)
                    inc(ins, "pe_m")
                if pstage >= 5:
                    chk("pe_m", B["pe_m"] + 102)
                for j in range(4 if pstage >= 6 else 0):
                    pe_ptile(t, 0, 0, j, B["dve_e"] + j)
                if pstage >= 6:
                    chk("pe_p", B["pe_p"] + 4)
                pdone(t, "pe_p" if pstage >= 6 else "pe_m")

            @block.vector
            def _(v):
                v.wait_ge(sems["phz"], bt())
                inc(v.memset(Qa[0][64:66, :], 0.0), "dve_m")
                inc(v.memset(Qa[1][64:66, :], 0.0), "dve_m")
                for p in range(16):
                    c = p % 8
                    wge(v, "pe_m", B["pe_m"] + 1 + p)
                    if p >= 8:
                        wge(v, "dma_q", B["dma_q"] + 16 * (c + 1))
                    bank = (PA if p % 2 == 0 else PB)[:, 0:512]
                    dst = Ka[0] if p < 8 else Qa[0]
                    bc = bkc if p < 8 else bqc
                    scl = 0.125 if p < 8 else None
                    op1 = ALU.mult if p < 8 else ALU.add
                    if p < 8:
                        v.tensor_scalar(dst[0:64, c * 512 : (c + 1) * 512],
                                        bank[0:64, :], bc[0:64, hp : hp + 1],
                                        scl, ALU.add, op1)
                        ins = v.tensor_scalar(QKstg[c][64:128, :],
                                              bank[64:128, :],
                                              bc[64:128, hp : hp + 1],
                                              scl, ALU.add, op1)
                    else:
                        v.tensor_scalar(dst[0:64, c * 512 : (c + 1) * 512],
                                        bank[0:64, :], bc[0:64, hp : hp + 1],
                                        None, ALU.add)
                        ins = v.tensor_scalar(QKstg[c][64:128, :],
                                              bank[64:128, :],
                                              bc[64:128, hp : hp + 1],
                                              None, ALU.add)
                    inc(ins, "dve_m")
                chk("dve_m", B["dve_m"] + 18)
                for h in range(2 if pstage >= 2 else 0):
                    src = Ka[h]
                    for ch in range(KC):
                        n = 8 * h + ch
                        if n >= 2:
                            pm = 17 + (n - 2) if n - 2 < 8 else 26 + (n - 2 - 8)
                            wge(v, "pe_m", B["pe_m"] + pm)
                        if h == 1:
                            wge(v, "dma_q", B["dma_q"] + 16 * (ch + 1))
                        inc(v.tensor_mul(Asb[ch % 2][0:64, :],
                                         src[0:64, ch * 512 : (ch + 1) * 512],
                                         src[0:64, ch * 512 : (ch + 1) * 512]),
                            "dve_m")
                    wge(v, "pe_m", B["pe_m"] + 24 + 9 * h)
                    inc(v.tensor_reduce(sml[:, 8 + h : 9 + h],
                                        MISCP[:, 32 * h : 32 * h + 32],
                                        axis=AX.X, op=ALU.max), "dve_m")
                    wge(v, "pe_m", B["pe_m"] + 25 + 9 * h)
                    inc(v.tensor_reduce(sml[0:1, 10 + h : 11 + h],
                                        MISCP[0:1, 256:384], axis=AX.X,
                                        op=ALU.max), "dve_m")
                if pstage >= 2:
                    chk("dve_m", B["dve_m"] + 38)
                for h in range(2 if pstage >= 2 else 0):
                    wge(v, "act_m", B["act_m"] + 1 + h)
                    inc(v.tensor_add(sml[0:1, 16 + h : 17 + h],
                                     sml[0:1, 14 + h : 15 + h], maxT[:, :]),
                        "dve_m")
                for h in range(2 if pstage >= 2 else 0):
                    wge(v, "pe_m", B["pe_m"] + 35 + h)
                    inc(v.tensor_copy(sml[:, 22 + h : 23 + h],
                                      MISCP[:, 100 + h : 101 + h]), "dve_m")
                for h in range(2 if pstage >= 3 else 0):
                    src = Qa[h]
                    for ch in range(KC):
                        n = 16 + 8 * h + ch
                        pm = 26 + (n - 2 - 8) if n - 2 < 16 else 37 + (n - 2 - 16)
                        wge(v, "pe_m", B["pe_m"] + pm)
                        if h == 1:
                            wge(v, "dma_q", B["dma_q"] + 128 + 16 * (ch + 1))
                        inc(v.tensor_mul(Asb[ch % 2][0:64, :],
                                         src[0:64, ch * 512 : (ch + 1) * 512],
                                         src[0:64, ch * 512 : (ch + 1) * 512]),
                            "dve_m")
                if pstage >= 3:
                    chk("dve_m", B["dve_m"] + 58)
                for h in range(2 if pstage >= 3 else 0):
                    wge(v, "act_m", B["act_m"] + 5 + h)
                    inc(v.tensor_scalar(nmall[h][:, :], mallb[:, :], -1.0,
                                        None, ALU.mult), "dve_m")
                    v.wait_ge(sems["dve_m"], cnt["dve_m"])
                    inc(v.tensor_scalar(nmall[h][:, :], nmall[h][:, :], 1.0,
                                        None, ALU.mult), "dve_m")
                    wge(v, "pe_m", B["pe_m"] + 53 + h)
                    if h == 1:
                        wge(v, "dma_q", B["dma_q"] + 272)
                    inc(v.tensor_copy(rowstg[0:32, :], MISCP[0:32, 256:384]),
                        "dve_m")
                if pstage >= 3:
                    chk("dve_m", B["dve_m"] + 64)
                for h in range(2 if pstage >= 4 else 0):
                    for c in range(8):
                        wge(v, "pe_m", B["pe_m"] + 55 + 8 * h + c)
                        wge(v, "dma_q", B["dma_q"] + 272 + 16 * h)
                        inc(v.tensor_tensor(
                            Qa[h][64:66, c * 512 : (c + 1) * 512],
                            MISCP[0:2, 0:512],
                            Qa[h][64:66, c * 512 : (c + 1) * 512],
                            ALU.add), "dve_m")
                if pstage >= 4:
                    chk("dve_m", B["dve_m"] + 80)
                for kt in range(32 if pstage >= 5 else 0):
                    wge(v, "pe_m", B["pe_m"] + 71 + kt)
                    sub = (PA if kt % 2 == 0 else PB)[:, 0:128]
                    v.tensor_copy(Va[0][:, kt * 65 : kt * 65 + 64], sub[:, 0:64])
                    ins = v.tensor_copy(Va[1][:, kt * 65 : kt * 65 + 64],
                                        sub[:, 64:128])
                    inc(ins, "dve_m")
                if pstage >= 5:
                    chk("dve_m", B["dve_m"] + 112)
                for j in range(4 if pstage >= 6 else 0):
                    dve_etile(v, 0, 0, j, B["pe_p"] + j + 1,
                              B["dma_e"] + 16 * (j - 1), j % 2)
                if pstage >= 6:
                    chk("dve_e", B["dve_e"] + 4)
                pdone(v, "dve_e" if pstage >= 6 else "dve_m")

            @block.scalar
            def _(sc):
                sc.wait_ge(sems["phz"], bt())
                for h in range(2 if pstage >= 2 else 0):
                    wge(sc, "dve_m", B["dve_m"] + 28 + 10 * h)
                    sc.activation(sml[0:1, 12 + h : 13 + h],
                                  sml[0:1, 10 + h : 11 + h], AF.Ln)
                    inc(sc.activation(sml[0:1, 14 + h : 15 + h],
                                      sml[0:1, 12 + h : 13 + h], AF.Exp,
                                      scale=0.5), "act_m")
                for h in range(2 if pstage >= 2 else 0):
                    wge(sc, "dve_m", B["dve_m"] + 39 + h)
                    sc.activation(sml[0:1, 18 + h : 19 + h],
                                  sml[0:1, 16 + h : 17 + h], AF.Ln)
                    inc(sc.activation(lncbf[0:1, h : h + 1],
                                      sml[0:1, 18 + h : 19 + h], AF.Copy),
                        "act_m")
                for h in range(2 if pstage >= 3 else 0):
                    wge(sc, "pe_m", B["pe_m"] + 52)
                    wge(sc, "dve_m", B["dve_m"] + 41 + h)
                    if h == 1:
                        wge(sc, "dve_m", B["dve_m"] + 59)
                    sc.activation(lnqq[:, :], MISCP[:, 32 * h : 32 * h + 32],
                                  AF.Ln)
                    inc(sc.activation(mallb[:, :], lnqq[:, :], AF.Exp,
                                      scale=0.5, bias=sml[:, 22 + h : 23 + h]),
                        "act_m")
                pdone(sc, "act_m")

            @block.gpsimd
            def _(g):
                g.wait_ge(sems["phz"], bt())
                for p in range(16):
                    c = p % 8
                    dst = Ka[1] if p < 8 else Qa[1]
                    wge(g, "dve_m", B["dve_m"] + 3 + p)
                    inc(g.dma_start(dst[0:64, c * 512 : (c + 1) * 512],
                                    QKstg[c][64:128, :]), "dma_q", 16)
                for h in range(2 if pstage >= 3 else 0):
                    wge(g, "dve_m", B["dve_m"] + 61 + 3 * h)
                    inc(g.dma_start(Qa[h][64:65, :], rowstg[0:32, :]),
                        "dma_q", 16)
                if pstage >= 3:
                    chk("dma_q", B["dma_q"] + 288)
                g.wait_ge(sems["dma_q"], cnt["dma_q"])
                pdone(g)

            @block.sync
            def _(s):
                s.wait_ge(sems["phz"], bt())
                for j in range(4 if pstage >= 6 else 0):
                    sync_etile(s, 0, j, 0,
                               B["dve_e"] + j + 1, 0,
                               B["dma_e"] + 16 * (j + 1), 0, j % 2)
                s.wait_ge(sems["dma_g"], cnt["dma_g"])
                pdone(s)

        end_phase()

        if prod_only:
            continue
        with nc.Block() as block:
            B = snap()

            def _ext(c, kt):
                lo, hi, Wb, EW, tiles = QG[c]
                return 65 if kt < lo else (66 if kt > hi else 64)

            def _gbts(t, idx, kt, bank, lo):
                ins = None
                for j in range(4):
                    ins = t.matmul(
                        bank[:, j * 128 : (j + 1) * 128],
                        GBs[idx % 2][:, j * 1536 + (kt - lo) * 128 :
                                     j * 1536 + (kt - lo + 1) * 128],
                        id_bf[:, :], start=False, stop=(j == 3),
                        skip_group_check=True)
                return ins

            def fills_pe(t, idx, kt, pair=False):
                h, c = idx // 8, idx % 8
                lo, hi, Wb, EW, tiles = QG[c]
                n_f = 2 if pair else 1
                g = 32 * idx + kt
                wge(t, "act_e", B["act_e"] + g + n_f - 3)
                if kt <= lo < kt + n_f and idx >= 1:
                    wge(t, "dma_g", B["dma_g"] + 64 * idx)
                bank = (PA, PB, PC)[g % 3][:, 0:512]
                ext = _ext(c, kt)
                ins = t.matmul(bank, Ka[h][0:ext, kt * 128 : (kt + 1) * 128],
                               Qa[h][0:ext, c * 512 : (c + 1) * 512],
                               start=True, stop=(ext != 64),
                               skip_group_check=True)
                if pair:
                    bank2 = (PA, PB, PC)[(g + 1) % 3][:, 0:512]
                    t.matmul(bank2,
                             Ka[h][0:64, (kt + 1) * 128 : (kt + 2) * 128],
                             Qa[h][0:64, c * 512 : (c + 1) * 512],
                             start=True, stop=False,
                             skip_group_check=True)
                if ext == 64:
                    ins = _gbts(t, idx, kt, bank, lo)
                inc(ins, "pe_f")
                chk("pe_f", B["pe_f"] + 32 * idx + kt + 1)
                if pair:
                    ins = _gbts(t, idx, kt + 1, bank2, lo)
                    inc(ins, "pe_f")
                    chk("pe_f", B["pe_f"] + 32 * idx + kt + 2)

            @block.tensor
            def _(t):
                t.wait_ge(sems["phz"], bt())
                for idx in range(16):
                    h, c = idx // 8, idx % 8
                    lo0, hi0 = QG[idx % 8][0], QG[idx % 8][1]
                    if lo0 <= 0 and 1 <= hi0:
                        fills_pe(t, idx, 0, pair=True)
                    else:
                        fills_pe(t, idx, 0)
                        fills_pe(t, idx, 1)
                    for kt in range(34):
                        if kt >= 2:
                            ktA = kt - 2
                            gA = 32 * idx + ktA
                            wge(t, "act_e", B["act_e"] + gA + 1)
                            if ktA == 0:
                                wge(t, "dve_c", B["dve_c"] + idx)
                            inc(t.matmul(
                                CTX[0:65, 0:512],
                                Va[h][:, ktA * 65 : ktA * 65 + 65],
                                Asb[gA % 3][:, :],
                                start=(ktA == 0), stop=(ktA == 31)), "pe_av")
                        if kt % 2 == 0 and kt + 3 <= 31:
                            kf = kt + 2
                            if _ext(c, kf) == 64 and _ext(c, kf + 1) == 64:
                                fills_pe(t, idx, kf, pair=True)
                            else:
                                fills_pe(t, idx, kf)
                                fills_pe(t, idx, kf + 1)
                        if kt == 4 and idx >= 1:
                            hN = (idx - 1) // 8
                            wge(t, "act_l", B["act_l"] + idx)
                            wge(t, "dve_n", B["dve_n"] + idx - 1)
                            inc(t.matmul(MISCP[64 * hN : 64 * hN + 64, 0:512],
                                         ones_bf[0:1, 0:64],
                                         rl_sb[0:1, 0:512],
                                         start=True, stop=True), "pe_x")
                        if idx + 1 <= 15 and kt in (2, 6, 10, 14):
                            j = {2: 0, 6: 1, 10: 2, 14: 3}[kt]
                            h2, c2 = (idx + 1) // 8, (idx + 1) % 8
                            pe_ptile(t, h2, c2, j, B["dve_e"] + 4 * idx + j)
                    chk("pe_av", B["pe_av"] + 32 * (idx + 1))
                wge(t, "act_l", B["act_l"] + 16)
                wge(t, "dve_n", B["dve_n"] + 15)
                inc(t.matmul(MISCP[64:128, 0:512], ones_bf[0:1, 0:64],
                             rl_sb[0:1, 0:512], start=True, stop=True), "pe_x")
                chk("pe_x", B["pe_x"] + 16)
                pdone(t, "pe_x")

            @block.scalar
            def _(sc):
                sc.wait_ge(sems["phz"], bt())
                for idx in range(16):
                    for kt in range(32):
                        g = 32 * idx + kt
                        wge(sc, "pe_f", B["pe_f"] + g + 1)
                        wge(sc, "pe_av", B["pe_av"] + g - 2)
                        bank = (PA, PB, PC)[g % 3][:, 0:512]
                        inc(sc.activation(Asb[g % 3][:, :], bank, AF.Exp),
                            "act_e")
                        if kt == 1 and idx >= 1:
                            wge(sc, "dve_c", B["dve_c"] + idx)
                            sc.activation(rlt[0:1, :],
                                          lrow[(idx - 1) % 2][0:1, :], AF.Ln)
                            inc(sc.activation(rl_sb[0:1, :], rlt[0:1, :],
                                              AF.Exp, scale=-1.0), "act_l")
                wge(sc, "dve_c", B["dve_c"] + 16)
                sc.activation(rlt[0:1, :], lrow[1][0:1, :], AF.Ln)
                inc(sc.activation(rl_sb[0:1, :], rlt[0:1, :], AF.Exp,
                                  scale=-1.0), "act_l")
                chk("act_e", B["act_e"] + 512)
                chk("act_l", B["act_l"] + 16)
                pdone(sc, "act_l")

            def ct_slice(k):
                hP, cP = k // 8, k % 8
                return CTs[64 * hP : 64 * hP + 64, cP * 512 : (cP + 1) * 512]

            @block.vector
            def _(v):
                v.wait_ge(sems["phz"], bt())
                for idx in range(16):
                    if idx >= 1:
                        wge(v, "pe_av", B["pe_av"] + 32 * idx)
                        wge(v, "act_l", B["act_l"] + idx - 2)
                        v.tensor_copy(ct_slice(idx - 1), CTX[0:64, 0:512])
                        inc(v.tensor_copy(lrow[(idx - 1) % 2][0:1, :],
                                          CTX[64:65, 0:512]), "dve_c")
                    if idx >= 2:
                        hN = (idx - 2) // 8
                        wge(v, "pe_x", B["pe_x"] + idx - 1)
                        selfedge(v, v.tensor_copy(
                            rlb_sb[64 * hN : 64 * hN + 64, :],
                            MISCP[64 * hN : 64 * hN + 64, 0:512]))
                        inc(v.tensor_tensor(ct_slice(idx - 2), ct_slice(idx - 2),
                                            rlb_sb[64 * hN : 64 * hN + 64, :],
                                            ALU.mult), "dve_n")
                    if idx + 1 <= 15:
                        h2, c2 = (idx + 1) // 8, (idx + 1) % 8
                        for j in range(4):
                            n = 4 * idx + j
                            dve_etile(v, h2, c2, j,
                                      B["pe_p"] + n + 1,
                                      B["dma_e"] + 16 * (n - 1), n % 2)
                wge(v, "pe_av", B["pe_av"] + 512)
                wge(v, "act_l", B["act_l"] + 14)
                v.tensor_copy(ct_slice(15), CTX[0:64, 0:512])
                inc(v.tensor_copy(lrow[1][0:1, :], CTX[64:65, 0:512]), "dve_c")
                for k in (14, 15):
                    wge(v, "pe_x", B["pe_x"] + k + 1)
                    selfedge(v, v.tensor_copy(rlb_sb[64:128, :],
                                              MISCP[64:128, 0:512]))
                    inc(v.tensor_tensor(ct_slice(k), ct_slice(k),
                                        rlb_sb[64:128, :], ALU.mult), "dve_n")
                chk("dve_c", B["dve_c"] + 16)
                chk("dve_n", B["dve_n"] + 16)
                chk("dve_e", B["dve_e"] + 60)
                pdone(v, "dve_n")

            @block.sync
            def _(s):
                s.wait_ge(sems["phz"], bt())
                for idx in range(15):
                    c2 = (idx + 1) % 8
                    sp = (idx + 1) % 2
                    for j in range(4):
                        n = 4 * idx + j
                        dmag_guard = (B["dma_g"] + 16 * (4 * idx - 4)
                                      if idx >= 2 else 0)
                        sync_etile(s, c2, j, sp,
                                   B["dve_e"] + n + 1,
                                   dmag_guard,
                                   B["dma_e"] + 16 * (n + 1),
                                   B["pe_f"] + 32 * idx if idx >= 1 else 0,
                                   n % 2)
                chk("dma_g", B["dma_g"] + 16 * 60)
                pdone(s)

            @block.gpsimd
            def _(g):
                g.wait_ge(sems["phz"], bt())
                if hp + 1 < 8:
                    for w0, wsrc in ((WQH_Q, Wq), (WQH_K, Wk), (WQH_V, Wv)):
                        for kcc in range(KC):
                            inc(g.dma_start(
                                wqh[(hp + 1) % 2][:, w0 + kcc * 128 :
                                                  w0 + (kcc + 1) * 128],
                                wsrc[kcc * 128 : (kcc + 1) * 128,
                                     (hp + 1) * 128 : (hp + 2) * 128]),
                                "dma_w", 16)
                if dbg and hp == 0:
                    wge(g, "dve_n", B["dve_n"] + 16)
                    inc(g.dma_start(dbg_qa, Qa[0][:, :]), "dma_st", 16)
                    inc(g.dma_start(dbg_ka, Ka[0][:, :]), "dma_st", 16)
                    inc(g.dma_start(dbg_va, Va[0][:, :]), "dma_st", 16)
                    inc(g.dma_start(dbg_e, Esb[0][:, :]), "dma_st", 16)
                    inc(g.dma_start(dbg_gb, GBs[0][:, :]), "dma_st", 16)
                    inc(g.dma_start(dbg_a, Asb[1][:, :]), "dma_st", 16)
                    inc(g.dma_start(dbg_ct, CTs[:, :]), "dma_st", 16)
                    inc(g.dma_start(dbg_sml, sml[:, :]), "dma_st", 16)
                wge(g, "dve_n", B["dve_n"] + 16)
                inc(g.dma_start(
                    bass.AP(ct_dram, hp * 128 * S, [[S, 128], [1, S]]),
                    CTs[:, :]), "dma_ct", 16)
                g.wait_ge(sems["dma_ct"], B["dma_ct"] + 16)
                g.wait_ge(sems["dma_st"], cnt["dma_st"])
                pdone(g)

        end_phase()

    if nhp is not None:
        with nc.Block() as block:
            @block.gpsimd
            def _(g):
                g.wait_ge(sems["phz"], bt())
                for i in range(4):
                    if not prod_only:
                        inc(g.dma_start(out[i * 128 : (i + 1) * 128, :],
                                        CTs[:, i * 1024 : (i + 1) * 1024]),
                            "dma_st", 16)
                    inc(g.dma_start(out[512 + i * 128 : 512 + (i + 1) * 128, :],
                                    Qa[0][:, i * 1024 : (i + 1) * 1024]),
                        "dma_st", 16)
                    inc(g.dma_start(out[1024 + i * 128 : 1024 + (i + 1) * 128, :],
                                    Ka[0][:, i * 1024 : (i + 1) * 1024]),
                        "dma_st", 16)
                g.wait_ge(sems["dma_st"], cnt["dma_st"])
                pdone(g)
            for eng_name in ("tensor", "vector", "scalar", "sync"):
                def _mk(en):
                    def _f(e):
                        e.wait_ge(sems["phz"], bt())
                        pdone(e)
                    return _f
                getattr(block, eng_name)(_mk(eng_name))
        end_phase()
        s1p.close()
        s1.close()
        base_st.close()
        return nc

    s1p.close()
    s1.close()

    tail_st = contextlib.ExitStack()
    mtl = tail_st.enter_context
    hTs = [mtl(nc.sbuf_tensor(f"hTs{i}", [128, D], BF16)) for i in range(2)]
    xob = [mtl(nc.sbuf_tensor(f"xob{i}", [128, D], F32)) for i in range(2)]
    X = mtl(nc.psum_tensor("X", [128, 1536], F32))
    Y = mtl(nc.psum_tensor("Y", [128, 1024], F32))
    stg = [mtl(nc.psum_tensor(f"stg{i}", [128, 512], BF16)) for i in range(2)]
    misc = MISCP

    s2 = contextlib.ExitStack()
    m2 = s2.enter_context
    wo_sb = m2(nc.sbuf_tensor("wo_sb", [128, 8 * 1024], BF16))
    ct_in = [m2(nc.sbuf_tensor(f"ct_in{i}", [128, 1024], BF16)) for i in range(2)]
    dst1 = out if upto == 1 else x1_dram

    with nc.Block() as block:
        B = snap()

        @block.gpsimd
        def _(g):
            g.wait_ge(sems["phz"], bt())
            for c8 in range(8):
                inc(g.dma_start(wo_sb[:, c8 * 1024 : (c8 + 1) * 1024],
                                Wo[c8 * 128 : (c8 + 1) * 128, :]), "dma_w", 16)
            ins = None
            for tt in range(NT):
                if tt >= 2:
                    wge(g, "pe_m", B["pe_m"] + tt - 1)
                inc(g.dma_start(
                    bass.AP(ct_in[tt % 2], 0, [[1024, 128], [128, 8], [1, 128]]),
                    bass.AP(ct_dram, tt * 128,
                            [[S, 128], [128 * S, 8], [1, 128]])), "dma_x", 16)
                if tt >= 2:
                    wge(g, "dve_m", B["dve_m"] + tt - 1)
                inc(g.dma_start(xbuf[tt % 2][:, :],
                                x0[tt * 128 : (tt + 1) * 128, :]), "dma_x", 16)
                if tt >= 1:
                    wge(g, "dve_m", B["dve_m"] + tt)
                    ins = inc(g.dma_start(dst1[(tt - 1) * 128 : tt * 128, :],
                                          xob[(tt - 1) % 2][:, :]), "dma_st", 16)
            wge(g, "dve_m", B["dve_m"] + NT)
            ins = inc(g.dma_start(dst1[(NT - 1) * 128 : NT * 128, :],
                                  xob[(NT - 1) % 2][:, :]), "dma_st", 16)
            g.wait_ge(sems["dma_st"], B["dma_st"] + 16 * NT)
            pdone(g)

        @block.tensor
        def _(t):
            t.wait_ge(sems["phz"], bt())
            wge(t, "dma_w", B["dma_w"] + 128)
            ins = None
            for tt in range(NT):
                wge(t, "dma_x", B["dma_x"] + 32 * tt + 16)
                if tt >= 2:
                    wge(t, "dve_m", B["dve_m"] + tt - 1)
                par = (X if tt % 2 == 0 else Y)[:, 0:1024]
                for half in range(2):
                    for c8 in range(8):
                        t.matmul(par[:, half * 512 : (half + 1) * 512],
                                 ct_in[tt % 2][:, c8 * 128 : (c8 + 1) * 128],
                                 wo_sb[:, c8 * 1024 + half * 512 :
                                       c8 * 1024 + (half + 1) * 512],
                                 start=(c8 == 0), stop=False)
                    ins = t.matmul(par[:, half * 512 : (half + 1) * 512],
                                   ones_bf[0:1, 0:128],
                                   bor[0:1, half * 512 : (half + 1) * 512],
                                   start=False, stop=True)
                inc(ins, "pe_m")
            pdone(t, "pe_m")

        @block.vector
        def _(v):
            v.wait_ge(sems["phz"], bt())
            ins = None
            for tt in range(NT):
                wge(v, "pe_m", B["pe_m"] + tt + 1)
                wge(v, "dma_x", B["dma_x"] + 32 * (tt + 1))
                if tt >= 2:
                    wge(v, "dma_st", B["dma_st"] + 16 * (tt - 1))
                par = (X if tt % 2 == 0 else Y)[:, 0:1024]
                ins = inc(v.tensor_add(xob[tt % 2][:, :], par,
                                       xbuf[tt % 2][:, :]), "dve_m")
            pdone(v, "dve_m")

        @block.scalar
        def _(sc):
            sc.wait_ge(sems["phz"], bt())
            pdone(sc)

        @block.sync
        def _(s):
            s.wait_ge(sems["phz"], bt())
            pdone(s)

    end_phase()
    s2.close()

    if upto == 1:
        tail_st.close()
        base_st.close()
        return nc

    def ffn_phase(src, dst, tag):
        s3 = contextlib.ExitStack()
        m3 = s3.enter_context
        bw_sb = m3(nc.sbuf_tensor(f"bw_sb{tag}", [128, 8 * 1024], BF16))
        gw_sb = m3(nc.sbuf_tensor(f"gw_sb{tag}", [128, 8 * 16], BF16))
        aA_sb = m3(nc.sbuf_tensor(f"aA_sb{tag}", [128, 8 * 128], BF16))
        bB_sb = m3(nc.sbuf_tensor(f"bB_sb{tag}", [128, 1024], BF16))
        e_sb = m3(nc.sbuf_tensor(f"e_sb{tag}", [128, 16], F32))
        tp_sb = m3(nc.sbuf_tensor(f"tp_sb{tag}", [128, 16], F32))
        ew_sb = m3(nc.sbuf_tensor(f"ew_sb{tag}", [128, 8], F32))
        ewx_sb = m3(nc.sbuf_tensor(f"ewx_sb{tag}", [128, 128], F32))
        zw_bf = m3(nc.sbuf_tensor(f"zw_bf{tag}", [128, 128], BF16))
        zwT_sb = m3(nc.sbuf_tensor(f"zwT_sb{tag}", [128, 128], BF16))
        tp2_sb = m3(nc.sbuf_tensor(f"tp2_sb{tag}", [128, 16], F32))
        bbr = m3(nc.sbuf_tensor(f"bbr{tag}", [1, D], BF16))

        PD, PS = {}, {}
        pd = [0]
        def _d(lbl):
            pd[0] += 1
            PD[lbl] = pd[0]
        ps = [0]
        def _s(lbl):
            ps[0] += 1
            PS[lbl] = ps[0]
        _d(("A", 0)); _d(("B", 0))
        for k in range(NT):
            if k + 1 < NT:
                _d(("A", k + 1)); _d(("B", k + 1))
            for r in ("C", "r4", "r5", "r6", "r7", "r8", "r9"):
                _d((r, k))
        _s(("s1", 0))
        for k in range(NT):
            if k + 1 < NT:
                _s(("s1", k + 1))
            _s(("s2", k)); _s(("s3", k))

        with nc.Block() as block:
            B = snap()

            @block.gpsimd
            def _(g):
                g.wait_ge(sems["phz"], bt())
                for c8 in range(8):
                    inc(g.dma_start(bw_sb[:, c8 * 1024 : (c8 + 1) * 1024],
                                    basewT[c8 * 128 : (c8 + 1) * 128, :]),
                        "dma_w", 16)
                    inc(g.dma_start(gw_sb[:, c8 * 16 : c8 * 16 + 10],
                                    gatewT[c8 * 128 : (c8 + 1) * 128, :]),
                        "dma_w", 16)
                    inc(g.dma_start(aA_sb[:, c8 * 128 : (c8 + 1) * 128],
                                    AcatT[c8 * 128 : (c8 + 1) * 128, :]),
                        "dma_w", 16)
                inc(g.dma_start(bB_sb[:, :], BcatT), "dma_w", 16)
                inc(g.dma_start(bbr[:, :], baseb_row), "dma_w", 16)
                ins = None
                for tt in range(NT):
                    if tt >= 2:
                        wge(g, "dve_m", B["dve_m"] + PD[("r9", tt - 2)])
                    inc(g.dma_start(xbuf[tt % 2][:, :],
                                    src[tt * 128 : (tt + 1) * 128, :]),
                        "dma_x", 16)
                    if tt >= 1:
                        wge(g, "dve_m", B["dve_m"] + PD[("r9", tt - 1)])
                        ins = inc(g.dma_start(dst[(tt - 1) * 128 : tt * 128, :],
                                              xob[(tt - 1) % 2][:, :]),
                                  "dma_st", 16)
                wge(g, "dve_m", B["dve_m"] + PD[("r9", NT - 1)])
                inc(g.dma_start(dst[(NT - 1) * 128 : NT * 128, :],
                                xob[(NT - 1) % 2][:, :]), "dma_st", 16)
                g.wait_ge(sems["dma_st"], cnt["dma_st"])
                pdone(g)

            @block.vector
            def _(v):
                v.wait_ge(sems["phz"], bt())

                def emit_stats(k):
                    o = 24 * (k % 2)
                    wge(v, "dma_x", B["dma_x"] + 16 * (k + 1))
                    if k >= 2:
                        wge(v, "act_m", B["act_m"] + PS[("s1", k - 2)])
                    inc(ln_dve_stats(v, xbuf[k % 2][:, :], o), "dve_m")
                    chk("dve_m", B["dve_m"] + PD[("A", k)])

                def emit_apply(k):
                    o = 24 * (k % 2)
                    wge(v, "act_m", B["act_m"] + PS[("s1", k)])
                    if k >= 2:
                        wge(v, "pe_m", B["pe_m"] + 5 * (k - 2) + 1)
                    inc(ln_dve_apply(v, xbuf[k % 2][:, :], hbuf[k % 2][:, :],
                                     o), "dve_m")
                    chk("dve_m", B["dve_m"] + PD[("B", k)])

                emit_stats(0)
                emit_apply(0)
                ins = None
                for tt in range(NT):
                    if tt + 1 < NT:
                        emit_stats(tt + 1)
                        emit_apply(tt + 1)
                    wge(v, "pe_m", B["pe_m"] + 5 * tt + 1)
                    for k8 in range(KC):
                        ins = v.tensor_copy(
                            hTs[tt % 2][:, k8 * 128 : (k8 + 1) * 128],
                            stg[k8 // 4][:, (k8 % 4) * 128 : (k8 % 4 + 1) * 128])
                    inc(ins, "dve_m")
                    chk("dve_m", B["dve_m"] + PD[("C", tt)])
                    wge(v, "pe_m", B["pe_m"] + 5 * tt + 2)
                    inc(v.reduce_max(sml[:, 8:9], misc[:, 0:10],
                                     axis=AX.X, negate=True), "dve_m")
                    chk("dve_m", B["dve_m"] + PD[("r4", tt)])
                    wge(v, "act_m", B["act_m"] + PS[("s2", tt)])
                    selfedge(v, v.reduce_max(sml[:, 9:10], e_sb[:, 0:10],
                                              axis=AX.X))
                    selfedge(v, v.tensor_scalar(tp2_sb[:, 0:10], ones32[:, 0:10],
                                    sml[:, 9:10], None, ALU.mult))
                    selfedge(v, v.tensor_tensor(ewx_sb[:, 0:10], e_sb[:, 0:10],
                                    tp2_sb[:, 0:10], ALU.is_equal))
                    selfedge(v, v.scalar_tensor_tensor(tp_sb[:, 0:10],
                                    ewx_sb[:, 0:10], -2.0,
                                    e_sb[:, 0:10], ALU.mult, ALU.add))
                    selfedge(v, v.reduce_max(sml[:, 10:11], tp_sb[:, 0:10],
                                             axis=AX.X))
                    selfedge(v, v.tensor_scalar(tp2_sb[:, 0:10], ones32[:, 0:10],
                                    sml[:, 10:11], None, ALU.mult))
                    selfedge(v, v.tensor_tensor(ewx_sb[:, 16:26], tp_sb[:, 0:10],
                                    tp2_sb[:, 0:10], ALU.is_equal))
                    selfedge(v, v.tensor_add(tp_sb[:, 0:10], ewx_sb[:, 16:26],
                                             ewx_sb[:, 0:10]))
                    selfedge(v, v.tensor_mul(ew_sb[:, 0:8], tp_sb[:, 0:8],
                                             e_sb[:, 0:8]))
                    selfedge(v, v.reduce_sum(sml[:, 12:13], ew_sb[:, 0:8],
                                             axis=AX.X))
                    selfedge(v, v.tensor_scalar(sml[:, 14:15], sml[:, 12:13],
                                                0.0, None, ALU.is_equal))
                    ins = inc(v.tensor_add(sml[:, 14:15], sml[:, 14:15],
                                           sml[:, 12:13]), "dve_m")
                    chk("dve_m", B["dve_m"] + PD[("r5", tt)])
                    wge(v, "act_m", B["act_m"] + PS[("s3", tt)])
                    selfedge(v, v.tensor_scalar(ew_sb[:, 0:8], ew_sb[:, 0:8],
                                                sml[:, 15:16], None, ALU.mult))
                    for e8 in range(8):
                        ins = v.tensor_scalar(ewx_sb[:, e8 * 16 : (e8 + 1) * 16],
                                              ones32[:, 0:16],
                                              ew_sb[:, e8 : e8 + 1], None,
                                              ALU.mult)
                    inc(ins, "dve_m")
                    chk("dve_m", B["dve_m"] + PD[("r6", tt)])
                    wge(v, "dve_m", cnt["dve_m"])
                    wge(v, "pe_m", B["pe_m"] + 5 * tt + 3)
                    zreg = X[:, 1024:1152]
                    ins = inc(v.tensor_tensor(zw_bf[:, :], zreg,
                                              ewx_sb[:, :], ALU.mult), "dve_m")
                    chk("dve_m", B["dve_m"] + PD[("r7", tt)])
                    wge(v, "pe_m", B["pe_m"] + 5 * tt + 4)
                    ins = inc(v.tensor_copy(zwT_sb[:, :], stg[0][:, 0:128]),
                              "dve_m")
                    chk("dve_m", B["dve_m"] + PD[("r8", tt)])
                    wge(v, "pe_m", B["pe_m"] + 5 * tt + 5)
                    if tt >= 2:
                        wge(v, "dma_st", B["dma_st"] + 16 * (tt - 1))
                    par = (X if tt % 2 == 0 else Y)[:, 0:1024]
                    ins = inc(v.tensor_add(xob[tt % 2][:, :], par,
                                           xbuf[tt % 2][:, :]), "dve_m")
                    chk("dve_m", B["dve_m"] + PD[("r9", tt)])
                pdone(v, "dve_m")

            @block.scalar
            def _(sc):
                sc.wait_ge(sems["phz"], bt())

                def emit_s1(k):
                    o = 24 * (k % 2)
                    wge(sc, "dve_m", B["dve_m"] + PD[("A", k)])
                    inc(ln_act_rstd(sc, o), "act_m")
                    chk("act_m", B["act_m"] + PS[("s1", k)])

                emit_s1(0)
                ins = None
                for tt in range(NT):
                    if tt + 1 < NT:
                        emit_s1(tt + 1)
                    wge(sc, "dve_m", B["dve_m"] + PD[("r4", tt)])
                    ins = inc(sc.activation(e_sb[:, 0:10], misc[:, 0:10],
                                            AF.Exp, bias=sml[:, 8:9]),
                              "act_m")
                    chk("act_m", B["act_m"] + PS[("s2", tt)])
                    wge(sc, "dve_m", B["dve_m"] + PD[("r5", tt)])
                    sc.activation(sml[:, 16:17], sml[:, 14:15], AF.Ln)
                    ins = inc(sc.activation(sml[:, 15:16], sml[:, 16:17],
                                            AF.Exp, scale=-1.0,
                                            bias=sml[:, 7:8]), "act_m")
                    chk("act_m", B["act_m"] + PS[("s3", tt)])
                pdone(sc, "act_m")

            @block.tensor
            def _(t):
                t.wait_ge(sems["phz"], bt())
                wge(t, "dma_w", B["dma_w"] + 26 * 16)
                ins = None
                for tt in range(NT):
                    wge(t, "dve_m", B["dve_m"] + PD[("B", tt)])
                    if tt >= 1:
                        wge(t, "dve_m", B["dve_m"] + PD[("r8", tt - 1)])
                    for k8 in range(KC):
                        ins = tpose(t,
                            stg[k8 // 4][:, (k8 % 4) * 128 : (k8 % 4 + 1) * 128],
                            hbuf[tt % 2][:, k8 * 128 : (k8 + 1) * 128],
                            id_bf[:, :])
                    inc(ins, "pe_m")
                    wge(t, "dve_m", B["dve_m"] + PD[("C", tt)])
                    if tt >= 1:
                        wge(t, "dve_m", B["dve_m"] + PD[("r7", tt - 1)])
                        wge(t, "act_m", B["act_m"] + PS[("s2", tt - 1)])
                    for k8 in range(KC):
                        ins = t.matmul(misc[:, 0:10],
                                       hTs[tt % 2][:, k8 * 128 : (k8 + 1) * 128],
                                       gw_sb[:, k8 * 16 : k8 * 16 + 10],
                                       start=(k8 == 0), stop=(k8 == 7))
                    inc(ins, "pe_m")
                    for k8 in range(KC):
                        ins = t.matmul(X[:, 1024:1152],
                                       hTs[tt % 2][:, k8 * 128 : (k8 + 1) * 128],
                                       aA_sb[:, k8 * 128 : (k8 + 1) * 128],
                                       start=(k8 == 0), stop=(k8 == 7))
                    inc(ins, "pe_m")
                    wge(t, "dve_m", B["dve_m"] + PD[("r7", tt)])
                    ins = inc(tpose(t, stg[0][:, 0:128], zw_bf[:, :],
                                          id_bf[:, :]), "pe_m")
                    wge(t, "dve_m", B["dve_m"] + PD[("r8", tt)])
                    if tt >= 2:
                        wge(t, "dve_m", B["dve_m"] + PD[("r9", tt - 2)])
                    par = (X if tt % 2 == 0 else Y)[:, 0:1024]
                    for half in range(2):
                        sl = par[:, half * 512 : (half + 1) * 512]
                        for k8 in range(KC):
                            t.matmul(sl,
                                     hTs[tt % 2][:, k8 * 128 : (k8 + 1) * 128],
                                     bw_sb[:, k8 * 1024 + half * 512 :
                                           k8 * 1024 + (half + 1) * 512],
                                     start=(k8 == 0), stop=False)
                        t.matmul(sl, ones_bf[0:1, 0:128],
                                 bbr[0:1, half * 512 : (half + 1) * 512],
                                 start=False, stop=False)
                        ins = t.matmul(sl, zwT_sb[:, :],
                                       bB_sb[:, half * 512 : (half + 1) * 512],
                                       start=False, stop=True)
                    inc(ins, "pe_m")
                pdone(t, "pe_m")

            @block.sync
            def _(s):
                s.wait_ge(sems["phz"], bt())
                pdone(s)

        end_phase()
        s3.close()

    ffn_phase(x1_dram, out if upto == 2 else x2_dram, "f1")
    if upto == 2:
        tail_st.close()
        base_st.close()
        return nc

    s4 = contextlib.ExitStack()
    m4 = s4.enter_context
    w2_sb = m4(nc.sbuf_tensor("w2_sb", [128, 16 * 1024], BF16))
    k_sb = m4(nc.sbuf_tensor("k_sb", [128, 32 * 1024], BF16))
    v_sb = m4(nc.sbuf_tensor("v_sb", [128, 32 * 1024], BF16))
    kv_sb = m4(nc.sbuf_tensor("kv_sb", [128, 8 * 1024], BF16))
    qT_sb = m4(nc.sbuf_tensor("qT_sb", [128, 1024], BF16))

    with nc.Block() as block:
        B = snap()

        @block.gpsimd
        def _(g):
            g.wait_ge(sems["phz"], bt())
            for c8 in range(8):
                inc(g.dma_start(w2_sb[:, c8 * 1024 : (c8 + 1) * 1024],
                                w_k2[c8 * 128 : (c8 + 1) * 128, :]), "dma_w", 16)
                inc(g.dma_start(w2_sb[:, 8192 + c8 * 1024 : 8192 + (c8 + 1) * 1024],
                                w_v2[c8 * 128 : (c8 + 1) * 128, :]), "dma_w", 16)
            ins = None
            for tt in range(NT):
                if tt >= 2:
                    wge(g, "dve_m", B["dve_m"] + 5 * (tt - 2) + 2)
                ins = inc(g.dma_start(xbuf[tt % 2][:, :],
                                      x2_dram[tt * 128 : (tt + 1) * 128, :]),
                          "dma_x", 16)
                wge(g, "dve_m", B["dve_m"] + 5 * tt + 3)
                inc(g.dma_start(h2t_dram[tt * 128 : (tt + 1) * 128, :],
                                hTs[tt % 2][:, :]), "dma_st", 16)
            g.wait_ge(sems["dma_st"], B["dma_st"] + 16 * NT)
            pdone(g)

        @block.vector
        def _(v):
            v.wait_ge(sems["phz"], bt())
            ins = None
            for tt in range(NT):
                wge(v, "dma_x", B["dma_x"] + 16 * (tt + 1))
                inc(ln_dve_stats(v, xbuf[tt % 2][:, :]), "dve_m")
                wge(v, "act_m", B["act_m"] + tt + 1)
                if tt >= 2:
                    wge(v, "pe_m", B["pe_m"] + 3 * (tt - 2) + 1)
                inc(ln_dve_apply(v, xbuf[tt % 2][:, :], hbuf[tt % 2][:, :]),
                    "dve_m")
                wge(v, "pe_m", B["pe_m"] + 3 * tt + 1)
                if tt >= 2:
                    wge(v, "dma_st", B["dma_st"] + 16 * (tt - 1))
                for k8 in range(KC):
                    ins = v.tensor_copy(
                        hTs[tt % 2][:, k8 * 128 : (k8 + 1) * 128],
                        stg[k8 // 4][:, (k8 % 4) * 128 : (k8 % 4 + 1) * 128])
                inc(ins, "dve_m")
                wge(v, "pe_m", B["pe_m"] + 3 * tt + 2)
                par = (X if tt % 2 == 0 else Y)[:, 0:1024]
                v.tensor_copy(k_sb[:, tt * 1024 : tt * 1024 + 512], par[:, 0:512])
                ins = v.tensor_copy(k_sb[:, tt * 1024 + 512 : (tt + 1) * 1024],
                                    par[:, 512:1024])
                inc(ins, "dve_m")
                wge(v, "pe_m", B["pe_m"] + 3 * tt + 3)
                anti = (Y if tt % 2 == 0 else X)[:, 0:1024]
                v.tensor_copy(v_sb[:, tt * 1024 : tt * 1024 + 512], anti[:, 0:512])
                ins = v.tensor_copy(v_sb[:, tt * 1024 + 512 : (tt + 1) * 1024],
                                    anti[:, 512:1024])
                inc(ins, "dve_m")
            pdone(v, "dve_m")

        @block.scalar
        def _(sc):
            sc.wait_ge(sems["phz"], bt())
            ins = None
            for tt in range(NT):
                wge(sc, "dve_m", B["dve_m"] + 5 * tt + 1)
                ins = inc(ln_act_rstd(sc), "act_m")
            pdone(sc, "act_m")

        @block.tensor
        def _(t):
            t.wait_ge(sems["phz"], bt())
            wge(t, "dma_w", B["dma_w"] + 16 * 16)
            ins = None
            for tt in range(NT):
                wge(t, "dve_m", B["dve_m"] + 5 * tt + 2)
                if tt >= 1:
                    wge(t, "dve_m", B["dve_m"] + 5 * (tt - 1) + 3)
                for k8 in range(KC):
                    ins = tpose(t, 
                        stg[k8 // 4][:, (k8 % 4) * 128 : (k8 % 4 + 1) * 128],
                        hbuf[tt % 2][:, k8 * 128 : (k8 + 1) * 128], id_bf[:, :])
                inc(ins, "pe_m")
                wge(t, "dve_m", B["dve_m"] + 5 * tt + 3)
                if tt >= 1:
                    wge(t, "dve_m", B["dve_m"] + 5 * (tt - 1) + 5)
                par = (X if tt % 2 == 0 else Y)[:, 0:1024]
                for half in range(2):
                    for k8 in range(KC):
                        ins = t.matmul(par[:, half * 512 : (half + 1) * 512],
                                       hTs[tt % 2][:, k8 * 128 : (k8 + 1) * 128],
                                       w2_sb[:, k8 * 1024 + half * 512 :
                                             k8 * 1024 + (half + 1) * 512],
                                       start=(k8 == 0), stop=(k8 == 7))
                inc(ins, "pe_m")
                if tt >= 1:
                    wge(t, "dve_m", B["dve_m"] + 5 * (tt - 1) + 4)
                anti = (Y if tt % 2 == 0 else X)[:, 0:1024]
                for half in range(2):
                    for k8 in range(KC):
                        ins = t.matmul(anti[:, half * 512 : (half + 1) * 512],
                                       hTs[tt % 2][:, k8 * 128 : (k8 + 1) * 128],
                                       w2_sb[:, 8192 + k8 * 1024 + half * 512 :
                                             8192 + k8 * 1024 + (half + 1) * 512],
                                       start=(k8 == 0), stop=(k8 == 7))
                inc(ins, "pe_m")
            pdone(t, "pe_m")

        @block.sync
        def _(s):
            s.wait_ge(sems["phz"], bt())
            pdone(s)

    end_phase()

    with nc.Block() as block:
        B = snap()

        @block.tensor
        def _(t):
            t.wait_ge(sems["phz"], bt())
            ins = None
            for r in range(16):
                ac, bh = r // 2, r % 2
                if r >= 2:
                    wge(t, "dve_m", B["dve_m"] + r - 1)
                par = (X if r % 2 == 0 else Y)[:, 0:512]
                for tt in range(NT):
                    ins = t.matmul(par,
                                   k_sb[:, tt * 1024 + ac * 128 :
                                        tt * 1024 + (ac + 1) * 128],
                                   v_sb[:, tt * 1024 + bh * 512 :
                                        tt * 1024 + (bh + 1) * 512],
                                   start=(tt == 0), stop=(tt == NT - 1))
                inc(ins, "pe_m")
            pdone(t, "pe_m")

        @block.vector
        def _(v):
            v.wait_ge(sems["phz"], bt())
            ins = None
            for r in range(16):
                ac, bh = r // 2, r % 2
                wge(v, "pe_m", B["pe_m"] + r + 1)
                par = (X if r % 2 == 0 else Y)[:, 0:512]
                ins = inc(v.tensor_copy(
                    kv_sb[:, ac * 1024 + bh * 512 : ac * 1024 + (bh + 1) * 512],
                    par), "dve_m")
            pdone(v, "dve_m")

        @block.scalar
        def _(sc):
            sc.wait_ge(sems["phz"], bt())
            pdone(sc)

        @block.gpsimd
        def _(g):
            g.wait_ge(sems["phz"], bt())
            for c8 in range(8):
                inc(g.dma_start(w2_sb[:, c8 * 1024 : (c8 + 1) * 1024],
                                w_q2[c8 * 128 : (c8 + 1) * 128, :]), "dma_w", 16)
            g.wait_ge(sems["dma_w"], B["dma_w"] + 8 * 16)
            pdone(g)

        @block.sync
        def _(s):
            s.wait_ge(sems["phz"], bt())
            pdone(s)

    end_phase()

    dst3 = out if upto == 3 else x3_dram
    with nc.Block() as block:
        B = snap()

        @block.gpsimd
        def _(g):
            g.wait_ge(sems["phz"], bt())
            ins = None
            for tt in range(NT):
                if tt >= 2:
                    wge(g, "pe_m", B["pe_m"] + 2 * (tt - 2) + 1)
                inc(g.dma_start(hTs[tt % 2][:, :],
                                h2t_dram[tt * 128 : (tt + 1) * 128, :]),
                    "dma_x", 16)
                if tt >= 2:
                    wge(g, "dve_m", B["dve_m"] + 2 * (tt - 2) + 2)
                inc(g.dma_start(xbuf[tt % 2][:, :],
                                x2_dram[tt * 128 : (tt + 1) * 128, :]),
                    "dma_x", 16)
                if tt >= 1:
                    wge(g, "dve_m", B["dve_m"] + 2 * (tt - 1) + 2)
                    ins = inc(g.dma_start(dst3[(tt - 1) * 128 : tt * 128, :],
                                          xob[(tt - 1) % 2][:, :]), "dma_st", 16)
            wge(g, "dve_m", B["dve_m"] + 2 * (NT - 1) + 2)
            inc(g.dma_start(dst3[(NT - 1) * 128 : NT * 128, :],
                            xob[(NT - 1) % 2][:, :]), "dma_st", 16)
            g.wait_ge(sems["dma_st"], B["dma_st"] + 16 * NT)
            pdone(g)

        @block.vector
        def _(v):
            v.wait_ge(sems["phz"], bt())
            ins = None
            for tt in range(NT):
                wge(v, "pe_m", B["pe_m"] + 2 * tt + 1)
                if tt >= 1:
                    wge(v, "pe_m", B["pe_m"] + 2 * (tt - 1) + 2)
                par = (X if tt % 2 == 0 else Y)[:, 0:1024]
                v.tensor_copy(qT_sb[:, 0:512], par[:, 0:512])
                ins = v.tensor_copy(qT_sb[:, 512:1024], par[:, 512:1024])
                inc(ins, "dve_m")
                wge(v, "pe_m", B["pe_m"] + 2 * tt + 2)
                wge(v, "dma_x", B["dma_x"] + 32 * (tt + 1))
                if tt >= 2:
                    wge(v, "dma_st", B["dma_st"] + 16 * (tt - 1))
                anti = (Y if tt % 2 == 0 else X)[:, 0:1024]
                ins = inc(v.scalar_tensor_tensor(xob[tt % 2][:, :], anti, 32.0,
                                                 xbuf[tt % 2][:, :], ALU.mult,
                                                 ALU.add), "dve_m")
            pdone(v, "dve_m")

        @block.scalar
        def _(sc):
            sc.wait_ge(sems["phz"], bt())
            pdone(sc)

        @block.tensor
        def _(t):
            t.wait_ge(sems["phz"], bt())
            ins = None
            for tt in range(NT):
                wge(t, "dma_x", B["dma_x"] + 32 * tt + 16)
                if tt >= 2:
                    wge(t, "dve_m", B["dve_m"] + 2 * (tt - 2) + 1)
                if tt >= 1:
                    wge(t, "dve_m", B["dve_m"] + 2 * (tt - 1) + 2)
                par = (X if tt % 2 == 0 else Y)[:, 0:1024]
                for at in range(8):
                    for k8 in range(KC):
                        ins = t.matmul(par[:, at * 128 : (at + 1) * 128],
                                       w2_sb[:, k8 * 1024 + at * 128 :
                                             k8 * 1024 + (at + 1) * 128],
                                       hTs[tt % 2][:, k8 * 128 : (k8 + 1) * 128],
                                       start=(k8 == 0), stop=(k8 == 7))
                inc(ins, "pe_m")
                wge(t, "dve_m", B["dve_m"] + 2 * tt + 1)
                if tt >= 2:
                    wge(t, "dve_m", B["dve_m"] + 2 * (tt - 2) + 2)
                anti = (Y if tt % 2 == 0 else X)[:, 0:1024]
                for half in range(2):
                    for ac in range(8):
                        ins = t.matmul(anti[:, half * 512 : (half + 1) * 512],
                                       qT_sb[:, ac * 128 : (ac + 1) * 128],
                                       kv_sb[:, ac * 1024 + half * 512 :
                                             ac * 1024 + (half + 1) * 512],
                                       start=(ac == 0), stop=(ac == 7))
                inc(ins, "pe_m")
            pdone(t, "pe_m")

        @block.sync
        def _(s):
            s.wait_ge(sems["phz"], bt())
            pdone(s)

    end_phase()

    if upto == 3:
        s4.close()
        tail_st.close()
        base_st.close()
        return nc

    s4.close()

    ffn_phase(x3_dram, out, "f2")

    tail_st.close()
    base_st.close()
    return nc




def kernel(L0_attn_a_q, L0_attn_a_Wq, L0_attn_a_bq, L0_attn_a_Wk, L0_attn_a_bk,
           L0_attn_a_Wv, L0_attn_a_bv, L0_attn_a_Wo, L0_attn_a_bo,
           L0_attn_a_rel_table, L0_attn_b_w_q, L0_attn_b_w_k, L0_attn_b_w_v,
           L0_ffn_base_w, L0_ffn_base_b, L0_ffn_gate_w, L0_ffn_lora_A,
           L0_ffn_lora_B, upto=4, dbg=False):
    f = np.float32
    x = np.ascontiguousarray(np.asarray(L0_attn_a_q, dtype=f))
    Wq = np.ascontiguousarray(np.asarray(L0_attn_a_Wq, dtype=f))
    Wk = np.ascontiguousarray(np.asarray(L0_attn_a_Wk, dtype=f))
    Wv = np.ascontiguousarray(np.asarray(L0_attn_a_Wv, dtype=f))
    Wo = np.ascontiguousarray(np.asarray(L0_attn_a_Wo, dtype=f))
    bq = np.asarray(L0_attn_a_bq, dtype=f)
    bk = np.asarray(L0_attn_a_bk, dtype=f)
    bv = np.asarray(L0_attn_a_bv, dtype=f)
    bo = np.asarray(L0_attn_a_bo, dtype=f)
    rel = np.asarray(L0_attn_a_rel_table, dtype=f)
    wq2 = np.ascontiguousarray(np.asarray(L0_attn_b_w_q, dtype=f))
    wk2 = np.ascontiguousarray(np.asarray(L0_attn_b_w_k, dtype=f))
    wv2 = np.ascontiguousarray(np.asarray(L0_attn_b_w_v, dtype=f))
    base_w = np.asarray(L0_ffn_base_w, dtype=f)
    base_b = np.asarray(L0_ffn_base_b, dtype=f)
    gate_w = np.asarray(L0_ffn_gate_w, dtype=f)
    lora_A = np.asarray(L0_ffn_lora_A, dtype=f)
    lora_B = np.asarray(L0_ffn_lora_B, dtype=f)

    relT = np.ascontiguousarray(rel.T)
    relT2 = np.ascontiguousarray(np.concatenate([relT, relT], axis=0))
    common = {
        "Wq": Wq, "Wk": Wk, "Wv": Wv, "Wo": Wo,
        "bq_pp": np.ascontiguousarray(bq.reshape(8, 128).T),
        "bk_pp": np.ascontiguousarray(bk.reshape(8, 128).T),
        "bv_row": np.ascontiguousarray(bv[None, :]),
        "bo_row": np.ascontiguousarray(bo[None, :]),
        "relT2": relT2,
        "w_q2": wq2, "w_k2": wk2, "w_v2": wv2,
        "basewT": np.ascontiguousarray(base_w.T),
        "baseb_row": np.ascontiguousarray(base_b[None, :]),
        "gatewT": np.ascontiguousarray(gate_w.T),
        "AcatT": np.ascontiguousarray(lora_A.reshape(128, D).T),
        "BcatT": np.ascontiguousarray(lora_B.transpose(0, 2, 1).reshape(128, D)),
        "ident_in": np.eye(128, dtype=f),
    }
    nc = build(upto, dbg=dbg)
    in_maps = [dict(common, x0=np.ascontiguousarray(x[b])) for b in range(8)]
    res = run_bass_kernel_spmd(nc, in_maps, core_ids=list(range(8)))
    if dbg:
        return res.results
    return np.stack([res.results[b]["out"] for b in range(8)], axis=0)


# version 13
# speedup vs baseline: 1.0160x; 1.0160x over previous
import contextlib
import numpy as np
import concourse.bass as bass
import concourse.mybir as mybir
from concourse.bass_utils import run_bass_kernel_spmd

F32 = mybir.dt.float32
BF16 = mybir.dt.bfloat16
AF = mybir.ActivationFunctionType
ALU = mybir.AluOpType
AX = mybir.AxisListType

S, D = 4096, 1024
NT = 32
KC = 8
EWMAX = 1664


def _qgeom(c):
    lo = max(0, 4 * c - 4)
    hi = min(31, 4 * c + 7)
    Wb = 128 * (hi - lo + 1)
    EW = Wb + 128
    tiles = []
    for j in range(4):
        qt = 4 * c + j
        i0 = 128 * qt
        t_min = 128 * lo - i0 + 385
        Lp = max(0, -t_min)
        t_lo = max(0, t_min)
        mid = min(1024 - t_lo, EW - Lp)
        Rp = EW - Lp - mid
        tiles.append((qt, i0, Lp, t_lo, mid, Rp))
    return lo, hi, Wb, EW, tiles


QG = [_qgeom(c) for c in range(8)]


def build(upto=4, dbg=False, nhp=None, prod_only=False, pstage=9):
    nc = bass.Bass("TRN2", target_bir_lowering=False)
    dt = nc.dram_tensor

    x0 = dt("x0", [S, D], F32, kind="ExternalInput").ap()
    Wq = dt("Wq", [D, D], F32, kind="ExternalInput").ap()
    Wk = dt("Wk", [D, D], F32, kind="ExternalInput").ap()
    Wv = dt("Wv", [D, D], F32, kind="ExternalInput").ap()
    Wo = dt("Wo", [D, D], F32, kind="ExternalInput").ap()
    bq_pp = dt("bq_pp", [128, 8], F32, kind="ExternalInput").ap()
    bk_pp = dt("bk_pp", [128, 8], F32, kind="ExternalInput").ap()
    bv_row = dt("bv_row", [1, D], F32, kind="ExternalInput").ap()
    bo_row = dt("bo_row", [1, D], F32, kind="ExternalInput").ap()
    relT2 = dt("relT2", [128, 1025], F32, kind="ExternalInput").ap()
    w_q2 = dt("w_q2", [D, D], F32, kind="ExternalInput").ap()
    w_k2 = dt("w_k2", [D, D], F32, kind="ExternalInput").ap()
    w_v2 = dt("w_v2", [D, D], F32, kind="ExternalInput").ap()
    basewT = dt("basewT", [D, D], F32, kind="ExternalInput").ap()
    baseb_row = dt("baseb_row", [1, D], F32, kind="ExternalInput").ap()
    gatewT = dt("gatewT", [D, 10], F32, kind="ExternalInput").ap()
    AcatT = dt("AcatT", [D, 128], F32, kind="ExternalInput").ap()
    BcatT = dt("BcatT", [128, D], F32, kind="ExternalInput").ap()
    ident_in = dt("ident_in", [128, 128], F32, kind="ExternalInput").ap()
    out = dt("out", [S, D], F32, kind="ExternalOutput").ap()
    if dbg:
        dbg_qa = dt("dbg_qa", [128, S], BF16, kind="ExternalOutput").ap()
        dbg_ka = dt("dbg_ka", [128, S], BF16, kind="ExternalOutput").ap()
        dbg_va = dt("dbg_va", [128, 2080], BF16, kind="ExternalOutput").ap()
        dbg_e = dt("dbg_e", [128, EWMAX], BF16, kind="ExternalOutput").ap()
        dbg_gb = dt("dbg_gb", [128, 6144], BF16, kind="ExternalOutput").ap()
        dbg_a = dt("dbg_a", [128, 512], BF16, kind="ExternalOutput").ap()
        dbg_ct = dt("dbg_ct", [128, S], BF16, kind="ExternalOutput").ap()
        dbg_sml = dt("dbg_sml", [128, 64], F32, kind="ExternalOutput").ap()

    e_dram = dt("e_dram", [8 * 128 * EWMAX], BF16, kind="Internal")
    ct_dram = dt("ct_dram", [8 * 128 * S], BF16, kind="Internal")
    x1_dram = dt("x1_dram", [S, D], F32, kind="Internal").ap()
    x2_dram = dt("x2_dram", [S, D], F32, kind="Internal").ap()
    x3_dram = dt("x3_dram", [S, D], F32, kind="Internal").ap()
    h2t_dram = dt("h2t_dram", [S, D], BF16, kind="Internal").ap()

    base_st = contextlib.ExitStack()
    mk = base_st.enter_context

    id_bf = mk(nc.sbuf_tensor("id_bf", [128, 128], BF16))
    ones_bf = mk(nc.sbuf_tensor("ones_bf", [128, 512], BF16))
    ones32 = mk(nc.sbuf_tensor("ones32", [128, 16], F32))
    sml = mk(nc.sbuf_tensor("sml", [128, 64], F32))
    bor = mk(nc.sbuf_tensor("bor", [1, D], BF16))
    xbuf = [mk(nc.sbuf_tensor(f"xbuf{i}", [128, D], F32)) for i in range(2)]
    hbuf = [mk(nc.sbuf_tensor(f"hbuf{i}", [128, D], BF16)) for i in range(2)]
    scrb = mk(nc.sbuf_tensor("scrb", [128, D], BF16))

    MISCP = mk(nc.psum_tensor("MISCP", [128, 512], F32))

    sem_names = [
        "phz", "dma_w", "dma_x", "dma_s", "dma_st", "dma_e", "dma_g",
        "dma_ct", "dma_q", "dve_n",
        "pe_m", "pe_p", "pe_f", "pe_av", "pe_x",
        "act_m", "act_e", "act_l",
        "dve_m", "dve_e", "dve_c", "dve_x",
    ]
    sems = {n: mk(nc.semaphore(n)) for n in sem_names}
    cnt = {n: 0 for n in sem_names}
    nphase = [0]

    def inc(instr, name, k=1):
        instr.then_inc(sems[name], k)
        cnt[name] += k
        return instr

    def chk(name, v):
        assert cnt[name] == v, (name, cnt[name], v)

    def wge(eng, name, v):
        if v > 0:
            eng.wait_ge(sems[name], v)

    def snap():
        return dict(cnt)

    def selfedge(eng, instr, name="dve_x"):
        inc(instr, name)
        eng.wait_ge(sems[name], cnt[name])
        return instr

    def pdone(eng, sem=None):
        if sem is not None:
            eng.wait_ge(sems[sem], cnt[sem])
        eng.sem_inc(sems["phz"], 1)

    def end_phase():
        nphase[0] += 1
        cnt["phz"] += 5

    def bt():
        return 5 * nphase[0]

    def ln_dve_stats(v, xa, o=0):
        v.tensor_scalar(scrb[:, :], xa, 1.0, 0.0, ALU.mult, ALU.add,
                        accum_out=sml[:, o : o + 1])
        selfedge(v, v.scalar_tensor_tensor(scrb[:, :], xa, 1.0, xa, ALU.mult,
                                           ALU.mult,
                                           accum_out=sml[:, o + 1 : o + 2]))
        v.tensor_scalar(sml[:, o + 2 : o + 3], sml[:, o : o + 1], -1.0 / D,
                        None, ALU.mult)
        selfedge(v, v.tensor_scalar(sml[:, o + 3 : o + 4],
                                    sml[:, o + 1 : o + 2], 1.0 / D, None,
                                    ALU.mult))
        selfedge(v, v.tensor_mul(sml[:, o + 5 : o + 6], sml[:, o + 2 : o + 3],
                                 sml[:, o + 2 : o + 3]))
        return v.tensor_sub(sml[:, o + 3 : o + 4], sml[:, o + 3 : o + 4],
                            sml[:, o + 5 : o + 6])

    def ln_act_rstd(sc, o=0):
        sc.activation(sml[:, o + 4 : o + 5], sml[:, o + 3 : o + 4], AF.Ln,
                      bias=sml[:, 6:7])
        return sc.activation(sml[:, o + 4 : o + 5], sml[:, o + 4 : o + 5],
                             AF.Exp, scale=-0.5)

    def ln_dve_apply(v, xa, h_out, o=0):
        return v.tensor_scalar(h_out, xa, sml[:, o + 2 : o + 3],
                               sml[:, o + 4 : o + 5], ALU.add, ALU.mult)

    def tpose(t, out_ap, in_ap, ident):
        return t.matmul(out_ap, in_ap, ident, start=True, stop=True,
                        is_transpose=True, skip_group_check=True)

    s1 = contextlib.ExitStack()
    m1 = s1.enter_context
    id_f32 = m1(nc.sbuf_tensor("id_f32", [128, 128], F32))
    TT2 = m1(nc.sbuf_tensor("TT2", [128, 1025], BF16))
    bqc = m1(nc.sbuf_tensor("bqc", [128, 8], F32))
    bkc = m1(nc.sbuf_tensor("bkc", [128, 8], F32))
    bvr = m1(nc.sbuf_tensor("bvr", [1, D], BF16))
    maxT = m1(nc.sbuf_tensor("maxT", [1, 1], F32))
    TdT0 = m1(nc.sbuf_tensor("TdT0", [64, 2], BF16))
    tsq = m1(nc.sbuf_tensor("tsq", [64, 1025], BF16))

    wqh = [m1(nc.sbuf_tensor(f"wqh{i}", [128, 3072], BF16)) for i in range(2)]
    WQH_Q, WQH_K, WQH_V = 0, 1024, 2048
    h1T = m1(nc.sbuf_tensor("h1T", [128, KC * S], BF16))
    Qa = [m1(nc.sbuf_tensor(f"Qa{i}", [128, S], BF16)) for i in range(2)]
    Ka = [m1(nc.sbuf_tensor(f"Ka{i}", [128, S], BF16)) for i in range(2)]
    Va = [m1(nc.sbuf_tensor(f"Va{i}", [128, 32 * 65], BF16)) for i in range(2)]
    QKstg = [m1(nc.sbuf_tensor(f"QKstg{i}", [128, 512], BF16)) for i in range(8)]
    rowstg = m1(nc.sbuf_tensor("rowstg", [32, 128], BF16))
    nmall = [m1(nc.sbuf_tensor(f"nmall{i}", [128, 32], F32)) for i in range(2)]
    lnqq = m1(nc.sbuf_tensor("lnqq", [128, 32], F32))
    mallb = m1(nc.sbuf_tensor("mallb", [128, 32], F32))
    lncbf = m1(nc.sbuf_tensor("lncbf", [1, 2], BF16))
    Asb = [m1(nc.sbuf_tensor(f"Asb{i}", [128, 512], BF16)) for i in range(3)]
    lrow = [m1(nc.sbuf_tensor(f"lrow{i}", [1, 512], BF16)) for i in range(2)]
    Esb = [m1(nc.sbuf_tensor(f"Esb{i}", [128, EWMAX], BF16)) for i in range(2)]
    GBs = [m1(nc.sbuf_tensor(f"GBs{i}", [128, 4 * 1536], BF16)) for i in range(2)]
    pads = m1(nc.sbuf_tensor("pads", [128, 2], F32))
    rlt = m1(nc.sbuf_tensor("rlt", [1, 512], F32))
    rl_sb = m1(nc.sbuf_tensor("rl_sb", [1, 512], BF16))
    rlb_sb = m1(nc.sbuf_tensor("rlb_sb", [128, 512], BF16))
    CTs = m1(nc.sbuf_tensor("CTs", [128, S], BF16))

    with nc.Block() as block:
        B = snap()

        @block.gpsimd
        def _(g):
            inc(g.dma_start(id_bf[:, :], ident_in), "dma_w", 16)
            inc(g.dma_start(TT2[:, :], relT2), "dma_w", 16)
            inc(g.dma_start(bvr[:, :], bv_row), "dma_w", 16)
            inc(g.dma_start(bor[:, :], bo_row), "dma_w", 16)
            g.wait_ge(sems["dma_w"], B["dma_w"] + 64)
            g.wait_ge(sems["dma_s"], B["dma_s"] + 48)
            pdone(g)

        @block.sync
        def _(s):
            inc(s.dma_start(id_f32[:, :], ident_in), "dma_s", 16)
            inc(s.dma_start(bqc[:, :], bq_pp), "dma_s", 16)
            inc(s.dma_start(bkc[:, :], bk_pp), "dma_s", 16)
            s.wait_ge(sems["dma_w"], B["dma_w"] + 64)
            s.wait_ge(sems["dma_s"], B["dma_s"] + 48)
            pdone(s)

        @block.vector
        def _(v):
            v.memset(ones_bf[:, :], 1.0)
            v.memset(ones32[:, :], 1.0)
            v.memset(sml[:, :], 0.0)
            v.memset(sml[:, 6:7], 1e-5)
            v.memset(sml[:, 7:8], 0.6931471805599453)
            v.memset(Ka[0][64:128, :], 0.0)
            v.memset(Ka[1][64:128, :], 0.0)
            v.memset(Qa[0][64:128, :], 0.0)
            v.memset(Qa[1][64:128, :], 0.0)
            selfedge(v, v.memset(Ka[0][64:66, :], 1.0))
            v.memset(Ka[1][64:66, :], 1.0)
            v.memset(Va[0][:, :], 1.0)
            v.memset(Va[1][:, :], 1.0)
            v.wait_ge(sems["dma_w"], B["dma_w"] + 64)
            v.wait_ge(sems["dma_s"], B["dma_s"] + 48)
            v.tensor_copy(TdT0[0:64, 0:1], TT2[0:64, 0:1])
            ins = inc(v.tensor_sub(TdT0[0:64, 1:2], TT2[0:64, 1024:1025],
                                   TT2[0:64, 0:1]), "dve_m")
            pdone(v, "dve_m")

        @block.tensor
        def _(t):
            pdone(t)

        @block.scalar
        def _(sc):
            pdone(sc)

    end_phase()

    with nc.Block() as block:
        B = snap()

        @block.gpsimd
        def _(g):
            g.wait_ge(sems["phz"], bt())
            for w0, wsrc in ((WQH_Q, Wq), (WQH_K, Wk), (WQH_V, Wv)):
                for kcc in range(KC):
                    inc(g.dma_start(
                        wqh[0][:, w0 + kcc * 128 : w0 + (kcc + 1) * 128],
                        wsrc[kcc * 128 : (kcc + 1) * 128, 0:128]), "dma_w", 16)
            g.wait_ge(sems["dma_w"], B["dma_w"] + 24 * 16)
            pdone(g)

        @block.vector
        def _(v):
            v.wait_ge(sems["phz"], bt())
            inc(v.tensor_mul(tsq[0:64, 0:1025], TT2[0:64, :], TT2[0:64, :]),
                "dve_m")
            wge(v, "pe_m", B["pe_m"] + 1)
            inc(v.tensor_reduce(sml[:, 8:9], MISCP[:, 0:8], axis=AX.X,
                                op=ALU.max), "dve_m")
            wge(v, "pe_m", B["pe_m"] + 2)
            selfedge(v, v.tensor_reduce(sml[0:1, 9:10], MISCP[0:1, 256:384],
                                        axis=AX.X, op=ALU.max))
            inc(v.tensor_tensor(sml[0:1, 9:10], sml[0:1, 9:10],
                                MISCP[0:1, 8:9], ALU.max), "dve_m")
            pdone(v, "dve_m")

        @block.tensor
        def _(t):
            t.wait_ge(sems["phz"], bt())
            wge(t, "dve_m", B["dve_m"] + 1)
            ins = None
            for t9 in range(9):
                w = min(128, 1025 - t9 * 128)
                ins = t.matmul(MISCP[0:w, t9 : t9 + 1],
                               tsq[0:64, t9 * 128 : t9 * 128 + w],
                               ones_bf[0:64, 0:1], start=True, stop=True)
            inc(ins, "pe_m")
            wge(t, "dve_m", B["dve_m"] + 2)
            ins = inc(tpose(t, MISCP[0:1, 256:384], sml[:, 8:9],
                                  id_f32[:, :]), "pe_m")
            pdone(t, "pe_m")

        @block.scalar
        def _(sc):
            sc.wait_ge(sems["phz"], bt())
            wge(sc, "dve_m", B["dve_m"] + 3)
            sc.activation(sml[0:1, 10:11], sml[0:1, 9:10], AF.Ln)
            ins = inc(sc.activation(maxT[:, :], sml[0:1, 10:11], AF.Exp,
                                    scale=0.5), "act_m")
            pdone(sc, "act_m")

        @block.sync
        def _(s):
            s.wait_ge(sems["phz"], bt())
            pdone(s)

    end_phase()

    s1a = contextlib.ExitStack()
    stgbf = [s1a.enter_context(nc.psum_tensor(f"stgbf{i}", [128, 512], BF16))
             for i in range(2)]
    with nc.Block() as block:
        B = snap()

        @block.gpsimd
        def _(g):
            g.wait_ge(sems["phz"], bt())
            ins = None
            for tt in range(NT):
                if tt >= 2:
                    wge(g, "dve_m", B["dve_m"] + 3 * (tt - 2) + 2)
                ins = inc(g.dma_start(xbuf[tt % 2][:, :],
                                      x0[tt * 128 : (tt + 1) * 128, :]),
                          "dma_x", 16)
            pdone(g)

        @block.vector
        def _(v):
            v.wait_ge(sems["phz"], bt())
            ins = None
            for tt in range(NT):
                wge(v, "dma_x", B["dma_x"] + 16 * (tt + 1))
                inc(ln_dve_stats(v, xbuf[tt % 2][:, :]), "dve_m")
                wge(v, "act_m", B["act_m"] + tt + 1)
                if tt >= 2:
                    wge(v, "pe_m", B["pe_m"] + tt - 1)
                inc(ln_dve_apply(v, xbuf[tt % 2][:, :], hbuf[tt % 2][:, :]),
                    "dve_m")
                wge(v, "pe_m", B["pe_m"] + tt + 1)
                for k8 in range(KC):
                    ins = v.tensor_copy(
                        h1T[:, k8 * S + tt * 128 : k8 * S + (tt + 1) * 128],
                        stgbf[k8 // 4][:, (k8 % 4) * 128 : (k8 % 4 + 1) * 128])
                inc(ins, "dve_m")
            pdone(v, "dve_m")

        @block.scalar
        def _(sc):
            sc.wait_ge(sems["phz"], bt())
            ins = None
            for tt in range(NT):
                wge(sc, "dve_m", B["dve_m"] + 3 * tt + 1)
                ins = inc(ln_act_rstd(sc), "act_m")
            pdone(sc, "act_m")

        @block.tensor
        def _(t):
            t.wait_ge(sems["phz"], bt())
            ins = None
            for tt in range(NT):
                wge(t, "dve_m", B["dve_m"] + 3 * tt + 2)
                if tt >= 1:
                    wge(t, "dve_m", B["dve_m"] + 3 * (tt - 1) + 3)
                for k8 in range(KC):
                    ins = tpose(t, 
                        stgbf[k8 // 4][:, (k8 % 4) * 128 : (k8 % 4 + 1) * 128],
                        hbuf[tt % 2][:, k8 * 128 : (k8 + 1) * 128], id_bf[:, :])
                inc(ins, "pe_m")
            pdone(t, "pe_m")

        @block.sync
        def _(s):
            s.wait_ge(sems["phz"], bt())
            pdone(s)

    end_phase()
    s1a.close()

    s1p = contextlib.ExitStack()
    mp = s1p.enter_context
    PA = mp(nc.psum_tensor("PA", [128, 512], F32))
    PB = mp(nc.psum_tensor("PB", [128, 512], F32))
    CTX = mp(nc.psum_tensor("CTX", [128, 512], F32))
    Pp = mp(nc.psum_tensor("Pp", [128, 1536], F32))
    PC = mp(nc.psum_tensor("PC", [128, 512], F32))

    def pe_ptile(t, h_, c_, j, dve_e_wait):
        lo, hi, Wb, EW, tiles = QG[c_]
        qt, i0, Lp, t_lo, mid, Rp = tiles[j]
        qa = Qa[h_]
        wge(t, "dve_e", dve_e_wait)
        t.matmul(Pp[:, 0:512], qa[0:64, i0 : i0 + 128], TT2[0:64, 0:512],
                 start=True, stop=True)
        t.matmul(Pp[:, 512:1024], qa[0:64, i0 : i0 + 128], TT2[0:64, 512:1024],
                 start=True, stop=True)
        ins = t.matmul(Pp[:, 1024:1025], qa[0:64, i0 : i0 + 128],
                       TT2[0:64, 1024:1025], start=True, stop=True)
        inc(ins, "pe_p")

    def dve_etile(v, h_, c_, j, pe_p_wait, dma_e_wait, slot):
        lo, hi, Wb, EW, tiles = QG[c_]
        qt, i0, Lp, t_lo, mid, Rp = tiles[j]
        Eb = Esb[slot]
        wge(v, "pe_p", pe_p_wait)
        wge(v, "dma_e", dma_e_wait)
        selfedge(v, v.tensor_scalar(pads[:, 0:1], Pp[:, 0:1],
                                    nmall[h_][:, qt : qt + 1], None, ALU.add))
        selfedge(v, v.tensor_scalar(pads[:, 1:2], Pp[:, 1024:1025],
                                    nmall[h_][:, qt : qt + 1], None, ALU.add))
        ins = v.tensor_scalar(Eb[:, Lp : Lp + mid], Pp[:, t_lo : t_lo + mid],
                              nmall[h_][:, qt : qt + 1], None, ALU.add)
        off = 0
        while off < Lp:
            w = min(512, Lp - off)
            ins = v.tensor_scalar(Eb[:, off : off + w], ones_bf[:, 0:w],
                                  pads[:, 0:1], None, ALU.mult)
            off += w
        off = Lp + mid
        while off < EW:
            w = min(512, EW - off)
            ins = v.tensor_scalar(Eb[:, off : off + w], ones_bf[:, 0:w],
                                  pads[:, 1:2], None, ALU.mult)
            off += w
        inc(ins, "dve_e")

    def sync_etile(s, c_, j, sp, dve_e_wait, dmag_guard, dmae_wait, pef_guard,
                   slot):
        lo, hi, Wb, EW, tiles = QG[c_]
        slot_off = (sp * 4 + j) * 128 * EWMAX
        wge(s, "dve_e", dve_e_wait)
        wge(s, "dma_g", dmag_guard)
        inc(s.dma_start(bass.AP(e_dram, slot_off, [[EWMAX, 128], [1, EW]]),
                        Esb[slot][:, 0:EW]), "dma_e", 16)
        wge(s, "dma_e", dmae_wait)
        wge(s, "pe_f", pef_guard)
        inc(s.dma_start(GBs[sp][:, j * 1536 : j * 1536 + Wb],
                        bass.AP(e_dram, slot_off + 127,
                                [[EWMAX - 1, 128], [1, Wb]])), "dma_g", 16)

    for hp in range(8 if nhp is None else nhp):
        with nc.Block() as block:
            B = snap()
            wsl = hp % 2
            dma_w_rdy = cnt["dma_w"]

            @block.tensor
            def _(t):
                t.wait_ge(sems["phz"], bt())
                wge(t, "dma_w", dma_w_rdy)
                for p in range(16):
                    c = p % 8
                    w0 = WQH_K if p < 8 else WQH_Q
                    if p >= 2:
                        wge(t, "dve_m", B["dve_m"] + 1 + p)
                    bank = (PA if p % 2 == 0 else PB)[:, 0:512]
                    ins = None
                    for kcc in range(KC):
                        ins = t.matmul(
                            bank,
                            wqh[wsl][:, w0 + kcc * 128 : w0 + (kcc + 1) * 128],
                            h1T[:, kcc * S + c * 512 : kcc * S + (c + 1) * 512],
                            start=(kcc == 0), stop=(kcc == KC - 1))
                    inc(ins, "pe_m")
                for h in range(2 if pstage >= 2 else 0):
                    for ch in range(KC):
                        wge(t, "dve_m", B["dve_m"] + 19 + 10 * h + ch)
                        ins = None
                        for i4 in range(4):
                            qt = 4 * ch + i4
                            ins = t.matmul(
                                MISCP[:, 32 * h + qt : 32 * h + qt + 1],
                                Asb[ch % 2][0:64, i4 * 128 : (i4 + 1) * 128],
                                ones_bf[0:64, 0:1], start=True, stop=True,
                                skip_group_check=True)
                        inc(ins, "pe_m")
                    wge(t, "dve_m", B["dve_m"] + 27 + 10 * h)
                    inc(tpose(t, MISCP[0:1, 256:384], sml[:, 8 + h : 9 + h],
                                    id_f32[:, :]), "pe_m")
                if pstage >= 2:
                    chk("pe_m", B["pe_m"] + 34)
                for h in range(2 if pstage >= 2 else 0):
                    wge(t, "act_m", B["act_m"] + 3 + h)
                    inc(t.matmul(MISCP[:, 100 + h : 101 + h],
                                 ones_bf[0:1, 0:128], lncbf[0:1, h : h + 1],
                                 start=True, stop=True,
                                 skip_group_check=True), "pe_m")
                for h in range(2 if pstage >= 3 else 0):
                    for ch in range(KC):
                        wge(t, "dve_m", B["dve_m"] + 43 + 8 * h + ch)
                        if ch == 0:
                            wge(t, "dve_m", B["dve_m"] + 27 + 10 * h)
                        ins = None
                        for i4 in range(4):
                            qt = 4 * ch + i4
                            ins = t.matmul(
                                MISCP[:, 32 * h + qt : 32 * h + qt + 1],
                                Asb[ch % 2][0:64, i4 * 128 : (i4 + 1) * 128],
                                ones_bf[0:64, 0:1], start=True, stop=True,
                                skip_group_check=True)
                        inc(ins, "pe_m")
                if pstage >= 3:
                    chk("pe_m", B["pe_m"] + 52)
                for h in range(2 if pstage >= 3 else 0):
                    wge(t, "dve_m", B["dve_m"] + 60 + 3 * h)
                    wge(t, "act_m", B["act_m"] + 6)
                    inc(tpose(t, MISCP[0:32, 256:384], nmall[h][:, 0:32],
                                    id_f32[:, :]), "pe_m")
                for h in range(2 if pstage >= 4 else 0):
                    for c in range(8):
                        pidx = 8 * h + c
                        if pidx == 0:
                            wge(t, "act_m", B["act_m"] + 6)
                            wge(t, "dve_m", B["dve_m"] + 64)
                        else:
                            wge(t, "dve_m", B["dve_m"] + 64 + pidx)
                        inc(t.matmul(MISCP[0:2, 0:512], TdT0[0:64, 0:2],
                                     Qa[h][0:64, c * 512 : (c + 1) * 512],
                                     start=True, stop=True,
                                     skip_group_check=True), "pe_m")
                if pstage >= 4:
                    chk("pe_m", B["pe_m"] + 70)
                for kt in range(32 if pstage >= 5 else 0):
                    if kt >= 2:
                        wge(t, "dve_m", B["dve_m"] + 81 + kt - 2)
                    sub = (PA if kt % 2 == 0 else PB)[:, 0:128]
                    for kcc in range(KC):
                        t.matmul(sub,
                                 h1T[:, kcc * S + kt * 128 : kcc * S + (kt + 1) * 128],
                                 wqh[wsl][:, WQH_V + kcc * 128 : WQH_V + (kcc + 1) * 128],
                                 start=(kcc == 0), stop=False,
                                 skip_group_check=True)
                    ins = t.matmul(sub, ones_bf[0:1, 0:128],
                                   bvr[0:1, hp * 128 : (hp + 1) * 128],
                                   start=False, stop=True,
                                   skip_group_check=True)
                    inc(ins, "pe_m")
                if pstage >= 5:
                    chk("pe_m", B["pe_m"] + 102)
                for j in range(4 if pstage >= 6 else 0):
                    pe_ptile(t, 0, 0, j, B["dve_e"] + j)
                if pstage >= 6:
                    chk("pe_p", B["pe_p"] + 4)
                pdone(t, "pe_p" if pstage >= 6 else "pe_m")

            @block.vector
            def _(v):
                v.wait_ge(sems["phz"], bt())
                inc(v.memset(Qa[0][64:66, :], 0.0), "dve_m")
                inc(v.memset(Qa[1][64:66, :], 0.0), "dve_m")
                for p in range(16):
                    c = p % 8
                    wge(v, "pe_m", B["pe_m"] + 1 + p)
                    if p >= 8:
                        wge(v, "dma_q", B["dma_q"] + 16 * (c + 1))
                    bank = (PA if p % 2 == 0 else PB)[:, 0:512]
                    dst = Ka[0] if p < 8 else Qa[0]
                    bc = bkc if p < 8 else bqc
                    scl = 0.125 if p < 8 else None
                    op1 = ALU.mult if p < 8 else ALU.add
                    if p < 8:
                        v.tensor_scalar(dst[0:64, c * 512 : (c + 1) * 512],
                                        bank[0:64, :], bc[0:64, hp : hp + 1],
                                        scl, ALU.add, op1)
                        ins = v.tensor_scalar(QKstg[c][64:128, :],
                                              bank[64:128, :],
                                              bc[64:128, hp : hp + 1],
                                              scl, ALU.add, op1)
                    else:
                        v.tensor_scalar(dst[0:64, c * 512 : (c + 1) * 512],
                                        bank[0:64, :], bc[0:64, hp : hp + 1],
                                        None, ALU.add)
                        ins = v.tensor_scalar(QKstg[c][64:128, :],
                                              bank[64:128, :],
                                              bc[64:128, hp : hp + 1],
                                              None, ALU.add)
                    inc(ins, "dve_m")
                chk("dve_m", B["dve_m"] + 18)
                for h in range(2 if pstage >= 2 else 0):
                    src = Ka[h]
                    for ch in range(KC):
                        n = 8 * h + ch
                        if n >= 2:
                            pm = 17 + (n - 2) if n - 2 < 8 else 26 + (n - 2 - 8)
                            wge(v, "pe_m", B["pe_m"] + pm)
                        if h == 1:
                            wge(v, "dma_q", B["dma_q"] + 16 * (ch + 1))
                        inc(v.tensor_mul(Asb[ch % 2][0:64, :],
                                         src[0:64, ch * 512 : (ch + 1) * 512],
                                         src[0:64, ch * 512 : (ch + 1) * 512]),
                            "dve_m")
                    wge(v, "pe_m", B["pe_m"] + 24 + 9 * h)
                    inc(v.tensor_reduce(sml[:, 8 + h : 9 + h],
                                        MISCP[:, 32 * h : 32 * h + 32],
                                        axis=AX.X, op=ALU.max), "dve_m")
                    wge(v, "pe_m", B["pe_m"] + 25 + 9 * h)
                    inc(v.tensor_reduce(sml[0:1, 10 + h : 11 + h],
                                        MISCP[0:1, 256:384], axis=AX.X,
                                        op=ALU.max), "dve_m")
                if pstage >= 2:
                    chk("dve_m", B["dve_m"] + 38)
                for h in range(2 if pstage >= 2 else 0):
                    wge(v, "act_m", B["act_m"] + 1 + h)
                    inc(v.tensor_add(sml[0:1, 16 + h : 17 + h],
                                     sml[0:1, 14 + h : 15 + h], maxT[:, :]),
                        "dve_m")
                for h in range(2 if pstage >= 2 else 0):
                    wge(v, "pe_m", B["pe_m"] + 35 + h)
                    inc(v.tensor_copy(sml[:, 22 + h : 23 + h],
                                      MISCP[:, 100 + h : 101 + h]), "dve_m")
                for h in range(2 if pstage >= 3 else 0):
                    src = Qa[h]
                    for ch in range(KC):
                        n = 16 + 8 * h + ch
                        pm = 26 + (n - 2 - 8) if n - 2 < 16 else 37 + (n - 2 - 16)
                        wge(v, "pe_m", B["pe_m"] + pm)
                        if h == 1:
                            wge(v, "dma_q", B["dma_q"] + 128 + 16 * (ch + 1))
                        inc(v.tensor_mul(Asb[ch % 2][0:64, :],
                                         src[0:64, ch * 512 : (ch + 1) * 512],
                                         src[0:64, ch * 512 : (ch + 1) * 512]),
                            "dve_m")
                if pstage >= 3:
                    chk("dve_m", B["dve_m"] + 58)
                for h in range(2 if pstage >= 3 else 0):
                    wge(v, "act_m", B["act_m"] + 5 + h)
                    inc(v.tensor_scalar(nmall[h][:, :], mallb[:, :], -1.0,
                                        None, ALU.mult), "dve_m")
                    v.wait_ge(sems["dve_m"], cnt["dve_m"])
                    inc(v.tensor_scalar(nmall[h][:, :], nmall[h][:, :], 1.0,
                                        None, ALU.mult), "dve_m")
                    wge(v, "pe_m", B["pe_m"] + 53 + h)
                    if h == 1:
                        wge(v, "dma_q", B["dma_q"] + 272)
                    inc(v.tensor_copy(rowstg[0:32, :], MISCP[0:32, 256:384]),
                        "dve_m")
                if pstage >= 3:
                    chk("dve_m", B["dve_m"] + 64)
                for h in range(2 if pstage >= 4 else 0):
                    for c in range(8):
                        wge(v, "pe_m", B["pe_m"] + 55 + 8 * h + c)
                        wge(v, "dma_q", B["dma_q"] + 272 + 16 * h)
                        inc(v.tensor_tensor(
                            Qa[h][64:66, c * 512 : (c + 1) * 512],
                            MISCP[0:2, 0:512],
                            Qa[h][64:66, c * 512 : (c + 1) * 512],
                            ALU.add), "dve_m")
                if pstage >= 4:
                    chk("dve_m", B["dve_m"] + 80)
                for kt in range(32 if pstage >= 5 else 0):
                    wge(v, "pe_m", B["pe_m"] + 71 + kt)
                    sub = (PA if kt % 2 == 0 else PB)[:, 0:128]
                    v.tensor_copy(Va[0][:, kt * 65 : kt * 65 + 64], sub[:, 0:64])
                    ins = v.tensor_copy(Va[1][:, kt * 65 : kt * 65 + 64],
                                        sub[:, 64:128])
                    inc(ins, "dve_m")
                if pstage >= 5:
                    chk("dve_m", B["dve_m"] + 112)
                for j in range(4 if pstage >= 6 else 0):
                    dve_etile(v, 0, 0, j, B["pe_p"] + j + 1,
                              B["dma_e"] + 16 * (j - 1), j % 2)
                if pstage >= 6:
                    chk("dve_e", B["dve_e"] + 4)
                pdone(v, "dve_e" if pstage >= 6 else "dve_m")

            @block.scalar
            def _(sc):
                sc.wait_ge(sems["phz"], bt())
                for h in range(2 if pstage >= 2 else 0):
                    wge(sc, "dve_m", B["dve_m"] + 28 + 10 * h)
                    sc.activation(sml[0:1, 12 + h : 13 + h],
                                  sml[0:1, 10 + h : 11 + h], AF.Ln)
                    inc(sc.activation(sml[0:1, 14 + h : 15 + h],
                                      sml[0:1, 12 + h : 13 + h], AF.Exp,
                                      scale=0.5), "act_m")
                for h in range(2 if pstage >= 2 else 0):
                    wge(sc, "dve_m", B["dve_m"] + 39 + h)
                    sc.activation(sml[0:1, 18 + h : 19 + h],
                                  sml[0:1, 16 + h : 17 + h], AF.Ln)
                    inc(sc.activation(lncbf[0:1, h : h + 1],
                                      sml[0:1, 18 + h : 19 + h], AF.Copy),
                        "act_m")
                for h in range(2 if pstage >= 3 else 0):
                    wge(sc, "pe_m", B["pe_m"] + 52)
                    wge(sc, "dve_m", B["dve_m"] + 41 + h)
                    if h == 1:
                        wge(sc, "dve_m", B["dve_m"] + 59)
                    sc.activation(lnqq[:, :], MISCP[:, 32 * h : 32 * h + 32],
                                  AF.Ln)
                    inc(sc.activation(mallb[:, :], lnqq[:, :], AF.Exp,
                                      scale=0.5, bias=sml[:, 22 + h : 23 + h]),
                        "act_m")
                pdone(sc, "act_m")

            @block.gpsimd
            def _(g):
                g.wait_ge(sems["phz"], bt())
                for p in range(16):
                    c = p % 8
                    dst = Ka[1] if p < 8 else Qa[1]
                    wge(g, "dve_m", B["dve_m"] + 3 + p)
                    inc(g.dma_start(dst[0:64, c * 512 : (c + 1) * 512],
                                    QKstg[c][64:128, :]), "dma_q", 16)
                for h in range(2 if pstage >= 3 else 0):
                    wge(g, "dve_m", B["dve_m"] + 61 + 3 * h)
                    inc(g.dma_start(Qa[h][64:65, :], rowstg[0:32, :]),
                        "dma_q", 16)
                if pstage >= 3:
                    chk("dma_q", B["dma_q"] + 288)
                g.wait_ge(sems["dma_q"], cnt["dma_q"])
                pdone(g)

            @block.sync
            def _(s):
                s.wait_ge(sems["phz"], bt())
                for j in range(4 if pstage >= 6 else 0):
                    sync_etile(s, 0, j, 0,
                               B["dve_e"] + j + 1, 0,
                               B["dma_e"] + 16 * (j + 1), 0, j % 2)
                s.wait_ge(sems["dma_g"], cnt["dma_g"])
                pdone(s)

        end_phase()

        if prod_only:
            continue
        with nc.Block() as block:
            B = snap()

            def _ext(c, kt):
                lo, hi, Wb, EW, tiles = QG[c]
                return 65 if kt < lo else (66 if kt > hi else 64)

            def _gbts(t, idx, kt, bank, lo):
                ins = None
                for j in range(4):
                    ins = t.matmul(
                        bank[:, j * 128 : (j + 1) * 128],
                        GBs[idx % 2][:, j * 1536 + (kt - lo) * 128 :
                                     j * 1536 + (kt - lo + 1) * 128],
                        id_bf[:, :], start=False, stop=(j == 3),
                        skip_group_check=True)
                return ins

            def fills_pe(t, idx, kt, pair=False):
                h, c = idx // 8, idx % 8
                lo, hi, Wb, EW, tiles = QG[c]
                n_f = 2 if pair else 1
                g = 32 * idx + kt
                wge(t, "act_e", B["act_e"] + g + n_f - 3)
                if kt <= lo < kt + n_f and idx >= 1:
                    wge(t, "dma_g", B["dma_g"] + 64 * idx)
                bank = (PA, PB, PC)[g % 3][:, 0:512]
                ext = _ext(c, kt)
                ins = t.matmul(bank, Ka[h][0:ext, kt * 128 : (kt + 1) * 128],
                               Qa[h][0:ext, c * 512 : (c + 1) * 512],
                               start=True, stop=(ext != 64),
                               skip_group_check=True)
                if pair:
                    bank2 = (PA, PB, PC)[(g + 1) % 3][:, 0:512]
                    t.matmul(bank2,
                             Ka[h][0:64, (kt + 1) * 128 : (kt + 2) * 128],
                             Qa[h][0:64, c * 512 : (c + 1) * 512],
                             start=True, stop=False,
                             skip_group_check=True)
                if ext == 64:
                    ins = _gbts(t, idx, kt, bank, lo)
                inc(ins, "pe_f")
                chk("pe_f", B["pe_f"] + 32 * idx + kt + 1)
                if pair:
                    ins = _gbts(t, idx, kt + 1, bank2, lo)
                    inc(ins, "pe_f")
                    chk("pe_f", B["pe_f"] + 32 * idx + kt + 2)

            @block.tensor
            def _(t):
                t.wait_ge(sems["phz"], bt())
                for idx in range(16):
                    h, c = idx // 8, idx % 8
                    lo0, hi0 = QG[idx % 8][0], QG[idx % 8][1]
                    if lo0 <= 0 and 1 <= hi0:
                        fills_pe(t, idx, 0, pair=True)
                    else:
                        fills_pe(t, idx, 0)
                        fills_pe(t, idx, 1)
                    for kt in range(34):
                        if kt >= 2:
                            ktA = kt - 2
                            gA = 32 * idx + ktA
                            wge(t, "act_e", B["act_e"] + gA + 1)
                            if ktA == 0:
                                wge(t, "dve_c", B["dve_c"] + idx)
                            inc(t.matmul(
                                CTX[0:65, 0:512],
                                Va[h][:, ktA * 65 : ktA * 65 + 65],
                                Asb[gA % 3][:, :],
                                start=(ktA == 0), stop=(ktA == 31)), "pe_av")
                        if kt % 2 == 0 and kt + 3 <= 31:
                            kf = kt + 2
                            if _ext(c, kf) == 64 and _ext(c, kf + 1) == 64:
                                fills_pe(t, idx, kf, pair=True)
                            else:
                                fills_pe(t, idx, kf)
                                fills_pe(t, idx, kf + 1)
                        if kt == 4 and idx >= 1:
                            hN = (idx - 1) // 8
                            wge(t, "act_l", B["act_l"] + idx)
                            wge(t, "dve_n", B["dve_n"] + idx - 1)
                            inc(t.matmul(MISCP[64 * hN : 64 * hN + 64, 0:512],
                                         ones_bf[0:1, 0:64],
                                         rl_sb[0:1, 0:512],
                                         start=True, stop=True), "pe_x")
                        if idx + 1 <= 15 and kt in (2, 6, 10, 14):
                            j = {2: 0, 6: 1, 10: 2, 14: 3}[kt]
                            h2, c2 = (idx + 1) // 8, (idx + 1) % 8
                            pe_ptile(t, h2, c2, j, B["dve_e"] + 4 * idx + j)
                    chk("pe_av", B["pe_av"] + 32 * (idx + 1))
                wge(t, "act_l", B["act_l"] + 16)
                wge(t, "dve_n", B["dve_n"] + 15)
                inc(t.matmul(MISCP[64:128, 0:512], ones_bf[0:1, 0:64],
                             rl_sb[0:1, 0:512], start=True, stop=True), "pe_x")
                chk("pe_x", B["pe_x"] + 16)
                pdone(t, "pe_x")

            @block.scalar
            def _(sc):
                sc.wait_ge(sems["phz"], bt())
                for idx in range(16):
                    for kt in range(32):
                        g = 32 * idx + kt
                        wge(sc, "pe_f", B["pe_f"] + g + 1)
                        wge(sc, "pe_av", B["pe_av"] + g - 2)
                        bank = (PA, PB, PC)[g % 3][:, 0:512]
                        inc(sc.activation(Asb[g % 3][:, :], bank, AF.Exp),
                            "act_e")
                        if kt == 1 and idx >= 1:
                            wge(sc, "dve_c", B["dve_c"] + idx)
                            sc.activation(rlt[0:1, :],
                                          lrow[(idx - 1) % 2][0:1, :], AF.Ln)
                            inc(sc.activation(rl_sb[0:1, :], rlt[0:1, :],
                                              AF.Exp, scale=-1.0), "act_l")
                wge(sc, "dve_c", B["dve_c"] + 16)
                sc.activation(rlt[0:1, :], lrow[1][0:1, :], AF.Ln)
                inc(sc.activation(rl_sb[0:1, :], rlt[0:1, :], AF.Exp,
                                  scale=-1.0), "act_l")
                chk("act_e", B["act_e"] + 512)
                chk("act_l", B["act_l"] + 16)
                pdone(sc, "act_l")

            def ct_slice(k):
                hP, cP = k // 8, k % 8
                return CTs[64 * hP : 64 * hP + 64, cP * 512 : (cP + 1) * 512]

            @block.vector
            def _(v):
                v.wait_ge(sems["phz"], bt())
                for idx in range(16):
                    if idx >= 1:
                        wge(v, "pe_av", B["pe_av"] + 32 * idx)
                        wge(v, "act_l", B["act_l"] + idx - 2)
                        v.tensor_copy(ct_slice(idx - 1), CTX[0:64, 0:512])
                        inc(v.tensor_copy(lrow[(idx - 1) % 2][0:1, :],
                                          CTX[64:65, 0:512]), "dve_c")
                    if idx >= 2:
                        hN = (idx - 2) // 8
                        wge(v, "pe_x", B["pe_x"] + idx - 1)
                        selfedge(v, v.tensor_copy(
                            rlb_sb[64 * hN : 64 * hN + 64, :],
                            MISCP[64 * hN : 64 * hN + 64, 0:512]))
                        inc(v.tensor_tensor(ct_slice(idx - 2), ct_slice(idx - 2),
                                            rlb_sb[64 * hN : 64 * hN + 64, :],
                                            ALU.mult), "dve_n")
                    if idx + 1 <= 15:
                        h2, c2 = (idx + 1) // 8, (idx + 1) % 8
                        for j in range(4):
                            n = 4 * idx + j
                            dve_etile(v, h2, c2, j,
                                      B["pe_p"] + n + 1,
                                      B["dma_e"] + 16 * (n - 1), n % 2)
                wge(v, "pe_av", B["pe_av"] + 512)
                wge(v, "act_l", B["act_l"] + 14)
                v.tensor_copy(ct_slice(15), CTX[0:64, 0:512])
                inc(v.tensor_copy(lrow[1][0:1, :], CTX[64:65, 0:512]), "dve_c")
                for k in (14, 15):
                    wge(v, "pe_x", B["pe_x"] + k + 1)
                    selfedge(v, v.tensor_copy(rlb_sb[64:128, :],
                                              MISCP[64:128, 0:512]))
                    inc(v.tensor_tensor(ct_slice(k), ct_slice(k),
                                        rlb_sb[64:128, :], ALU.mult), "dve_n")
                chk("dve_c", B["dve_c"] + 16)
                chk("dve_n", B["dve_n"] + 16)
                chk("dve_e", B["dve_e"] + 60)
                pdone(v, "dve_n")

            @block.sync
            def _(s):
                s.wait_ge(sems["phz"], bt())
                for idx in range(15):
                    c2 = (idx + 1) % 8
                    sp = (idx + 1) % 2
                    for j in range(4):
                        n = 4 * idx + j
                        dmag_guard = (B["dma_g"] + 16 * (4 * idx - 4)
                                      if idx >= 2 else 0)
                        sync_etile(s, c2, j, sp,
                                   B["dve_e"] + n + 1,
                                   dmag_guard,
                                   B["dma_e"] + 16 * (n + 1),
                                   B["pe_f"] + 32 * idx if idx >= 1 else 0,
                                   n % 2)
                chk("dma_g", B["dma_g"] + 16 * 60)
                pdone(s)

            @block.gpsimd
            def _(g):
                g.wait_ge(sems["phz"], bt())
                if hp + 1 < 8:
                    for w0, wsrc in ((WQH_Q, Wq), (WQH_K, Wk), (WQH_V, Wv)):
                        for kcc in range(KC):
                            inc(g.dma_start(
                                wqh[(hp + 1) % 2][:, w0 + kcc * 128 :
                                                  w0 + (kcc + 1) * 128],
                                wsrc[kcc * 128 : (kcc + 1) * 128,
                                     (hp + 1) * 128 : (hp + 2) * 128]),
                                "dma_w", 16)
                if dbg and hp == 0:
                    wge(g, "dve_n", B["dve_n"] + 16)
                    inc(g.dma_start(dbg_qa, Qa[0][:, :]), "dma_st", 16)
                    inc(g.dma_start(dbg_ka, Ka[0][:, :]), "dma_st", 16)
                    inc(g.dma_start(dbg_va, Va[0][:, :]), "dma_st", 16)
                    inc(g.dma_start(dbg_e, Esb[0][:, :]), "dma_st", 16)
                    inc(g.dma_start(dbg_gb, GBs[0][:, :]), "dma_st", 16)
                    inc(g.dma_start(dbg_a, Asb[1][:, :]), "dma_st", 16)
                    inc(g.dma_start(dbg_ct, CTs[:, :]), "dma_st", 16)
                    inc(g.dma_start(dbg_sml, sml[:, :]), "dma_st", 16)
                wge(g, "dve_n", B["dve_n"] + 16)
                inc(g.dma_start(
                    bass.AP(ct_dram, hp * 128 * S, [[S, 128], [1, S]]),
                    CTs[:, :]), "dma_ct", 16)
                g.wait_ge(sems["dma_ct"], B["dma_ct"] + 16)
                g.wait_ge(sems["dma_st"], cnt["dma_st"])
                pdone(g)

        end_phase()

    if nhp is not None:
        with nc.Block() as block:
            @block.gpsimd
            def _(g):
                g.wait_ge(sems["phz"], bt())
                for i in range(4):
                    if not prod_only:
                        inc(g.dma_start(out[i * 128 : (i + 1) * 128, :],
                                        CTs[:, i * 1024 : (i + 1) * 1024]),
                            "dma_st", 16)
                    inc(g.dma_start(out[512 + i * 128 : 512 + (i + 1) * 128, :],
                                    Qa[0][:, i * 1024 : (i + 1) * 1024]),
                        "dma_st", 16)
                    inc(g.dma_start(out[1024 + i * 128 : 1024 + (i + 1) * 128, :],
                                    Ka[0][:, i * 1024 : (i + 1) * 1024]),
                        "dma_st", 16)
                g.wait_ge(sems["dma_st"], cnt["dma_st"])
                pdone(g)
            for eng_name in ("tensor", "vector", "scalar", "sync"):
                def _mk(en):
                    def _f(e):
                        e.wait_ge(sems["phz"], bt())
                        pdone(e)
                    return _f
                getattr(block, eng_name)(_mk(eng_name))
        end_phase()
        s1p.close()
        s1.close()
        base_st.close()
        return nc

    s1p.close()
    s1.close()

    tail_st = contextlib.ExitStack()
    mtl = tail_st.enter_context
    hTs = [mtl(nc.sbuf_tensor(f"hTs{i}", [128, D], BF16)) for i in range(2)]
    xob = [mtl(nc.sbuf_tensor(f"xob{i}", [128, D], F32)) for i in range(2)]
    X = mtl(nc.psum_tensor("X", [128, 1536], F32))
    Y = mtl(nc.psum_tensor("Y", [128, 1024], F32))
    stg = [mtl(nc.psum_tensor(f"stg{i}", [128, 512], BF16)) for i in range(2)]
    misc = MISCP

    s2 = contextlib.ExitStack()
    m2 = s2.enter_context
    wo_sb = m2(nc.sbuf_tensor("wo_sb", [128, 8 * 1024], BF16))
    ct_in = [m2(nc.sbuf_tensor(f"ct_in{i}", [128, 1024], BF16)) for i in range(2)]
    dst1 = out if upto == 1 else x1_dram

    with nc.Block() as block:
        B = snap()

        @block.gpsimd
        def _(g):
            g.wait_ge(sems["phz"], bt())
            for c8 in range(8):
                inc(g.dma_start(wo_sb[:, c8 * 1024 : (c8 + 1) * 1024],
                                Wo[c8 * 128 : (c8 + 1) * 128, :]), "dma_w", 16)
            ins = None
            for tt in range(NT):
                if tt >= 2:
                    wge(g, "pe_m", B["pe_m"] + tt - 1)
                inc(g.dma_start(
                    bass.AP(ct_in[tt % 2], 0, [[1024, 128], [128, 8], [1, 128]]),
                    bass.AP(ct_dram, tt * 128,
                            [[S, 128], [128 * S, 8], [1, 128]])), "dma_x", 16)
                if tt >= 2:
                    wge(g, "dve_m", B["dve_m"] + tt - 1)
                inc(g.dma_start(xbuf[tt % 2][:, :],
                                x0[tt * 128 : (tt + 1) * 128, :]), "dma_x", 16)
                if tt >= 1:
                    wge(g, "dve_m", B["dve_m"] + tt)
                    ins = inc(g.dma_start(dst1[(tt - 1) * 128 : tt * 128, :],
                                          xob[(tt - 1) % 2][:, :]), "dma_st", 16)
            wge(g, "dve_m", B["dve_m"] + NT)
            ins = inc(g.dma_start(dst1[(NT - 1) * 128 : NT * 128, :],
                                  xob[(NT - 1) % 2][:, :]), "dma_st", 16)
            g.wait_ge(sems["dma_st"], B["dma_st"] + 16 * NT)
            pdone(g)

        @block.tensor
        def _(t):
            t.wait_ge(sems["phz"], bt())
            wge(t, "dma_w", B["dma_w"] + 128)
            ins = None
            for tt in range(NT):
                wge(t, "dma_x", B["dma_x"] + 32 * tt + 16)
                if tt >= 2:
                    wge(t, "dve_m", B["dve_m"] + tt - 1)
                par = (X if tt % 2 == 0 else Y)[:, 0:1024]
                for half in range(2):
                    for c8 in range(8):
                        t.matmul(par[:, half * 512 : (half + 1) * 512],
                                 ct_in[tt % 2][:, c8 * 128 : (c8 + 1) * 128],
                                 wo_sb[:, c8 * 1024 + half * 512 :
                                       c8 * 1024 + (half + 1) * 512],
                                 start=(c8 == 0), stop=False)
                    ins = t.matmul(par[:, half * 512 : (half + 1) * 512],
                                   ones_bf[0:1, 0:128],
                                   bor[0:1, half * 512 : (half + 1) * 512],
                                   start=False, stop=True)
                inc(ins, "pe_m")
            pdone(t, "pe_m")

        @block.vector
        def _(v):
            v.wait_ge(sems["phz"], bt())
            ins = None
            for tt in range(NT):
                wge(v, "pe_m", B["pe_m"] + tt + 1)
                wge(v, "dma_x", B["dma_x"] + 32 * (tt + 1))
                if tt >= 2:
                    wge(v, "dma_st", B["dma_st"] + 16 * (tt - 1))
                par = (X if tt % 2 == 0 else Y)[:, 0:1024]
                ins = inc(v.tensor_add(xob[tt % 2][:, :], par,
                                       xbuf[tt % 2][:, :]), "dve_m")
            pdone(v, "dve_m")

        @block.scalar
        def _(sc):
            sc.wait_ge(sems["phz"], bt())
            pdone(sc)

        @block.sync
        def _(s):
            s.wait_ge(sems["phz"], bt())
            pdone(s)

    end_phase()
    s2.close()

    if upto == 1:
        tail_st.close()
        base_st.close()
        return nc

    def ffn_phase(src, dst, tag):
        s3 = contextlib.ExitStack()
        m3 = s3.enter_context
        bw_sb = m3(nc.sbuf_tensor(f"bw_sb{tag}", [128, 8 * 1024], BF16))
        gw_sb = m3(nc.sbuf_tensor(f"gw_sb{tag}", [128, 8 * 16], BF16))
        aA_sb = m3(nc.sbuf_tensor(f"aA_sb{tag}", [128, 8 * 128], BF16))
        bB_sb = m3(nc.sbuf_tensor(f"bB_sb{tag}", [128, 1024], BF16))
        e_sb = m3(nc.sbuf_tensor(f"e_sb{tag}", [128, 16], F32))
        tp_sb = m3(nc.sbuf_tensor(f"tp_sb{tag}", [128, 16], F32))
        ew_sb = m3(nc.sbuf_tensor(f"ew_sb{tag}", [128, 8], F32))
        ewx_sb = m3(nc.sbuf_tensor(f"ewx_sb{tag}", [128, 128], F32))
        zw_bf = m3(nc.sbuf_tensor(f"zw_bf{tag}", [128, 128], BF16))
        zwT_sb = m3(nc.sbuf_tensor(f"zwT_sb{tag}", [128, 128], BF16))
        tp2_sb = m3(nc.sbuf_tensor(f"tp2_sb{tag}", [128, 16], F32))
        bbr = m3(nc.sbuf_tensor(f"bbr{tag}", [1, D], BF16))

        PD, PS = {}, {}
        pd = [0]
        def _d(lbl):
            pd[0] += 1
            PD[lbl] = pd[0]
        ps = [0]
        def _s(lbl):
            ps[0] += 1
            PS[lbl] = ps[0]
        _d(("A", 0)); _d(("B", 0))
        for k in range(NT):
            _d(("C", k))
            if k + 1 < NT:
                _d(("A", k + 1)); _d(("B", k + 1))
            for r in ("r4", "r5", "r6", "r7", "r8", "r9"):
                _d((r, k))
        _s(("s1", 0))
        for k in range(NT):
            if k + 1 < NT:
                _s(("s1", k + 1))
            _s(("s2", k)); _s(("s3", k))

        with nc.Block() as block:
            B = snap()

            @block.gpsimd
            def _(g):
                g.wait_ge(sems["phz"], bt())
                for c8 in range(8):
                    inc(g.dma_start(bw_sb[:, c8 * 1024 : (c8 + 1) * 1024],
                                    basewT[c8 * 128 : (c8 + 1) * 128, :]),
                        "dma_w", 16)
                    inc(g.dma_start(gw_sb[:, c8 * 16 : c8 * 16 + 10],
                                    gatewT[c8 * 128 : (c8 + 1) * 128, :]),
                        "dma_w", 16)
                    inc(g.dma_start(aA_sb[:, c8 * 128 : (c8 + 1) * 128],
                                    AcatT[c8 * 128 : (c8 + 1) * 128, :]),
                        "dma_w", 16)
                inc(g.dma_start(bB_sb[:, :], BcatT), "dma_w", 16)
                inc(g.dma_start(bbr[:, :], baseb_row), "dma_w", 16)
                ins = None
                for tt in range(NT):
                    if tt >= 2:
                        wge(g, "dve_m", B["dve_m"] + PD[("r9", tt - 2)])
                    inc(g.dma_start(xbuf[tt % 2][:, :],
                                    src[tt * 128 : (tt + 1) * 128, :]),
                        "dma_x", 16)
                    if tt >= 1:
                        wge(g, "dve_m", B["dve_m"] + PD[("r9", tt - 1)])
                        ins = inc(g.dma_start(dst[(tt - 1) * 128 : tt * 128, :],
                                              xob[(tt - 1) % 2][:, :]),
                                  "dma_st", 16)
                wge(g, "dve_m", B["dve_m"] + PD[("r9", NT - 1)])
                inc(g.dma_start(dst[(NT - 1) * 128 : NT * 128, :],
                                xob[(NT - 1) % 2][:, :]), "dma_st", 16)
                g.wait_ge(sems["dma_st"], cnt["dma_st"])
                pdone(g)

            @block.vector
            def _(v):
                v.wait_ge(sems["phz"], bt())

                def emit_stats(k):
                    o = 24 * (k % 2)
                    wge(v, "dma_x", B["dma_x"] + 16 * (k + 1))
                    if k >= 2:
                        wge(v, "act_m", B["act_m"] + PS[("s1", k - 2)])
                    inc(ln_dve_stats(v, xbuf[k % 2][:, :], o), "dve_m")
                    chk("dve_m", B["dve_m"] + PD[("A", k)])

                def emit_apply(k):
                    o = 24 * (k % 2)
                    wge(v, "act_m", B["act_m"] + PS[("s1", k)])
                    if k >= 2:
                        wge(v, "pe_m", B["pe_m"] + 5 * (k - 2) + 1)
                    inc(ln_dve_apply(v, xbuf[k % 2][:, :], hbuf[k % 2][:, :],
                                     o), "dve_m")
                    chk("dve_m", B["dve_m"] + PD[("B", k)])

                emit_stats(0)
                emit_apply(0)
                ins = None
                for tt in range(NT):
                    wge(v, "pe_m", B["pe_m"] + 5 * tt + 1)
                    for k8 in range(KC):
                        ins = v.tensor_copy(
                            hTs[tt % 2][:, k8 * 128 : (k8 + 1) * 128],
                            stg[k8 // 4][:, (k8 % 4) * 128 : (k8 % 4 + 1) * 128])
                    inc(ins, "dve_m")
                    chk("dve_m", B["dve_m"] + PD[("C", tt)])
                    if tt + 1 < NT:
                        emit_stats(tt + 1)
                        emit_apply(tt + 1)
                    wge(v, "pe_m", B["pe_m"] + 5 * tt + 2)
                    inc(v.reduce_max(sml[:, 8:9], misc[:, 0:10],
                                     axis=AX.X, negate=True), "dve_m")
                    chk("dve_m", B["dve_m"] + PD[("r4", tt)])
                    wge(v, "act_m", B["act_m"] + PS[("s2", tt)])
                    selfedge(v, v.reduce_max(sml[:, 9:10], e_sb[:, 0:10],
                                              axis=AX.X))
                    selfedge(v, v.tensor_scalar(tp2_sb[:, 0:10], ones32[:, 0:10],
                                    sml[:, 9:10], None, ALU.mult))
                    selfedge(v, v.tensor_tensor(ewx_sb[:, 0:10], e_sb[:, 0:10],
                                    tp2_sb[:, 0:10], ALU.is_equal))
                    selfedge(v, v.scalar_tensor_tensor(tp_sb[:, 0:10],
                                    ewx_sb[:, 0:10], -2.0,
                                    e_sb[:, 0:10], ALU.mult, ALU.add))
                    selfedge(v, v.reduce_max(sml[:, 10:11], tp_sb[:, 0:10],
                                             axis=AX.X))
                    selfedge(v, v.tensor_scalar(tp2_sb[:, 0:10], ones32[:, 0:10],
                                    sml[:, 10:11], None, ALU.mult))
                    selfedge(v, v.tensor_tensor(ewx_sb[:, 16:26], tp_sb[:, 0:10],
                                    tp2_sb[:, 0:10], ALU.is_equal))
                    selfedge(v, v.tensor_add(tp_sb[:, 0:10], ewx_sb[:, 16:26],
                                             ewx_sb[:, 0:10]))
                    selfedge(v, v.tensor_mul(ew_sb[:, 0:8], tp_sb[:, 0:8],
                                             e_sb[:, 0:8]))
                    selfedge(v, v.reduce_sum(sml[:, 12:13], ew_sb[:, 0:8],
                                             axis=AX.X))
                    selfedge(v, v.tensor_scalar(sml[:, 14:15], sml[:, 12:13],
                                                0.0, None, ALU.is_equal))
                    ins = inc(v.tensor_add(sml[:, 14:15], sml[:, 14:15],
                                           sml[:, 12:13]), "dve_m")
                    chk("dve_m", B["dve_m"] + PD[("r5", tt)])
                    wge(v, "act_m", B["act_m"] + PS[("s3", tt)])
                    selfedge(v, v.tensor_scalar(ew_sb[:, 0:8], ew_sb[:, 0:8],
                                                sml[:, 15:16], None, ALU.mult))
                    for e8 in range(8):
                        ins = v.tensor_scalar(ewx_sb[:, e8 * 16 : (e8 + 1) * 16],
                                              ones32[:, 0:16],
                                              ew_sb[:, e8 : e8 + 1], None,
                                              ALU.mult)
                    inc(ins, "dve_m")
                    chk("dve_m", B["dve_m"] + PD[("r6", tt)])
                    wge(v, "dve_m", cnt["dve_m"])
                    wge(v, "pe_m", B["pe_m"] + 5 * tt + 3)
                    zreg = X[:, 1024:1152]
                    ins = inc(v.tensor_tensor(zw_bf[:, :], zreg,
                                              ewx_sb[:, :], ALU.mult), "dve_m")
                    chk("dve_m", B["dve_m"] + PD[("r7", tt)])
                    wge(v, "pe_m", B["pe_m"] + 5 * tt + 4)
                    ins = inc(v.tensor_copy(zwT_sb[:, :], stg[0][:, 0:128]),
                              "dve_m")
                    chk("dve_m", B["dve_m"] + PD[("r8", tt)])
                    wge(v, "pe_m", B["pe_m"] + 5 * tt + 5)
                    if tt >= 2:
                        wge(v, "dma_st", B["dma_st"] + 16 * (tt - 1))
                    par = (X if tt % 2 == 0 else Y)[:, 0:1024]
                    ins = inc(v.tensor_add(xob[tt % 2][:, :], par,
                                           xbuf[tt % 2][:, :]), "dve_m")
                    chk("dve_m", B["dve_m"] + PD[("r9", tt)])
                pdone(v, "dve_m")

            @block.scalar
            def _(sc):
                sc.wait_ge(sems["phz"], bt())

                def emit_s1(k):
                    o = 24 * (k % 2)
                    wge(sc, "dve_m", B["dve_m"] + PD[("A", k)])
                    inc(ln_act_rstd(sc, o), "act_m")
                    chk("act_m", B["act_m"] + PS[("s1", k)])

                emit_s1(0)
                ins = None
                for tt in range(NT):
                    if tt + 1 < NT:
                        emit_s1(tt + 1)
                    wge(sc, "dve_m", B["dve_m"] + PD[("r4", tt)])
                    ins = inc(sc.activation(e_sb[:, 0:10], misc[:, 0:10],
                                            AF.Exp, bias=sml[:, 8:9]),
                              "act_m")
                    chk("act_m", B["act_m"] + PS[("s2", tt)])
                    wge(sc, "dve_m", B["dve_m"] + PD[("r5", tt)])
                    sc.activation(sml[:, 16:17], sml[:, 14:15], AF.Ln)
                    ins = inc(sc.activation(sml[:, 15:16], sml[:, 16:17],
                                            AF.Exp, scale=-1.0,
                                            bias=sml[:, 7:8]), "act_m")
                    chk("act_m", B["act_m"] + PS[("s3", tt)])
                pdone(sc, "act_m")

            @block.tensor
            def _(t):
                t.wait_ge(sems["phz"], bt())
                wge(t, "dma_w", B["dma_w"] + 26 * 16)
                ins = None
                for tt in range(NT):
                    wge(t, "dve_m", B["dve_m"] + PD[("B", tt)])
                    if tt >= 1:
                        wge(t, "dve_m", B["dve_m"] + PD[("r8", tt - 1)])
                    for k8 in range(KC):
                        ins = tpose(t,
                            stg[k8 // 4][:, (k8 % 4) * 128 : (k8 % 4 + 1) * 128],
                            hbuf[tt % 2][:, k8 * 128 : (k8 + 1) * 128],
                            id_bf[:, :])
                    inc(ins, "pe_m")
                    wge(t, "dve_m", B["dve_m"] + PD[("C", tt)])
                    if tt >= 1:
                        wge(t, "dve_m", B["dve_m"] + PD[("r7", tt - 1)])
                        wge(t, "act_m", B["act_m"] + PS[("s2", tt - 1)])
                    for k8 in range(KC):
                        ins = t.matmul(misc[:, 0:10],
                                       hTs[tt % 2][:, k8 * 128 : (k8 + 1) * 128],
                                       gw_sb[:, k8 * 16 : k8 * 16 + 10],
                                       start=(k8 == 0), stop=(k8 == 7))
                    inc(ins, "pe_m")
                    for k8 in range(KC):
                        ins = t.matmul(X[:, 1024:1152],
                                       hTs[tt % 2][:, k8 * 128 : (k8 + 1) * 128],
                                       aA_sb[:, k8 * 128 : (k8 + 1) * 128],
                                       start=(k8 == 0), stop=(k8 == 7))
                    inc(ins, "pe_m")
                    wge(t, "dve_m", B["dve_m"] + PD[("r7", tt)])
                    ins = inc(tpose(t, stg[0][:, 0:128], zw_bf[:, :],
                                          id_bf[:, :]), "pe_m")
                    wge(t, "dve_m", B["dve_m"] + PD[("r8", tt)])
                    if tt >= 2:
                        wge(t, "dve_m", B["dve_m"] + PD[("r9", tt - 2)])
                    par = (X if tt % 2 == 0 else Y)[:, 0:1024]
                    for half in range(2):
                        sl = par[:, half * 512 : (half + 1) * 512]
                        for k8 in range(KC):
                            t.matmul(sl,
                                     hTs[tt % 2][:, k8 * 128 : (k8 + 1) * 128],
                                     bw_sb[:, k8 * 1024 + half * 512 :
                                           k8 * 1024 + (half + 1) * 512],
                                     start=(k8 == 0), stop=False)
                        t.matmul(sl, ones_bf[0:1, 0:128],
                                 bbr[0:1, half * 512 : (half + 1) * 512],
                                 start=False, stop=False)
                        ins = t.matmul(sl, zwT_sb[:, :],
                                       bB_sb[:, half * 512 : (half + 1) * 512],
                                       start=False, stop=True)
                    inc(ins, "pe_m")
                pdone(t, "pe_m")

            @block.sync
            def _(s):
                s.wait_ge(sems["phz"], bt())
                pdone(s)

        end_phase()
        s3.close()

    ffn_phase(x1_dram, out if upto == 2 else x2_dram, "f1")
    if upto == 2:
        tail_st.close()
        base_st.close()
        return nc

    s4 = contextlib.ExitStack()
    m4 = s4.enter_context
    w2_sb = m4(nc.sbuf_tensor("w2_sb", [128, 16 * 1024], BF16))
    k_sb = m4(nc.sbuf_tensor("k_sb", [128, 32 * 1024], BF16))
    v_sb = m4(nc.sbuf_tensor("v_sb", [128, 32 * 1024], BF16))
    kv_sb = m4(nc.sbuf_tensor("kv_sb", [128, 8 * 1024], BF16))
    qT_sb = m4(nc.sbuf_tensor("qT_sb", [128, 1024], BF16))

    with nc.Block() as block:
        B = snap()

        @block.gpsimd
        def _(g):
            g.wait_ge(sems["phz"], bt())
            for c8 in range(8):
                inc(g.dma_start(w2_sb[:, c8 * 1024 : (c8 + 1) * 1024],
                                w_k2[c8 * 128 : (c8 + 1) * 128, :]), "dma_w", 16)
                inc(g.dma_start(w2_sb[:, 8192 + c8 * 1024 : 8192 + (c8 + 1) * 1024],
                                w_v2[c8 * 128 : (c8 + 1) * 128, :]), "dma_w", 16)
            ins = None
            for tt in range(NT):
                if tt >= 2:
                    wge(g, "dve_m", B["dve_m"] + 5 * (tt - 2) + 2)
                ins = inc(g.dma_start(xbuf[tt % 2][:, :],
                                      x2_dram[tt * 128 : (tt + 1) * 128, :]),
                          "dma_x", 16)
                wge(g, "dve_m", B["dve_m"] + 5 * tt + 3)
                inc(g.dma_start(h2t_dram[tt * 128 : (tt + 1) * 128, :],
                                hTs[tt % 2][:, :]), "dma_st", 16)
            g.wait_ge(sems["dma_st"], B["dma_st"] + 16 * NT)
            pdone(g)

        @block.vector
        def _(v):
            v.wait_ge(sems["phz"], bt())
            ins = None
            for tt in range(NT):
                wge(v, "dma_x", B["dma_x"] + 16 * (tt + 1))
                inc(ln_dve_stats(v, xbuf[tt % 2][:, :]), "dve_m")
                wge(v, "act_m", B["act_m"] + tt + 1)
                if tt >= 2:
                    wge(v, "pe_m", B["pe_m"] + 3 * (tt - 2) + 1)
                inc(ln_dve_apply(v, xbuf[tt % 2][:, :], hbuf[tt % 2][:, :]),
                    "dve_m")
                wge(v, "pe_m", B["pe_m"] + 3 * tt + 1)
                if tt >= 2:
                    wge(v, "dma_st", B["dma_st"] + 16 * (tt - 1))
                for k8 in range(KC):
                    ins = v.tensor_copy(
                        hTs[tt % 2][:, k8 * 128 : (k8 + 1) * 128],
                        stg[k8 // 4][:, (k8 % 4) * 128 : (k8 % 4 + 1) * 128])
                inc(ins, "dve_m")
                wge(v, "pe_m", B["pe_m"] + 3 * tt + 2)
                par = (X if tt % 2 == 0 else Y)[:, 0:1024]
                v.tensor_copy(k_sb[:, tt * 1024 : tt * 1024 + 512], par[:, 0:512])
                ins = v.tensor_copy(k_sb[:, tt * 1024 + 512 : (tt + 1) * 1024],
                                    par[:, 512:1024])
                inc(ins, "dve_m")
                wge(v, "pe_m", B["pe_m"] + 3 * tt + 3)
                anti = (Y if tt % 2 == 0 else X)[:, 0:1024]
                v.tensor_copy(v_sb[:, tt * 1024 : tt * 1024 + 512], anti[:, 0:512])
                ins = v.tensor_copy(v_sb[:, tt * 1024 + 512 : (tt + 1) * 1024],
                                    anti[:, 512:1024])
                inc(ins, "dve_m")
            pdone(v, "dve_m")

        @block.scalar
        def _(sc):
            sc.wait_ge(sems["phz"], bt())
            ins = None
            for tt in range(NT):
                wge(sc, "dve_m", B["dve_m"] + 5 * tt + 1)
                ins = inc(ln_act_rstd(sc), "act_m")
            pdone(sc, "act_m")

        @block.tensor
        def _(t):
            t.wait_ge(sems["phz"], bt())
            wge(t, "dma_w", B["dma_w"] + 16 * 16)
            ins = None
            for tt in range(NT):
                wge(t, "dve_m", B["dve_m"] + 5 * tt + 2)
                if tt >= 1:
                    wge(t, "dve_m", B["dve_m"] + 5 * (tt - 1) + 3)
                for k8 in range(KC):
                    ins = tpose(t, 
                        stg[k8 // 4][:, (k8 % 4) * 128 : (k8 % 4 + 1) * 128],
                        hbuf[tt % 2][:, k8 * 128 : (k8 + 1) * 128], id_bf[:, :])
                inc(ins, "pe_m")
                wge(t, "dve_m", B["dve_m"] + 5 * tt + 3)
                if tt >= 1:
                    wge(t, "dve_m", B["dve_m"] + 5 * (tt - 1) + 5)
                par = (X if tt % 2 == 0 else Y)[:, 0:1024]
                for half in range(2):
                    for k8 in range(KC):
                        ins = t.matmul(par[:, half * 512 : (half + 1) * 512],
                                       hTs[tt % 2][:, k8 * 128 : (k8 + 1) * 128],
                                       w2_sb[:, k8 * 1024 + half * 512 :
                                             k8 * 1024 + (half + 1) * 512],
                                       start=(k8 == 0), stop=(k8 == 7))
                inc(ins, "pe_m")
                if tt >= 1:
                    wge(t, "dve_m", B["dve_m"] + 5 * (tt - 1) + 4)
                anti = (Y if tt % 2 == 0 else X)[:, 0:1024]
                for half in range(2):
                    for k8 in range(KC):
                        ins = t.matmul(anti[:, half * 512 : (half + 1) * 512],
                                       hTs[tt % 2][:, k8 * 128 : (k8 + 1) * 128],
                                       w2_sb[:, 8192 + k8 * 1024 + half * 512 :
                                             8192 + k8 * 1024 + (half + 1) * 512],
                                       start=(k8 == 0), stop=(k8 == 7))
                inc(ins, "pe_m")
            pdone(t, "pe_m")

        @block.sync
        def _(s):
            s.wait_ge(sems["phz"], bt())
            pdone(s)

    end_phase()

    with nc.Block() as block:
        B = snap()

        @block.tensor
        def _(t):
            t.wait_ge(sems["phz"], bt())
            ins = None
            for r in range(16):
                ac, bh = r // 2, r % 2
                if r >= 2:
                    wge(t, "dve_m", B["dve_m"] + r - 1)
                par = (X if r % 2 == 0 else Y)[:, 0:512]
                for tt in range(NT):
                    ins = t.matmul(par,
                                   k_sb[:, tt * 1024 + ac * 128 :
                                        tt * 1024 + (ac + 1) * 128],
                                   v_sb[:, tt * 1024 + bh * 512 :
                                        tt * 1024 + (bh + 1) * 512],
                                   start=(tt == 0), stop=(tt == NT - 1))
                inc(ins, "pe_m")
            pdone(t, "pe_m")

        @block.vector
        def _(v):
            v.wait_ge(sems["phz"], bt())
            ins = None
            for r in range(16):
                ac, bh = r // 2, r % 2
                wge(v, "pe_m", B["pe_m"] + r + 1)
                par = (X if r % 2 == 0 else Y)[:, 0:512]
                ins = inc(v.tensor_copy(
                    kv_sb[:, ac * 1024 + bh * 512 : ac * 1024 + (bh + 1) * 512],
                    par), "dve_m")
            pdone(v, "dve_m")

        @block.scalar
        def _(sc):
            sc.wait_ge(sems["phz"], bt())
            pdone(sc)

        @block.gpsimd
        def _(g):
            g.wait_ge(sems["phz"], bt())
            for c8 in range(8):
                inc(g.dma_start(w2_sb[:, c8 * 1024 : (c8 + 1) * 1024],
                                w_q2[c8 * 128 : (c8 + 1) * 128, :]), "dma_w", 16)
            g.wait_ge(sems["dma_w"], B["dma_w"] + 8 * 16)
            pdone(g)

        @block.sync
        def _(s):
            s.wait_ge(sems["phz"], bt())
            pdone(s)

    end_phase()

    dst3 = out if upto == 3 else x3_dram
    with nc.Block() as block:
        B = snap()

        @block.gpsimd
        def _(g):
            g.wait_ge(sems["phz"], bt())
            ins = None
            for tt in range(NT):
                if tt >= 2:
                    wge(g, "pe_m", B["pe_m"] + 2 * (tt - 2) + 1)
                inc(g.dma_start(hTs[tt % 2][:, :],
                                h2t_dram[tt * 128 : (tt + 1) * 128, :]),
                    "dma_x", 16)
                if tt >= 2:
                    wge(g, "dve_m", B["dve_m"] + 2 * (tt - 2) + 2)
                inc(g.dma_start(xbuf[tt % 2][:, :],
                                x2_dram[tt * 128 : (tt + 1) * 128, :]),
                    "dma_x", 16)
                if tt >= 1:
                    wge(g, "dve_m", B["dve_m"] + 2 * (tt - 1) + 2)
                    ins = inc(g.dma_start(dst3[(tt - 1) * 128 : tt * 128, :],
                                          xob[(tt - 1) % 2][:, :]), "dma_st", 16)
            wge(g, "dve_m", B["dve_m"] + 2 * (NT - 1) + 2)
            inc(g.dma_start(dst3[(NT - 1) * 128 : NT * 128, :],
                            xob[(NT - 1) % 2][:, :]), "dma_st", 16)
            g.wait_ge(sems["dma_st"], B["dma_st"] + 16 * NT)
            pdone(g)

        @block.vector
        def _(v):
            v.wait_ge(sems["phz"], bt())
            ins = None
            for tt in range(NT):
                wge(v, "pe_m", B["pe_m"] + 2 * tt + 1)
                if tt >= 1:
                    wge(v, "pe_m", B["pe_m"] + 2 * (tt - 1) + 2)
                par = (X if tt % 2 == 0 else Y)[:, 0:1024]
                v.tensor_copy(qT_sb[:, 0:512], par[:, 0:512])
                ins = v.tensor_copy(qT_sb[:, 512:1024], par[:, 512:1024])
                inc(ins, "dve_m")
                wge(v, "pe_m", B["pe_m"] + 2 * tt + 2)
                wge(v, "dma_x", B["dma_x"] + 32 * (tt + 1))
                if tt >= 2:
                    wge(v, "dma_st", B["dma_st"] + 16 * (tt - 1))
                anti = (Y if tt % 2 == 0 else X)[:, 0:1024]
                ins = inc(v.scalar_tensor_tensor(xob[tt % 2][:, :], anti, 32.0,
                                                 xbuf[tt % 2][:, :], ALU.mult,
                                                 ALU.add), "dve_m")
            pdone(v, "dve_m")

        @block.scalar
        def _(sc):
            sc.wait_ge(sems["phz"], bt())
            pdone(sc)

        @block.tensor
        def _(t):
            t.wait_ge(sems["phz"], bt())
            ins = None
            for tt in range(NT):
                wge(t, "dma_x", B["dma_x"] + 32 * tt + 16)
                if tt >= 2:
                    wge(t, "dve_m", B["dve_m"] + 2 * (tt - 2) + 1)
                if tt >= 1:
                    wge(t, "dve_m", B["dve_m"] + 2 * (tt - 1) + 2)
                par = (X if tt % 2 == 0 else Y)[:, 0:1024]
                for at in range(8):
                    for k8 in range(KC):
                        ins = t.matmul(par[:, at * 128 : (at + 1) * 128],
                                       w2_sb[:, k8 * 1024 + at * 128 :
                                             k8 * 1024 + (at + 1) * 128],
                                       hTs[tt % 2][:, k8 * 128 : (k8 + 1) * 128],
                                       start=(k8 == 0), stop=(k8 == 7))
                inc(ins, "pe_m")
                wge(t, "dve_m", B["dve_m"] + 2 * tt + 1)
                if tt >= 2:
                    wge(t, "dve_m", B["dve_m"] + 2 * (tt - 2) + 2)
                anti = (Y if tt % 2 == 0 else X)[:, 0:1024]
                for half in range(2):
                    for ac in range(8):
                        ins = t.matmul(anti[:, half * 512 : (half + 1) * 512],
                                       qT_sb[:, ac * 128 : (ac + 1) * 128],
                                       kv_sb[:, ac * 1024 + half * 512 :
                                             ac * 1024 + (half + 1) * 512],
                                       start=(ac == 0), stop=(ac == 7))
                inc(ins, "pe_m")
            pdone(t, "pe_m")

        @block.sync
        def _(s):
            s.wait_ge(sems["phz"], bt())
            pdone(s)

    end_phase()

    if upto == 3:
        s4.close()
        tail_st.close()
        base_st.close()
        return nc

    s4.close()

    ffn_phase(x3_dram, out, "f2")

    tail_st.close()
    base_st.close()
    return nc




def kernel(L0_attn_a_q, L0_attn_a_Wq, L0_attn_a_bq, L0_attn_a_Wk, L0_attn_a_bk,
           L0_attn_a_Wv, L0_attn_a_bv, L0_attn_a_Wo, L0_attn_a_bo,
           L0_attn_a_rel_table, L0_attn_b_w_q, L0_attn_b_w_k, L0_attn_b_w_v,
           L0_ffn_base_w, L0_ffn_base_b, L0_ffn_gate_w, L0_ffn_lora_A,
           L0_ffn_lora_B, upto=4, dbg=False):
    f = np.float32
    x = np.ascontiguousarray(np.asarray(L0_attn_a_q, dtype=f))
    Wq = np.ascontiguousarray(np.asarray(L0_attn_a_Wq, dtype=f))
    Wk = np.ascontiguousarray(np.asarray(L0_attn_a_Wk, dtype=f))
    Wv = np.ascontiguousarray(np.asarray(L0_attn_a_Wv, dtype=f))
    Wo = np.ascontiguousarray(np.asarray(L0_attn_a_Wo, dtype=f))
    bq = np.asarray(L0_attn_a_bq, dtype=f)
    bk = np.asarray(L0_attn_a_bk, dtype=f)
    bv = np.asarray(L0_attn_a_bv, dtype=f)
    bo = np.asarray(L0_attn_a_bo, dtype=f)
    rel = np.asarray(L0_attn_a_rel_table, dtype=f)
    wq2 = np.ascontiguousarray(np.asarray(L0_attn_b_w_q, dtype=f))
    wk2 = np.ascontiguousarray(np.asarray(L0_attn_b_w_k, dtype=f))
    wv2 = np.ascontiguousarray(np.asarray(L0_attn_b_w_v, dtype=f))
    base_w = np.asarray(L0_ffn_base_w, dtype=f)
    base_b = np.asarray(L0_ffn_base_b, dtype=f)
    gate_w = np.asarray(L0_ffn_gate_w, dtype=f)
    lora_A = np.asarray(L0_ffn_lora_A, dtype=f)
    lora_B = np.asarray(L0_ffn_lora_B, dtype=f)

    relT = np.ascontiguousarray(rel.T)
    relT2 = np.ascontiguousarray(np.concatenate([relT, relT], axis=0))
    common = {
        "Wq": Wq, "Wk": Wk, "Wv": Wv, "Wo": Wo,
        "bq_pp": np.ascontiguousarray(bq.reshape(8, 128).T),
        "bk_pp": np.ascontiguousarray(bk.reshape(8, 128).T),
        "bv_row": np.ascontiguousarray(bv[None, :]),
        "bo_row": np.ascontiguousarray(bo[None, :]),
        "relT2": relT2,
        "w_q2": wq2, "w_k2": wk2, "w_v2": wv2,
        "basewT": np.ascontiguousarray(base_w.T),
        "baseb_row": np.ascontiguousarray(base_b[None, :]),
        "gatewT": np.ascontiguousarray(gate_w.T),
        "AcatT": np.ascontiguousarray(lora_A.reshape(128, D).T),
        "BcatT": np.ascontiguousarray(lora_B.transpose(0, 2, 1).reshape(128, D)),
        "ident_in": np.eye(128, dtype=f),
    }
    nc = build(upto, dbg=dbg)
    in_maps = [dict(common, x0=np.ascontiguousarray(x[b])) for b in range(8)]
    res = run_bass_kernel_spmd(nc, in_maps, core_ids=list(range(8)))
    if dbg:
        return res.results
    return np.stack([res.results[b]["out"] for b in range(8)], axis=0)


# version 14
# speedup vs baseline: 1.1531x; 1.1350x over previous
import contextlib
import numpy as np
import concourse.bass as bass
import concourse.mybir as mybir
from concourse.bass_utils import run_bass_kernel_spmd

F32 = mybir.dt.float32
BF16 = mybir.dt.bfloat16
AF = mybir.ActivationFunctionType
ALU = mybir.AluOpType
AX = mybir.AxisListType

S, D = 4096, 1024
NT = 32
KC = 8
EWMAX = 1664


def _qgeom(c):
    lo = max(0, 4 * c - 4)
    hi = min(31, 4 * c + 7)
    Wb = 128 * (hi - lo + 1)
    EW = Wb + 128
    tiles = []
    for j in range(4):
        qt = 4 * c + j
        i0 = 128 * qt
        t_min = 128 * lo - i0 + 385
        Lp = max(0, -t_min)
        t_lo = max(0, t_min)
        mid = min(1024 - t_lo, EW - Lp)
        Rp = EW - Lp - mid
        tiles.append((qt, i0, Lp, t_lo, mid, Rp))
    return lo, hi, Wb, EW, tiles


QG = [_qgeom(c) for c in range(8)]


def build(upto=4, dbg=False, nhp=None, prod_only=False, pstage=9):
    nc = bass.Bass("TRN2", target_bir_lowering=False)
    dt = nc.dram_tensor

    x0 = dt("x0", [S, D], F32, kind="ExternalInput").ap()
    Wq = dt("Wq", [D, D], F32, kind="ExternalInput").ap()
    Wk = dt("Wk", [D, D], F32, kind="ExternalInput").ap()
    Wv = dt("Wv", [D, D], F32, kind="ExternalInput").ap()
    Wo = dt("Wo", [D, D], F32, kind="ExternalInput").ap()
    bq_pp = dt("bq_pp", [128, 8], F32, kind="ExternalInput").ap()
    bk_pp = dt("bk_pp", [128, 8], F32, kind="ExternalInput").ap()
    bv_row = dt("bv_row", [1, D], F32, kind="ExternalInput").ap()
    bo_row = dt("bo_row", [1, D], F32, kind="ExternalInput").ap()
    relT2 = dt("relT2", [128, 1025], F32, kind="ExternalInput").ap()
    w_q2 = dt("w_q2", [D, D], F32, kind="ExternalInput").ap()
    w_k2 = dt("w_k2", [D, D], F32, kind="ExternalInput").ap()
    w_v2 = dt("w_v2", [D, D], F32, kind="ExternalInput").ap()
    basewT = dt("basewT", [D, D], F32, kind="ExternalInput").ap()
    baseb_row = dt("baseb_row", [1, D], F32, kind="ExternalInput").ap()
    gatewT = dt("gatewT", [D, 10], F32, kind="ExternalInput").ap()
    AcatT = dt("AcatT", [D, 128], F32, kind="ExternalInput").ap()
    BcatT = dt("BcatT", [128, D], F32, kind="ExternalInput").ap()
    ident_in = dt("ident_in", [128, 128], F32, kind="ExternalInput").ap()
    out = dt("out", [S, D], F32, kind="ExternalOutput").ap()
    if dbg:
        dbg_qa = dt("dbg_qa", [128, S], BF16, kind="ExternalOutput").ap()
        dbg_ka = dt("dbg_ka", [128, S], BF16, kind="ExternalOutput").ap()
        dbg_va = dt("dbg_va", [128, 2080], BF16, kind="ExternalOutput").ap()
        dbg_e = dt("dbg_e", [128, EWMAX], BF16, kind="ExternalOutput").ap()
        dbg_gb = dt("dbg_gb", [128, 6144], BF16, kind="ExternalOutput").ap()
        dbg_a = dt("dbg_a", [128, 512], BF16, kind="ExternalOutput").ap()
        dbg_ct = dt("dbg_ct", [128, S], BF16, kind="ExternalOutput").ap()
        dbg_sml = dt("dbg_sml", [128, 64], F32, kind="ExternalOutput").ap()

    e_dram = dt("e_dram", [8 * 128 * EWMAX], BF16, kind="Internal")
    ct_dram = dt("ct_dram", [8 * 128 * S], BF16, kind="Internal")
    x1_dram = dt("x1_dram", [S, D], F32, kind="Internal").ap()
    x2_dram = dt("x2_dram", [S, D], F32, kind="Internal").ap()
    x3_dram = dt("x3_dram", [S, D], F32, kind="Internal").ap()
    h2t_dram = dt("h2t_dram", [S, D], BF16, kind="Internal").ap()

    base_st = contextlib.ExitStack()
    mk = base_st.enter_context

    id_bf = mk(nc.sbuf_tensor("id_bf", [128, 128], BF16))
    ones_bf = mk(nc.sbuf_tensor("ones_bf", [128, 512], BF16))
    ones32 = mk(nc.sbuf_tensor("ones32", [128, 16], F32))
    sml = mk(nc.sbuf_tensor("sml", [128, 64], F32))
    bor = mk(nc.sbuf_tensor("bor", [1, D], BF16))
    xbuf = [mk(nc.sbuf_tensor(f"xbuf{i}", [128, D], F32)) for i in range(2)]
    hbuf = [mk(nc.sbuf_tensor(f"hbuf{i}", [128, D], BF16)) for i in range(2)]
    scrb = mk(nc.sbuf_tensor("scrb", [128, D], BF16))

    MISCP = mk(nc.psum_tensor("MISCP", [128, 512], F32))

    sem_names = [
        "phz", "dma_w", "dma_x", "dma_s", "dma_st", "dma_e", "dma_g",
        "dma_ct", "dma_q", "dve_n",
        "pe_m", "pe_p", "pe_f", "pe_av", "pe_x",
        "act_m", "act_e", "act_l",
        "dve_m", "dve_e", "dve_c", "dve_x",
    ]
    sems = {n: mk(nc.semaphore(n)) for n in sem_names}
    cnt = {n: 0 for n in sem_names}
    nphase = [0]

    def inc(instr, name, k=1):
        instr.then_inc(sems[name], k)
        cnt[name] += k
        return instr

    def chk(name, v):
        assert cnt[name] == v, (name, cnt[name], v)

    def wge(eng, name, v):
        if v > 0:
            eng.wait_ge(sems[name], v)

    def snap():
        return dict(cnt)

    def selfedge(eng, instr, name="dve_x"):
        inc(instr, name)
        eng.wait_ge(sems[name], cnt[name])
        return instr

    def pdone(eng, sem=None):
        if sem is not None:
            eng.wait_ge(sems[sem], cnt[sem])
        eng.sem_inc(sems["phz"], 1)

    def end_phase():
        nphase[0] += 1
        cnt["phz"] += 5

    def bt():
        return 5 * nphase[0]

    def ln_dve_stats(v, xa):
        v.tensor_scalar(scrb[:, :], xa, 1.0, 0.0, ALU.mult, ALU.add,
                        accum_out=sml[:, 0:1])
        selfedge(v, v.scalar_tensor_tensor(scrb[:, :], xa, 1.0, xa, ALU.mult,
                                           ALU.mult, accum_out=sml[:, 1:2]))
        v.tensor_scalar(sml[:, 2:3], sml[:, 0:1], -1.0 / D, None, ALU.mult)
        selfedge(v, v.tensor_scalar(sml[:, 3:4], sml[:, 1:2], 1.0 / D, None,
                                    ALU.mult))
        selfedge(v, v.tensor_mul(sml[:, 5:6], sml[:, 2:3], sml[:, 2:3]))
        return v.tensor_sub(sml[:, 3:4], sml[:, 3:4], sml[:, 5:6])

    def ln_act_rstd(sc):
        sc.activation(sml[:, 4:5], sml[:, 3:4], AF.Ln, bias=sml[:, 6:7])
        return sc.activation(sml[:, 4:5], sml[:, 4:5], AF.Exp, scale=-0.5)

    def ln_dve_apply(v, xa, h_out):
        return v.tensor_scalar(h_out, xa, sml[:, 2:3], sml[:, 4:5],
                               ALU.add, ALU.mult)

    def tpose(t, out_ap, in_ap, ident):
        return t.matmul(out_ap, in_ap, ident, start=True, stop=True,
                        is_transpose=True, skip_group_check=True)

    s1 = contextlib.ExitStack()
    m1 = s1.enter_context
    id_f32 = m1(nc.sbuf_tensor("id_f32", [128, 128], F32))
    TT2 = m1(nc.sbuf_tensor("TT2", [128, 1025], BF16))
    bqc = m1(nc.sbuf_tensor("bqc", [128, 8], F32))
    bkc = m1(nc.sbuf_tensor("bkc", [128, 8], F32))
    bvr = m1(nc.sbuf_tensor("bvr", [1, D], BF16))
    maxT = m1(nc.sbuf_tensor("maxT", [1, 1], F32))
    TdT0 = m1(nc.sbuf_tensor("TdT0", [64, 2], BF16))
    tsq = m1(nc.sbuf_tensor("tsq", [64, 1025], BF16))

    wqh = [m1(nc.sbuf_tensor(f"wqh{i}", [128, 3072], BF16)) for i in range(2)]
    WQH_Q, WQH_K, WQH_V = 0, 1024, 2048
    h1T = m1(nc.sbuf_tensor("h1T", [128, KC * S], BF16))
    Qa = [m1(nc.sbuf_tensor(f"Qa{i}", [128, S], BF16)) for i in range(2)]
    Ka = [m1(nc.sbuf_tensor(f"Ka{i}", [128, S], BF16)) for i in range(2)]
    Va = [m1(nc.sbuf_tensor(f"Va{i}", [128, 32 * 65], BF16)) for i in range(2)]
    QKstg = [m1(nc.sbuf_tensor(f"QKstg{i}", [128, 512], BF16)) for i in range(8)]
    rowstg = m1(nc.sbuf_tensor("rowstg", [32, 128], BF16))
    nmall = [m1(nc.sbuf_tensor(f"nmall{i}", [128, 32], F32)) for i in range(2)]
    lnqq = m1(nc.sbuf_tensor("lnqq", [128, 32], F32))
    mallb = m1(nc.sbuf_tensor("mallb", [128, 32], F32))
    lncbf = m1(nc.sbuf_tensor("lncbf", [1, 2], BF16))
    Asb = [m1(nc.sbuf_tensor(f"Asb{i}", [128, 512], BF16)) for i in range(3)]
    lrow = [m1(nc.sbuf_tensor(f"lrow{i}", [1, 512], BF16)) for i in range(2)]
    Esb = [m1(nc.sbuf_tensor(f"Esb{i}", [128, EWMAX], BF16)) for i in range(2)]
    GBs = [m1(nc.sbuf_tensor(f"GBs{i}", [128, 4 * 1536], BF16)) for i in range(2)]
    pads = m1(nc.sbuf_tensor("pads", [128, 2], F32))
    rlt = m1(nc.sbuf_tensor("rlt", [1, 512], F32))
    rl_sb = m1(nc.sbuf_tensor("rl_sb", [1, 512], BF16))
    rlb_sb = m1(nc.sbuf_tensor("rlb_sb", [128, 512], BF16))
    CTs = m1(nc.sbuf_tensor("CTs", [128, S], BF16))

    with nc.Block() as block:
        B = snap()

        @block.gpsimd
        def _(g):
            inc(g.dma_start(id_bf[:, :], ident_in), "dma_w", 16)
            inc(g.dma_start(TT2[:, :], relT2), "dma_w", 16)
            inc(g.dma_start(bvr[:, :], bv_row), "dma_w", 16)
            inc(g.dma_start(bor[:, :], bo_row), "dma_w", 16)
            g.wait_ge(sems["dma_w"], B["dma_w"] + 64)
            g.wait_ge(sems["dma_s"], B["dma_s"] + 48)
            pdone(g)

        @block.sync
        def _(s):
            inc(s.dma_start(id_f32[:, :], ident_in), "dma_s", 16)
            inc(s.dma_start(bqc[:, :], bq_pp), "dma_s", 16)
            inc(s.dma_start(bkc[:, :], bk_pp), "dma_s", 16)
            s.wait_ge(sems["dma_w"], B["dma_w"] + 64)
            s.wait_ge(sems["dma_s"], B["dma_s"] + 48)
            pdone(s)

        @block.vector
        def _(v):
            v.memset(ones_bf[:, :], 1.0)
            v.memset(ones32[:, :], 1.0)
            v.memset(sml[:, :], 0.0)
            v.memset(sml[:, 6:7], 1e-5)
            v.memset(sml[:, 7:8], 0.6931471805599453)
            v.memset(Ka[0][64:128, :], 0.0)
            v.memset(Ka[1][64:128, :], 0.0)
            v.memset(Qa[0][64:128, :], 0.0)
            v.memset(Qa[1][64:128, :], 0.0)
            selfedge(v, v.memset(Ka[0][64:66, :], 1.0))
            v.memset(Ka[1][64:66, :], 1.0)
            v.memset(Va[0][:, :], 1.0)
            v.memset(Va[1][:, :], 1.0)
            v.wait_ge(sems["dma_w"], B["dma_w"] + 64)
            v.wait_ge(sems["dma_s"], B["dma_s"] + 48)
            v.tensor_copy(TdT0[0:64, 0:1], TT2[0:64, 0:1])
            ins = inc(v.tensor_sub(TdT0[0:64, 1:2], TT2[0:64, 1024:1025],
                                   TT2[0:64, 0:1]), "dve_m")
            pdone(v, "dve_m")

        @block.tensor
        def _(t):
            pdone(t)

        @block.scalar
        def _(sc):
            pdone(sc)

    end_phase()

    with nc.Block() as block:
        B = snap()

        @block.gpsimd
        def _(g):
            g.wait_ge(sems["phz"], bt())
            for w0, wsrc in ((WQH_Q, Wq), (WQH_K, Wk), (WQH_V, Wv)):
                for kcc in range(KC):
                    inc(g.dma_start(
                        wqh[0][:, w0 + kcc * 128 : w0 + (kcc + 1) * 128],
                        wsrc[kcc * 128 : (kcc + 1) * 128, 0:128]), "dma_w", 16)
            g.wait_ge(sems["dma_w"], B["dma_w"] + 24 * 16)
            pdone(g)

        @block.vector
        def _(v):
            v.wait_ge(sems["phz"], bt())
            inc(v.tensor_mul(tsq[0:64, 0:1025], TT2[0:64, :], TT2[0:64, :]),
                "dve_m")
            wge(v, "pe_m", B["pe_m"] + 1)
            inc(v.tensor_reduce(sml[:, 8:9], MISCP[:, 0:8], axis=AX.X,
                                op=ALU.max), "dve_m")
            wge(v, "pe_m", B["pe_m"] + 2)
            selfedge(v, v.tensor_reduce(sml[0:1, 9:10], MISCP[0:1, 256:384],
                                        axis=AX.X, op=ALU.max))
            inc(v.tensor_tensor(sml[0:1, 9:10], sml[0:1, 9:10],
                                MISCP[0:1, 8:9], ALU.max), "dve_m")
            pdone(v, "dve_m")

        @block.tensor
        def _(t):
            t.wait_ge(sems["phz"], bt())
            wge(t, "dve_m", B["dve_m"] + 1)
            ins = None
            for t9 in range(9):
                w = min(128, 1025 - t9 * 128)
                ins = t.matmul(MISCP[0:w, t9 : t9 + 1],
                               tsq[0:64, t9 * 128 : t9 * 128 + w],
                               ones_bf[0:64, 0:1], start=True, stop=True)
            inc(ins, "pe_m")
            wge(t, "dve_m", B["dve_m"] + 2)
            ins = inc(tpose(t, MISCP[0:1, 256:384], sml[:, 8:9],
                                  id_f32[:, :]), "pe_m")
            pdone(t, "pe_m")

        @block.scalar
        def _(sc):
            sc.wait_ge(sems["phz"], bt())
            wge(sc, "dve_m", B["dve_m"] + 3)
            sc.activation(sml[0:1, 10:11], sml[0:1, 9:10], AF.Ln)
            ins = inc(sc.activation(maxT[:, :], sml[0:1, 10:11], AF.Exp,
                                    scale=0.5), "act_m")
            pdone(sc, "act_m")

        @block.sync
        def _(s):
            s.wait_ge(sems["phz"], bt())
            pdone(s)

    end_phase()

    s1a = contextlib.ExitStack()
    stgbf = [s1a.enter_context(nc.psum_tensor(f"stgbf{i}", [128, 512], BF16))
             for i in range(2)]
    with nc.Block() as block:
        B = snap()

        @block.gpsimd
        def _(g):
            g.wait_ge(sems["phz"], bt())
            ins = None
            for tt in range(NT):
                if tt >= 2:
                    wge(g, "dve_m", B["dve_m"] + 3 * (tt - 2) + 2)
                ins = inc(g.dma_start(xbuf[tt % 2][:, :],
                                      x0[tt * 128 : (tt + 1) * 128, :]),
                          "dma_x", 16)
            pdone(g)

        @block.vector
        def _(v):
            v.wait_ge(sems["phz"], bt())
            ins = None
            for tt in range(NT):
                wge(v, "dma_x", B["dma_x"] + 16 * (tt + 1))
                inc(ln_dve_stats(v, xbuf[tt % 2][:, :]), "dve_m")
                wge(v, "act_m", B["act_m"] + tt + 1)
                if tt >= 2:
                    wge(v, "pe_m", B["pe_m"] + tt - 1)
                inc(ln_dve_apply(v, xbuf[tt % 2][:, :], hbuf[tt % 2][:, :]),
                    "dve_m")
                wge(v, "pe_m", B["pe_m"] + tt + 1)
                for k8 in range(KC):
                    ins = v.tensor_copy(
                        h1T[:, k8 * S + tt * 128 : k8 * S + (tt + 1) * 128],
                        stgbf[k8 // 4][:, (k8 % 4) * 128 : (k8 % 4 + 1) * 128])
                inc(ins, "dve_m")
            pdone(v, "dve_m")

        @block.scalar
        def _(sc):
            sc.wait_ge(sems["phz"], bt())
            ins = None
            for tt in range(NT):
                wge(sc, "dve_m", B["dve_m"] + 3 * tt + 1)
                ins = inc(ln_act_rstd(sc), "act_m")
            pdone(sc, "act_m")

        @block.tensor
        def _(t):
            t.wait_ge(sems["phz"], bt())
            ins = None
            for tt in range(NT):
                wge(t, "dve_m", B["dve_m"] + 3 * tt + 2)
                if tt >= 1:
                    wge(t, "dve_m", B["dve_m"] + 3 * (tt - 1) + 3)
                for k8 in range(KC):
                    ins = tpose(t, 
                        stgbf[k8 // 4][:, (k8 % 4) * 128 : (k8 % 4 + 1) * 128],
                        hbuf[tt % 2][:, k8 * 128 : (k8 + 1) * 128], id_bf[:, :])
                inc(ins, "pe_m")
            pdone(t, "pe_m")

        @block.sync
        def _(s):
            s.wait_ge(sems["phz"], bt())
            pdone(s)

    end_phase()
    s1a.close()

    s1p = contextlib.ExitStack()
    mp = s1p.enter_context
    PA = mp(nc.psum_tensor("PA", [128, 512], F32))
    PB = mp(nc.psum_tensor("PB", [128, 512], F32))
    CTX = mp(nc.psum_tensor("CTX", [128, 512], F32))
    Pp = mp(nc.psum_tensor("Pp", [128, 1536], F32))
    PC = mp(nc.psum_tensor("PC", [128, 512], F32))

    def pe_ptile(t, h_, c_, j, dve_e_wait):
        lo, hi, Wb, EW, tiles = QG[c_]
        qt, i0, Lp, t_lo, mid, Rp = tiles[j]
        qa = Qa[h_]
        wge(t, "dve_e", dve_e_wait)
        t.matmul(Pp[:, 0:512], qa[0:64, i0 : i0 + 128], TT2[0:64, 0:512],
                 start=True, stop=True)
        t.matmul(Pp[:, 512:1024], qa[0:64, i0 : i0 + 128], TT2[0:64, 512:1024],
                 start=True, stop=True)
        ins = t.matmul(Pp[:, 1024:1025], qa[0:64, i0 : i0 + 128],
                       TT2[0:64, 1024:1025], start=True, stop=True)
        inc(ins, "pe_p")

    def dve_etile(v, h_, c_, j, pe_p_wait, dma_e_wait, slot):
        lo, hi, Wb, EW, tiles = QG[c_]
        qt, i0, Lp, t_lo, mid, Rp = tiles[j]
        Eb = Esb[slot]
        wge(v, "pe_p", pe_p_wait)
        wge(v, "dma_e", dma_e_wait)
        selfedge(v, v.tensor_scalar(pads[:, 0:1], Pp[:, 0:1],
                                    nmall[h_][:, qt : qt + 1], None, ALU.add))
        selfedge(v, v.tensor_scalar(pads[:, 1:2], Pp[:, 1024:1025],
                                    nmall[h_][:, qt : qt + 1], None, ALU.add))
        ins = v.tensor_scalar(Eb[:, Lp : Lp + mid], Pp[:, t_lo : t_lo + mid],
                              nmall[h_][:, qt : qt + 1], None, ALU.add)
        off = 0
        while off < Lp:
            w = min(512, Lp - off)
            ins = v.tensor_scalar(Eb[:, off : off + w], ones_bf[:, 0:w],
                                  pads[:, 0:1], None, ALU.mult)
            off += w
        off = Lp + mid
        while off < EW:
            w = min(512, EW - off)
            ins = v.tensor_scalar(Eb[:, off : off + w], ones_bf[:, 0:w],
                                  pads[:, 1:2], None, ALU.mult)
            off += w
        inc(ins, "dve_e")

    def sync_etile(s, c_, j, sp, dve_e_wait, dmag_guard, dmae_wait, pef_guard,
                   slot):
        lo, hi, Wb, EW, tiles = QG[c_]
        slot_off = (sp * 4 + j) * 128 * EWMAX
        wge(s, "dve_e", dve_e_wait)
        wge(s, "dma_g", dmag_guard)
        inc(s.dma_start(bass.AP(e_dram, slot_off, [[EWMAX, 128], [1, EW]]),
                        Esb[slot][:, 0:EW]), "dma_e", 16)
        wge(s, "dma_e", dmae_wait)
        wge(s, "pe_f", pef_guard)
        inc(s.dma_start(GBs[sp][:, j * 1536 : j * 1536 + Wb],
                        bass.AP(e_dram, slot_off + 127,
                                [[EWMAX - 1, 128], [1, Wb]])), "dma_g", 16)

    for hp in range(8 if nhp is None else nhp):
        with nc.Block() as block:
            B = snap()
            wsl = hp % 2
            dma_w_rdy = cnt["dma_w"]

            @block.tensor
            def _(t):
                t.wait_ge(sems["phz"], bt())
                wge(t, "dma_w", dma_w_rdy)
                for p in range(16):
                    c = p % 8
                    w0 = WQH_K if p < 8 else WQH_Q
                    if p >= 2:
                        wge(t, "dve_m", B["dve_m"] + 1 + p)
                    bank = (PA if p % 2 == 0 else PB)[:, 0:512]
                    ins = None
                    for kcc in range(KC):
                        ins = t.matmul(
                            bank,
                            wqh[wsl][:, w0 + kcc * 128 : w0 + (kcc + 1) * 128],
                            h1T[:, kcc * S + c * 512 : kcc * S + (c + 1) * 512],
                            start=(kcc == 0), stop=(kcc == KC - 1))
                    inc(ins, "pe_m")
                for h in range(2 if pstage >= 2 else 0):
                    for ch in range(KC):
                        wge(t, "dve_m", B["dve_m"] + 19 + 10 * h + ch)
                        ins = None
                        for i4 in range(4):
                            qt = 4 * ch + i4
                            ins = t.matmul(
                                MISCP[:, 32 * h + qt : 32 * h + qt + 1],
                                Asb[ch % 2][0:64, i4 * 128 : (i4 + 1) * 128],
                                ones_bf[0:64, 0:1], start=True, stop=True,
                                skip_group_check=True)
                        inc(ins, "pe_m")
                    wge(t, "dve_m", B["dve_m"] + 27 + 10 * h)
                    inc(tpose(t, MISCP[0:1, 256:384], sml[:, 8 + h : 9 + h],
                                    id_f32[:, :]), "pe_m")
                if pstage >= 2:
                    chk("pe_m", B["pe_m"] + 34)
                for h in range(2 if pstage >= 2 else 0):
                    wge(t, "act_m", B["act_m"] + 3 + h)
                    inc(t.matmul(MISCP[:, 100 + h : 101 + h],
                                 ones_bf[0:1, 0:128], lncbf[0:1, h : h + 1],
                                 start=True, stop=True,
                                 skip_group_check=True), "pe_m")
                for h in range(2 if pstage >= 3 else 0):
                    for ch in range(KC):
                        wge(t, "dve_m", B["dve_m"] + 43 + 8 * h + ch)
                        if ch == 0:
                            wge(t, "dve_m", B["dve_m"] + 27 + 10 * h)
                        ins = None
                        for i4 in range(4):
                            qt = 4 * ch + i4
                            ins = t.matmul(
                                MISCP[:, 32 * h + qt : 32 * h + qt + 1],
                                Asb[ch % 2][0:64, i4 * 128 : (i4 + 1) * 128],
                                ones_bf[0:64, 0:1], start=True, stop=True,
                                skip_group_check=True)
                        inc(ins, "pe_m")
                if pstage >= 3:
                    chk("pe_m", B["pe_m"] + 52)
                for h in range(2 if pstage >= 3 else 0):
                    wge(t, "dve_m", B["dve_m"] + 60 + 3 * h)
                    wge(t, "act_m", B["act_m"] + 6)
                    inc(tpose(t, MISCP[0:32, 256:384], nmall[h][:, 0:32],
                                    id_f32[:, :]), "pe_m")
                for h in range(2 if pstage >= 4 else 0):
                    for c in range(8):
                        pidx = 8 * h + c
                        if pidx == 0:
                            wge(t, "act_m", B["act_m"] + 6)
                            wge(t, "dve_m", B["dve_m"] + 64)
                        else:
                            wge(t, "dve_m", B["dve_m"] + 64 + pidx)
                        inc(t.matmul(MISCP[0:2, 0:512], TdT0[0:64, 0:2],
                                     Qa[h][0:64, c * 512 : (c + 1) * 512],
                                     start=True, stop=True,
                                     skip_group_check=True), "pe_m")
                if pstage >= 4:
                    chk("pe_m", B["pe_m"] + 70)
                for kt in range(32 if pstage >= 5 else 0):
                    if kt >= 2:
                        wge(t, "dve_m", B["dve_m"] + 81 + kt - 2)
                    sub = (PA if kt % 2 == 0 else PB)[:, 0:128]
                    for kcc in range(KC):
                        t.matmul(sub,
                                 h1T[:, kcc * S + kt * 128 : kcc * S + (kt + 1) * 128],
                                 wqh[wsl][:, WQH_V + kcc * 128 : WQH_V + (kcc + 1) * 128],
                                 start=(kcc == 0), stop=False,
                                 skip_group_check=True)
                    ins = t.matmul(sub, ones_bf[0:1, 0:128],
                                   bvr[0:1, hp * 128 : (hp + 1) * 128],
                                   start=False, stop=True,
                                   skip_group_check=True)
                    inc(ins, "pe_m")
                if pstage >= 5:
                    chk("pe_m", B["pe_m"] + 102)
                for j in range(4 if pstage >= 6 else 0):
                    pe_ptile(t, 0, 0, j, B["dve_e"] + j)
                if pstage >= 6:
                    chk("pe_p", B["pe_p"] + 4)
                pdone(t, "pe_p" if pstage >= 6 else "pe_m")

            @block.vector
            def _(v):
                v.wait_ge(sems["phz"], bt())
                inc(v.memset(Qa[0][64:66, :], 0.0), "dve_m")
                inc(v.memset(Qa[1][64:66, :], 0.0), "dve_m")
                for p in range(16):
                    c = p % 8
                    wge(v, "pe_m", B["pe_m"] + 1 + p)
                    if p >= 8:
                        wge(v, "dma_q", B["dma_q"] + 16 * (c + 1))
                    bank = (PA if p % 2 == 0 else PB)[:, 0:512]
                    dst = Ka[0] if p < 8 else Qa[0]
                    bc = bkc if p < 8 else bqc
                    scl = 0.125 if p < 8 else None
                    op1 = ALU.mult if p < 8 else ALU.add
                    if p < 8:
                        v.tensor_scalar(dst[0:64, c * 512 : (c + 1) * 512],
                                        bank[0:64, :], bc[0:64, hp : hp + 1],
                                        scl, ALU.add, op1)
                        ins = v.tensor_scalar(QKstg[c][64:128, :],
                                              bank[64:128, :],
                                              bc[64:128, hp : hp + 1],
                                              scl, ALU.add, op1)
                    else:
                        v.tensor_scalar(dst[0:64, c * 512 : (c + 1) * 512],
                                        bank[0:64, :], bc[0:64, hp : hp + 1],
                                        None, ALU.add)
                        ins = v.tensor_scalar(QKstg[c][64:128, :],
                                              bank[64:128, :],
                                              bc[64:128, hp : hp + 1],
                                              None, ALU.add)
                    inc(ins, "dve_m")
                chk("dve_m", B["dve_m"] + 18)
                for h in range(2 if pstage >= 2 else 0):
                    src = Ka[h]
                    for ch in range(KC):
                        n = 8 * h + ch
                        if n >= 2:
                            pm = 17 + (n - 2) if n - 2 < 8 else 26 + (n - 2 - 8)
                            wge(v, "pe_m", B["pe_m"] + pm)
                        if h == 1:
                            wge(v, "dma_q", B["dma_q"] + 16 * (ch + 1))
                        inc(v.tensor_mul(Asb[ch % 2][0:64, :],
                                         src[0:64, ch * 512 : (ch + 1) * 512],
                                         src[0:64, ch * 512 : (ch + 1) * 512]),
                            "dve_m")
                    wge(v, "pe_m", B["pe_m"] + 24 + 9 * h)
                    inc(v.tensor_reduce(sml[:, 8 + h : 9 + h],
                                        MISCP[:, 32 * h : 32 * h + 32],
                                        axis=AX.X, op=ALU.max), "dve_m")
                    wge(v, "pe_m", B["pe_m"] + 25 + 9 * h)
                    inc(v.tensor_reduce(sml[0:1, 10 + h : 11 + h],
                                        MISCP[0:1, 256:384], axis=AX.X,
                                        op=ALU.max), "dve_m")
                if pstage >= 2:
                    chk("dve_m", B["dve_m"] + 38)
                for h in range(2 if pstage >= 2 else 0):
                    wge(v, "act_m", B["act_m"] + 1 + h)
                    inc(v.tensor_add(sml[0:1, 16 + h : 17 + h],
                                     sml[0:1, 14 + h : 15 + h], maxT[:, :]),
                        "dve_m")
                for h in range(2 if pstage >= 2 else 0):
                    wge(v, "pe_m", B["pe_m"] + 35 + h)
                    inc(v.tensor_copy(sml[:, 22 + h : 23 + h],
                                      MISCP[:, 100 + h : 101 + h]), "dve_m")
                for h in range(2 if pstage >= 3 else 0):
                    src = Qa[h]
                    for ch in range(KC):
                        n = 16 + 8 * h + ch
                        pm = 26 + (n - 2 - 8) if n - 2 < 16 else 37 + (n - 2 - 16)
                        wge(v, "pe_m", B["pe_m"] + pm)
                        if h == 1:
                            wge(v, "dma_q", B["dma_q"] + 128 + 16 * (ch + 1))
                        inc(v.tensor_mul(Asb[ch % 2][0:64, :],
                                         src[0:64, ch * 512 : (ch + 1) * 512],
                                         src[0:64, ch * 512 : (ch + 1) * 512]),
                            "dve_m")
                if pstage >= 3:
                    chk("dve_m", B["dve_m"] + 58)
                for h in range(2 if pstage >= 3 else 0):
                    wge(v, "act_m", B["act_m"] + 5 + h)
                    inc(v.tensor_scalar(nmall[h][:, :], mallb[:, :], -1.0,
                                        None, ALU.mult), "dve_m")
                    v.wait_ge(sems["dve_m"], cnt["dve_m"])
                    inc(v.tensor_scalar(nmall[h][:, :], nmall[h][:, :], 1.0,
                                        None, ALU.mult), "dve_m")
                    wge(v, "pe_m", B["pe_m"] + 53 + h)
                    if h == 1:
                        wge(v, "dma_q", B["dma_q"] + 272)
                    inc(v.tensor_copy(rowstg[0:32, :], MISCP[0:32, 256:384]),
                        "dve_m")
                if pstage >= 3:
                    chk("dve_m", B["dve_m"] + 64)
                for h in range(2 if pstage >= 4 else 0):
                    for c in range(8):
                        wge(v, "pe_m", B["pe_m"] + 55 + 8 * h + c)
                        wge(v, "dma_q", B["dma_q"] + 272 + 16 * h)
                        inc(v.tensor_tensor(
                            Qa[h][64:66, c * 512 : (c + 1) * 512],
                            MISCP[0:2, 0:512],
                            Qa[h][64:66, c * 512 : (c + 1) * 512],
                            ALU.add), "dve_m")
                if pstage >= 4:
                    chk("dve_m", B["dve_m"] + 80)
                for kt in range(32 if pstage >= 5 else 0):
                    wge(v, "pe_m", B["pe_m"] + 71 + kt)
                    sub = (PA if kt % 2 == 0 else PB)[:, 0:128]
                    v.tensor_copy(Va[0][:, kt * 65 : kt * 65 + 64], sub[:, 0:64])
                    ins = v.tensor_copy(Va[1][:, kt * 65 : kt * 65 + 64],
                                        sub[:, 64:128])
                    inc(ins, "dve_m")
                if pstage >= 5:
                    chk("dve_m", B["dve_m"] + 112)
                for j in range(4 if pstage >= 6 else 0):
                    dve_etile(v, 0, 0, j, B["pe_p"] + j + 1,
                              B["dma_e"] + 16 * (j - 1), j % 2)
                if pstage >= 6:
                    chk("dve_e", B["dve_e"] + 4)
                pdone(v, "dve_e" if pstage >= 6 else "dve_m")

            @block.scalar
            def _(sc):
                sc.wait_ge(sems["phz"], bt())
                for h in range(2 if pstage >= 2 else 0):
                    wge(sc, "dve_m", B["dve_m"] + 28 + 10 * h)
                    sc.activation(sml[0:1, 12 + h : 13 + h],
                                  sml[0:1, 10 + h : 11 + h], AF.Ln)
                    inc(sc.activation(sml[0:1, 14 + h : 15 + h],
                                      sml[0:1, 12 + h : 13 + h], AF.Exp,
                                      scale=0.5), "act_m")
                for h in range(2 if pstage >= 2 else 0):
                    wge(sc, "dve_m", B["dve_m"] + 39 + h)
                    sc.activation(sml[0:1, 18 + h : 19 + h],
                                  sml[0:1, 16 + h : 17 + h], AF.Ln)
                    inc(sc.activation(lncbf[0:1, h : h + 1],
                                      sml[0:1, 18 + h : 19 + h], AF.Copy),
                        "act_m")
                for h in range(2 if pstage >= 3 else 0):
                    wge(sc, "pe_m", B["pe_m"] + 52)
                    wge(sc, "dve_m", B["dve_m"] + 41 + h)
                    if h == 1:
                        wge(sc, "dve_m", B["dve_m"] + 59)
                    sc.activation(lnqq[:, :], MISCP[:, 32 * h : 32 * h + 32],
                                  AF.Ln)
                    inc(sc.activation(mallb[:, :], lnqq[:, :], AF.Exp,
                                      scale=0.5, bias=sml[:, 22 + h : 23 + h]),
                        "act_m")
                pdone(sc, "act_m")

            @block.gpsimd
            def _(g):
                g.wait_ge(sems["phz"], bt())
                for p in range(16):
                    c = p % 8
                    dst = Ka[1] if p < 8 else Qa[1]
                    wge(g, "dve_m", B["dve_m"] + 3 + p)
                    inc(g.dma_start(dst[0:64, c * 512 : (c + 1) * 512],
                                    QKstg[c][64:128, :]), "dma_q", 16)
                for h in range(2 if pstage >= 3 else 0):
                    wge(g, "dve_m", B["dve_m"] + 61 + 3 * h)
                    inc(g.dma_start(Qa[h][64:65, :], rowstg[0:32, :]),
                        "dma_q", 16)
                if pstage >= 3:
                    chk("dma_q", B["dma_q"] + 288)
                g.wait_ge(sems["dma_q"], cnt["dma_q"])
                pdone(g)

            @block.sync
            def _(s):
                s.wait_ge(sems["phz"], bt())
                for j in range(4 if pstage >= 6 else 0):
                    sync_etile(s, 0, j, 0,
                               B["dve_e"] + j + 1, 0,
                               B["dma_e"] + 16 * (j + 1), 0, j % 2)
                s.wait_ge(sems["dma_g"], cnt["dma_g"])
                pdone(s)

        end_phase()

        if prod_only:
            continue
        with nc.Block() as block:
            B = snap()

            def fills_pe(t, idx, kt):
                h, c = idx // 8, idx % 8
                lo, hi, Wb, EW, tiles = QG[c]
                g = 32 * idx + kt
                wge(t, "act_e", B["act_e"] + g - 2)
                if kt == lo and idx >= 1:
                    wge(t, "dma_g", B["dma_g"] + 64 * idx)
                bank = (PA, PB, PC)[g % 3][:, 0:512]
                if kt < lo:
                    ext = 65
                elif kt > hi:
                    ext = 66
                else:
                    ext = 64
                ins = t.matmul(bank, Ka[h][0:ext, kt * 128 : (kt + 1) * 128],
                               Qa[h][0:ext, c * 512 : (c + 1) * 512],
                               start=True, stop=(ext != 64),
                               skip_group_check=True)
                if ext == 64:
                    for j in range(4):
                        ins = t.matmul(
                            bank[:, j * 128 : (j + 1) * 128],
                            GBs[idx % 2][:, j * 1536 + (kt - lo) * 128 :
                                         j * 1536 + (kt - lo + 1) * 128],
                            id_bf[:, :], start=False, stop=(j == 3),
                            skip_group_check=True)
                inc(ins, "pe_f")
                chk("pe_f", B["pe_f"] + 32 * idx + kt + 1)

            @block.tensor
            def _(t):
                t.wait_ge(sems["phz"], bt())
                for idx in range(16):
                    h, c = idx // 8, idx % 8
                    fills_pe(t, idx, 0)
                    fills_pe(t, idx, 1)
                    for kt in range(34):
                        if kt >= 2:
                            ktA = kt - 2
                            gA = 32 * idx + ktA
                            wge(t, "act_e", B["act_e"] + gA + 1)
                            if ktA == 0:
                                wge(t, "dve_c", B["dve_c"] + idx)
                            inc(t.matmul(
                                CTX[0:65, 0:512],
                                Va[h][:, ktA * 65 : ktA * 65 + 65],
                                Asb[gA % 3][:, :],
                                start=(ktA == 0), stop=(ktA == 31)), "pe_av")
                        if kt + 2 <= 31:
                            fills_pe(t, idx, kt + 2)
                        if kt == 4 and idx >= 1:
                            hN = (idx - 1) // 8
                            wge(t, "act_l", B["act_l"] + idx)
                            wge(t, "dve_n", B["dve_n"] + idx - 1)
                            inc(t.matmul(MISCP[64 * hN : 64 * hN + 64, 0:512],
                                         ones_bf[0:1, 0:64],
                                         rl_sb[0:1, 0:512],
                                         start=True, stop=True), "pe_x")
                        if idx + 1 <= 15 and kt in (2, 6, 10, 14):
                            j = {2: 0, 6: 1, 10: 2, 14: 3}[kt]
                            h2, c2 = (idx + 1) // 8, (idx + 1) % 8
                            pe_ptile(t, h2, c2, j, B["dve_e"] + 4 * idx + j)
                    chk("pe_av", B["pe_av"] + 32 * (idx + 1))
                wge(t, "act_l", B["act_l"] + 16)
                wge(t, "dve_n", B["dve_n"] + 15)
                inc(t.matmul(MISCP[64:128, 0:512], ones_bf[0:1, 0:64],
                             rl_sb[0:1, 0:512], start=True, stop=True), "pe_x")
                chk("pe_x", B["pe_x"] + 16)
                pdone(t, "pe_x")

            @block.scalar
            def _(sc):
                sc.wait_ge(sems["phz"], bt())
                for idx in range(16):
                    for kt in range(32):
                        g = 32 * idx + kt
                        wge(sc, "pe_f", B["pe_f"] + g + 1)
                        wge(sc, "pe_av", B["pe_av"] + g - 2)
                        bank = (PA, PB, PC)[g % 3][:, 0:512]
                        inc(sc.activation(Asb[g % 3][:, :], bank, AF.Exp),
                            "act_e")
                        if kt == 1 and idx >= 1:
                            wge(sc, "dve_c", B["dve_c"] + idx)
                            sc.activation(rlt[0:1, :],
                                          lrow[(idx - 1) % 2][0:1, :], AF.Ln)
                            inc(sc.activation(rl_sb[0:1, :], rlt[0:1, :],
                                              AF.Exp, scale=-1.0), "act_l")
                wge(sc, "dve_c", B["dve_c"] + 16)
                sc.activation(rlt[0:1, :], lrow[1][0:1, :], AF.Ln)
                inc(sc.activation(rl_sb[0:1, :], rlt[0:1, :], AF.Exp,
                                  scale=-1.0), "act_l")
                chk("act_e", B["act_e"] + 512)
                chk("act_l", B["act_l"] + 16)
                pdone(sc, "act_l")

            def ct_slice(k):
                hP, cP = k // 8, k % 8
                return CTs[64 * hP : 64 * hP + 64, cP * 512 : (cP + 1) * 512]

            @block.vector
            def _(v):
                v.wait_ge(sems["phz"], bt())
                for idx in range(16):
                    if idx >= 1:
                        wge(v, "pe_av", B["pe_av"] + 32 * idx)
                        wge(v, "act_l", B["act_l"] + idx - 2)
                        v.tensor_copy(ct_slice(idx - 1), CTX[0:64, 0:512])
                        inc(v.tensor_copy(lrow[(idx - 1) % 2][0:1, :],
                                          CTX[64:65, 0:512]), "dve_c")
                    if idx >= 2:
                        hN = (idx - 2) // 8
                        wge(v, "pe_x", B["pe_x"] + idx - 1)
                        selfedge(v, v.tensor_copy(
                            rlb_sb[64 * hN : 64 * hN + 64, :],
                            MISCP[64 * hN : 64 * hN + 64, 0:512]))
                        inc(v.tensor_tensor(ct_slice(idx - 2), ct_slice(idx - 2),
                                            rlb_sb[64 * hN : 64 * hN + 64, :],
                                            ALU.mult), "dve_n")
                    if idx + 1 <= 15:
                        h2, c2 = (idx + 1) // 8, (idx + 1) % 8
                        for j in range(4):
                            n = 4 * idx + j
                            dve_etile(v, h2, c2, j,
                                      B["pe_p"] + n + 1,
                                      B["dma_e"] + 16 * (n - 1), n % 2)
                wge(v, "pe_av", B["pe_av"] + 512)
                wge(v, "act_l", B["act_l"] + 14)
                v.tensor_copy(ct_slice(15), CTX[0:64, 0:512])
                inc(v.tensor_copy(lrow[1][0:1, :], CTX[64:65, 0:512]), "dve_c")
                for k in (14, 15):
                    wge(v, "pe_x", B["pe_x"] + k + 1)
                    selfedge(v, v.tensor_copy(rlb_sb[64:128, :],
                                              MISCP[64:128, 0:512]))
                    inc(v.tensor_tensor(ct_slice(k), ct_slice(k),
                                        rlb_sb[64:128, :], ALU.mult), "dve_n")
                chk("dve_c", B["dve_c"] + 16)
                chk("dve_n", B["dve_n"] + 16)
                chk("dve_e", B["dve_e"] + 60)
                pdone(v, "dve_n")

            @block.sync
            def _(s):
                s.wait_ge(sems["phz"], bt())
                for idx in range(15):
                    c2 = (idx + 1) % 8
                    sp = (idx + 1) % 2
                    for j in range(4):
                        n = 4 * idx + j
                        dmag_guard = (B["dma_g"] + 16 * (4 * idx - 4)
                                      if idx >= 2 else 0)
                        sync_etile(s, c2, j, sp,
                                   B["dve_e"] + n + 1,
                                   dmag_guard,
                                   B["dma_e"] + 16 * (n + 1),
                                   B["pe_f"] + 32 * idx if idx >= 1 else 0,
                                   n % 2)
                chk("dma_g", B["dma_g"] + 16 * 60)
                pdone(s)

            @block.gpsimd
            def _(g):
                g.wait_ge(sems["phz"], bt())
                if hp + 1 < 8:
                    for w0, wsrc in ((WQH_Q, Wq), (WQH_K, Wk), (WQH_V, Wv)):
                        for kcc in range(KC):
                            inc(g.dma_start(
                                wqh[(hp + 1) % 2][:, w0 + kcc * 128 :
                                                  w0 + (kcc + 1) * 128],
                                wsrc[kcc * 128 : (kcc + 1) * 128,
                                     (hp + 1) * 128 : (hp + 2) * 128]),
                                "dma_w", 16)
                if dbg and hp == 0:
                    wge(g, "dve_n", B["dve_n"] + 16)
                    inc(g.dma_start(dbg_qa, Qa[0][:, :]), "dma_st", 16)
                    inc(g.dma_start(dbg_ka, Ka[0][:, :]), "dma_st", 16)
                    inc(g.dma_start(dbg_va, Va[0][:, :]), "dma_st", 16)
                    inc(g.dma_start(dbg_e, Esb[0][:, :]), "dma_st", 16)
                    inc(g.dma_start(dbg_gb, GBs[0][:, :]), "dma_st", 16)
                    inc(g.dma_start(dbg_a, Asb[1][:, :]), "dma_st", 16)
                    inc(g.dma_start(dbg_ct, CTs[:, :]), "dma_st", 16)
                    inc(g.dma_start(dbg_sml, sml[:, :]), "dma_st", 16)
                wge(g, "dve_n", B["dve_n"] + 16)
                inc(g.dma_start(
                    bass.AP(ct_dram, hp * 128,
                            [[1024, 128], [131072, 32], [1, 128]]),
                    bass.AP(CTs, 0, [[4096, 128], [128, 32], [1, 128]])),
                    "dma_ct", 16)
                g.wait_ge(sems["dma_ct"], B["dma_ct"] + 16)
                g.wait_ge(sems["dma_st"], cnt["dma_st"])
                pdone(g)

        end_phase()

    if nhp is not None:
        with nc.Block() as block:
            @block.gpsimd
            def _(g):
                g.wait_ge(sems["phz"], bt())
                for i in range(4):
                    if not prod_only:
                        inc(g.dma_start(out[i * 128 : (i + 1) * 128, :],
                                        CTs[:, i * 1024 : (i + 1) * 1024]),
                            "dma_st", 16)
                    inc(g.dma_start(out[512 + i * 128 : 512 + (i + 1) * 128, :],
                                    Qa[0][:, i * 1024 : (i + 1) * 1024]),
                        "dma_st", 16)
                    inc(g.dma_start(out[1024 + i * 128 : 1024 + (i + 1) * 128, :],
                                    Ka[0][:, i * 1024 : (i + 1) * 1024]),
                        "dma_st", 16)
                g.wait_ge(sems["dma_st"], cnt["dma_st"])
                pdone(g)
            for eng_name in ("tensor", "vector", "scalar", "sync"):
                def _mk(en):
                    def _f(e):
                        e.wait_ge(sems["phz"], bt())
                        pdone(e)
                    return _f
                getattr(block, eng_name)(_mk(eng_name))
        end_phase()
        s1p.close()
        s1.close()
        base_st.close()
        return nc

    s1p.close()
    s1.close()

    tail_st = contextlib.ExitStack()
    mtl = tail_st.enter_context
    hTs = [mtl(nc.sbuf_tensor(f"hTs{i}", [128, D], BF16)) for i in range(2)]
    xob = [mtl(nc.sbuf_tensor(f"xob{i}", [128, D], F32)) for i in range(2)]
    X = mtl(nc.psum_tensor("X", [128, 1536], F32))
    Y = mtl(nc.psum_tensor("Y", [128, 1024], F32))
    stg = [mtl(nc.psum_tensor(f"stg{i}", [128, 512], BF16)) for i in range(2)]
    misc = MISCP

    s2 = contextlib.ExitStack()
    m2 = s2.enter_context
    wo_sb = m2(nc.sbuf_tensor("wo_sb", [128, 8 * 1024], BF16))
    ct_in = [m2(nc.sbuf_tensor(f"ct_in{i}", [128, 1024], BF16)) for i in range(2)]
    dst1 = out if upto == 1 else x1_dram

    with nc.Block() as block:
        B = snap()

        @block.gpsimd
        def _(g):
            g.wait_ge(sems["phz"], bt())
            for c8 in range(8):
                inc(g.dma_start(wo_sb[:, c8 * 1024 : (c8 + 1) * 1024],
                                Wo[c8 * 128 : (c8 + 1) * 128, :]), "dma_w", 16)
            ins = None
            for tt in range(NT):
                if tt >= 2:
                    wge(g, "pe_m", B["pe_m"] + tt - 1)
                inc(g.dma_start(
                    ct_in[tt % 2][:, :],
                    bass.AP(ct_dram, tt * 131072,
                            [[1024, 128], [1, 1024]])), "dma_x", 16)
                if tt >= 2:
                    wge(g, "dve_m", B["dve_m"] + tt - 1)
                inc(g.dma_start(xbuf[tt % 2][:, :],
                                x0[tt * 128 : (tt + 1) * 128, :]), "dma_x", 16)
                if tt >= 1:
                    wge(g, "dve_m", B["dve_m"] + tt)
                    ins = inc(g.dma_start(dst1[(tt - 1) * 128 : tt * 128, :],
                                          xob[(tt - 1) % 2][:, :]), "dma_st", 16)
            wge(g, "dve_m", B["dve_m"] + NT)
            ins = inc(g.dma_start(dst1[(NT - 1) * 128 : NT * 128, :],
                                  xob[(NT - 1) % 2][:, :]), "dma_st", 16)
            g.wait_ge(sems["dma_st"], B["dma_st"] + 16 * NT)
            pdone(g)

        @block.tensor
        def _(t):
            t.wait_ge(sems["phz"], bt())
            wge(t, "dma_w", B["dma_w"] + 128)
            ins = None
            for tt in range(NT):
                wge(t, "dma_x", B["dma_x"] + 32 * tt + 16)
                if tt >= 2:
                    wge(t, "dve_m", B["dve_m"] + tt - 1)
                par = (X if tt % 2 == 0 else Y)[:, 0:1024]
                for half in range(2):
                    for c8 in range(8):
                        t.matmul(par[:, half * 512 : (half + 1) * 512],
                                 ct_in[tt % 2][:, c8 * 128 : (c8 + 1) * 128],
                                 wo_sb[:, c8 * 1024 + half * 512 :
                                       c8 * 1024 + (half + 1) * 512],
                                 start=(c8 == 0), stop=False)
                    ins = t.matmul(par[:, half * 512 : (half + 1) * 512],
                                   ones_bf[0:1, 0:128],
                                   bor[0:1, half * 512 : (half + 1) * 512],
                                   start=False, stop=True)
                inc(ins, "pe_m")
            pdone(t, "pe_m")

        @block.vector
        def _(v):
            v.wait_ge(sems["phz"], bt())
            ins = None
            for tt in range(NT):
                wge(v, "pe_m", B["pe_m"] + tt + 1)
                wge(v, "dma_x", B["dma_x"] + 32 * (tt + 1))
                if tt >= 2:
                    wge(v, "dma_st", B["dma_st"] + 16 * (tt - 1))
                par = (X if tt % 2 == 0 else Y)[:, 0:1024]
                ins = inc(v.tensor_add(xob[tt % 2][:, :], par,
                                       xbuf[tt % 2][:, :]), "dve_m")
            pdone(v, "dve_m")

        @block.scalar
        def _(sc):
            sc.wait_ge(sems["phz"], bt())
            pdone(sc)

        @block.sync
        def _(s):
            s.wait_ge(sems["phz"], bt())
            pdone(s)

    end_phase()
    s2.close()

    if upto == 1:
        tail_st.close()
        base_st.close()
        return nc

    def ffn_phase(src, dst, tag):
        s3 = contextlib.ExitStack()
        m3 = s3.enter_context
        bw_sb = m3(nc.sbuf_tensor(f"bw_sb{tag}", [128, 8 * 1024], BF16))
        gw_sb = m3(nc.sbuf_tensor(f"gw_sb{tag}", [128, 8 * 16], BF16))
        aA_sb = m3(nc.sbuf_tensor(f"aA_sb{tag}", [128, 8 * 128], BF16))
        bB_sb = m3(nc.sbuf_tensor(f"bB_sb{tag}", [128, 1024], BF16))
        e_sb = m3(nc.sbuf_tensor(f"e_sb{tag}", [128, 16], F32))
        tp_sb = m3(nc.sbuf_tensor(f"tp_sb{tag}", [128, 16], F32))
        ew_sb = m3(nc.sbuf_tensor(f"ew_sb{tag}", [128, 8], F32))
        ewx_sb = m3(nc.sbuf_tensor(f"ewx_sb{tag}", [128, 128], F32))
        zw_bf = m3(nc.sbuf_tensor(f"zw_bf{tag}", [128, 128], BF16))
        zwT_sb = m3(nc.sbuf_tensor(f"zwT_sb{tag}", [128, 128], BF16))
        tp2_sb = m3(nc.sbuf_tensor(f"tp2_sb{tag}", [128, 16], F32))
        bbr = m3(nc.sbuf_tensor(f"bbr{tag}", [1, D], BF16))

        with nc.Block() as block:
            B = snap()

            @block.gpsimd
            def _(g):
                g.wait_ge(sems["phz"], bt())
                for c8 in range(8):
                    inc(g.dma_start(bw_sb[:, c8 * 1024 : (c8 + 1) * 1024],
                                    basewT[c8 * 128 : (c8 + 1) * 128, :]),
                        "dma_w", 16)
                    inc(g.dma_start(gw_sb[:, c8 * 16 : c8 * 16 + 10],
                                    gatewT[c8 * 128 : (c8 + 1) * 128, :]),
                        "dma_w", 16)
                    inc(g.dma_start(aA_sb[:, c8 * 128 : (c8 + 1) * 128],
                                    AcatT[c8 * 128 : (c8 + 1) * 128, :]),
                        "dma_w", 16)
                inc(g.dma_start(bB_sb[:, :], BcatT), "dma_w", 16)
                inc(g.dma_start(bbr[:, :], baseb_row), "dma_w", 16)
                ins = None
                for tt in range(NT):
                    if tt >= 2:
                        wge(g, "dve_m", B["dve_m"] + 9 * (tt - 2) + 9)
                    inc(g.dma_start(xbuf[tt % 2][:, :],
                                    src[tt * 128 : (tt + 1) * 128, :]),
                        "dma_x", 16)
                    if tt >= 1:
                        wge(g, "dve_m", B["dve_m"] + 9 * (tt - 1) + 9)
                        ins = inc(g.dma_start(dst[(tt - 1) * 128 : tt * 128, :],
                                              xob[(tt - 1) % 2][:, :]),
                                  "dma_st", 16)
                wge(g, "dve_m", B["dve_m"] + 9 * (NT - 1) + 9)
                inc(g.dma_start(dst[(NT - 1) * 128 : NT * 128, :],
                                xob[(NT - 1) % 2][:, :]), "dma_st", 16)
                g.wait_ge(sems["dma_st"], cnt["dma_st"])
                pdone(g)

            @block.vector
            def _(v):
                v.wait_ge(sems["phz"], bt())
                ins = None
                for tt in range(NT):
                    wge(v, "dma_x", B["dma_x"] + 16 * (tt + 1))
                    inc(ln_dve_stats(v, xbuf[tt % 2][:, :]), "dve_m")
                    wge(v, "act_m", B["act_m"] + 3 * tt + 1)
                    if tt >= 2:
                        wge(v, "pe_m", B["pe_m"] + 5 * (tt - 2) + 1)
                    inc(ln_dve_apply(v, xbuf[tt % 2][:, :],
                                     hbuf[tt % 2][:, :]), "dve_m")
                    wge(v, "pe_m", B["pe_m"] + 5 * tt + 1)
                    for k8 in range(KC):
                        ins = v.tensor_copy(
                            hTs[tt % 2][:, k8 * 128 : (k8 + 1) * 128],
                            stg[k8 // 4][:, (k8 % 4) * 128 : (k8 % 4 + 1) * 128])
                    inc(ins, "dve_m")
                    wge(v, "pe_m", B["pe_m"] + 5 * tt + 2)
                    ins = inc(v.reduce_max(sml[:, 8:9], misc[:, 0:10],
                                           axis=AX.X, negate=True), "dve_m")
                    wge(v, "act_m", B["act_m"] + 3 * tt + 2)
                    selfedge(v, v.reduce_max(sml[:, 9:10], e_sb[:, 0:10],
                                              axis=AX.X))
                    selfedge(v, v.tensor_scalar(tp2_sb[:, 0:10], ones32[:, 0:10],
                                    sml[:, 9:10], None, ALU.mult))
                    selfedge(v, v.tensor_tensor(ewx_sb[:, 0:10], e_sb[:, 0:10],
                                    tp2_sb[:, 0:10], ALU.is_equal))
                    selfedge(v, v.scalar_tensor_tensor(tp_sb[:, 0:10],
                                    ewx_sb[:, 0:10], -2.0,
                                    e_sb[:, 0:10], ALU.mult, ALU.add))
                    selfedge(v, v.reduce_max(sml[:, 10:11], tp_sb[:, 0:10],
                                             axis=AX.X))
                    selfedge(v, v.tensor_scalar(tp2_sb[:, 0:10], ones32[:, 0:10],
                                    sml[:, 10:11], None, ALU.mult))
                    selfedge(v, v.tensor_tensor(ewx_sb[:, 16:26], tp_sb[:, 0:10],
                                    tp2_sb[:, 0:10], ALU.is_equal))
                    selfedge(v, v.tensor_add(tp_sb[:, 0:10], ewx_sb[:, 16:26],
                                             ewx_sb[:, 0:10]))
                    selfedge(v, v.tensor_mul(ew_sb[:, 0:8], tp_sb[:, 0:8],
                                             e_sb[:, 0:8]))
                    selfedge(v, v.reduce_sum(sml[:, 12:13], ew_sb[:, 0:8],
                                             axis=AX.X))
                    selfedge(v, v.tensor_scalar(sml[:, 13:14], sml[:, 12:13],
                                                1e30, None, ALU.mult))
                    selfedge(v, v.tensor_tensor(sml[:, 11:12], sml[:, 13:14],
                                                ones32[:, 0:1], ALU.min))
                    selfedge(v, v.scalar_tensor_tensor(sml[:, 14:15],
                                    sml[:, 11:12], -1.0,
                                    ones32[:, 0:1], ALU.mult, ALU.add))
                    ins = inc(v.tensor_add(sml[:, 14:15], sml[:, 14:15],
                                           sml[:, 12:13]), "dve_m")
                    wge(v, "act_m", B["act_m"] + 3 * tt + 3)
                    selfedge(v, v.tensor_scalar(ew_sb[:, 0:8], ew_sb[:, 0:8],
                                                sml[:, 15:16], None, ALU.mult))
                    for e8 in range(8):
                        ins = v.tensor_scalar(ewx_sb[:, e8 * 16 : (e8 + 1) * 16],
                                              ones32[:, 0:16],
                                              ew_sb[:, e8 : e8 + 1], None,
                                              ALU.mult)
                    inc(ins, "dve_m")
                    wge(v, "dve_m", cnt["dve_m"])
                    wge(v, "pe_m", B["pe_m"] + 5 * tt + 3)
                    ins = inc(v.tensor_tensor(zw_bf[:, :], X[:, 1024:1152],
                                              ewx_sb[:, :], ALU.mult), "dve_m")
                    wge(v, "pe_m", B["pe_m"] + 5 * tt + 4)
                    ins = inc(v.tensor_copy(zwT_sb[:, :], stg[0][:, 0:128]),
                              "dve_m")
                    wge(v, "pe_m", B["pe_m"] + 5 * tt + 5)
                    if tt >= 2:
                        wge(v, "dma_st", B["dma_st"] + 16 * (tt - 1))
                    par = (X if tt % 2 == 0 else Y)[:, 0:1024]
                    ins = inc(v.tensor_add(xob[tt % 2][:, :], par,
                                           xbuf[tt % 2][:, :]), "dve_m")
                pdone(v, "dve_m")

            @block.scalar
            def _(sc):
                sc.wait_ge(sems["phz"], bt())
                ins = None
                for tt in range(NT):
                    wge(sc, "dve_m", B["dve_m"] + 9 * tt + 1)
                    ins = inc(ln_act_rstd(sc), "act_m")
                    wge(sc, "dve_m", B["dve_m"] + 9 * tt + 4)
                    ins = inc(sc.activation(e_sb[:, 0:10], misc[:, 0:10],
                                            AF.Exp, bias=sml[:, 8:9]),
                              "act_m")
                    wge(sc, "dve_m", B["dve_m"] + 9 * tt + 5)
                    sc.activation(sml[:, 16:17], sml[:, 14:15], AF.Ln)
                    ins = inc(sc.activation(sml[:, 15:16], sml[:, 16:17],
                                            AF.Exp, scale=-1.0,
                                            bias=sml[:, 7:8]), "act_m")
                pdone(sc, "act_m")

            @block.tensor
            def _(t):
                t.wait_ge(sems["phz"], bt())
                wge(t, "dma_w", B["dma_w"] + 26 * 16)
                ins = None
                for tt in range(NT):
                    wge(t, "dve_m", B["dve_m"] + 9 * tt + 2)
                    if tt >= 1:
                        wge(t, "dve_m", B["dve_m"] + 9 * (tt - 1) + 8)
                    for k8 in range(KC):
                        ins = tpose(t, 
                            stg[k8 // 4][:, (k8 % 4) * 128 : (k8 % 4 + 1) * 128],
                            hbuf[tt % 2][:, k8 * 128 : (k8 + 1) * 128],
                            id_bf[:, :])
                    inc(ins, "pe_m")
                    wge(t, "dve_m", B["dve_m"] + 9 * tt + 3)
                    if tt >= 1:
                        wge(t, "dve_m", B["dve_m"] + 9 * (tt - 1) + 7)
                        wge(t, "act_m", B["act_m"] + 3 * (tt - 1) + 2)
                    for k8 in range(KC):
                        ins = t.matmul(misc[:, 0:10],
                                       hTs[tt % 2][:, k8 * 128 : (k8 + 1) * 128],
                                       gw_sb[:, k8 * 16 : k8 * 16 + 10],
                                       start=(k8 == 0), stop=(k8 == 7))
                    inc(ins, "pe_m")
                    for k8 in range(KC):
                        ins = t.matmul(X[:, 1024:1152],
                                       hTs[tt % 2][:, k8 * 128 : (k8 + 1) * 128],
                                       aA_sb[:, k8 * 128 : (k8 + 1) * 128],
                                       start=(k8 == 0), stop=(k8 == 7))
                    inc(ins, "pe_m")
                    wge(t, "dve_m", B["dve_m"] + 9 * tt + 7)
                    ins = inc(tpose(t, stg[0][:, 0:128], zw_bf[:, :],
                                          id_bf[:, :]), "pe_m")
                    wge(t, "dve_m", B["dve_m"] + 9 * tt + 8)
                    if tt >= 2:
                        wge(t, "dve_m", B["dve_m"] + 9 * (tt - 2) + 9)
                    par = (X if tt % 2 == 0 else Y)[:, 0:1024]
                    for half in range(2):
                        sl = par[:, half * 512 : (half + 1) * 512]
                        for k8 in range(KC):
                            t.matmul(sl,
                                     hTs[tt % 2][:, k8 * 128 : (k8 + 1) * 128],
                                     bw_sb[:, k8 * 1024 + half * 512 :
                                           k8 * 1024 + (half + 1) * 512],
                                     start=(k8 == 0), stop=False)
                        t.matmul(sl, ones_bf[0:1, 0:128],
                                 bbr[0:1, half * 512 : (half + 1) * 512],
                                 start=False, stop=False)
                        ins = t.matmul(sl, zwT_sb[:, :],
                                       bB_sb[:, half * 512 : (half + 1) * 512],
                                       start=False, stop=True)
                    inc(ins, "pe_m")
                pdone(t, "pe_m")

            @block.sync
            def _(s):
                s.wait_ge(sems["phz"], bt())
                pdone(s)

        end_phase()
        s3.close()

    ffn_phase(x1_dram, out if upto == 2 else x2_dram, "f1")
    if upto == 2:
        tail_st.close()
        base_st.close()
        return nc

    s4 = contextlib.ExitStack()
    m4 = s4.enter_context
    w2_sb = m4(nc.sbuf_tensor("w2_sb", [128, 16 * 1024], BF16))
    k_sb = m4(nc.sbuf_tensor("k_sb", [128, 32 * 1024], BF16))
    v_sb = m4(nc.sbuf_tensor("v_sb", [128, 32 * 1024], BF16))
    kv_sb = m4(nc.sbuf_tensor("kv_sb", [128, 8 * 1024], BF16))
    qT_sb = m4(nc.sbuf_tensor("qT_sb", [128, 1024], BF16))

    with nc.Block() as block:
        B = snap()

        @block.gpsimd
        def _(g):
            g.wait_ge(sems["phz"], bt())
            for c8 in range(8):
                inc(g.dma_start(w2_sb[:, c8 * 1024 : (c8 + 1) * 1024],
                                w_k2[c8 * 128 : (c8 + 1) * 128, :]), "dma_w", 16)
                inc(g.dma_start(w2_sb[:, 8192 + c8 * 1024 : 8192 + (c8 + 1) * 1024],
                                w_v2[c8 * 128 : (c8 + 1) * 128, :]), "dma_w", 16)
            ins = None
            for tt in range(NT):
                if tt >= 2:
                    wge(g, "dve_m", B["dve_m"] + 5 * (tt - 2) + 2)
                ins = inc(g.dma_start(xbuf[tt % 2][:, :],
                                      x2_dram[tt * 128 : (tt + 1) * 128, :]),
                          "dma_x", 16)
                wge(g, "dve_m", B["dve_m"] + 5 * tt + 3)
                inc(g.dma_start(h2t_dram[tt * 128 : (tt + 1) * 128, :],
                                hTs[tt % 2][:, :]), "dma_st", 16)
            g.wait_ge(sems["dma_st"], B["dma_st"] + 16 * NT)
            pdone(g)

        @block.vector
        def _(v):
            v.wait_ge(sems["phz"], bt())
            ins = None
            for tt in range(NT):
                wge(v, "dma_x", B["dma_x"] + 16 * (tt + 1))
                inc(ln_dve_stats(v, xbuf[tt % 2][:, :]), "dve_m")
                wge(v, "act_m", B["act_m"] + tt + 1)
                if tt >= 2:
                    wge(v, "pe_m", B["pe_m"] + 3 * (tt - 2) + 1)
                inc(ln_dve_apply(v, xbuf[tt % 2][:, :], hbuf[tt % 2][:, :]),
                    "dve_m")
                wge(v, "pe_m", B["pe_m"] + 3 * tt + 1)
                if tt >= 2:
                    wge(v, "dma_st", B["dma_st"] + 16 * (tt - 1))
                for k8 in range(KC):
                    ins = v.tensor_copy(
                        hTs[tt % 2][:, k8 * 128 : (k8 + 1) * 128],
                        stg[k8 // 4][:, (k8 % 4) * 128 : (k8 % 4 + 1) * 128])
                inc(ins, "dve_m")
                wge(v, "pe_m", B["pe_m"] + 3 * tt + 2)
                par = (X if tt % 2 == 0 else Y)[:, 0:1024]
                v.tensor_copy(k_sb[:, tt * 1024 : tt * 1024 + 512], par[:, 0:512])
                ins = v.tensor_copy(k_sb[:, tt * 1024 + 512 : (tt + 1) * 1024],
                                    par[:, 512:1024])
                inc(ins, "dve_m")
                wge(v, "pe_m", B["pe_m"] + 3 * tt + 3)
                anti = (Y if tt % 2 == 0 else X)[:, 0:1024]
                v.tensor_copy(v_sb[:, tt * 1024 : tt * 1024 + 512], anti[:, 0:512])
                ins = v.tensor_copy(v_sb[:, tt * 1024 + 512 : (tt + 1) * 1024],
                                    anti[:, 512:1024])
                inc(ins, "dve_m")
            pdone(v, "dve_m")

        @block.scalar
        def _(sc):
            sc.wait_ge(sems["phz"], bt())
            ins = None
            for tt in range(NT):
                wge(sc, "dve_m", B["dve_m"] + 5 * tt + 1)
                ins = inc(ln_act_rstd(sc), "act_m")
            pdone(sc, "act_m")

        @block.tensor
        def _(t):
            t.wait_ge(sems["phz"], bt())
            wge(t, "dma_w", B["dma_w"] + 16 * 16)
            ins = None
            for tt in range(NT):
                wge(t, "dve_m", B["dve_m"] + 5 * tt + 2)
                if tt >= 1:
                    wge(t, "dve_m", B["dve_m"] + 5 * (tt - 1) + 3)
                for k8 in range(KC):
                    ins = tpose(t, 
                        stg[k8 // 4][:, (k8 % 4) * 128 : (k8 % 4 + 1) * 128],
                        hbuf[tt % 2][:, k8 * 128 : (k8 + 1) * 128], id_bf[:, :])
                inc(ins, "pe_m")
                wge(t, "dve_m", B["dve_m"] + 5 * tt + 3)
                if tt >= 1:
                    wge(t, "dve_m", B["dve_m"] + 5 * (tt - 1) + 5)
                par = (X if tt % 2 == 0 else Y)[:, 0:1024]
                for half in range(2):
                    for k8 in range(KC):
                        ins = t.matmul(par[:, half * 512 : (half + 1) * 512],
                                       hTs[tt % 2][:, k8 * 128 : (k8 + 1) * 128],
                                       w2_sb[:, k8 * 1024 + half * 512 :
                                             k8 * 1024 + (half + 1) * 512],
                                       start=(k8 == 0), stop=(k8 == 7))
                inc(ins, "pe_m")
                if tt >= 1:
                    wge(t, "dve_m", B["dve_m"] + 5 * (tt - 1) + 4)
                anti = (Y if tt % 2 == 0 else X)[:, 0:1024]
                for half in range(2):
                    for k8 in range(KC):
                        ins = t.matmul(anti[:, half * 512 : (half + 1) * 512],
                                       hTs[tt % 2][:, k8 * 128 : (k8 + 1) * 128],
                                       w2_sb[:, 8192 + k8 * 1024 + half * 512 :
                                             8192 + k8 * 1024 + (half + 1) * 512],
                                       start=(k8 == 0), stop=(k8 == 7))
                inc(ins, "pe_m")
            pdone(t, "pe_m")

        @block.sync
        def _(s):
            s.wait_ge(sems["phz"], bt())
            pdone(s)

    end_phase()

    with nc.Block() as block:
        B = snap()

        @block.tensor
        def _(t):
            t.wait_ge(sems["phz"], bt())
            ins = None
            for r in range(16):
                ac, bh = r // 2, r % 2
                if r >= 2:
                    wge(t, "dve_m", B["dve_m"] + r - 1)
                par = (X if r % 2 == 0 else Y)[:, 0:512]
                for tt in range(NT):
                    ins = t.matmul(par,
                                   k_sb[:, tt * 1024 + ac * 128 :
                                        tt * 1024 + (ac + 1) * 128],
                                   v_sb[:, tt * 1024 + bh * 512 :
                                        tt * 1024 + (bh + 1) * 512],
                                   start=(tt == 0), stop=(tt == NT - 1))
                inc(ins, "pe_m")
            pdone(t, "pe_m")

        @block.vector
        def _(v):
            v.wait_ge(sems["phz"], bt())
            ins = None
            for r in range(16):
                ac, bh = r // 2, r % 2
                wge(v, "pe_m", B["pe_m"] + r + 1)
                par = (X if r % 2 == 0 else Y)[:, 0:512]
                ins = inc(v.tensor_copy(
                    kv_sb[:, ac * 1024 + bh * 512 : ac * 1024 + (bh + 1) * 512],
                    par), "dve_m")
            pdone(v, "dve_m")

        @block.scalar
        def _(sc):
            sc.wait_ge(sems["phz"], bt())
            pdone(sc)

        @block.gpsimd
        def _(g):
            g.wait_ge(sems["phz"], bt())
            for c8 in range(8):
                inc(g.dma_start(w2_sb[:, c8 * 1024 : (c8 + 1) * 1024],
                                w_q2[c8 * 128 : (c8 + 1) * 128, :]), "dma_w", 16)
            g.wait_ge(sems["dma_w"], B["dma_w"] + 8 * 16)
            pdone(g)

        @block.sync
        def _(s):
            s.wait_ge(sems["phz"], bt())
            pdone(s)

    end_phase()

    dst3 = out if upto == 3 else x3_dram
    with nc.Block() as block:
        B = snap()

        @block.gpsimd
        def _(g):
            g.wait_ge(sems["phz"], bt())
            ins = None
            for tt in range(NT):
                if tt >= 2:
                    wge(g, "pe_m", B["pe_m"] + 2 * (tt - 2) + 1)
                inc(g.dma_start(hTs[tt % 2][:, :],
                                h2t_dram[tt * 128 : (tt + 1) * 128, :]),
                    "dma_x", 16)
                if tt >= 2:
                    wge(g, "dve_m", B["dve_m"] + 2 * (tt - 2) + 2)
                inc(g.dma_start(xbuf[tt % 2][:, :],
                                x2_dram[tt * 128 : (tt + 1) * 128, :]),
                    "dma_x", 16)
                if tt >= 1:
                    wge(g, "dve_m", B["dve_m"] + 2 * (tt - 1) + 2)
                    ins = inc(g.dma_start(dst3[(tt - 1) * 128 : tt * 128, :],
                                          xob[(tt - 1) % 2][:, :]), "dma_st", 16)
            wge(g, "dve_m", B["dve_m"] + 2 * (NT - 1) + 2)
            inc(g.dma_start(dst3[(NT - 1) * 128 : NT * 128, :],
                            xob[(NT - 1) % 2][:, :]), "dma_st", 16)
            g.wait_ge(sems["dma_st"], B["dma_st"] + 16 * NT)
            pdone(g)

        @block.vector
        def _(v):
            v.wait_ge(sems["phz"], bt())
            ins = None
            for tt in range(NT):
                wge(v, "pe_m", B["pe_m"] + 2 * tt + 1)
                if tt >= 1:
                    wge(v, "pe_m", B["pe_m"] + 2 * (tt - 1) + 2)
                par = (X if tt % 2 == 0 else Y)[:, 0:1024]
                v.tensor_copy(qT_sb[:, 0:512], par[:, 0:512])
                ins = v.tensor_copy(qT_sb[:, 512:1024], par[:, 512:1024])
                inc(ins, "dve_m")
                wge(v, "pe_m", B["pe_m"] + 2 * tt + 2)
                wge(v, "dma_x", B["dma_x"] + 32 * (tt + 1))
                if tt >= 2:
                    wge(v, "dma_st", B["dma_st"] + 16 * (tt - 1))
                anti = (Y if tt % 2 == 0 else X)[:, 0:1024]
                ins = inc(v.scalar_tensor_tensor(xob[tt % 2][:, :], anti, 32.0,
                                                 xbuf[tt % 2][:, :], ALU.mult,
                                                 ALU.add), "dve_m")
            pdone(v, "dve_m")

        @block.scalar
        def _(sc):
            sc.wait_ge(sems["phz"], bt())
            pdone(sc)

        @block.tensor
        def _(t):
            t.wait_ge(sems["phz"], bt())
            ins = None
            for tt in range(NT):
                wge(t, "dma_x", B["dma_x"] + 32 * tt + 16)
                if tt >= 2:
                    wge(t, "dve_m", B["dve_m"] + 2 * (tt - 2) + 1)
                if tt >= 1:
                    wge(t, "dve_m", B["dve_m"] + 2 * (tt - 1) + 2)
                par = (X if tt % 2 == 0 else Y)[:, 0:1024]
                for at in range(8):
                    for k8 in range(KC):
                        ins = t.matmul(par[:, at * 128 : (at + 1) * 128],
                                       w2_sb[:, k8 * 1024 + at * 128 :
                                             k8 * 1024 + (at + 1) * 128],
                                       hTs[tt % 2][:, k8 * 128 : (k8 + 1) * 128],
                                       start=(k8 == 0), stop=(k8 == 7))
                inc(ins, "pe_m")
                wge(t, "dve_m", B["dve_m"] + 2 * tt + 1)
                if tt >= 2:
                    wge(t, "dve_m", B["dve_m"] + 2 * (tt - 2) + 2)
                anti = (Y if tt % 2 == 0 else X)[:, 0:1024]
                for half in range(2):
                    for ac in range(8):
                        ins = t.matmul(anti[:, half * 512 : (half + 1) * 512],
                                       qT_sb[:, ac * 128 : (ac + 1) * 128],
                                       kv_sb[:, ac * 1024 + half * 512 :
                                             ac * 1024 + (half + 1) * 512],
                                       start=(ac == 0), stop=(ac == 7))
                inc(ins, "pe_m")
            pdone(t, "pe_m")

        @block.sync
        def _(s):
            s.wait_ge(sems["phz"], bt())
            pdone(s)

    end_phase()

    if upto == 3:
        s4.close()
        tail_st.close()
        base_st.close()
        return nc

    s4.close()

    ffn_phase(x3_dram, out, "f2")

    tail_st.close()
    base_st.close()
    return nc




def kernel(L0_attn_a_q, L0_attn_a_Wq, L0_attn_a_bq, L0_attn_a_Wk, L0_attn_a_bk,
           L0_attn_a_Wv, L0_attn_a_bv, L0_attn_a_Wo, L0_attn_a_bo,
           L0_attn_a_rel_table, L0_attn_b_w_q, L0_attn_b_w_k, L0_attn_b_w_v,
           L0_ffn_base_w, L0_ffn_base_b, L0_ffn_gate_w, L0_ffn_lora_A,
           L0_ffn_lora_B, upto=4, dbg=False):
    f = np.float32
    x = np.ascontiguousarray(np.asarray(L0_attn_a_q, dtype=f))
    Wq = np.ascontiguousarray(np.asarray(L0_attn_a_Wq, dtype=f))
    Wk = np.ascontiguousarray(np.asarray(L0_attn_a_Wk, dtype=f))
    Wv = np.ascontiguousarray(np.asarray(L0_attn_a_Wv, dtype=f))
    Wo = np.ascontiguousarray(np.asarray(L0_attn_a_Wo, dtype=f))
    bq = np.asarray(L0_attn_a_bq, dtype=f)
    bk = np.asarray(L0_attn_a_bk, dtype=f)
    bv = np.asarray(L0_attn_a_bv, dtype=f)
    bo = np.asarray(L0_attn_a_bo, dtype=f)
    rel = np.asarray(L0_attn_a_rel_table, dtype=f)
    wq2 = np.ascontiguousarray(np.asarray(L0_attn_b_w_q, dtype=f))
    wk2 = np.ascontiguousarray(np.asarray(L0_attn_b_w_k, dtype=f))
    wv2 = np.ascontiguousarray(np.asarray(L0_attn_b_w_v, dtype=f))
    base_w = np.asarray(L0_ffn_base_w, dtype=f)
    base_b = np.asarray(L0_ffn_base_b, dtype=f)
    gate_w = np.asarray(L0_ffn_gate_w, dtype=f)
    lora_A = np.asarray(L0_ffn_lora_A, dtype=f)
    lora_B = np.asarray(L0_ffn_lora_B, dtype=f)

    relT = np.ascontiguousarray(rel.T)
    relT2 = np.ascontiguousarray(np.concatenate([relT, relT], axis=0))
    common = {
        "Wq": Wq, "Wk": Wk, "Wv": Wv, "Wo": Wo,
        "bq_pp": np.ascontiguousarray(bq.reshape(8, 128).T),
        "bk_pp": np.ascontiguousarray(bk.reshape(8, 128).T),
        "bv_row": np.ascontiguousarray(bv[None, :]),
        "bo_row": np.ascontiguousarray(bo[None, :]),
        "relT2": relT2,
        "w_q2": wq2, "w_k2": wk2, "w_v2": wv2,
        "basewT": np.ascontiguousarray(base_w.T),
        "baseb_row": np.ascontiguousarray(base_b[None, :]),
        "gatewT": np.ascontiguousarray(gate_w.T),
        "AcatT": np.ascontiguousarray(lora_A.reshape(128, D).T),
        "BcatT": np.ascontiguousarray(lora_B.transpose(0, 2, 1).reshape(128, D)),
        "ident_in": np.eye(128, dtype=f),
    }
    nc = build(upto, dbg=dbg)
    in_maps = [dict(common, x0=np.ascontiguousarray(x[b])) for b in range(8)]
    res = run_bass_kernel_spmd(nc, in_maps, core_ids=list(range(8)))
    if dbg:
        return res.results
    return np.stack([res.results[b]["out"] for b in range(8)], axis=0)


# version 15
# speedup vs baseline: 1.1646x; 1.0099x over previous
import contextlib
import numpy as np
import concourse.bass as bass
import concourse.mybir as mybir
from concourse.bass_utils import run_bass_kernel_spmd

F32 = mybir.dt.float32
BF16 = mybir.dt.bfloat16
AF = mybir.ActivationFunctionType
ALU = mybir.AluOpType
AX = mybir.AxisListType

S, D = 4096, 1024
NT = 32
KC = 8
EWMAX = 1664


def _qgeom(c):
    lo = max(0, 4 * c - 4)
    hi = min(31, 4 * c + 7)
    Wb = 128 * (hi - lo + 1)
    EW = Wb + 128
    tiles = []
    for j in range(4):
        qt = 4 * c + j
        i0 = 128 * qt
        t_min = 128 * lo - i0 + 385
        Lp = max(0, -t_min)
        t_lo = max(0, t_min)
        mid = min(1024 - t_lo, EW - Lp)
        Rp = EW - Lp - mid
        tiles.append((qt, i0, Lp, t_lo, mid, Rp))
    return lo, hi, Wb, EW, tiles


QG = [_qgeom(c) for c in range(8)]


def build(upto=4, dbg=False, nhp=None, prod_only=False, pstage=9):
    nc = bass.Bass("TRN2", target_bir_lowering=False)
    dt = nc.dram_tensor

    x0 = dt("x0", [S, D], F32, kind="ExternalInput").ap()
    Wq = dt("Wq", [D, D], F32, kind="ExternalInput").ap()
    Wk = dt("Wk", [D, D], F32, kind="ExternalInput").ap()
    Wv = dt("Wv", [D, D], F32, kind="ExternalInput").ap()
    Wo = dt("Wo", [D, D], F32, kind="ExternalInput").ap()
    bq_pp = dt("bq_pp", [128, 8], F32, kind="ExternalInput").ap()
    bk_pp = dt("bk_pp", [128, 8], F32, kind="ExternalInput").ap()
    bv_row = dt("bv_row", [1, D], F32, kind="ExternalInput").ap()
    bo_row = dt("bo_row", [1, D], F32, kind="ExternalInput").ap()
    relT2 = dt("relT2", [128, 1025], F32, kind="ExternalInput").ap()
    w_q2 = dt("w_q2", [D, D], F32, kind="ExternalInput").ap()
    w_k2 = dt("w_k2", [D, D], F32, kind="ExternalInput").ap()
    w_v2 = dt("w_v2", [D, D], F32, kind="ExternalInput").ap()
    basewT = dt("basewT", [D, D], F32, kind="ExternalInput").ap()
    baseb_row = dt("baseb_row", [1, D], F32, kind="ExternalInput").ap()
    gatewT = dt("gatewT", [D, 10], F32, kind="ExternalInput").ap()
    AcatT = dt("AcatT", [D, 128], F32, kind="ExternalInput").ap()
    BcatT = dt("BcatT", [128, D], F32, kind="ExternalInput").ap()
    ident_in = dt("ident_in", [128, 128], F32, kind="ExternalInput").ap()
    out = dt("out", [S, D], F32, kind="ExternalOutput").ap()
    if dbg:
        dbg_qa = dt("dbg_qa", [128, S], BF16, kind="ExternalOutput").ap()
        dbg_ka = dt("dbg_ka", [128, S], BF16, kind="ExternalOutput").ap()
        dbg_va = dt("dbg_va", [128, 2080], BF16, kind="ExternalOutput").ap()
        dbg_e = dt("dbg_e", [128, EWMAX], BF16, kind="ExternalOutput").ap()
        dbg_gb = dt("dbg_gb", [128, 6144], BF16, kind="ExternalOutput").ap()
        dbg_a = dt("dbg_a", [128, 512], BF16, kind="ExternalOutput").ap()
        dbg_ct = dt("dbg_ct", [128, S], BF16, kind="ExternalOutput").ap()
        dbg_sml = dt("dbg_sml", [128, 64], F32, kind="ExternalOutput").ap()

    e_dram = dt("e_dram", [8 * 128 * EWMAX], BF16, kind="Internal")
    ct_dram = dt("ct_dram", [8 * 128 * S], BF16, kind="Internal")
    x1_dram = dt("x1_dram", [S, D], F32, kind="Internal").ap()
    x2_dram = dt("x2_dram", [S, D], F32, kind="Internal").ap()
    x3_dram = dt("x3_dram", [S, D], F32, kind="Internal").ap()
    h2t_dram = dt("h2t_dram", [S, D], BF16, kind="Internal").ap()

    base_st = contextlib.ExitStack()
    mk = base_st.enter_context

    id_bf = mk(nc.sbuf_tensor("id_bf", [128, 128], BF16))
    ones_bf = mk(nc.sbuf_tensor("ones_bf", [128, 512], BF16))
    ones32 = mk(nc.sbuf_tensor("ones32", [128, 16], F32))
    sml = mk(nc.sbuf_tensor("sml", [128, 64], F32))
    bor = mk(nc.sbuf_tensor("bor", [1, D], BF16))
    xbuf = [mk(nc.sbuf_tensor(f"xbuf{i}", [128, D], F32)) for i in range(2)]
    hbuf = [mk(nc.sbuf_tensor(f"hbuf{i}", [128, D], BF16)) for i in range(2)]
    scrb = mk(nc.sbuf_tensor("scrb", [128, D], BF16))

    MISCP = mk(nc.psum_tensor("MISCP", [128, 512], F32))

    sem_names = [
        "phz", "dma_w", "dma_x", "dma_s", "dma_st", "dma_e", "dma_g",
        "dma_ct", "dma_q", "dve_n",
        "pe_m", "pe_p", "pe_f", "pe_av", "pe_x",
        "act_m", "act_e", "act_l",
        "dve_m", "dve_e", "dve_c", "dve_x",
    ]
    sems = {n: mk(nc.semaphore(n)) for n in sem_names}
    cnt = {n: 0 for n in sem_names}
    nphase = [0]

    def inc(instr, name, k=1):
        instr.then_inc(sems[name], k)
        cnt[name] += k
        return instr

    def chk(name, v):
        assert cnt[name] == v, (name, cnt[name], v)

    def wge(eng, name, v):
        if v > 0:
            eng.wait_ge(sems[name], v)

    def snap():
        return dict(cnt)

    def selfedge(eng, instr, name="dve_x"):
        inc(instr, name)
        eng.wait_ge(sems[name], cnt[name])
        return instr

    def pdone(eng, sem=None):
        if sem is not None:
            eng.wait_ge(sems[sem], cnt[sem])
        eng.sem_inc(sems["phz"], 1)

    def end_phase():
        nphase[0] += 1
        cnt["phz"] += 5

    def bt():
        return 5 * nphase[0]

    def ln_dve_stats(v, xa):
        v.tensor_scalar(scrb[:, :], xa, 1.0, 0.0, ALU.mult, ALU.add,
                        accum_out=sml[:, 0:1])
        selfedge(v, v.scalar_tensor_tensor(scrb[:, :], xa, 1.0, xa, ALU.mult,
                                           ALU.mult, accum_out=sml[:, 1:2]))
        v.tensor_scalar(sml[:, 2:3], sml[:, 0:1], -1.0 / D, None, ALU.mult)
        selfedge(v, v.tensor_scalar(sml[:, 3:4], sml[:, 1:2], 1.0 / D, None,
                                    ALU.mult))
        selfedge(v, v.tensor_mul(sml[:, 5:6], sml[:, 2:3], sml[:, 2:3]))
        return v.tensor_sub(sml[:, 3:4], sml[:, 3:4], sml[:, 5:6])

    def ln_act_rstd(sc):
        sc.activation(sml[:, 4:5], sml[:, 3:4], AF.Ln, bias=sml[:, 6:7])
        return sc.activation(sml[:, 4:5], sml[:, 4:5], AF.Exp, scale=-0.5)

    def ln_dve_apply(v, xa, h_out):
        return v.tensor_scalar(h_out, xa, sml[:, 2:3], sml[:, 4:5],
                               ALU.add, ALU.mult)

    def tpose(t, out_ap, in_ap, ident):
        return t.matmul(out_ap, in_ap, ident, start=True, stop=True,
                        is_transpose=True, skip_group_check=True)

    s1 = contextlib.ExitStack()
    m1 = s1.enter_context
    id_f32 = m1(nc.sbuf_tensor("id_f32", [128, 128], F32))
    TT2 = m1(nc.sbuf_tensor("TT2", [128, 1025], BF16))
    bqc = m1(nc.sbuf_tensor("bqc", [128, 8], F32))
    bkc = m1(nc.sbuf_tensor("bkc", [128, 8], F32))
    bvr = m1(nc.sbuf_tensor("bvr", [1, D], BF16))
    maxT = m1(nc.sbuf_tensor("maxT", [1, 1], F32))
    TdT0 = m1(nc.sbuf_tensor("TdT0", [64, 2], BF16))
    tsq = m1(nc.sbuf_tensor("tsq", [64, 1025], BF16))

    wqh = [m1(nc.sbuf_tensor(f"wqh{i}", [128, 3072], BF16)) for i in range(2)]
    WQH_Q, WQH_K, WQH_V = 0, 1024, 2048
    h1T = m1(nc.sbuf_tensor("h1T", [128, KC * S], BF16))
    Qa = [m1(nc.sbuf_tensor(f"Qa{i}", [128, S], BF16)) for i in range(2)]
    Ka = [m1(nc.sbuf_tensor(f"Ka{i}", [128, S], BF16)) for i in range(2)]
    Va = [m1(nc.sbuf_tensor(f"Va{i}", [128, 32 * 65], BF16)) for i in range(2)]
    QKstg = [m1(nc.sbuf_tensor(f"QKstg{i}", [128, 512], BF16)) for i in range(8)]
    rowstg = m1(nc.sbuf_tensor("rowstg", [32, 128], BF16))
    nmall = [m1(nc.sbuf_tensor(f"nmall{i}", [128, 32], F32)) for i in range(2)]
    lnqq = m1(nc.sbuf_tensor("lnqq", [128, 32], F32))
    mallb = m1(nc.sbuf_tensor("mallb", [128, 32], F32))
    lncbf = m1(nc.sbuf_tensor("lncbf", [1, 2], BF16))
    Asb = [m1(nc.sbuf_tensor(f"Asb{i}", [128, 512], BF16)) for i in range(3)]
    lrow = [m1(nc.sbuf_tensor(f"lrow{i}", [1, 512], BF16)) for i in range(2)]
    Esb = [m1(nc.sbuf_tensor(f"Esb{i}", [128, EWMAX], BF16)) for i in range(2)]
    GBs = [m1(nc.sbuf_tensor(f"GBs{i}", [128, 4 * 1536], BF16)) for i in range(2)]
    pads = m1(nc.sbuf_tensor("pads", [128, 2], F32))
    rlt = m1(nc.sbuf_tensor("rlt", [1, 512], F32))
    rl_sb = m1(nc.sbuf_tensor("rl_sb", [1, 512], BF16))
    rlb_sb = m1(nc.sbuf_tensor("rlb_sb", [128, 512], BF16))
    CTs = m1(nc.sbuf_tensor("CTs", [128, S], BF16))

    with nc.Block() as block:
        B = snap()

        @block.gpsimd
        def _(g):
            inc(g.dma_start(id_bf[:, :], ident_in), "dma_w", 16)
            inc(g.dma_start(TT2[:, :], relT2), "dma_w", 16)
            inc(g.dma_start(bvr[:, :], bv_row), "dma_w", 16)
            inc(g.dma_start(bor[:, :], bo_row), "dma_w", 16)
            g.wait_ge(sems["dma_w"], B["dma_w"] + 64)
            g.wait_ge(sems["dma_s"], B["dma_s"] + 48)
            pdone(g)

        @block.sync
        def _(s):
            inc(s.dma_start(id_f32[:, :], ident_in), "dma_s", 16)
            inc(s.dma_start(bqc[:, :], bq_pp), "dma_s", 16)
            inc(s.dma_start(bkc[:, :], bk_pp), "dma_s", 16)
            s.wait_ge(sems["dma_w"], B["dma_w"] + 64)
            s.wait_ge(sems["dma_s"], B["dma_s"] + 48)
            pdone(s)

        @block.vector
        def _(v):
            v.memset(ones_bf[:, :], 1.0)
            v.memset(ones32[:, :], 1.0)
            v.memset(sml[:, :], 0.0)
            v.memset(sml[:, 6:7], 1e-5)
            v.memset(sml[:, 7:8], 0.6931471805599453)
            v.memset(Ka[0][64:128, :], 0.0)
            v.memset(Ka[1][64:128, :], 0.0)
            v.memset(Qa[0][64:128, :], 0.0)
            v.memset(Qa[1][64:128, :], 0.0)
            selfedge(v, v.memset(Ka[0][64:66, :], 1.0))
            v.memset(Ka[1][64:66, :], 1.0)
            v.memset(Va[0][:, :], 1.0)
            v.memset(Va[1][:, :], 1.0)
            v.wait_ge(sems["dma_w"], B["dma_w"] + 64)
            v.wait_ge(sems["dma_s"], B["dma_s"] + 48)
            v.tensor_copy(TdT0[0:64, 0:1], TT2[0:64, 0:1])
            ins = inc(v.tensor_sub(TdT0[0:64, 1:2], TT2[0:64, 1024:1025],
                                   TT2[0:64, 0:1]), "dve_m")
            pdone(v, "dve_m")

        @block.tensor
        def _(t):
            pdone(t)

        @block.scalar
        def _(sc):
            pdone(sc)

    end_phase()

    with nc.Block() as block:
        B = snap()

        @block.gpsimd
        def _(g):
            g.wait_ge(sems["phz"], bt())
            for w0, wsrc in ((WQH_Q, Wq), (WQH_K, Wk), (WQH_V, Wv)):
                for kcc in range(KC):
                    inc(g.dma_start(
                        wqh[0][:, w0 + kcc * 128 : w0 + (kcc + 1) * 128],
                        wsrc[kcc * 128 : (kcc + 1) * 128, 0:128]), "dma_w", 16)
            g.wait_ge(sems["dma_w"], B["dma_w"] + 24 * 16)
            pdone(g)

        @block.vector
        def _(v):
            v.wait_ge(sems["phz"], bt())
            inc(v.tensor_mul(tsq[0:64, 0:1025], TT2[0:64, :], TT2[0:64, :]),
                "dve_m")
            wge(v, "pe_m", B["pe_m"] + 1)
            inc(v.tensor_reduce(sml[:, 8:9], MISCP[:, 0:8], axis=AX.X,
                                op=ALU.max), "dve_m")
            wge(v, "pe_m", B["pe_m"] + 2)
            selfedge(v, v.tensor_reduce(sml[0:1, 9:10], MISCP[0:1, 256:384],
                                        axis=AX.X, op=ALU.max))
            inc(v.tensor_tensor(sml[0:1, 9:10], sml[0:1, 9:10],
                                MISCP[0:1, 8:9], ALU.max), "dve_m")
            pdone(v, "dve_m")

        @block.tensor
        def _(t):
            t.wait_ge(sems["phz"], bt())
            wge(t, "dve_m", B["dve_m"] + 1)
            ins = None
            for t9 in range(9):
                w = min(128, 1025 - t9 * 128)
                ins = t.matmul(MISCP[0:w, t9 : t9 + 1],
                               tsq[0:64, t9 * 128 : t9 * 128 + w],
                               ones_bf[0:64, 0:1], start=True, stop=True)
            inc(ins, "pe_m")
            wge(t, "dve_m", B["dve_m"] + 2)
            ins = inc(tpose(t, MISCP[0:1, 256:384], sml[:, 8:9],
                                  id_f32[:, :]), "pe_m")
            pdone(t, "pe_m")

        @block.scalar
        def _(sc):
            sc.wait_ge(sems["phz"], bt())
            wge(sc, "dve_m", B["dve_m"] + 3)
            sc.activation(sml[0:1, 10:11], sml[0:1, 9:10], AF.Ln)
            ins = inc(sc.activation(maxT[:, :], sml[0:1, 10:11], AF.Exp,
                                    scale=0.5), "act_m")
            pdone(sc, "act_m")

        @block.sync
        def _(s):
            s.wait_ge(sems["phz"], bt())
            pdone(s)

    end_phase()

    s1a = contextlib.ExitStack()
    stgbf = [s1a.enter_context(nc.psum_tensor(f"stgbf{i}", [128, 512], BF16))
             for i in range(2)]
    with nc.Block() as block:
        B = snap()

        @block.gpsimd
        def _(g):
            g.wait_ge(sems["phz"], bt())
            ins = None
            for tt in range(NT):
                if tt >= 2:
                    wge(g, "dve_m", B["dve_m"] + 3 * (tt - 2) + 2)
                ins = inc(g.dma_start(xbuf[tt % 2][:, :],
                                      x0[tt * 128 : (tt + 1) * 128, :]),
                          "dma_x", 16)
            pdone(g)

        @block.vector
        def _(v):
            v.wait_ge(sems["phz"], bt())
            ins = None
            for tt in range(NT):
                wge(v, "dma_x", B["dma_x"] + 16 * (tt + 1))
                inc(ln_dve_stats(v, xbuf[tt % 2][:, :]), "dve_m")
                wge(v, "act_m", B["act_m"] + tt + 1)
                if tt >= 2:
                    wge(v, "pe_m", B["pe_m"] + tt - 1)
                inc(ln_dve_apply(v, xbuf[tt % 2][:, :], hbuf[tt % 2][:, :]),
                    "dve_m")
                wge(v, "pe_m", B["pe_m"] + tt + 1)
                for k8 in range(KC):
                    ins = v.tensor_copy(
                        h1T[:, k8 * S + tt * 128 : k8 * S + (tt + 1) * 128],
                        stgbf[k8 // 4][:, (k8 % 4) * 128 : (k8 % 4 + 1) * 128])
                inc(ins, "dve_m")
            pdone(v, "dve_m")

        @block.scalar
        def _(sc):
            sc.wait_ge(sems["phz"], bt())
            ins = None
            for tt in range(NT):
                wge(sc, "dve_m", B["dve_m"] + 3 * tt + 1)
                ins = inc(ln_act_rstd(sc), "act_m")
            pdone(sc, "act_m")

        @block.tensor
        def _(t):
            t.wait_ge(sems["phz"], bt())
            ins = None
            for tt in range(NT):
                wge(t, "dve_m", B["dve_m"] + 3 * tt + 2)
                if tt >= 1:
                    wge(t, "dve_m", B["dve_m"] + 3 * (tt - 1) + 3)
                for k8 in range(KC):
                    ins = tpose(t, 
                        stgbf[k8 // 4][:, (k8 % 4) * 128 : (k8 % 4 + 1) * 128],
                        hbuf[tt % 2][:, k8 * 128 : (k8 + 1) * 128], id_bf[:, :])
                inc(ins, "pe_m")
            pdone(t, "pe_m")

        @block.sync
        def _(s):
            s.wait_ge(sems["phz"], bt())
            pdone(s)

    end_phase()
    s1a.close()

    s1p = contextlib.ExitStack()
    mp = s1p.enter_context
    PA = mp(nc.psum_tensor("PA", [128, 512], F32))
    PB = mp(nc.psum_tensor("PB", [128, 512], F32))
    CTX = mp(nc.psum_tensor("CTX", [128, 512], F32))
    Pp = mp(nc.psum_tensor("Pp", [128, 1536], F32))
    PC = mp(nc.psum_tensor("PC", [128, 512], F32))

    def pe_ptile(t, h_, c_, j, dve_e_wait):
        lo, hi, Wb, EW, tiles = QG[c_]
        qt, i0, Lp, t_lo, mid, Rp = tiles[j]
        qa = Qa[h_]
        wge(t, "dve_e", dve_e_wait)
        t.matmul(Pp[:, 0:512], qa[0:64, i0 : i0 + 128], TT2[0:64, 0:512],
                 start=True, stop=True)
        t.matmul(Pp[:, 512:1024], qa[0:64, i0 : i0 + 128], TT2[0:64, 512:1024],
                 start=True, stop=True)
        ins = t.matmul(Pp[:, 1024:1025], qa[0:64, i0 : i0 + 128],
                       TT2[0:64, 1024:1025], start=True, stop=True)
        inc(ins, "pe_p")

    def dve_etile(v, h_, c_, j, pe_p_wait, dma_e_wait, slot):
        lo, hi, Wb, EW, tiles = QG[c_]
        qt, i0, Lp, t_lo, mid, Rp = tiles[j]
        Eb = Esb[slot]
        wge(v, "pe_p", pe_p_wait)
        wge(v, "dma_e", dma_e_wait)
        selfedge(v, v.tensor_scalar(pads[:, 0:1], Pp[:, 0:1],
                                    nmall[h_][:, qt : qt + 1], None, ALU.add))
        selfedge(v, v.tensor_scalar(pads[:, 1:2], Pp[:, 1024:1025],
                                    nmall[h_][:, qt : qt + 1], None, ALU.add))
        ins = v.tensor_scalar(Eb[:, Lp : Lp + mid], Pp[:, t_lo : t_lo + mid],
                              nmall[h_][:, qt : qt + 1], None, ALU.add)
        off = 0
        while off < Lp:
            w = min(512, Lp - off)
            ins = v.tensor_scalar(Eb[:, off : off + w], ones_bf[:, 0:w],
                                  pads[:, 0:1], None, ALU.mult)
            off += w
        off = Lp + mid
        while off < EW:
            w = min(512, EW - off)
            ins = v.tensor_scalar(Eb[:, off : off + w], ones_bf[:, 0:w],
                                  pads[:, 1:2], None, ALU.mult)
            off += w
        inc(ins, "dve_e")

    def sync_etile(s, c_, j, sp, dve_e_wait, dmag_guard, dmae_wait, pef_guard,
                   slot):
        lo, hi, Wb, EW, tiles = QG[c_]
        slot_off = (sp * 4 + j) * 128 * EWMAX
        wge(s, "dve_e", dve_e_wait)
        wge(s, "dma_g", dmag_guard)
        inc(s.dma_start(bass.AP(e_dram, slot_off, [[EWMAX, 128], [1, EW]]),
                        Esb[slot][:, 0:EW]), "dma_e", 16)
        wge(s, "dma_e", dmae_wait)
        wge(s, "pe_f", pef_guard)
        inc(s.dma_start(GBs[sp][:, j * 1536 : j * 1536 + Wb],
                        bass.AP(e_dram, slot_off + 127,
                                [[EWMAX - 1, 128], [1, Wb]])), "dma_g", 16)

    for hp in range(8 if nhp is None else nhp):
        with nc.Block() as block:
            B = snap()
            wsl = hp % 2
            dma_w_rdy = cnt["dma_w"]

            @block.tensor
            def _(t):
                t.wait_ge(sems["phz"], bt())
                wge(t, "dma_w", dma_w_rdy)
                for p in range(16):
                    c = p % 8
                    w0 = WQH_K if p < 8 else WQH_Q
                    if p >= 2:
                        wge(t, "dve_m", B["dve_m"] + 1 + p)
                    bank = (PA if p % 2 == 0 else PB)[:, 0:512]
                    ins = None
                    for kcc in range(KC):
                        ins = t.matmul(
                            bank,
                            wqh[wsl][:, w0 + kcc * 128 : w0 + (kcc + 1) * 128],
                            h1T[:, kcc * S + c * 512 : kcc * S + (c + 1) * 512],
                            start=(kcc == 0), stop=(kcc == KC - 1))
                    inc(ins, "pe_m")
                for h in range(2 if pstage >= 2 else 0):
                    for ch in range(KC):
                        wge(t, "dve_m", B["dve_m"] + 19 + 10 * h + ch)
                        ins = None
                        for i4 in range(4):
                            qt = 4 * ch + i4
                            ins = t.matmul(
                                MISCP[:, 32 * h + qt : 32 * h + qt + 1],
                                Asb[ch % 2][0:64, i4 * 128 : (i4 + 1) * 128],
                                ones_bf[0:64, 0:1], start=True, stop=True,
                                skip_group_check=True)
                        inc(ins, "pe_m")
                    wge(t, "dve_m", B["dve_m"] + 27 + 10 * h)
                    inc(tpose(t, MISCP[0:1, 256:384], sml[:, 8 + h : 9 + h],
                                    id_f32[:, :]), "pe_m")
                if pstage >= 2:
                    chk("pe_m", B["pe_m"] + 34)
                for h in range(2 if pstage >= 2 else 0):
                    wge(t, "act_m", B["act_m"] + 3 + h)
                    inc(t.matmul(MISCP[:, 100 + h : 101 + h],
                                 ones_bf[0:1, 0:128], lncbf[0:1, h : h + 1],
                                 start=True, stop=True,
                                 skip_group_check=True), "pe_m")
                for h in range(2 if pstage >= 3 else 0):
                    for ch in range(KC):
                        wge(t, "dve_m", B["dve_m"] + 43 + 8 * h + ch)
                        if ch == 0:
                            wge(t, "dve_m", B["dve_m"] + 27 + 10 * h)
                        ins = None
                        for i4 in range(4):
                            qt = 4 * ch + i4
                            ins = t.matmul(
                                MISCP[:, 32 * h + qt : 32 * h + qt + 1],
                                Asb[ch % 2][0:64, i4 * 128 : (i4 + 1) * 128],
                                ones_bf[0:64, 0:1], start=True, stop=True,
                                skip_group_check=True)
                        inc(ins, "pe_m")
                if pstage >= 3:
                    chk("pe_m", B["pe_m"] + 52)
                for h in range(2 if pstage >= 3 else 0):
                    wge(t, "dve_m", B["dve_m"] + 60 + 3 * h)
                    wge(t, "act_m", B["act_m"] + 6)
                    inc(tpose(t, MISCP[0:32, 256:384], nmall[h][:, 0:32],
                                    id_f32[:, :]), "pe_m")
                for h in range(2 if pstage >= 4 else 0):
                    for c in range(8):
                        pidx = 8 * h + c
                        if pidx == 0:
                            wge(t, "act_m", B["act_m"] + 6)
                            wge(t, "dve_m", B["dve_m"] + 64)
                        else:
                            wge(t, "dve_m", B["dve_m"] + 64 + pidx)
                        inc(t.matmul(MISCP[0:2, 0:512], TdT0[0:64, 0:2],
                                     Qa[h][0:64, c * 512 : (c + 1) * 512],
                                     start=True, stop=True,
                                     skip_group_check=True), "pe_m")
                if pstage >= 4:
                    chk("pe_m", B["pe_m"] + 70)
                for kt in range(32 if pstage >= 5 else 0):
                    if kt >= 2:
                        wge(t, "dve_m", B["dve_m"] + 81 + kt - 2)
                    sub = (PA if kt % 2 == 0 else PB)[:, 0:128]
                    for kcc in range(KC):
                        t.matmul(sub,
                                 h1T[:, kcc * S + kt * 128 : kcc * S + (kt + 1) * 128],
                                 wqh[wsl][:, WQH_V + kcc * 128 : WQH_V + (kcc + 1) * 128],
                                 start=(kcc == 0), stop=False,
                                 skip_group_check=True)
                    ins = t.matmul(sub, ones_bf[0:1, 0:128],
                                   bvr[0:1, hp * 128 : (hp + 1) * 128],
                                   start=False, stop=True,
                                   skip_group_check=True)
                    inc(ins, "pe_m")
                if pstage >= 5:
                    chk("pe_m", B["pe_m"] + 102)
                for j in range(4 if pstage >= 6 else 0):
                    pe_ptile(t, 0, 0, j, B["dve_e"] + j)
                if pstage >= 6:
                    chk("pe_p", B["pe_p"] + 4)
                pdone(t, "pe_p" if pstage >= 6 else "pe_m")

            @block.vector
            def _(v):
                v.wait_ge(sems["phz"], bt())
                inc(v.memset(Qa[0][64:66, :], 0.0), "dve_m")
                inc(v.memset(Qa[1][64:66, :], 0.0), "dve_m")
                for p in range(16):
                    c = p % 8
                    wge(v, "pe_m", B["pe_m"] + 1 + p)
                    if p >= 8:
                        wge(v, "dma_q", B["dma_q"] + 16 * (c + 1))
                    bank = (PA if p % 2 == 0 else PB)[:, 0:512]
                    dst = Ka[0] if p < 8 else Qa[0]
                    bc = bkc if p < 8 else bqc
                    scl = 0.125 if p < 8 else None
                    op1 = ALU.mult if p < 8 else ALU.add
                    if p < 8:
                        v.tensor_scalar(dst[0:64, c * 512 : (c + 1) * 512],
                                        bank[0:64, :], bc[0:64, hp : hp + 1],
                                        scl, ALU.add, op1)
                        ins = v.tensor_scalar(QKstg[c][64:128, :],
                                              bank[64:128, :],
                                              bc[64:128, hp : hp + 1],
                                              scl, ALU.add, op1)
                    else:
                        v.tensor_scalar(dst[0:64, c * 512 : (c + 1) * 512],
                                        bank[0:64, :], bc[0:64, hp : hp + 1],
                                        None, ALU.add)
                        ins = v.tensor_scalar(QKstg[c][64:128, :],
                                              bank[64:128, :],
                                              bc[64:128, hp : hp + 1],
                                              None, ALU.add)
                    inc(ins, "dve_m")
                chk("dve_m", B["dve_m"] + 18)
                for h in range(2 if pstage >= 2 else 0):
                    src = Ka[h]
                    for ch in range(KC):
                        n = 8 * h + ch
                        if n >= 2:
                            pm = 17 + (n - 2) if n - 2 < 8 else 26 + (n - 2 - 8)
                            wge(v, "pe_m", B["pe_m"] + pm)
                        if h == 1:
                            wge(v, "dma_q", B["dma_q"] + 16 * (ch + 1))
                        inc(v.tensor_mul(Asb[ch % 2][0:64, :],
                                         src[0:64, ch * 512 : (ch + 1) * 512],
                                         src[0:64, ch * 512 : (ch + 1) * 512]),
                            "dve_m")
                    wge(v, "pe_m", B["pe_m"] + 24 + 9 * h)
                    inc(v.tensor_reduce(sml[:, 8 + h : 9 + h],
                                        MISCP[:, 32 * h : 32 * h + 32],
                                        axis=AX.X, op=ALU.max), "dve_m")
                    wge(v, "pe_m", B["pe_m"] + 25 + 9 * h)
                    inc(v.tensor_reduce(sml[0:1, 10 + h : 11 + h],
                                        MISCP[0:1, 256:384], axis=AX.X,
                                        op=ALU.max), "dve_m")
                if pstage >= 2:
                    chk("dve_m", B["dve_m"] + 38)
                for h in range(2 if pstage >= 2 else 0):
                    wge(v, "act_m", B["act_m"] + 1 + h)
                    inc(v.tensor_add(sml[0:1, 16 + h : 17 + h],
                                     sml[0:1, 14 + h : 15 + h], maxT[:, :]),
                        "dve_m")
                for h in range(2 if pstage >= 2 else 0):
                    wge(v, "pe_m", B["pe_m"] + 35 + h)
                    inc(v.tensor_copy(sml[:, 22 + h : 23 + h],
                                      MISCP[:, 100 + h : 101 + h]), "dve_m")
                for h in range(2 if pstage >= 3 else 0):
                    src = Qa[h]
                    for ch in range(KC):
                        n = 16 + 8 * h + ch
                        pm = 26 + (n - 2 - 8) if n - 2 < 16 else 37 + (n - 2 - 16)
                        wge(v, "pe_m", B["pe_m"] + pm)
                        if h == 1:
                            wge(v, "dma_q", B["dma_q"] + 128 + 16 * (ch + 1))
                        inc(v.tensor_mul(Asb[ch % 2][0:64, :],
                                         src[0:64, ch * 512 : (ch + 1) * 512],
                                         src[0:64, ch * 512 : (ch + 1) * 512]),
                            "dve_m")
                if pstage >= 3:
                    chk("dve_m", B["dve_m"] + 58)
                for h in range(2 if pstage >= 3 else 0):
                    wge(v, "act_m", B["act_m"] + 5 + h)
                    inc(v.tensor_scalar(nmall[h][:, :], mallb[:, :], -1.0,
                                        None, ALU.mult), "dve_m")
                    v.wait_ge(sems["dve_m"], cnt["dve_m"])
                    inc(v.tensor_scalar(nmall[h][:, :], nmall[h][:, :], 1.0,
                                        None, ALU.mult), "dve_m")
                    wge(v, "pe_m", B["pe_m"] + 53 + h)
                    if h == 1:
                        wge(v, "dma_q", B["dma_q"] + 272)
                    inc(v.tensor_copy(rowstg[0:32, :], MISCP[0:32, 256:384]),
                        "dve_m")
                if pstage >= 3:
                    chk("dve_m", B["dve_m"] + 64)
                for h in range(2 if pstage >= 4 else 0):
                    for c in range(8):
                        wge(v, "pe_m", B["pe_m"] + 55 + 8 * h + c)
                        wge(v, "dma_q", B["dma_q"] + 272 + 16 * h)
                        inc(v.tensor_tensor(
                            Qa[h][64:66, c * 512 : (c + 1) * 512],
                            MISCP[0:2, 0:512],
                            Qa[h][64:66, c * 512 : (c + 1) * 512],
                            ALU.add), "dve_m")
                if pstage >= 4:
                    chk("dve_m", B["dve_m"] + 80)
                for kt in range(32 if pstage >= 5 else 0):
                    wge(v, "pe_m", B["pe_m"] + 71 + kt)
                    sub = (PA if kt % 2 == 0 else PB)[:, 0:128]
                    v.tensor_copy(Va[0][:, kt * 65 : kt * 65 + 64], sub[:, 0:64])
                    ins = v.tensor_copy(Va[1][:, kt * 65 : kt * 65 + 64],
                                        sub[:, 64:128])
                    inc(ins, "dve_m")
                if pstage >= 5:
                    chk("dve_m", B["dve_m"] + 112)
                for j in range(4 if pstage >= 6 else 0):
                    dve_etile(v, 0, 0, j, B["pe_p"] + j + 1,
                              B["dma_e"] + 16 * (j - 1), j % 2)
                if pstage >= 6:
                    chk("dve_e", B["dve_e"] + 4)
                pdone(v, "dve_e" if pstage >= 6 else "dve_m")

            @block.scalar
            def _(sc):
                sc.wait_ge(sems["phz"], bt())
                for h in range(2 if pstage >= 2 else 0):
                    wge(sc, "dve_m", B["dve_m"] + 28 + 10 * h)
                    sc.activation(sml[0:1, 12 + h : 13 + h],
                                  sml[0:1, 10 + h : 11 + h], AF.Ln)
                    inc(sc.activation(sml[0:1, 14 + h : 15 + h],
                                      sml[0:1, 12 + h : 13 + h], AF.Exp,
                                      scale=0.5), "act_m")
                for h in range(2 if pstage >= 2 else 0):
                    wge(sc, "dve_m", B["dve_m"] + 39 + h)
                    sc.activation(sml[0:1, 18 + h : 19 + h],
                                  sml[0:1, 16 + h : 17 + h], AF.Ln)
                    inc(sc.activation(lncbf[0:1, h : h + 1],
                                      sml[0:1, 18 + h : 19 + h], AF.Copy),
                        "act_m")
                for h in range(2 if pstage >= 3 else 0):
                    wge(sc, "pe_m", B["pe_m"] + 52)
                    wge(sc, "dve_m", B["dve_m"] + 41 + h)
                    if h == 1:
                        wge(sc, "dve_m", B["dve_m"] + 59)
                    sc.activation(lnqq[:, :], MISCP[:, 32 * h : 32 * h + 32],
                                  AF.Ln)
                    inc(sc.activation(mallb[:, :], lnqq[:, :], AF.Exp,
                                      scale=0.5, bias=sml[:, 22 + h : 23 + h]),
                        "act_m")
                pdone(sc, "act_m")

            @block.gpsimd
            def _(g):
                g.wait_ge(sems["phz"], bt())
                for p in range(16):
                    c = p % 8
                    dst = Ka[1] if p < 8 else Qa[1]
                    wge(g, "dve_m", B["dve_m"] + 3 + p)
                    inc(g.dma_start(dst[0:64, c * 512 : (c + 1) * 512],
                                    QKstg[c][64:128, :]), "dma_q", 16)
                for h in range(2 if pstage >= 3 else 0):
                    wge(g, "dve_m", B["dve_m"] + 61 + 3 * h)
                    inc(g.dma_start(Qa[h][64:65, :], rowstg[0:32, :]),
                        "dma_q", 16)
                if pstage >= 3:
                    chk("dma_q", B["dma_q"] + 288)
                g.wait_ge(sems["dma_q"], cnt["dma_q"])
                pdone(g)

            @block.sync
            def _(s):
                s.wait_ge(sems["phz"], bt())
                for j in range(4 if pstage >= 6 else 0):
                    sync_etile(s, 0, j, 0,
                               B["dve_e"] + j + 1, 0,
                               B["dma_e"] + 16 * (j + 1), 0, j % 2)
                s.wait_ge(sems["dma_g"], cnt["dma_g"])
                pdone(s)

        end_phase()

        if prod_only:
            continue
        with nc.Block() as block:
            B = snap()

            def fills_pe(t, idx, kt):
                h, c = idx // 8, idx % 8
                lo, hi, Wb, EW, tiles = QG[c]
                g = 32 * idx + kt
                wge(t, "act_e", B["act_e"] + g - 2)
                if kt == lo and idx >= 1:
                    wge(t, "dma_g", B["dma_g"] + 64 * idx)
                bank = (PA, PB, PC)[g % 3][:, 0:512]
                if kt < lo:
                    ext = 65
                elif kt > hi:
                    ext = 66
                else:
                    ext = 64
                ins = t.matmul(bank, Ka[h][0:ext, kt * 128 : (kt + 1) * 128],
                               Qa[h][0:ext, c * 512 : (c + 1) * 512],
                               start=True, stop=(ext != 64),
                               skip_group_check=True)
                if ext == 64:
                    for j in range(4):
                        ins = t.matmul(
                            bank[:, j * 128 : (j + 1) * 128],
                            GBs[idx % 2][:, j * 1536 + (kt - lo) * 128 :
                                         j * 1536 + (kt - lo + 1) * 128],
                            id_bf[:, :], start=False, stop=(j == 3),
                            skip_group_check=True)
                inc(ins, "pe_f")
                chk("pe_f", B["pe_f"] + 32 * idx + kt + 1)

            @block.tensor
            def _(t):
                t.wait_ge(sems["phz"], bt())
                for idx in range(16):
                    h, c = idx // 8, idx % 8
                    fills_pe(t, idx, 0)
                    fills_pe(t, idx, 1)
                    for kt in range(34):
                        if kt >= 2:
                            ktA = kt - 2
                            gA = 32 * idx + ktA
                            wge(t, "act_e", B["act_e"] + gA + 1)
                            if ktA == 0:
                                wge(t, "dve_c", B["dve_c"] + idx)
                            inc(t.matmul(
                                CTX[0:65, 0:512],
                                Va[h][:, ktA * 65 : ktA * 65 + 65],
                                Asb[gA % 3][:, :],
                                start=(ktA == 0), stop=(ktA == 31)), "pe_av")
                        if kt + 2 <= 31:
                            fills_pe(t, idx, kt + 2)
                        if kt == 4 and idx >= 1:
                            hN = (idx - 1) // 8
                            wge(t, "act_l", B["act_l"] + idx)
                            wge(t, "dve_n", B["dve_n"] + idx - 1)
                            inc(t.matmul(MISCP[64 * hN : 64 * hN + 64, 0:512],
                                         ones_bf[0:1, 0:64],
                                         rl_sb[0:1, 0:512],
                                         start=True, stop=True), "pe_x")
                        if idx + 1 <= 15 and kt in (2, 6, 10, 14):
                            j = {2: 0, 6: 1, 10: 2, 14: 3}[kt]
                            h2, c2 = (idx + 1) // 8, (idx + 1) % 8
                            pe_ptile(t, h2, c2, j, B["dve_e"] + 4 * idx + j)
                    chk("pe_av", B["pe_av"] + 32 * (idx + 1))
                wge(t, "act_l", B["act_l"] + 16)
                wge(t, "dve_n", B["dve_n"] + 15)
                inc(t.matmul(MISCP[64:128, 0:512], ones_bf[0:1, 0:64],
                             rl_sb[0:1, 0:512], start=True, stop=True), "pe_x")
                chk("pe_x", B["pe_x"] + 16)
                pdone(t, "pe_x")

            @block.scalar
            def _(sc):
                sc.wait_ge(sems["phz"], bt())
                for idx in range(16):
                    for kt in range(32):
                        g = 32 * idx + kt
                        wge(sc, "pe_f", B["pe_f"] + g + 1)
                        wge(sc, "pe_av", B["pe_av"] + g - 2)
                        bank = (PA, PB, PC)[g % 3][:, 0:512]
                        inc(sc.activation(Asb[g % 3][:, :], bank, AF.Exp),
                            "act_e")
                        if kt == 1 and idx >= 1:
                            wge(sc, "dve_c", B["dve_c"] + idx)
                            sc.activation(rlt[0:1, :],
                                          lrow[(idx - 1) % 2][0:1, :], AF.Ln)
                            inc(sc.activation(rl_sb[0:1, :], rlt[0:1, :],
                                              AF.Exp, scale=-1.0), "act_l")
                wge(sc, "dve_c", B["dve_c"] + 16)
                sc.activation(rlt[0:1, :], lrow[1][0:1, :], AF.Ln)
                inc(sc.activation(rl_sb[0:1, :], rlt[0:1, :], AF.Exp,
                                  scale=-1.0), "act_l")
                chk("act_e", B["act_e"] + 512)
                chk("act_l", B["act_l"] + 16)
                pdone(sc, "act_l")

            def ct_slice(k):
                hP, cP = k // 8, k % 8
                return CTs[64 * hP : 64 * hP + 64, cP * 512 : (cP + 1) * 512]

            @block.vector
            def _(v):
                v.wait_ge(sems["phz"], bt())
                for idx in range(16):
                    if idx >= 1:
                        wge(v, "pe_av", B["pe_av"] + 32 * idx)
                        wge(v, "act_l", B["act_l"] + idx - 2)
                        v.tensor_copy(ct_slice(idx - 1), CTX[0:64, 0:512])
                        inc(v.tensor_copy(lrow[(idx - 1) % 2][0:1, :],
                                          CTX[64:65, 0:512]), "dve_c")
                    if idx >= 2:
                        hN = (idx - 2) // 8
                        wge(v, "pe_x", B["pe_x"] + idx - 1)
                        selfedge(v, v.tensor_copy(
                            rlb_sb[64 * hN : 64 * hN + 64, :],
                            MISCP[64 * hN : 64 * hN + 64, 0:512]))
                        inc(v.tensor_tensor(ct_slice(idx - 2), ct_slice(idx - 2),
                                            rlb_sb[64 * hN : 64 * hN + 64, :],
                                            ALU.mult), "dve_n")
                    if idx + 1 <= 15:
                        h2, c2 = (idx + 1) // 8, (idx + 1) % 8
                        for j in range(4):
                            n = 4 * idx + j
                            dve_etile(v, h2, c2, j,
                                      B["pe_p"] + n + 1,
                                      B["dma_e"] + 16 * (n - 1), n % 2)
                wge(v, "pe_av", B["pe_av"] + 512)
                wge(v, "act_l", B["act_l"] + 14)
                v.tensor_copy(ct_slice(15), CTX[0:64, 0:512])
                inc(v.tensor_copy(lrow[1][0:1, :], CTX[64:65, 0:512]), "dve_c")
                for k in (14, 15):
                    wge(v, "pe_x", B["pe_x"] + k + 1)
                    selfedge(v, v.tensor_copy(rlb_sb[64:128, :],
                                              MISCP[64:128, 0:512]))
                    inc(v.tensor_tensor(ct_slice(k), ct_slice(k),
                                        rlb_sb[64:128, :], ALU.mult), "dve_n")
                chk("dve_c", B["dve_c"] + 16)
                chk("dve_n", B["dve_n"] + 16)
                chk("dve_e", B["dve_e"] + 60)
                pdone(v, "dve_n")

            @block.sync
            def _(s):
                s.wait_ge(sems["phz"], bt())
                for idx in range(15):
                    c2 = (idx + 1) % 8
                    sp = (idx + 1) % 2
                    for j in range(4):
                        n = 4 * idx + j
                        dmag_guard = (B["dma_g"] + 16 * (4 * idx - 4)
                                      if idx >= 2 else 0)
                        sync_etile(s, c2, j, sp,
                                   B["dve_e"] + n + 1,
                                   dmag_guard,
                                   B["dma_e"] + 16 * (n + 1),
                                   B["pe_f"] + 32 * idx if idx >= 1 else 0,
                                   n % 2)
                chk("dma_g", B["dma_g"] + 16 * 60)
                pdone(s)

            @block.gpsimd
            def _(g):
                g.wait_ge(sems["phz"], bt())
                if hp + 1 < 8:
                    for w0, wsrc in ((WQH_Q, Wq), (WQH_K, Wk), (WQH_V, Wv)):
                        for kcc in range(KC):
                            inc(g.dma_start(
                                wqh[(hp + 1) % 2][:, w0 + kcc * 128 :
                                                  w0 + (kcc + 1) * 128],
                                wsrc[kcc * 128 : (kcc + 1) * 128,
                                     (hp + 1) * 128 : (hp + 2) * 128]),
                                "dma_w", 16)
                if dbg and hp == 0:
                    wge(g, "dve_n", B["dve_n"] + 16)
                    inc(g.dma_start(dbg_qa, Qa[0][:, :]), "dma_st", 16)
                    inc(g.dma_start(dbg_ka, Ka[0][:, :]), "dma_st", 16)
                    inc(g.dma_start(dbg_va, Va[0][:, :]), "dma_st", 16)
                    inc(g.dma_start(dbg_e, Esb[0][:, :]), "dma_st", 16)
                    inc(g.dma_start(dbg_gb, GBs[0][:, :]), "dma_st", 16)
                    inc(g.dma_start(dbg_a, Asb[1][:, :]), "dma_st", 16)
                    inc(g.dma_start(dbg_ct, CTs[:, :]), "dma_st", 16)
                    inc(g.dma_start(dbg_sml, sml[:, :]), "dma_st", 16)
                wge(g, "dve_n", B["dve_n"] + 16)
                inc(g.dma_start(
                    bass.AP(ct_dram, hp * 128,
                            [[1024, 128], [131072, 32], [1, 128]]),
                    bass.AP(CTs, 0, [[4096, 128], [128, 32], [1, 128]])),
                    "dma_ct", 16)
                g.wait_ge(sems["dma_ct"], B["dma_ct"] + 16)
                g.wait_ge(sems["dma_st"], cnt["dma_st"])
                pdone(g)

        end_phase()

    if nhp is not None:
        with nc.Block() as block:
            @block.gpsimd
            def _(g):
                g.wait_ge(sems["phz"], bt())
                for i in range(4):
                    if not prod_only:
                        inc(g.dma_start(out[i * 128 : (i + 1) * 128, :],
                                        CTs[:, i * 1024 : (i + 1) * 1024]),
                            "dma_st", 16)
                    inc(g.dma_start(out[512 + i * 128 : 512 + (i + 1) * 128, :],
                                    Qa[0][:, i * 1024 : (i + 1) * 1024]),
                        "dma_st", 16)
                    inc(g.dma_start(out[1024 + i * 128 : 1024 + (i + 1) * 128, :],
                                    Ka[0][:, i * 1024 : (i + 1) * 1024]),
                        "dma_st", 16)
                g.wait_ge(sems["dma_st"], cnt["dma_st"])
                pdone(g)
            for eng_name in ("tensor", "vector", "scalar", "sync"):
                def _mk(en):
                    def _f(e):
                        e.wait_ge(sems["phz"], bt())
                        pdone(e)
                    return _f
                getattr(block, eng_name)(_mk(eng_name))
        end_phase()
        s1p.close()
        s1.close()
        base_st.close()
        return nc

    s1p.close()
    s1.close()

    tail_st = contextlib.ExitStack()
    mtl = tail_st.enter_context
    hTs = [mtl(nc.sbuf_tensor(f"hTs{i}", [128, D], BF16)) for i in range(2)]
    xob = [mtl(nc.sbuf_tensor(f"xob{i}", [128, D], F32)) for i in range(2)]
    X = mtl(nc.psum_tensor("X", [128, 1536], F32))
    Y = mtl(nc.psum_tensor("Y", [128, 1024], F32))
    stg = [mtl(nc.psum_tensor(f"stg{i}", [128, 512], BF16)) for i in range(2)]
    misc = MISCP

    s2 = contextlib.ExitStack()
    m2 = s2.enter_context
    wo_sb = m2(nc.sbuf_tensor("wo_sb", [128, 8 * 1024], BF16))
    ct_in = [m2(nc.sbuf_tensor(f"ct_in{i}", [128, 1024], BF16)) for i in range(2)]
    dst1 = out if upto == 1 else x1_dram

    with nc.Block() as block:
        B = snap()

        @block.gpsimd
        def _(g):
            g.wait_ge(sems["phz"], bt())
            for c8 in range(8):
                inc(g.dma_start(wo_sb[:, c8 * 1024 : (c8 + 1) * 1024],
                                Wo[c8 * 128 : (c8 + 1) * 128, :]), "dma_w", 16)
            ins = None
            for tt in range(NT):
                if tt >= 2:
                    wge(g, "pe_m", B["pe_m"] + tt - 1)
                inc(g.dma_start(
                    ct_in[tt % 2][:, :],
                    bass.AP(ct_dram, tt * 131072,
                            [[1024, 128], [1, 1024]])), "dma_x", 16)
            g.wait_ge(sems["dma_x"], B["dma_x"] + 16 * NT)
            pdone(g)

        @block.tensor
        def _(t):
            t.wait_ge(sems["phz"], bt())
            wge(t, "dma_w", B["dma_w"] + 128)
            ins = None
            for tt in range(NT):
                wge(t, "dma_x", B["dma_x"] + 16 * (tt + 1))
                if tt >= 2:
                    wge(t, "dve_m", B["dve_m"] + tt - 1)
                par = (X if tt % 2 == 0 else Y)[:, 0:1024]
                for half in range(2):
                    for c8 in range(8):
                        t.matmul(par[:, half * 512 : (half + 1) * 512],
                                 ct_in[tt % 2][:, c8 * 128 : (c8 + 1) * 128],
                                 wo_sb[:, c8 * 1024 + half * 512 :
                                       c8 * 1024 + (half + 1) * 512],
                                 start=(c8 == 0), stop=False)
                    ins = t.matmul(par[:, half * 512 : (half + 1) * 512],
                                   ones_bf[0:1, 0:128],
                                   bor[0:1, half * 512 : (half + 1) * 512],
                                   start=False, stop=True)
                inc(ins, "pe_m")
            pdone(t, "pe_m")

        @block.vector
        def _(v):
            v.wait_ge(sems["phz"], bt())
            ins = None
            for tt in range(NT):
                wge(v, "pe_m", B["pe_m"] + tt + 1)
                wge(v, "dma_s", B["dma_s"] + 16 * (tt + 1))
                if tt >= 2:
                    wge(v, "dma_e", B["dma_e"] + 16 * (tt - 1))
                par = (X if tt % 2 == 0 else Y)[:, 0:1024]
                ins = inc(v.tensor_add(xob[tt % 2][:, :], par,
                                       xbuf[tt % 2][:, :]), "dve_m")
            pdone(v, "dve_m")

        @block.scalar
        def _(sc):
            sc.wait_ge(sems["phz"], bt())
            pdone(sc)

        @block.sync
        def _(s):
            s.wait_ge(sems["phz"], bt())
            ins = None
            for tt in range(NT):
                if tt >= 2:
                    wge(s, "dve_m", B["dve_m"] + tt - 1)
                inc(s.dma_start(xbuf[tt % 2][:, :],
                                x0[tt * 128 : (tt + 1) * 128, :]), "dma_s", 16)
                if tt >= 1:
                    wge(s, "dve_m", B["dve_m"] + tt)
                    ins = inc(s.dma_start(dst1[(tt - 1) * 128 : tt * 128, :],
                                          xob[(tt - 1) % 2][:, :]), "dma_e", 16)
            wge(s, "dve_m", B["dve_m"] + NT)
            inc(s.dma_start(dst1[(NT - 1) * 128 : NT * 128, :],
                            xob[(NT - 1) % 2][:, :]), "dma_e", 16)
            s.wait_ge(sems["dma_e"], B["dma_e"] + 16 * NT)
            pdone(s)

    end_phase()
    s2.close()

    if upto == 1:
        tail_st.close()
        base_st.close()
        return nc

    def ffn_phase(src, dst, tag):
        s3 = contextlib.ExitStack()
        m3 = s3.enter_context
        bw_sb = m3(nc.sbuf_tensor(f"bw_sb{tag}", [128, 8 * 1024], BF16))
        gw_sb = m3(nc.sbuf_tensor(f"gw_sb{tag}", [128, 8 * 16], BF16))
        aA_sb = m3(nc.sbuf_tensor(f"aA_sb{tag}", [128, 8 * 128], BF16))
        bB_sb = m3(nc.sbuf_tensor(f"bB_sb{tag}", [128, 1024], BF16))
        e_sb = m3(nc.sbuf_tensor(f"e_sb{tag}", [128, 16], F32))
        tp_sb = m3(nc.sbuf_tensor(f"tp_sb{tag}", [128, 16], F32))
        ew_sb = m3(nc.sbuf_tensor(f"ew_sb{tag}", [128, 8], F32))
        ewx_sb = m3(nc.sbuf_tensor(f"ewx_sb{tag}", [128, 128], F32))
        zw_bf = m3(nc.sbuf_tensor(f"zw_bf{tag}", [128, 128], BF16))
        zwT_sb = m3(nc.sbuf_tensor(f"zwT_sb{tag}", [128, 128], BF16))
        tp2_sb = m3(nc.sbuf_tensor(f"tp2_sb{tag}", [128, 16], F32))
        bbr = m3(nc.sbuf_tensor(f"bbr{tag}", [1, D], BF16))

        with nc.Block() as block:
            B = snap()

            @block.gpsimd
            def _(g):
                g.wait_ge(sems["phz"], bt())
                for c8 in range(8):
                    inc(g.dma_start(bw_sb[:, c8 * 1024 : (c8 + 1) * 1024],
                                    basewT[c8 * 128 : (c8 + 1) * 128, :]),
                        "dma_w", 16)
                    inc(g.dma_start(gw_sb[:, c8 * 16 : c8 * 16 + 10],
                                    gatewT[c8 * 128 : (c8 + 1) * 128, :]),
                        "dma_w", 16)
                    inc(g.dma_start(aA_sb[:, c8 * 128 : (c8 + 1) * 128],
                                    AcatT[c8 * 128 : (c8 + 1) * 128, :]),
                        "dma_w", 16)
                inc(g.dma_start(bB_sb[:, :], BcatT), "dma_w", 16)
                inc(g.dma_start(bbr[:, :], baseb_row), "dma_w", 16)
                ins = None
                for tt in range(NT):
                    if tt >= 2:
                        wge(g, "dve_m", B["dve_m"] + 9 * (tt - 2) + 9)
                    inc(g.dma_start(xbuf[tt % 2][:, :],
                                    src[tt * 128 : (tt + 1) * 128, :]),
                        "dma_x", 16)
                    if tt >= 1:
                        wge(g, "dve_m", B["dve_m"] + 9 * (tt - 1) + 9)
                        ins = inc(g.dma_start(dst[(tt - 1) * 128 : tt * 128, :],
                                              xob[(tt - 1) % 2][:, :]),
                                  "dma_st", 16)
                wge(g, "dve_m", B["dve_m"] + 9 * (NT - 1) + 9)
                inc(g.dma_start(dst[(NT - 1) * 128 : NT * 128, :],
                                xob[(NT - 1) % 2][:, :]), "dma_st", 16)
                g.wait_ge(sems["dma_st"], cnt["dma_st"])
                pdone(g)

            @block.vector
            def _(v):
                v.wait_ge(sems["phz"], bt())
                ins = None
                for tt in range(NT):
                    wge(v, "dma_x", B["dma_x"] + 16 * (tt + 1))
                    inc(ln_dve_stats(v, xbuf[tt % 2][:, :]), "dve_m")
                    wge(v, "act_m", B["act_m"] + 3 * tt + 1)
                    if tt >= 2:
                        wge(v, "pe_m", B["pe_m"] + 5 * (tt - 2) + 1)
                    inc(ln_dve_apply(v, xbuf[tt % 2][:, :],
                                     hbuf[tt % 2][:, :]), "dve_m")
                    wge(v, "pe_m", B["pe_m"] + 5 * tt + 1)
                    for k8 in range(KC):
                        ins = v.tensor_copy(
                            hTs[tt % 2][:, k8 * 128 : (k8 + 1) * 128],
                            stg[k8 // 4][:, (k8 % 4) * 128 : (k8 % 4 + 1) * 128])
                    inc(ins, "dve_m")
                    wge(v, "pe_m", B["pe_m"] + 5 * tt + 2)
                    ins = inc(v.reduce_max(sml[:, 8:9], misc[:, 0:10],
                                           axis=AX.X, negate=True), "dve_m")
                    wge(v, "act_m", B["act_m"] + 3 * tt + 2)
                    selfedge(v, v.reduce_max(sml[:, 9:10], e_sb[:, 0:10],
                                              axis=AX.X))
                    selfedge(v, v.tensor_scalar(tp2_sb[:, 0:10], ones32[:, 0:10],
                                    sml[:, 9:10], None, ALU.mult))
                    selfedge(v, v.tensor_tensor(ewx_sb[:, 0:10], e_sb[:, 0:10],
                                    tp2_sb[:, 0:10], ALU.is_equal))
                    selfedge(v, v.scalar_tensor_tensor(tp_sb[:, 0:10],
                                    ewx_sb[:, 0:10], -2.0,
                                    e_sb[:, 0:10], ALU.mult, ALU.add))
                    selfedge(v, v.reduce_max(sml[:, 10:11], tp_sb[:, 0:10],
                                             axis=AX.X))
                    selfedge(v, v.tensor_scalar(tp2_sb[:, 0:10], ones32[:, 0:10],
                                    sml[:, 10:11], None, ALU.mult))
                    selfedge(v, v.tensor_tensor(ewx_sb[:, 16:26], tp_sb[:, 0:10],
                                    tp2_sb[:, 0:10], ALU.is_equal))
                    selfedge(v, v.tensor_add(tp_sb[:, 0:10], ewx_sb[:, 16:26],
                                             ewx_sb[:, 0:10]))
                    selfedge(v, v.tensor_mul(ew_sb[:, 0:8], tp_sb[:, 0:8],
                                             e_sb[:, 0:8]))
                    selfedge(v, v.reduce_sum(sml[:, 12:13], ew_sb[:, 0:8],
                                             axis=AX.X))
                    selfedge(v, v.tensor_scalar(sml[:, 13:14], sml[:, 12:13],
                                                1e30, None, ALU.mult))
                    selfedge(v, v.tensor_tensor(sml[:, 11:12], sml[:, 13:14],
                                                ones32[:, 0:1], ALU.min))
                    selfedge(v, v.scalar_tensor_tensor(sml[:, 14:15],
                                    sml[:, 11:12], -1.0,
                                    ones32[:, 0:1], ALU.mult, ALU.add))
                    ins = inc(v.tensor_add(sml[:, 14:15], sml[:, 14:15],
                                           sml[:, 12:13]), "dve_m")
                    wge(v, "act_m", B["act_m"] + 3 * tt + 3)
                    selfedge(v, v.tensor_scalar(ew_sb[:, 0:8], ew_sb[:, 0:8],
                                                sml[:, 15:16], None, ALU.mult))
                    for e8 in range(8):
                        ins = v.tensor_scalar(ewx_sb[:, e8 * 16 : (e8 + 1) * 16],
                                              ones32[:, 0:16],
                                              ew_sb[:, e8 : e8 + 1], None,
                                              ALU.mult)
                    inc(ins, "dve_m")
                    wge(v, "dve_m", cnt["dve_m"])
                    wge(v, "pe_m", B["pe_m"] + 5 * tt + 3)
                    ins = inc(v.tensor_tensor(zw_bf[:, :], X[:, 1024:1152],
                                              ewx_sb[:, :], ALU.mult), "dve_m")
                    wge(v, "pe_m", B["pe_m"] + 5 * tt + 4)
                    ins = inc(v.tensor_copy(zwT_sb[:, :], stg[0][:, 0:128]),
                              "dve_m")
                    wge(v, "pe_m", B["pe_m"] + 5 * tt + 5)
                    if tt >= 2:
                        wge(v, "dma_st", B["dma_st"] + 16 * (tt - 1))
                    par = (X if tt % 2 == 0 else Y)[:, 0:1024]
                    ins = inc(v.tensor_add(xob[tt % 2][:, :], par,
                                           xbuf[tt % 2][:, :]), "dve_m")
                pdone(v, "dve_m")

            @block.scalar
            def _(sc):
                sc.wait_ge(sems["phz"], bt())
                ins = None
                for tt in range(NT):
                    wge(sc, "dve_m", B["dve_m"] + 9 * tt + 1)
                    ins = inc(ln_act_rstd(sc), "act_m")
                    wge(sc, "dve_m", B["dve_m"] + 9 * tt + 4)
                    ins = inc(sc.activation(e_sb[:, 0:10], misc[:, 0:10],
                                            AF.Exp, bias=sml[:, 8:9]),
                              "act_m")
                    wge(sc, "dve_m", B["dve_m"] + 9 * tt + 5)
                    sc.activation(sml[:, 16:17], sml[:, 14:15], AF.Ln)
                    ins = inc(sc.activation(sml[:, 15:16], sml[:, 16:17],
                                            AF.Exp, scale=-1.0,
                                            bias=sml[:, 7:8]), "act_m")
                pdone(sc, "act_m")

            @block.tensor
            def _(t):
                t.wait_ge(sems["phz"], bt())
                wge(t, "dma_w", B["dma_w"] + 26 * 16)
                ins = None
                for tt in range(NT):
                    wge(t, "dve_m", B["dve_m"] + 9 * tt + 2)
                    if tt >= 1:
                        wge(t, "dve_m", B["dve_m"] + 9 * (tt - 1) + 8)
                    for k8 in range(KC):
                        ins = tpose(t, 
                            stg[k8 // 4][:, (k8 % 4) * 128 : (k8 % 4 + 1) * 128],
                            hbuf[tt % 2][:, k8 * 128 : (k8 + 1) * 128],
                            id_bf[:, :])
                    inc(ins, "pe_m")
                    wge(t, "dve_m", B["dve_m"] + 9 * tt + 3)
                    if tt >= 1:
                        wge(t, "dve_m", B["dve_m"] + 9 * (tt - 1) + 7)
                        wge(t, "act_m", B["act_m"] + 3 * (tt - 1) + 2)
                    for k8 in range(KC):
                        ins = t.matmul(misc[:, 0:10],
                                       hTs[tt % 2][:, k8 * 128 : (k8 + 1) * 128],
                                       gw_sb[:, k8 * 16 : k8 * 16 + 10],
                                       start=(k8 == 0), stop=(k8 == 7))
                    inc(ins, "pe_m")
                    for k8 in range(KC):
                        ins = t.matmul(X[:, 1024:1152],
                                       hTs[tt % 2][:, k8 * 128 : (k8 + 1) * 128],
                                       aA_sb[:, k8 * 128 : (k8 + 1) * 128],
                                       start=(k8 == 0), stop=(k8 == 7))
                    inc(ins, "pe_m")
                    wge(t, "dve_m", B["dve_m"] + 9 * tt + 7)
                    ins = inc(tpose(t, stg[0][:, 0:128], zw_bf[:, :],
                                          id_bf[:, :]), "pe_m")
                    wge(t, "dve_m", B["dve_m"] + 9 * tt + 8)
                    if tt >= 2:
                        wge(t, "dve_m", B["dve_m"] + 9 * (tt - 2) + 9)
                    par = (X if tt % 2 == 0 else Y)[:, 0:1024]
                    for half in range(2):
                        sl = par[:, half * 512 : (half + 1) * 512]
                        for k8 in range(KC):
                            t.matmul(sl,
                                     hTs[tt % 2][:, k8 * 128 : (k8 + 1) * 128],
                                     bw_sb[:, k8 * 1024 + half * 512 :
                                           k8 * 1024 + (half + 1) * 512],
                                     start=(k8 == 0), stop=False)
                        t.matmul(sl, ones_bf[0:1, 0:128],
                                 bbr[0:1, half * 512 : (half + 1) * 512],
                                 start=False, stop=False)
                        ins = t.matmul(sl, zwT_sb[:, :],
                                       bB_sb[:, half * 512 : (half + 1) * 512],
                                       start=False, stop=True)
                    inc(ins, "pe_m")
                pdone(t, "pe_m")

            @block.sync
            def _(s):
                s.wait_ge(sems["phz"], bt())
                pdone(s)

        end_phase()
        s3.close()

    ffn_phase(x1_dram, out if upto == 2 else x2_dram, "f1")
    if upto == 2:
        tail_st.close()
        base_st.close()
        return nc

    s4 = contextlib.ExitStack()
    m4 = s4.enter_context
    w2_sb = m4(nc.sbuf_tensor("w2_sb", [128, 16 * 1024], BF16))
    k_sb = m4(nc.sbuf_tensor("k_sb", [128, 32 * 1024], BF16))
    v_sb = m4(nc.sbuf_tensor("v_sb", [128, 32 * 1024], BF16))
    kv_sb = m4(nc.sbuf_tensor("kv_sb", [128, 8 * 1024], BF16))
    qT_sb = m4(nc.sbuf_tensor("qT_sb", [128, 1024], BF16))

    with nc.Block() as block:
        B = snap()

        @block.gpsimd
        def _(g):
            g.wait_ge(sems["phz"], bt())
            for c8 in range(8):
                inc(g.dma_start(w2_sb[:, c8 * 1024 : (c8 + 1) * 1024],
                                w_k2[c8 * 128 : (c8 + 1) * 128, :]), "dma_w", 16)
                inc(g.dma_start(w2_sb[:, 8192 + c8 * 1024 : 8192 + (c8 + 1) * 1024],
                                w_v2[c8 * 128 : (c8 + 1) * 128, :]), "dma_w", 16)
            ins = None
            for tt in range(NT):
                if tt >= 2:
                    wge(g, "dve_m", B["dve_m"] + 5 * (tt - 2) + 2)
                ins = inc(g.dma_start(xbuf[tt % 2][:, :],
                                      x2_dram[tt * 128 : (tt + 1) * 128, :]),
                          "dma_x", 16)
                wge(g, "dve_m", B["dve_m"] + 5 * tt + 3)
                inc(g.dma_start(h2t_dram[tt * 128 : (tt + 1) * 128, :],
                                hTs[tt % 2][:, :]), "dma_st", 16)
            g.wait_ge(sems["dma_st"], B["dma_st"] + 16 * NT)
            pdone(g)

        @block.vector
        def _(v):
            v.wait_ge(sems["phz"], bt())
            ins = None
            for tt in range(NT):
                wge(v, "dma_x", B["dma_x"] + 16 * (tt + 1))
                inc(ln_dve_stats(v, xbuf[tt % 2][:, :]), "dve_m")
                wge(v, "act_m", B["act_m"] + tt + 1)
                if tt >= 2:
                    wge(v, "pe_m", B["pe_m"] + 3 * (tt - 2) + 1)
                inc(ln_dve_apply(v, xbuf[tt % 2][:, :], hbuf[tt % 2][:, :]),
                    "dve_m")
                wge(v, "pe_m", B["pe_m"] + 3 * tt + 1)
                if tt >= 2:
                    wge(v, "dma_st", B["dma_st"] + 16 * (tt - 1))
                for k8 in range(KC):
                    ins = v.tensor_copy(
                        hTs[tt % 2][:, k8 * 128 : (k8 + 1) * 128],
                        stg[k8 // 4][:, (k8 % 4) * 128 : (k8 % 4 + 1) * 128])
                inc(ins, "dve_m")
                wge(v, "pe_m", B["pe_m"] + 3 * tt + 2)
                par = (X if tt % 2 == 0 else Y)[:, 0:1024]
                v.tensor_copy(k_sb[:, tt * 1024 : tt * 1024 + 512], par[:, 0:512])
                ins = v.tensor_copy(k_sb[:, tt * 1024 + 512 : (tt + 1) * 1024],
                                    par[:, 512:1024])
                inc(ins, "dve_m")
                wge(v, "pe_m", B["pe_m"] + 3 * tt + 3)
                anti = (Y if tt % 2 == 0 else X)[:, 0:1024]
                v.tensor_copy(v_sb[:, tt * 1024 : tt * 1024 + 512], anti[:, 0:512])
                ins = v.tensor_copy(v_sb[:, tt * 1024 + 512 : (tt + 1) * 1024],
                                    anti[:, 512:1024])
                inc(ins, "dve_m")
            pdone(v, "dve_m")

        @block.scalar
        def _(sc):
            sc.wait_ge(sems["phz"], bt())
            ins = None
            for tt in range(NT):
                wge(sc, "dve_m", B["dve_m"] + 5 * tt + 1)
                ins = inc(ln_act_rstd(sc), "act_m")
            pdone(sc, "act_m")

        @block.tensor
        def _(t):
            t.wait_ge(sems["phz"], bt())
            wge(t, "dma_w", B["dma_w"] + 16 * 16)
            ins = None
            for tt in range(NT):
                wge(t, "dve_m", B["dve_m"] + 5 * tt + 2)
                if tt >= 1:
                    wge(t, "dve_m", B["dve_m"] + 5 * (tt - 1) + 3)
                for k8 in range(KC):
                    ins = tpose(t, 
                        stg[k8 // 4][:, (k8 % 4) * 128 : (k8 % 4 + 1) * 128],
                        hbuf[tt % 2][:, k8 * 128 : (k8 + 1) * 128], id_bf[:, :])
                inc(ins, "pe_m")
                wge(t, "dve_m", B["dve_m"] + 5 * tt + 3)
                if tt >= 1:
                    wge(t, "dve_m", B["dve_m"] + 5 * (tt - 1) + 5)
                par = (X if tt % 2 == 0 else Y)[:, 0:1024]
                for half in range(2):
                    for k8 in range(KC):
                        ins = t.matmul(par[:, half * 512 : (half + 1) * 512],
                                       hTs[tt % 2][:, k8 * 128 : (k8 + 1) * 128],
                                       w2_sb[:, k8 * 1024 + half * 512 :
                                             k8 * 1024 + (half + 1) * 512],
                                       start=(k8 == 0), stop=(k8 == 7))
                inc(ins, "pe_m")
                if tt >= 1:
                    wge(t, "dve_m", B["dve_m"] + 5 * (tt - 1) + 4)
                anti = (Y if tt % 2 == 0 else X)[:, 0:1024]
                for half in range(2):
                    for k8 in range(KC):
                        ins = t.matmul(anti[:, half * 512 : (half + 1) * 512],
                                       hTs[tt % 2][:, k8 * 128 : (k8 + 1) * 128],
                                       w2_sb[:, 8192 + k8 * 1024 + half * 512 :
                                             8192 + k8 * 1024 + (half + 1) * 512],
                                       start=(k8 == 0), stop=(k8 == 7))
                inc(ins, "pe_m")
            pdone(t, "pe_m")

        @block.sync
        def _(s):
            s.wait_ge(sems["phz"], bt())
            pdone(s)

    end_phase()

    with nc.Block() as block:
        B = snap()

        @block.tensor
        def _(t):
            t.wait_ge(sems["phz"], bt())
            ins = None
            for r in range(16):
                ac, bh = r // 2, r % 2
                if r >= 2:
                    wge(t, "dve_m", B["dve_m"] + r - 1)
                par = (X if r % 2 == 0 else Y)[:, 0:512]
                for tt in range(NT):
                    ins = t.matmul(par,
                                   k_sb[:, tt * 1024 + ac * 128 :
                                        tt * 1024 + (ac + 1) * 128],
                                   v_sb[:, tt * 1024 + bh * 512 :
                                        tt * 1024 + (bh + 1) * 512],
                                   start=(tt == 0), stop=(tt == NT - 1))
                inc(ins, "pe_m")
            pdone(t, "pe_m")

        @block.vector
        def _(v):
            v.wait_ge(sems["phz"], bt())
            ins = None
            for r in range(16):
                ac, bh = r // 2, r % 2
                wge(v, "pe_m", B["pe_m"] + r + 1)
                par = (X if r % 2 == 0 else Y)[:, 0:512]
                ins = inc(v.tensor_copy(
                    kv_sb[:, ac * 1024 + bh * 512 : ac * 1024 + (bh + 1) * 512],
                    par), "dve_m")
            pdone(v, "dve_m")

        @block.scalar
        def _(sc):
            sc.wait_ge(sems["phz"], bt())
            pdone(sc)

        @block.gpsimd
        def _(g):
            g.wait_ge(sems["phz"], bt())
            for c8 in range(8):
                inc(g.dma_start(w2_sb[:, c8 * 1024 : (c8 + 1) * 1024],
                                w_q2[c8 * 128 : (c8 + 1) * 128, :]), "dma_w", 16)
            g.wait_ge(sems["dma_w"], B["dma_w"] + 8 * 16)
            pdone(g)

        @block.sync
        def _(s):
            s.wait_ge(sems["phz"], bt())
            pdone(s)

    end_phase()

    dst3 = out if upto == 3 else x3_dram
    with nc.Block() as block:
        B = snap()

        @block.gpsimd
        def _(g):
            g.wait_ge(sems["phz"], bt())
            ins = None
            for tt in range(NT):
                if tt >= 2:
                    wge(g, "pe_m", B["pe_m"] + 2 * (tt - 2) + 1)
                inc(g.dma_start(hTs[tt % 2][:, :],
                                h2t_dram[tt * 128 : (tt + 1) * 128, :]),
                    "dma_x", 16)
                if tt >= 2:
                    wge(g, "dve_m", B["dve_m"] + 2 * (tt - 2) + 2)
                inc(g.dma_start(xbuf[tt % 2][:, :],
                                x2_dram[tt * 128 : (tt + 1) * 128, :]),
                    "dma_x", 16)
                if tt >= 1:
                    wge(g, "dve_m", B["dve_m"] + 2 * (tt - 1) + 2)
                    ins = inc(g.dma_start(dst3[(tt - 1) * 128 : tt * 128, :],
                                          xob[(tt - 1) % 2][:, :]), "dma_st", 16)
            wge(g, "dve_m", B["dve_m"] + 2 * (NT - 1) + 2)
            inc(g.dma_start(dst3[(NT - 1) * 128 : NT * 128, :],
                            xob[(NT - 1) % 2][:, :]), "dma_st", 16)
            g.wait_ge(sems["dma_st"], B["dma_st"] + 16 * NT)
            pdone(g)

        @block.vector
        def _(v):
            v.wait_ge(sems["phz"], bt())
            ins = None
            for tt in range(NT):
                wge(v, "pe_m", B["pe_m"] + 2 * tt + 1)
                if tt >= 1:
                    wge(v, "pe_m", B["pe_m"] + 2 * (tt - 1) + 2)
                par = (X if tt % 2 == 0 else Y)[:, 0:1024]
                v.tensor_copy(qT_sb[:, 0:512], par[:, 0:512])
                ins = v.tensor_copy(qT_sb[:, 512:1024], par[:, 512:1024])
                inc(ins, "dve_m")
                wge(v, "pe_m", B["pe_m"] + 2 * tt + 2)
                wge(v, "dma_x", B["dma_x"] + 32 * (tt + 1))
                if tt >= 2:
                    wge(v, "dma_st", B["dma_st"] + 16 * (tt - 1))
                anti = (Y if tt % 2 == 0 else X)[:, 0:1024]
                ins = inc(v.scalar_tensor_tensor(xob[tt % 2][:, :], anti, 32.0,
                                                 xbuf[tt % 2][:, :], ALU.mult,
                                                 ALU.add), "dve_m")
            pdone(v, "dve_m")

        @block.scalar
        def _(sc):
            sc.wait_ge(sems["phz"], bt())
            pdone(sc)

        @block.tensor
        def _(t):
            t.wait_ge(sems["phz"], bt())
            ins = None
            for tt in range(NT):
                wge(t, "dma_x", B["dma_x"] + 32 * tt + 16)
                if tt >= 2:
                    wge(t, "dve_m", B["dve_m"] + 2 * (tt - 2) + 1)
                if tt >= 1:
                    wge(t, "dve_m", B["dve_m"] + 2 * (tt - 1) + 2)
                par = (X if tt % 2 == 0 else Y)[:, 0:1024]
                for at in range(8):
                    for k8 in range(KC):
                        ins = t.matmul(par[:, at * 128 : (at + 1) * 128],
                                       w2_sb[:, k8 * 1024 + at * 128 :
                                             k8 * 1024 + (at + 1) * 128],
                                       hTs[tt % 2][:, k8 * 128 : (k8 + 1) * 128],
                                       start=(k8 == 0), stop=(k8 == 7))
                inc(ins, "pe_m")
                wge(t, "dve_m", B["dve_m"] + 2 * tt + 1)
                if tt >= 2:
                    wge(t, "dve_m", B["dve_m"] + 2 * (tt - 2) + 2)
                anti = (Y if tt % 2 == 0 else X)[:, 0:1024]
                for half in range(2):
                    for ac in range(8):
                        ins = t.matmul(anti[:, half * 512 : (half + 1) * 512],
                                       qT_sb[:, ac * 128 : (ac + 1) * 128],
                                       kv_sb[:, ac * 1024 + half * 512 :
                                             ac * 1024 + (half + 1) * 512],
                                       start=(ac == 0), stop=(ac == 7))
                inc(ins, "pe_m")
            pdone(t, "pe_m")

        @block.sync
        def _(s):
            s.wait_ge(sems["phz"], bt())
            pdone(s)

    end_phase()

    if upto == 3:
        s4.close()
        tail_st.close()
        base_st.close()
        return nc

    s4.close()

    ffn_phase(x3_dram, out, "f2")

    tail_st.close()
    base_st.close()
    return nc




def kernel(L0_attn_a_q, L0_attn_a_Wq, L0_attn_a_bq, L0_attn_a_Wk, L0_attn_a_bk,
           L0_attn_a_Wv, L0_attn_a_bv, L0_attn_a_Wo, L0_attn_a_bo,
           L0_attn_a_rel_table, L0_attn_b_w_q, L0_attn_b_w_k, L0_attn_b_w_v,
           L0_ffn_base_w, L0_ffn_base_b, L0_ffn_gate_w, L0_ffn_lora_A,
           L0_ffn_lora_B, upto=4, dbg=False):
    f = np.float32
    x = np.ascontiguousarray(np.asarray(L0_attn_a_q, dtype=f))
    Wq = np.ascontiguousarray(np.asarray(L0_attn_a_Wq, dtype=f))
    Wk = np.ascontiguousarray(np.asarray(L0_attn_a_Wk, dtype=f))
    Wv = np.ascontiguousarray(np.asarray(L0_attn_a_Wv, dtype=f))
    Wo = np.ascontiguousarray(np.asarray(L0_attn_a_Wo, dtype=f))
    bq = np.asarray(L0_attn_a_bq, dtype=f)
    bk = np.asarray(L0_attn_a_bk, dtype=f)
    bv = np.asarray(L0_attn_a_bv, dtype=f)
    bo = np.asarray(L0_attn_a_bo, dtype=f)
    rel = np.asarray(L0_attn_a_rel_table, dtype=f)
    wq2 = np.ascontiguousarray(np.asarray(L0_attn_b_w_q, dtype=f))
    wk2 = np.ascontiguousarray(np.asarray(L0_attn_b_w_k, dtype=f))
    wv2 = np.ascontiguousarray(np.asarray(L0_attn_b_w_v, dtype=f))
    base_w = np.asarray(L0_ffn_base_w, dtype=f)
    base_b = np.asarray(L0_ffn_base_b, dtype=f)
    gate_w = np.asarray(L0_ffn_gate_w, dtype=f)
    lora_A = np.asarray(L0_ffn_lora_A, dtype=f)
    lora_B = np.asarray(L0_ffn_lora_B, dtype=f)

    relT = np.ascontiguousarray(rel.T)
    relT2 = np.ascontiguousarray(np.concatenate([relT, relT], axis=0))
    common = {
        "Wq": Wq, "Wk": Wk, "Wv": Wv, "Wo": Wo,
        "bq_pp": np.ascontiguousarray(bq.reshape(8, 128).T),
        "bk_pp": np.ascontiguousarray(bk.reshape(8, 128).T),
        "bv_row": np.ascontiguousarray(bv[None, :]),
        "bo_row": np.ascontiguousarray(bo[None, :]),
        "relT2": relT2,
        "w_q2": wq2, "w_k2": wk2, "w_v2": wv2,
        "basewT": np.ascontiguousarray(base_w.T),
        "baseb_row": np.ascontiguousarray(base_b[None, :]),
        "gatewT": np.ascontiguousarray(gate_w.T),
        "AcatT": np.ascontiguousarray(lora_A.reshape(128, D).T),
        "BcatT": np.ascontiguousarray(lora_B.transpose(0, 2, 1).reshape(128, D)),
        "ident_in": np.eye(128, dtype=f),
    }
    nc = build(upto, dbg=dbg)
    in_maps = [dict(common, x0=np.ascontiguousarray(x[b])) for b in range(8)]
    res = run_bass_kernel_spmd(nc, in_maps, core_ids=list(range(8)))
    if dbg:
        return res.results
    return np.stack([res.results[b]["out"] for b in range(8)], axis=0)


# version 16 (final)
# speedup vs baseline: 1.1672x; 1.0023x over previous
import contextlib
import numpy as np
import concourse.bass as bass
import concourse.mybir as mybir
from concourse.bass_utils import run_bass_kernel_spmd

F32 = mybir.dt.float32
BF16 = mybir.dt.bfloat16
AF = mybir.ActivationFunctionType
ALU = mybir.AluOpType
AX = mybir.AxisListType

S, D = 4096, 1024
NT = 32
KC = 8
EWMAX = 1664


def _qgeom(c):
    lo = max(0, 4 * c - 4)
    hi = min(31, 4 * c + 7)
    Wb = 128 * (hi - lo + 1)
    EW = Wb + 128
    tiles = []
    for j in range(4):
        qt = 4 * c + j
        i0 = 128 * qt
        t_min = 128 * lo - i0 + 385
        Lp = max(0, -t_min)
        t_lo = max(0, t_min)
        mid = min(1024 - t_lo, EW - Lp)
        Rp = EW - Lp - mid
        tiles.append((qt, i0, Lp, t_lo, mid, Rp))
    return lo, hi, Wb, EW, tiles


QG = [_qgeom(c) for c in range(8)]


def build(upto=4, dbg=False, nhp=None, prod_only=False, pstage=9):
    nc = bass.Bass("TRN2", target_bir_lowering=False)
    dt = nc.dram_tensor

    x0 = dt("x0", [S, D], F32, kind="ExternalInput").ap()
    Wq = dt("Wq", [D, D], F32, kind="ExternalInput").ap()
    Wk = dt("Wk", [D, D], F32, kind="ExternalInput").ap()
    Wv = dt("Wv", [D, D], F32, kind="ExternalInput").ap()
    Wo = dt("Wo", [D, D], F32, kind="ExternalInput").ap()
    bq_pp = dt("bq_pp", [128, 8], F32, kind="ExternalInput").ap()
    bk_pp = dt("bk_pp", [128, 8], F32, kind="ExternalInput").ap()
    bv_row = dt("bv_row", [1, D], F32, kind="ExternalInput").ap()
    bo_row = dt("bo_row", [1, D], F32, kind="ExternalInput").ap()
    relT2 = dt("relT2", [128, 1025], F32, kind="ExternalInput").ap()
    w_q2 = dt("w_q2", [D, D], F32, kind="ExternalInput").ap()
    w_k2 = dt("w_k2", [D, D], F32, kind="ExternalInput").ap()
    w_v2 = dt("w_v2", [D, D], F32, kind="ExternalInput").ap()
    basewT = dt("basewT", [D, D], F32, kind="ExternalInput").ap()
    baseb_row = dt("baseb_row", [1, D], F32, kind="ExternalInput").ap()
    gatewT = dt("gatewT", [D, 10], F32, kind="ExternalInput").ap()
    AcatT = dt("AcatT", [D, 128], F32, kind="ExternalInput").ap()
    BcatT = dt("BcatT", [128, D], F32, kind="ExternalInput").ap()
    ident_in = dt("ident_in", [128, 128], F32, kind="ExternalInput").ap()
    out = dt("out", [S, D], F32, kind="ExternalOutput").ap()
    if dbg:
        dbg_qa = dt("dbg_qa", [128, S], BF16, kind="ExternalOutput").ap()
        dbg_ka = dt("dbg_ka", [128, S], BF16, kind="ExternalOutput").ap()
        dbg_va = dt("dbg_va", [128, 2080], BF16, kind="ExternalOutput").ap()
        dbg_e = dt("dbg_e", [128, EWMAX], BF16, kind="ExternalOutput").ap()
        dbg_gb = dt("dbg_gb", [128, 6144], BF16, kind="ExternalOutput").ap()
        dbg_a = dt("dbg_a", [128, 512], BF16, kind="ExternalOutput").ap()
        dbg_ct = dt("dbg_ct", [128, S], BF16, kind="ExternalOutput").ap()
        dbg_sml = dt("dbg_sml", [128, 64], F32, kind="ExternalOutput").ap()

    e_dram = dt("e_dram", [8 * 128 * EWMAX], BF16, kind="Internal")
    ct_dram = dt("ct_dram", [8 * 128 * S], BF16, kind="Internal")
    x1_dram = dt("x1_dram", [S, D], F32, kind="Internal").ap()
    x2_dram = dt("x2_dram", [S, D], F32, kind="Internal").ap()
    x3_dram = dt("x3_dram", [S, D], F32, kind="Internal").ap()
    h2t_dram = dt("h2t_dram", [S, D], BF16, kind="Internal").ap()

    base_st = contextlib.ExitStack()
    mk = base_st.enter_context

    id_bf = mk(nc.sbuf_tensor("id_bf", [128, 128], BF16))
    ones_bf = mk(nc.sbuf_tensor("ones_bf", [128, 512], BF16))
    ones32 = mk(nc.sbuf_tensor("ones32", [128, 16], F32))
    sml = mk(nc.sbuf_tensor("sml", [128, 64], F32))
    bor = mk(nc.sbuf_tensor("bor", [1, D], BF16))
    xbuf = [mk(nc.sbuf_tensor(f"xbuf{i}", [128, D], F32)) for i in range(2)]
    hbuf = [mk(nc.sbuf_tensor(f"hbuf{i}", [128, D], BF16)) for i in range(2)]
    scrb = mk(nc.sbuf_tensor("scrb", [128, D], BF16))

    MISCP = mk(nc.psum_tensor("MISCP", [128, 512], F32))

    sem_names = [
        "phz", "dma_w", "dma_x", "dma_s", "dma_st", "dma_e", "dma_g",
        "dma_ct", "dma_q", "dve_n",
        "pe_m", "pe_p", "pe_f", "pe_av", "pe_x",
        "act_m", "act_e", "act_l",
        "dve_m", "dve_e", "dve_c", "dve_x",
    ]
    sems = {n: mk(nc.semaphore(n)) for n in sem_names}
    cnt = {n: 0 for n in sem_names}
    nphase = [0]

    def inc(instr, name, k=1):
        instr.then_inc(sems[name], k)
        cnt[name] += k
        return instr

    def chk(name, v):
        assert cnt[name] == v, (name, cnt[name], v)

    def wge(eng, name, v):
        if v > 0:
            eng.wait_ge(sems[name], v)

    def snap():
        return dict(cnt)

    def selfedge(eng, instr, name="dve_x"):
        inc(instr, name)
        eng.wait_ge(sems[name], cnt[name])
        return instr

    def pdone(eng, sem=None):
        if sem is not None:
            eng.wait_ge(sems[sem], cnt[sem])
        eng.sem_inc(sems["phz"], 1)

    def end_phase():
        nphase[0] += 1
        cnt["phz"] += 5

    def bt():
        return 5 * nphase[0]

    def ln_dve_stats(v, xa):
        v.tensor_scalar(scrb[:, :], xa, 1.0, 0.0, ALU.mult, ALU.add,
                        accum_out=sml[:, 0:1])
        selfedge(v, v.scalar_tensor_tensor(scrb[:, :], xa, 1.0, xa, ALU.mult,
                                           ALU.mult, accum_out=sml[:, 1:2]))
        v.tensor_scalar(sml[:, 2:3], sml[:, 0:1], -1.0 / D, None, ALU.mult)
        selfedge(v, v.tensor_scalar(sml[:, 3:4], sml[:, 1:2], 1.0 / D, None,
                                    ALU.mult))
        selfedge(v, v.tensor_mul(sml[:, 5:6], sml[:, 2:3], sml[:, 2:3]))
        return v.tensor_sub(sml[:, 3:4], sml[:, 3:4], sml[:, 5:6])

    def ln_act_rstd(sc):
        sc.activation(sml[:, 4:5], sml[:, 3:4], AF.Ln, bias=sml[:, 6:7])
        return sc.activation(sml[:, 4:5], sml[:, 4:5], AF.Exp, scale=-0.5)

    def ln_dve_apply(v, xa, h_out):
        return v.tensor_scalar(h_out, xa, sml[:, 2:3], sml[:, 4:5],
                               ALU.add, ALU.mult)

    def tpose(t, out_ap, in_ap, ident):
        return t.matmul(out_ap, in_ap, ident, start=True, stop=True,
                        is_transpose=True, skip_group_check=True)

    s1 = contextlib.ExitStack()
    m1 = s1.enter_context
    id_f32 = m1(nc.sbuf_tensor("id_f32", [128, 128], F32))
    TT2 = m1(nc.sbuf_tensor("TT2", [128, 1025], BF16))
    bqc = m1(nc.sbuf_tensor("bqc", [128, 8], F32))
    bkc = m1(nc.sbuf_tensor("bkc", [128, 8], F32))
    bvr = m1(nc.sbuf_tensor("bvr", [1, D], BF16))
    maxT = m1(nc.sbuf_tensor("maxT", [1, 1], F32))
    TdT0 = m1(nc.sbuf_tensor("TdT0", [64, 2], BF16))
    tsq = m1(nc.sbuf_tensor("tsq", [64, 1025], BF16))

    wqh = [m1(nc.sbuf_tensor(f"wqh{i}", [128, 3072], BF16)) for i in range(2)]
    WQH_Q, WQH_K, WQH_V = 0, 1024, 2048
    h1T = m1(nc.sbuf_tensor("h1T", [128, KC * S], BF16))
    Qa = [m1(nc.sbuf_tensor(f"Qa{i}", [128, S], BF16)) for i in range(2)]
    Ka = [m1(nc.sbuf_tensor(f"Ka{i}", [128, S], BF16)) for i in range(2)]
    Va = [m1(nc.sbuf_tensor(f"Va{i}", [128, 32 * 65], BF16)) for i in range(2)]
    QKstg = [m1(nc.sbuf_tensor(f"QKstg{i}", [128, 512], BF16)) for i in range(8)]
    rowstg = m1(nc.sbuf_tensor("rowstg", [32, 128], BF16))
    nmall = [m1(nc.sbuf_tensor(f"nmall{i}", [128, 32], F32)) for i in range(2)]
    lnqq = m1(nc.sbuf_tensor("lnqq", [128, 32], F32))
    mallb = m1(nc.sbuf_tensor("mallb", [128, 32], F32))
    lncbf = m1(nc.sbuf_tensor("lncbf", [1, 2], BF16))
    Asb = [m1(nc.sbuf_tensor(f"Asb{i}", [128, 512], BF16)) for i in range(4)]
    lrow = [m1(nc.sbuf_tensor(f"lrow{i}", [1, 512], BF16)) for i in range(2)]
    Esb = [m1(nc.sbuf_tensor(f"Esb{i}", [128, EWMAX], BF16)) for i in range(2)]
    GBs = [m1(nc.sbuf_tensor(f"GBs{i}", [128, 4 * 1536], BF16)) for i in range(2)]
    pads = m1(nc.sbuf_tensor("pads", [128, 2], F32))
    rlt = m1(nc.sbuf_tensor("rlt", [1, 512], F32))
    rl_sb = m1(nc.sbuf_tensor("rl_sb", [1, 512], BF16))
    rlb_sb = m1(nc.sbuf_tensor("rlb_sb", [128, 512], BF16))
    CTs = m1(nc.sbuf_tensor("CTs", [128, S], BF16))

    with nc.Block() as block:
        B = snap()

        @block.gpsimd
        def _(g):
            inc(g.dma_start(id_bf[:, :], ident_in), "dma_w", 16)
            inc(g.dma_start(TT2[:, :], relT2), "dma_w", 16)
            inc(g.dma_start(bvr[:, :], bv_row), "dma_w", 16)
            inc(g.dma_start(bor[:, :], bo_row), "dma_w", 16)
            g.wait_ge(sems["dma_w"], B["dma_w"] + 64)
            g.wait_ge(sems["dma_s"], B["dma_s"] + 48)
            pdone(g)

        @block.sync
        def _(s):
            inc(s.dma_start(id_f32[:, :], ident_in), "dma_s", 16)
            inc(s.dma_start(bqc[:, :], bq_pp), "dma_s", 16)
            inc(s.dma_start(bkc[:, :], bk_pp), "dma_s", 16)
            s.wait_ge(sems["dma_w"], B["dma_w"] + 64)
            s.wait_ge(sems["dma_s"], B["dma_s"] + 48)
            pdone(s)

        @block.vector
        def _(v):
            v.memset(ones_bf[:, :], 1.0)
            v.memset(ones32[:, :], 1.0)
            v.memset(sml[:, :], 0.0)
            v.memset(sml[:, 6:7], 1e-5)
            v.memset(sml[:, 7:8], 0.6931471805599453)
            v.memset(Ka[0][64:128, :], 0.0)
            v.memset(Ka[1][64:128, :], 0.0)
            v.memset(Qa[0][64:128, :], 0.0)
            v.memset(Qa[1][64:128, :], 0.0)
            selfedge(v, v.memset(Ka[0][64:66, :], 1.0))
            v.memset(Ka[1][64:66, :], 1.0)
            v.memset(Va[0][:, :], 1.0)
            v.memset(Va[1][:, :], 1.0)
            v.wait_ge(sems["dma_w"], B["dma_w"] + 64)
            v.wait_ge(sems["dma_s"], B["dma_s"] + 48)
            v.tensor_copy(TdT0[0:64, 0:1], TT2[0:64, 0:1])
            ins = inc(v.tensor_sub(TdT0[0:64, 1:2], TT2[0:64, 1024:1025],
                                   TT2[0:64, 0:1]), "dve_m")
            pdone(v, "dve_m")

        @block.tensor
        def _(t):
            pdone(t)

        @block.scalar
        def _(sc):
            pdone(sc)

    end_phase()

    with nc.Block() as block:
        B = snap()

        @block.gpsimd
        def _(g):
            g.wait_ge(sems["phz"], bt())
            for w0, wsrc in ((WQH_Q, Wq), (WQH_K, Wk), (WQH_V, Wv)):
                for kcc in range(KC):
                    inc(g.dma_start(
                        wqh[0][:, w0 + kcc * 128 : w0 + (kcc + 1) * 128],
                        wsrc[kcc * 128 : (kcc + 1) * 128, 0:128]), "dma_w", 16)
            g.wait_ge(sems["dma_w"], B["dma_w"] + 24 * 16)
            pdone(g)

        @block.vector
        def _(v):
            v.wait_ge(sems["phz"], bt())
            inc(v.tensor_mul(tsq[0:64, 0:1025], TT2[0:64, :], TT2[0:64, :]),
                "dve_m")
            wge(v, "pe_m", B["pe_m"] + 1)
            inc(v.tensor_reduce(sml[:, 8:9], MISCP[:, 0:8], axis=AX.X,
                                op=ALU.max), "dve_m")
            wge(v, "pe_m", B["pe_m"] + 2)
            selfedge(v, v.tensor_reduce(sml[0:1, 9:10], MISCP[0:1, 256:384],
                                        axis=AX.X, op=ALU.max))
            inc(v.tensor_tensor(sml[0:1, 9:10], sml[0:1, 9:10],
                                MISCP[0:1, 8:9], ALU.max), "dve_m")
            pdone(v, "dve_m")

        @block.tensor
        def _(t):
            t.wait_ge(sems["phz"], bt())
            wge(t, "dve_m", B["dve_m"] + 1)
            ins = None
            for t9 in range(9):
                w = min(128, 1025 - t9 * 128)
                ins = t.matmul(MISCP[0:w, t9 : t9 + 1],
                               tsq[0:64, t9 * 128 : t9 * 128 + w],
                               ones_bf[0:64, 0:1], start=True, stop=True)
            inc(ins, "pe_m")
            wge(t, "dve_m", B["dve_m"] + 2)
            ins = inc(tpose(t, MISCP[0:1, 256:384], sml[:, 8:9],
                                  id_f32[:, :]), "pe_m")
            pdone(t, "pe_m")

        @block.scalar
        def _(sc):
            sc.wait_ge(sems["phz"], bt())
            wge(sc, "dve_m", B["dve_m"] + 3)
            sc.activation(sml[0:1, 10:11], sml[0:1, 9:10], AF.Ln)
            ins = inc(sc.activation(maxT[:, :], sml[0:1, 10:11], AF.Exp,
                                    scale=0.5), "act_m")
            pdone(sc, "act_m")

        @block.sync
        def _(s):
            s.wait_ge(sems["phz"], bt())
            pdone(s)

    end_phase()

    s1a = contextlib.ExitStack()
    stgbf = [s1a.enter_context(nc.psum_tensor(f"stgbf{i}", [128, 512], BF16))
             for i in range(2)]
    with nc.Block() as block:
        B = snap()

        @block.gpsimd
        def _(g):
            g.wait_ge(sems["phz"], bt())
            ins = None
            for tt in range(NT):
                if tt >= 2:
                    wge(g, "dve_m", B["dve_m"] + 3 * (tt - 2) + 2)
                ins = inc(g.dma_start(xbuf[tt % 2][:, :],
                                      x0[tt * 128 : (tt + 1) * 128, :]),
                          "dma_x", 16)
            pdone(g)

        @block.vector
        def _(v):
            v.wait_ge(sems["phz"], bt())
            ins = None
            for tt in range(NT):
                wge(v, "dma_x", B["dma_x"] + 16 * (tt + 1))
                inc(ln_dve_stats(v, xbuf[tt % 2][:, :]), "dve_m")
                wge(v, "act_m", B["act_m"] + tt + 1)
                if tt >= 2:
                    wge(v, "pe_m", B["pe_m"] + tt - 1)
                inc(ln_dve_apply(v, xbuf[tt % 2][:, :], hbuf[tt % 2][:, :]),
                    "dve_m")
                wge(v, "pe_m", B["pe_m"] + tt + 1)
                for k8 in range(KC):
                    ins = v.tensor_copy(
                        h1T[:, k8 * S + tt * 128 : k8 * S + (tt + 1) * 128],
                        stgbf[k8 // 4][:, (k8 % 4) * 128 : (k8 % 4 + 1) * 128])
                inc(ins, "dve_m")
            pdone(v, "dve_m")

        @block.scalar
        def _(sc):
            sc.wait_ge(sems["phz"], bt())
            ins = None
            for tt in range(NT):
                wge(sc, "dve_m", B["dve_m"] + 3 * tt + 1)
                ins = inc(ln_act_rstd(sc), "act_m")
            pdone(sc, "act_m")

        @block.tensor
        def _(t):
            t.wait_ge(sems["phz"], bt())
            ins = None
            for tt in range(NT):
                wge(t, "dve_m", B["dve_m"] + 3 * tt + 2)
                if tt >= 1:
                    wge(t, "dve_m", B["dve_m"] + 3 * (tt - 1) + 3)
                for k8 in range(KC):
                    ins = tpose(t, 
                        stgbf[k8 // 4][:, (k8 % 4) * 128 : (k8 % 4 + 1) * 128],
                        hbuf[tt % 2][:, k8 * 128 : (k8 + 1) * 128], id_bf[:, :])
                inc(ins, "pe_m")
            pdone(t, "pe_m")

        @block.sync
        def _(s):
            s.wait_ge(sems["phz"], bt())
            pdone(s)

    end_phase()
    s1a.close()

    s1p = contextlib.ExitStack()
    mp = s1p.enter_context
    PA = mp(nc.psum_tensor("PA", [128, 512], F32))
    PB = mp(nc.psum_tensor("PB", [128, 512], F32))
    CTX = mp(nc.psum_tensor("CTX", [128, 512], F32))
    Pp = mp(nc.psum_tensor("Pp", [128, 1536], F32))
    PC = mp(nc.psum_tensor("PC", [128, 512], F32))

    def pe_ptile(t, h_, c_, j, dve_e_wait):
        lo, hi, Wb, EW, tiles = QG[c_]
        qt, i0, Lp, t_lo, mid, Rp = tiles[j]
        qa = Qa[h_]
        wge(t, "dve_e", dve_e_wait)
        t.matmul(Pp[:, 0:512], qa[0:64, i0 : i0 + 128], TT2[0:64, 0:512],
                 start=True, stop=True)
        t.matmul(Pp[:, 512:1024], qa[0:64, i0 : i0 + 128], TT2[0:64, 512:1024],
                 start=True, stop=True)
        ins = t.matmul(Pp[:, 1024:1025], qa[0:64, i0 : i0 + 128],
                       TT2[0:64, 1024:1025], start=True, stop=True)
        inc(ins, "pe_p")

    def dve_etile(v, h_, c_, j, pe_p_wait, dma_e_wait, slot):
        lo, hi, Wb, EW, tiles = QG[c_]
        qt, i0, Lp, t_lo, mid, Rp = tiles[j]
        Eb = Esb[slot]
        wge(v, "pe_p", pe_p_wait)
        wge(v, "dma_e", dma_e_wait)
        selfedge(v, v.tensor_scalar(pads[:, 0:1], Pp[:, 0:1],
                                    nmall[h_][:, qt : qt + 1], None, ALU.add))
        selfedge(v, v.tensor_scalar(pads[:, 1:2], Pp[:, 1024:1025],
                                    nmall[h_][:, qt : qt + 1], None, ALU.add))
        ins = v.tensor_scalar(Eb[:, Lp : Lp + mid], Pp[:, t_lo : t_lo + mid],
                              nmall[h_][:, qt : qt + 1], None, ALU.add)
        off = 0
        while off < Lp:
            w = min(512, Lp - off)
            ins = v.tensor_scalar(Eb[:, off : off + w], ones_bf[:, 0:w],
                                  pads[:, 0:1], None, ALU.mult)
            off += w
        off = Lp + mid
        while off < EW:
            w = min(512, EW - off)
            ins = v.tensor_scalar(Eb[:, off : off + w], ones_bf[:, 0:w],
                                  pads[:, 1:2], None, ALU.mult)
            off += w
        inc(ins, "dve_e")

    def sync_etile(s, c_, j, sp, dve_e_wait, dmag_guard, dmae_wait, pef_guard,
                   slot):
        lo, hi, Wb, EW, tiles = QG[c_]
        slot_off = (sp * 4 + j) * 128 * EWMAX
        wge(s, "dve_e", dve_e_wait)
        wge(s, "dma_g", dmag_guard)
        inc(s.dma_start(bass.AP(e_dram, slot_off, [[EWMAX, 128], [1, EW]]),
                        Esb[slot][:, 0:EW]), "dma_e", 16)
        wge(s, "dma_e", dmae_wait)
        wge(s, "pe_f", pef_guard)
        inc(s.dma_start(GBs[sp][:, j * 1536 : j * 1536 + Wb],
                        bass.AP(e_dram, slot_off + 127,
                                [[EWMAX - 1, 128], [1, Wb]])), "dma_g", 16)

    for hp in range(8 if nhp is None else nhp):
        with nc.Block() as block:
            B = snap()
            wsl = hp % 2
            dma_w_rdy = cnt["dma_w"]

            @block.tensor
            def _(t):
                t.wait_ge(sems["phz"], bt())
                wge(t, "dma_w", dma_w_rdy)
                for p in range(16):
                    c = p % 8
                    w0 = WQH_K if p < 8 else WQH_Q
                    if p >= 2:
                        wge(t, "dve_m", B["dve_m"] + 1 + p)
                    bank = (PA if p % 2 == 0 else PB)[:, 0:512]
                    ins = None
                    for kcc in range(KC):
                        ins = t.matmul(
                            bank,
                            wqh[wsl][:, w0 + kcc * 128 : w0 + (kcc + 1) * 128],
                            h1T[:, kcc * S + c * 512 : kcc * S + (c + 1) * 512],
                            start=(kcc == 0), stop=(kcc == KC - 1))
                    inc(ins, "pe_m")
                for h in range(2 if pstage >= 2 else 0):
                    for ch in range(KC):
                        wge(t, "dve_m", B["dve_m"] + 19 + 10 * h + ch)
                        ins = None
                        for i4 in range(4):
                            qt = 4 * ch + i4
                            ins = t.matmul(
                                MISCP[:, 32 * h + qt : 32 * h + qt + 1],
                                Asb[ch % 2][0:64, i4 * 128 : (i4 + 1) * 128],
                                ones_bf[0:64, 0:1], start=True, stop=True,
                                skip_group_check=True)
                        inc(ins, "pe_m")
                    wge(t, "dve_m", B["dve_m"] + 27 + 10 * h)
                    inc(tpose(t, MISCP[0:1, 256:384], sml[:, 8 + h : 9 + h],
                                    id_f32[:, :]), "pe_m")
                if pstage >= 2:
                    chk("pe_m", B["pe_m"] + 34)
                for h in range(2 if pstage >= 2 else 0):
                    wge(t, "act_m", B["act_m"] + 3 + h)
                    inc(t.matmul(MISCP[:, 100 + h : 101 + h],
                                 ones_bf[0:1, 0:128], lncbf[0:1, h : h + 1],
                                 start=True, stop=True,
                                 skip_group_check=True), "pe_m")
                for h in range(2 if pstage >= 3 else 0):
                    for ch in range(KC):
                        wge(t, "dve_m", B["dve_m"] + 43 + 8 * h + ch)
                        if ch == 0:
                            wge(t, "dve_m", B["dve_m"] + 27 + 10 * h)
                        ins = None
                        for i4 in range(4):
                            qt = 4 * ch + i4
                            ins = t.matmul(
                                MISCP[:, 32 * h + qt : 32 * h + qt + 1],
                                Asb[ch % 2][0:64, i4 * 128 : (i4 + 1) * 128],
                                ones_bf[0:64, 0:1], start=True, stop=True,
                                skip_group_check=True)
                        inc(ins, "pe_m")
                if pstage >= 3:
                    chk("pe_m", B["pe_m"] + 52)
                for h in range(2 if pstage >= 3 else 0):
                    wge(t, "dve_m", B["dve_m"] + 60 + 3 * h)
                    wge(t, "act_m", B["act_m"] + 6)
                    inc(tpose(t, MISCP[0:32, 256:384], nmall[h][:, 0:32],
                                    id_f32[:, :]), "pe_m")
                for h in range(2 if pstage >= 4 else 0):
                    for c in range(8):
                        pidx = 8 * h + c
                        if pidx == 0:
                            wge(t, "act_m", B["act_m"] + 6)
                            wge(t, "dve_m", B["dve_m"] + 64)
                        else:
                            wge(t, "dve_m", B["dve_m"] + 64 + pidx)
                        inc(t.matmul(MISCP[0:2, 0:512], TdT0[0:64, 0:2],
                                     Qa[h][0:64, c * 512 : (c + 1) * 512],
                                     start=True, stop=True,
                                     skip_group_check=True), "pe_m")
                if pstage >= 4:
                    chk("pe_m", B["pe_m"] + 70)
                for kt in range(32 if pstage >= 5 else 0):
                    if kt >= 2:
                        wge(t, "dve_m", B["dve_m"] + 81 + kt - 2)
                    sub = (PA if kt % 2 == 0 else PB)[:, 0:128]
                    for kcc in range(KC):
                        t.matmul(sub,
                                 h1T[:, kcc * S + kt * 128 : kcc * S + (kt + 1) * 128],
                                 wqh[wsl][:, WQH_V + kcc * 128 : WQH_V + (kcc + 1) * 128],
                                 start=(kcc == 0), stop=False,
                                 skip_group_check=True)
                    ins = t.matmul(sub, ones_bf[0:1, 0:128],
                                   bvr[0:1, hp * 128 : (hp + 1) * 128],
                                   start=False, stop=True,
                                   skip_group_check=True)
                    inc(ins, "pe_m")
                if pstage >= 5:
                    chk("pe_m", B["pe_m"] + 102)
                for j in range(4 if pstage >= 6 else 0):
                    pe_ptile(t, 0, 0, j, B["dve_e"] + j)
                if pstage >= 6:
                    chk("pe_p", B["pe_p"] + 4)
                pdone(t, "pe_p" if pstage >= 6 else "pe_m")

            @block.vector
            def _(v):
                v.wait_ge(sems["phz"], bt())
                inc(v.memset(Qa[0][64:66, :], 0.0), "dve_m")
                inc(v.memset(Qa[1][64:66, :], 0.0), "dve_m")
                for p in range(16):
                    c = p % 8
                    wge(v, "pe_m", B["pe_m"] + 1 + p)
                    if p >= 8:
                        wge(v, "dma_q", B["dma_q"] + 16 * (c + 1))
                    bank = (PA if p % 2 == 0 else PB)[:, 0:512]
                    dst = Ka[0] if p < 8 else Qa[0]
                    bc = bkc if p < 8 else bqc
                    scl = 0.125 if p < 8 else None
                    op1 = ALU.mult if p < 8 else ALU.add
                    if p < 8:
                        v.tensor_scalar(dst[0:64, c * 512 : (c + 1) * 512],
                                        bank[0:64, :], bc[0:64, hp : hp + 1],
                                        scl, ALU.add, op1)
                        ins = v.tensor_scalar(QKstg[c][64:128, :],
                                              bank[64:128, :],
                                              bc[64:128, hp : hp + 1],
                                              scl, ALU.add, op1)
                    else:
                        v.tensor_scalar(dst[0:64, c * 512 : (c + 1) * 512],
                                        bank[0:64, :], bc[0:64, hp : hp + 1],
                                        None, ALU.add)
                        ins = v.tensor_scalar(QKstg[c][64:128, :],
                                              bank[64:128, :],
                                              bc[64:128, hp : hp + 1],
                                              None, ALU.add)
                    inc(ins, "dve_m")
                chk("dve_m", B["dve_m"] + 18)
                for h in range(2 if pstage >= 2 else 0):
                    src = Ka[h]
                    for ch in range(KC):
                        n = 8 * h + ch
                        if n >= 2:
                            pm = 17 + (n - 2) if n - 2 < 8 else 26 + (n - 2 - 8)
                            wge(v, "pe_m", B["pe_m"] + pm)
                        if h == 1:
                            wge(v, "dma_q", B["dma_q"] + 16 * (ch + 1))
                        inc(v.tensor_mul(Asb[ch % 2][0:64, :],
                                         src[0:64, ch * 512 : (ch + 1) * 512],
                                         src[0:64, ch * 512 : (ch + 1) * 512]),
                            "dve_m")
                    wge(v, "pe_m", B["pe_m"] + 24 + 9 * h)
                    inc(v.tensor_reduce(sml[:, 8 + h : 9 + h],
                                        MISCP[:, 32 * h : 32 * h + 32],
                                        axis=AX.X, op=ALU.max), "dve_m")
                    wge(v, "pe_m", B["pe_m"] + 25 + 9 * h)
                    inc(v.tensor_reduce(sml[0:1, 10 + h : 11 + h],
                                        MISCP[0:1, 256:384], axis=AX.X,
                                        op=ALU.max), "dve_m")
                if pstage >= 2:
                    chk("dve_m", B["dve_m"] + 38)
                for h in range(2 if pstage >= 2 else 0):
                    wge(v, "act_m", B["act_m"] + 1 + h)
                    inc(v.tensor_add(sml[0:1, 16 + h : 17 + h],
                                     sml[0:1, 14 + h : 15 + h], maxT[:, :]),
                        "dve_m")
                for h in range(2 if pstage >= 2 else 0):
                    wge(v, "pe_m", B["pe_m"] + 35 + h)
                    inc(v.tensor_copy(sml[:, 22 + h : 23 + h],
                                      MISCP[:, 100 + h : 101 + h]), "dve_m")
                for h in range(2 if pstage >= 3 else 0):
                    src = Qa[h]
                    for ch in range(KC):
                        n = 16 + 8 * h + ch
                        pm = 26 + (n - 2 - 8) if n - 2 < 16 else 37 + (n - 2 - 16)
                        wge(v, "pe_m", B["pe_m"] + pm)
                        if h == 1:
                            wge(v, "dma_q", B["dma_q"] + 128 + 16 * (ch + 1))
                        inc(v.tensor_mul(Asb[ch % 2][0:64, :],
                                         src[0:64, ch * 512 : (ch + 1) * 512],
                                         src[0:64, ch * 512 : (ch + 1) * 512]),
                            "dve_m")
                if pstage >= 3:
                    chk("dve_m", B["dve_m"] + 58)
                for h in range(2 if pstage >= 3 else 0):
                    wge(v, "act_m", B["act_m"] + 5 + h)
                    inc(v.tensor_scalar(nmall[h][:, :], mallb[:, :], -1.0,
                                        None, ALU.mult), "dve_m")
                    v.wait_ge(sems["dve_m"], cnt["dve_m"])
                    inc(v.tensor_scalar(nmall[h][:, :], nmall[h][:, :], 1.0,
                                        None, ALU.mult), "dve_m")
                    wge(v, "pe_m", B["pe_m"] + 53 + h)
                    if h == 1:
                        wge(v, "dma_q", B["dma_q"] + 272)
                    inc(v.tensor_copy(rowstg[0:32, :], MISCP[0:32, 256:384]),
                        "dve_m")
                if pstage >= 3:
                    chk("dve_m", B["dve_m"] + 64)
                for h in range(2 if pstage >= 4 else 0):
                    for c in range(8):
                        wge(v, "pe_m", B["pe_m"] + 55 + 8 * h + c)
                        wge(v, "dma_q", B["dma_q"] + 272 + 16 * h)
                        inc(v.tensor_tensor(
                            Qa[h][64:66, c * 512 : (c + 1) * 512],
                            MISCP[0:2, 0:512],
                            Qa[h][64:66, c * 512 : (c + 1) * 512],
                            ALU.add), "dve_m")
                if pstage >= 4:
                    chk("dve_m", B["dve_m"] + 80)
                for kt in range(32 if pstage >= 5 else 0):
                    wge(v, "pe_m", B["pe_m"] + 71 + kt)
                    sub = (PA if kt % 2 == 0 else PB)[:, 0:128]
                    v.tensor_copy(Va[0][:, kt * 65 : kt * 65 + 64], sub[:, 0:64])
                    ins = v.tensor_copy(Va[1][:, kt * 65 : kt * 65 + 64],
                                        sub[:, 64:128])
                    inc(ins, "dve_m")
                if pstage >= 5:
                    chk("dve_m", B["dve_m"] + 112)
                for j in range(4 if pstage >= 6 else 0):
                    dve_etile(v, 0, 0, j, B["pe_p"] + j + 1,
                              B["dma_e"] + 16 * (j - 1), j % 2)
                if pstage >= 6:
                    chk("dve_e", B["dve_e"] + 4)
                pdone(v, "dve_e" if pstage >= 6 else "dve_m")

            @block.scalar
            def _(sc):
                sc.wait_ge(sems["phz"], bt())
                for h in range(2 if pstage >= 2 else 0):
                    wge(sc, "dve_m", B["dve_m"] + 28 + 10 * h)
                    sc.activation(sml[0:1, 12 + h : 13 + h],
                                  sml[0:1, 10 + h : 11 + h], AF.Ln)
                    inc(sc.activation(sml[0:1, 14 + h : 15 + h],
                                      sml[0:1, 12 + h : 13 + h], AF.Exp,
                                      scale=0.5), "act_m")
                for h in range(2 if pstage >= 2 else 0):
                    wge(sc, "dve_m", B["dve_m"] + 39 + h)
                    sc.activation(sml[0:1, 18 + h : 19 + h],
                                  sml[0:1, 16 + h : 17 + h], AF.Ln)
                    inc(sc.activation(lncbf[0:1, h : h + 1],
                                      sml[0:1, 18 + h : 19 + h], AF.Copy),
                        "act_m")
                for h in range(2 if pstage >= 3 else 0):
                    wge(sc, "pe_m", B["pe_m"] + 52)
                    wge(sc, "dve_m", B["dve_m"] + 41 + h)
                    if h == 1:
                        wge(sc, "dve_m", B["dve_m"] + 59)
                    sc.activation(lnqq[:, :], MISCP[:, 32 * h : 32 * h + 32],
                                  AF.Ln)
                    inc(sc.activation(mallb[:, :], lnqq[:, :], AF.Exp,
                                      scale=0.5, bias=sml[:, 22 + h : 23 + h]),
                        "act_m")
                pdone(sc, "act_m")

            @block.gpsimd
            def _(g):
                g.wait_ge(sems["phz"], bt())
                for p in range(16):
                    c = p % 8
                    dst = Ka[1] if p < 8 else Qa[1]
                    wge(g, "dve_m", B["dve_m"] + 3 + p)
                    inc(g.dma_start(dst[0:64, c * 512 : (c + 1) * 512],
                                    QKstg[c][64:128, :]), "dma_q", 16)
                for h in range(2 if pstage >= 3 else 0):
                    wge(g, "dve_m", B["dve_m"] + 61 + 3 * h)
                    inc(g.dma_start(Qa[h][64:65, :], rowstg[0:32, :]),
                        "dma_q", 16)
                if pstage >= 3:
                    chk("dma_q", B["dma_q"] + 288)
                g.wait_ge(sems["dma_q"], cnt["dma_q"])
                pdone(g)

            @block.sync
            def _(s):
                s.wait_ge(sems["phz"], bt())
                for j in range(4 if pstage >= 6 else 0):
                    sync_etile(s, 0, j, 0,
                               B["dve_e"] + j + 1, 0,
                               B["dma_e"] + 16 * (j + 1), 0, j % 2)
                s.wait_ge(sems["dma_g"], cnt["dma_g"])
                pdone(s)

        end_phase()

        if prod_only:
            continue
        with nc.Block() as block:
            B = snap()

            def fills_pe(t, idx, kt):
                h, c = idx // 8, idx % 8
                lo, hi, Wb, EW, tiles = QG[c]
                g = 32 * idx + kt
                wge(t, "act_e", B["act_e"] + g - 2)
                if kt == lo and idx >= 1:
                    wge(t, "dma_g", B["dma_g"] + 64 * idx)
                bank = (PA, PB, PC)[g % 3][:, 0:512]
                if kt < lo:
                    ext = 65
                elif kt > hi:
                    ext = 66
                else:
                    ext = 64
                ins = t.matmul(bank, Ka[h][0:ext, kt * 128 : (kt + 1) * 128],
                               Qa[h][0:ext, c * 512 : (c + 1) * 512],
                               start=True, stop=(ext != 64),
                               skip_group_check=True)
                if ext == 64:
                    for j in range(4):
                        ins = t.matmul(
                            bank[:, j * 128 : (j + 1) * 128],
                            GBs[idx % 2][:, j * 1536 + (kt - lo) * 128 :
                                         j * 1536 + (kt - lo + 1) * 128],
                            id_bf[:, :], start=False, stop=(j == 3),
                            skip_group_check=True)
                inc(ins, "pe_f")
                chk("pe_f", B["pe_f"] + 32 * idx + kt + 1)

            @block.tensor
            def _(t):
                t.wait_ge(sems["phz"], bt())
                for idx in range(16):
                    h, c = idx // 8, idx % 8
                    fills_pe(t, idx, 0)
                    fills_pe(t, idx, 1)
                    for kt in range(34):
                        if kt >= 2:
                            ktA = kt - 2
                            gA = 32 * idx + ktA
                            wge(t, "act_e", B["act_e"] + gA + 1)
                            if ktA == 0:
                                wge(t, "dve_c", B["dve_c"] + idx)
                            inc(t.matmul(
                                CTX[0:65, 0:512],
                                Va[h][:, ktA * 65 : ktA * 65 + 65],
                                Asb[gA % 4][:, :],
                                start=(ktA == 0), stop=(ktA == 31)), "pe_av")
                        if kt + 2 <= 31:
                            fills_pe(t, idx, kt + 2)
                        if kt == 4 and idx >= 1:
                            hN = (idx - 1) // 8
                            wge(t, "act_l", B["act_l"] + idx)
                            wge(t, "dve_n", B["dve_n"] + idx - 1)
                            inc(t.matmul(MISCP[64 * hN : 64 * hN + 64, 0:512],
                                         ones_bf[0:1, 0:64],
                                         rl_sb[0:1, 0:512],
                                         start=True, stop=True), "pe_x")
                        if idx + 1 <= 15 and kt in (2, 6, 10, 14):
                            j = {2: 0, 6: 1, 10: 2, 14: 3}[kt]
                            h2, c2 = (idx + 1) // 8, (idx + 1) % 8
                            pe_ptile(t, h2, c2, j, B["dve_e"] + 4 * idx + j)
                    chk("pe_av", B["pe_av"] + 32 * (idx + 1))
                wge(t, "act_l", B["act_l"] + 16)
                wge(t, "dve_n", B["dve_n"] + 15)
                inc(t.matmul(MISCP[64:128, 0:512], ones_bf[0:1, 0:64],
                             rl_sb[0:1, 0:512], start=True, stop=True), "pe_x")
                chk("pe_x", B["pe_x"] + 16)
                pdone(t, "pe_x")

            @block.scalar
            def _(sc):
                sc.wait_ge(sems["phz"], bt())
                for idx in range(16):
                    for kt in range(32):
                        g = 32 * idx + kt
                        wge(sc, "pe_f", B["pe_f"] + g + 1)
                        wge(sc, "pe_av", B["pe_av"] + g - 3)
                        bank = (PA, PB, PC)[g % 3][:, 0:512]
                        inc(sc.activation(Asb[g % 4][:, :], bank, AF.Exp),
                            "act_e")
                        if kt == 1 and idx >= 1:
                            wge(sc, "dve_c", B["dve_c"] + idx)
                            sc.activation(rlt[0:1, :],
                                          lrow[(idx - 1) % 2][0:1, :], AF.Ln)
                            inc(sc.activation(rl_sb[0:1, :], rlt[0:1, :],
                                              AF.Exp, scale=-1.0), "act_l")
                wge(sc, "dve_c", B["dve_c"] + 16)
                sc.activation(rlt[0:1, :], lrow[1][0:1, :], AF.Ln)
                inc(sc.activation(rl_sb[0:1, :], rlt[0:1, :], AF.Exp,
                                  scale=-1.0), "act_l")
                chk("act_e", B["act_e"] + 512)
                chk("act_l", B["act_l"] + 16)
                pdone(sc, "act_l")

            def ct_slice(k):
                hP, cP = k // 8, k % 8
                return CTs[64 * hP : 64 * hP + 64, cP * 512 : (cP + 1) * 512]

            @block.vector
            def _(v):
                v.wait_ge(sems["phz"], bt())
                for idx in range(16):
                    if idx >= 1:
                        wge(v, "pe_av", B["pe_av"] + 32 * idx)
                        wge(v, "act_l", B["act_l"] + idx - 2)
                        v.tensor_copy(ct_slice(idx - 1), CTX[0:64, 0:512])
                        inc(v.tensor_copy(lrow[(idx - 1) % 2][0:1, :],
                                          CTX[64:65, 0:512]), "dve_c")
                    if idx >= 2:
                        hN = (idx - 2) // 8
                        wge(v, "pe_x", B["pe_x"] + idx - 1)
                        selfedge(v, v.tensor_copy(
                            rlb_sb[64 * hN : 64 * hN + 64, :],
                            MISCP[64 * hN : 64 * hN + 64, 0:512]))
                        inc(v.tensor_tensor(ct_slice(idx - 2), ct_slice(idx - 2),
                                            rlb_sb[64 * hN : 64 * hN + 64, :],
                                            ALU.mult), "dve_n")
                    if idx + 1 <= 15:
                        h2, c2 = (idx + 1) // 8, (idx + 1) % 8
                        for j in range(4):
                            n = 4 * idx + j
                            dve_etile(v, h2, c2, j,
                                      B["pe_p"] + n + 1,
                                      B["dma_e"] + 16 * (n - 1), n % 2)
                wge(v, "pe_av", B["pe_av"] + 512)
                wge(v, "act_l", B["act_l"] + 14)
                v.tensor_copy(ct_slice(15), CTX[0:64, 0:512])
                inc(v.tensor_copy(lrow[1][0:1, :], CTX[64:65, 0:512]), "dve_c")
                for k in (14, 15):
                    wge(v, "pe_x", B["pe_x"] + k + 1)
                    selfedge(v, v.tensor_copy(rlb_sb[64:128, :],
                                              MISCP[64:128, 0:512]))
                    inc(v.tensor_tensor(ct_slice(k), ct_slice(k),
                                        rlb_sb[64:128, :], ALU.mult), "dve_n")
                chk("dve_c", B["dve_c"] + 16)
                chk("dve_n", B["dve_n"] + 16)
                chk("dve_e", B["dve_e"] + 60)
                pdone(v, "dve_n")

            @block.sync
            def _(s):
                s.wait_ge(sems["phz"], bt())
                for idx in range(15):
                    c2 = (idx + 1) % 8
                    sp = (idx + 1) % 2
                    for j in range(4):
                        n = 4 * idx + j
                        dmag_guard = (B["dma_g"] + 16 * (4 * idx - 4)
                                      if idx >= 2 else 0)
                        sync_etile(s, c2, j, sp,
                                   B["dve_e"] + n + 1,
                                   dmag_guard,
                                   B["dma_e"] + 16 * (n + 1),
                                   B["pe_f"] + 32 * idx if idx >= 1 else 0,
                                   n % 2)
                chk("dma_g", B["dma_g"] + 16 * 60)
                pdone(s)

            @block.gpsimd
            def _(g):
                g.wait_ge(sems["phz"], bt())
                if hp + 1 < 8:
                    for w0, wsrc in ((WQH_Q, Wq), (WQH_K, Wk), (WQH_V, Wv)):
                        for kcc in range(KC):
                            inc(g.dma_start(
                                wqh[(hp + 1) % 2][:, w0 + kcc * 128 :
                                                  w0 + (kcc + 1) * 128],
                                wsrc[kcc * 128 : (kcc + 1) * 128,
                                     (hp + 1) * 128 : (hp + 2) * 128]),
                                "dma_w", 16)
                if dbg and hp == 0:
                    wge(g, "dve_n", B["dve_n"] + 16)
                    inc(g.dma_start(dbg_qa, Qa[0][:, :]), "dma_st", 16)
                    inc(g.dma_start(dbg_ka, Ka[0][:, :]), "dma_st", 16)
                    inc(g.dma_start(dbg_va, Va[0][:, :]), "dma_st", 16)
                    inc(g.dma_start(dbg_e, Esb[0][:, :]), "dma_st", 16)
                    inc(g.dma_start(dbg_gb, GBs[0][:, :]), "dma_st", 16)
                    inc(g.dma_start(dbg_a, Asb[1][:, :]), "dma_st", 16)
                    inc(g.dma_start(dbg_ct, CTs[:, :]), "dma_st", 16)
                    inc(g.dma_start(dbg_sml, sml[:, :]), "dma_st", 16)
                wge(g, "dve_n", B["dve_n"] + 16)
                inc(g.dma_start(
                    bass.AP(ct_dram, hp * 128 * S, [[S, 128], [1, S]]),
                    CTs[:, :]), "dma_ct", 16)
                g.wait_ge(sems["dma_ct"], B["dma_ct"] + 16)
                g.wait_ge(sems["dma_st"], cnt["dma_st"])
                pdone(g)

        end_phase()

    if nhp is not None:
        with nc.Block() as block:
            @block.gpsimd
            def _(g):
                g.wait_ge(sems["phz"], bt())
                for i in range(4):
                    if not prod_only:
                        inc(g.dma_start(out[i * 128 : (i + 1) * 128, :],
                                        CTs[:, i * 1024 : (i + 1) * 1024]),
                            "dma_st", 16)
                    inc(g.dma_start(out[512 + i * 128 : 512 + (i + 1) * 128, :],
                                    Qa[0][:, i * 1024 : (i + 1) * 1024]),
                        "dma_st", 16)
                    inc(g.dma_start(out[1024 + i * 128 : 1024 + (i + 1) * 128, :],
                                    Ka[0][:, i * 1024 : (i + 1) * 1024]),
                        "dma_st", 16)
                g.wait_ge(sems["dma_st"], cnt["dma_st"])
                pdone(g)
            for eng_name in ("tensor", "vector", "scalar", "sync"):
                def _mk(en):
                    def _f(e):
                        e.wait_ge(sems["phz"], bt())
                        pdone(e)
                    return _f
                getattr(block, eng_name)(_mk(eng_name))
        end_phase()
        s1p.close()
        s1.close()
        base_st.close()
        return nc

    s1p.close()
    s1.close()

    tail_st = contextlib.ExitStack()
    mtl = tail_st.enter_context
    hTs = [mtl(nc.sbuf_tensor(f"hTs{i}", [128, D], BF16)) for i in range(2)]
    xob = [mtl(nc.sbuf_tensor(f"xob{i}", [128, D], F32)) for i in range(2)]
    X = mtl(nc.psum_tensor("X", [128, 1536], F32))
    Y = mtl(nc.psum_tensor("Y", [128, 1024], F32))
    stg = [mtl(nc.psum_tensor(f"stg{i}", [128, 512], BF16)) for i in range(2)]
    misc = MISCP

    s2 = contextlib.ExitStack()
    m2 = s2.enter_context
    wo_sb = m2(nc.sbuf_tensor("wo_sb", [128, 8 * 1024], BF16))
    ct_in = [m2(nc.sbuf_tensor(f"ct_in{i}", [128, 1024], BF16)) for i in range(2)]
    dst1 = out if upto == 1 else x1_dram

    with nc.Block() as block:
        B = snap()

        @block.gpsimd
        def _(g):
            g.wait_ge(sems["phz"], bt())
            for c8 in range(8):
                inc(g.dma_start(wo_sb[:, c8 * 1024 : (c8 + 1) * 1024],
                                Wo[c8 * 128 : (c8 + 1) * 128, :]), "dma_w", 16)
            ins = None
            for tt in range(NT):
                if tt >= 2:
                    wge(g, "pe_m", B["pe_m"] + tt - 1)
                inc(g.dma_start(
                    bass.AP(ct_in[tt % 2], 0, [[1024, 128], [128, 8], [1, 128]]),
                    bass.AP(ct_dram, tt * 128,
                            [[S, 128], [128 * S, 8], [1, 128]])), "dma_x", 16)
                if tt >= 2:
                    wge(g, "dve_m", B["dve_m"] + tt - 1)
                inc(g.dma_start(xbuf[tt % 2][:, :],
                                x0[tt * 128 : (tt + 1) * 128, :]), "dma_x", 16)
                if tt >= 1:
                    wge(g, "dve_m", B["dve_m"] + tt)
                    ins = inc(g.dma_start(dst1[(tt - 1) * 128 : tt * 128, :],
                                          xob[(tt - 1) % 2][:, :]), "dma_st", 16)
            wge(g, "dve_m", B["dve_m"] + NT)
            ins = inc(g.dma_start(dst1[(NT - 1) * 128 : NT * 128, :],
                                  xob[(NT - 1) % 2][:, :]), "dma_st", 16)
            g.wait_ge(sems["dma_st"], B["dma_st"] + 16 * NT)
            pdone(g)

        @block.tensor
        def _(t):
            t.wait_ge(sems["phz"], bt())
            wge(t, "dma_w", B["dma_w"] + 128)
            ins = None
            for tt in range(NT):
                wge(t, "dma_x", B["dma_x"] + 32 * tt + 16)
                if tt >= 2:
                    wge(t, "dve_m", B["dve_m"] + tt - 1)
                par = (X if tt % 2 == 0 else Y)[:, 0:1024]
                for half in range(2):
                    for c8 in range(8):
                        t.matmul(par[:, half * 512 : (half + 1) * 512],
                                 ct_in[tt % 2][:, c8 * 128 : (c8 + 1) * 128],
                                 wo_sb[:, c8 * 1024 + half * 512 :
                                       c8 * 1024 + (half + 1) * 512],
                                 start=(c8 == 0), stop=False)
                    ins = t.matmul(par[:, half * 512 : (half + 1) * 512],
                                   ones_bf[0:1, 0:128],
                                   bor[0:1, half * 512 : (half + 1) * 512],
                                   start=False, stop=True)
                inc(ins, "pe_m")
            pdone(t, "pe_m")

        @block.vector
        def _(v):
            v.wait_ge(sems["phz"], bt())
            ins = None
            for tt in range(NT):
                wge(v, "pe_m", B["pe_m"] + tt + 1)
                wge(v, "dma_x", B["dma_x"] + 32 * (tt + 1))
                if tt >= 2:
                    wge(v, "dma_st", B["dma_st"] + 16 * (tt - 1))
                par = (X if tt % 2 == 0 else Y)[:, 0:1024]
                ins = inc(v.tensor_add(xob[tt % 2][:, :], par,
                                       xbuf[tt % 2][:, :]), "dve_m")
            pdone(v, "dve_m")

        @block.scalar
        def _(sc):
            sc.wait_ge(sems["phz"], bt())
            pdone(sc)

        @block.sync
        def _(s):
            s.wait_ge(sems["phz"], bt())
            pdone(s)

    end_phase()
    s2.close()

    if upto == 1:
        tail_st.close()
        base_st.close()
        return nc

    def ffn_phase(src, dst, tag):
        s3 = contextlib.ExitStack()
        m3 = s3.enter_context
        bw_sb = m3(nc.sbuf_tensor(f"bw_sb{tag}", [128, 8 * 1024], BF16))
        gw_sb = m3(nc.sbuf_tensor(f"gw_sb{tag}", [128, 8 * 16], BF16))
        aA_sb = m3(nc.sbuf_tensor(f"aA_sb{tag}", [128, 8 * 128], BF16))
        bB_sb = m3(nc.sbuf_tensor(f"bB_sb{tag}", [128, 1024], BF16))
        e_sb = m3(nc.sbuf_tensor(f"e_sb{tag}", [128, 16], F32))
        tp_sb = m3(nc.sbuf_tensor(f"tp_sb{tag}", [128, 16], F32))
        ew_sb = m3(nc.sbuf_tensor(f"ew_sb{tag}", [128, 8], F32))
        ewx_sb = m3(nc.sbuf_tensor(f"ewx_sb{tag}", [128, 128], F32))
        zw_bf = m3(nc.sbuf_tensor(f"zw_bf{tag}", [128, 128], BF16))
        zwT_sb = m3(nc.sbuf_tensor(f"zwT_sb{tag}", [128, 128], BF16))
        tp2_sb = m3(nc.sbuf_tensor(f"tp2_sb{tag}", [128, 16], F32))
        bbr = m3(nc.sbuf_tensor(f"bbr{tag}", [1, D], BF16))

        with nc.Block() as block:
            B = snap()

            @block.gpsimd
            def _(g):
                g.wait_ge(sems["phz"], bt())
                for c8 in range(8):
                    inc(g.dma_start(bw_sb[:, c8 * 1024 : (c8 + 1) * 1024],
                                    basewT[c8 * 128 : (c8 + 1) * 128, :]),
                        "dma_w", 16)
                    inc(g.dma_start(gw_sb[:, c8 * 16 : c8 * 16 + 10],
                                    gatewT[c8 * 128 : (c8 + 1) * 128, :]),
                        "dma_w", 16)
                    inc(g.dma_start(aA_sb[:, c8 * 128 : (c8 + 1) * 128],
                                    AcatT[c8 * 128 : (c8 + 1) * 128, :]),
                        "dma_w", 16)
                inc(g.dma_start(bB_sb[:, :], BcatT), "dma_w", 16)
                inc(g.dma_start(bbr[:, :], baseb_row), "dma_w", 16)
                ins = None
                for tt in range(NT):
                    if tt >= 2:
                        wge(g, "dve_m", B["dve_m"] + 9 * (tt - 2) + 9)
                    inc(g.dma_start(xbuf[tt % 2][:, :],
                                    src[tt * 128 : (tt + 1) * 128, :]),
                        "dma_x", 16)
                    if tt >= 1:
                        wge(g, "dve_m", B["dve_m"] + 9 * (tt - 1) + 9)
                        ins = inc(g.dma_start(dst[(tt - 1) * 128 : tt * 128, :],
                                              xob[(tt - 1) % 2][:, :]),
                                  "dma_st", 16)
                wge(g, "dve_m", B["dve_m"] + 9 * (NT - 1) + 9)
                inc(g.dma_start(dst[(NT - 1) * 128 : NT * 128, :],
                                xob[(NT - 1) % 2][:, :]), "dma_st", 16)
                g.wait_ge(sems["dma_st"], cnt["dma_st"])
                pdone(g)

            @block.vector
            def _(v):
                v.wait_ge(sems["phz"], bt())
                ins = None
                for tt in range(NT):
                    wge(v, "dma_x", B["dma_x"] + 16 * (tt + 1))
                    inc(ln_dve_stats(v, xbuf[tt % 2][:, :]), "dve_m")
                    wge(v, "act_m", B["act_m"] + 3 * tt + 1)
                    if tt >= 2:
                        wge(v, "pe_m", B["pe_m"] + 5 * (tt - 2) + 1)
                    inc(ln_dve_apply(v, xbuf[tt % 2][:, :],
                                     hbuf[tt % 2][:, :]), "dve_m")
                    wge(v, "pe_m", B["pe_m"] + 5 * tt + 1)
                    for k8 in range(KC):
                        ins = v.tensor_copy(
                            hTs[tt % 2][:, k8 * 128 : (k8 + 1) * 128],
                            stg[k8 // 4][:, (k8 % 4) * 128 : (k8 % 4 + 1) * 128])
                    inc(ins, "dve_m")
                    wge(v, "pe_m", B["pe_m"] + 5 * tt + 2)
                    ins = inc(v.reduce_max(sml[:, 8:9], misc[:, 0:10],
                                           axis=AX.X, negate=True), "dve_m")
                    wge(v, "act_m", B["act_m"] + 3 * tt + 2)
                    selfedge(v, v.reduce_max(sml[:, 9:10], e_sb[:, 0:10],
                                              axis=AX.X))
                    selfedge(v, v.tensor_scalar(tp2_sb[:, 0:10], ones32[:, 0:10],
                                    sml[:, 9:10], None, ALU.mult))
                    selfedge(v, v.tensor_tensor(ewx_sb[:, 0:10], e_sb[:, 0:10],
                                    tp2_sb[:, 0:10], ALU.is_equal))
                    selfedge(v, v.scalar_tensor_tensor(tp_sb[:, 0:10],
                                    ewx_sb[:, 0:10], -2.0,
                                    e_sb[:, 0:10], ALU.mult, ALU.add))
                    selfedge(v, v.reduce_max(sml[:, 10:11], tp_sb[:, 0:10],
                                             axis=AX.X))
                    selfedge(v, v.tensor_scalar(tp2_sb[:, 0:10], ones32[:, 0:10],
                                    sml[:, 10:11], None, ALU.mult))
                    selfedge(v, v.tensor_tensor(ewx_sb[:, 16:26], tp_sb[:, 0:10],
                                    tp2_sb[:, 0:10], ALU.is_equal))
                    selfedge(v, v.tensor_add(tp_sb[:, 0:10], ewx_sb[:, 16:26],
                                             ewx_sb[:, 0:10]))
                    selfedge(v, v.tensor_mul(ew_sb[:, 0:8], tp_sb[:, 0:8],
                                             e_sb[:, 0:8]))
                    selfedge(v, v.reduce_sum(sml[:, 12:13], ew_sb[:, 0:8],
                                             axis=AX.X))
                    selfedge(v, v.tensor_scalar(sml[:, 13:14], sml[:, 12:13],
                                                1e30, None, ALU.mult))
                    selfedge(v, v.tensor_tensor(sml[:, 11:12], sml[:, 13:14],
                                                ones32[:, 0:1], ALU.min))
                    selfedge(v, v.scalar_tensor_tensor(sml[:, 14:15],
                                    sml[:, 11:12], -1.0,
                                    ones32[:, 0:1], ALU.mult, ALU.add))
                    ins = inc(v.tensor_add(sml[:, 14:15], sml[:, 14:15],
                                           sml[:, 12:13]), "dve_m")
                    wge(v, "act_m", B["act_m"] + 3 * tt + 3)
                    selfedge(v, v.tensor_scalar(ew_sb[:, 0:8], ew_sb[:, 0:8],
                                                sml[:, 15:16], None, ALU.mult))
                    for e8 in range(8):
                        ins = v.tensor_scalar(ewx_sb[:, e8 * 16 : (e8 + 1) * 16],
                                              ones32[:, 0:16],
                                              ew_sb[:, e8 : e8 + 1], None,
                                              ALU.mult)
                    inc(ins, "dve_m")
                    wge(v, "dve_m", cnt["dve_m"])
                    wge(v, "pe_m", B["pe_m"] + 5 * tt + 3)
                    ins = inc(v.tensor_tensor(zw_bf[:, :], X[:, 1024:1152],
                                              ewx_sb[:, :], ALU.mult), "dve_m")
                    wge(v, "pe_m", B["pe_m"] + 5 * tt + 4)
                    ins = inc(v.tensor_copy(zwT_sb[:, :], stg[0][:, 0:128]),
                              "dve_m")
                    wge(v, "pe_m", B["pe_m"] + 5 * tt + 5)
                    if tt >= 2:
                        wge(v, "dma_st", B["dma_st"] + 16 * (tt - 1))
                    par = (X if tt % 2 == 0 else Y)[:, 0:1024]
                    ins = inc(v.tensor_add(xob[tt % 2][:, :], par,
                                           xbuf[tt % 2][:, :]), "dve_m")
                pdone(v, "dve_m")

            @block.scalar
            def _(sc):
                sc.wait_ge(sems["phz"], bt())
                ins = None
                for tt in range(NT):
                    wge(sc, "dve_m", B["dve_m"] + 9 * tt + 1)
                    ins = inc(ln_act_rstd(sc), "act_m")
                    wge(sc, "dve_m", B["dve_m"] + 9 * tt + 4)
                    ins = inc(sc.activation(e_sb[:, 0:10], misc[:, 0:10],
                                            AF.Exp, bias=sml[:, 8:9]),
                              "act_m")
                    wge(sc, "dve_m", B["dve_m"] + 9 * tt + 5)
                    sc.activation(sml[:, 16:17], sml[:, 14:15], AF.Ln)
                    ins = inc(sc.activation(sml[:, 15:16], sml[:, 16:17],
                                            AF.Exp, scale=-1.0,
                                            bias=sml[:, 7:8]), "act_m")
                pdone(sc, "act_m")

            @block.tensor
            def _(t):
                t.wait_ge(sems["phz"], bt())
                wge(t, "dma_w", B["dma_w"] + 26 * 16)
                ins = None
                for tt in range(NT):
                    wge(t, "dve_m", B["dve_m"] + 9 * tt + 2)
                    if tt >= 1:
                        wge(t, "dve_m", B["dve_m"] + 9 * (tt - 1) + 8)
                    for k8 in range(KC):
                        ins = tpose(t, 
                            stg[k8 // 4][:, (k8 % 4) * 128 : (k8 % 4 + 1) * 128],
                            hbuf[tt % 2][:, k8 * 128 : (k8 + 1) * 128],
                            id_bf[:, :])
                    inc(ins, "pe_m")
                    wge(t, "dve_m", B["dve_m"] + 9 * tt + 3)
                    if tt >= 1:
                        wge(t, "dve_m", B["dve_m"] + 9 * (tt - 1) + 7)
                        wge(t, "act_m", B["act_m"] + 3 * (tt - 1) + 2)
                    for k8 in range(KC):
                        ins = t.matmul(misc[:, 0:10],
                                       hTs[tt % 2][:, k8 * 128 : (k8 + 1) * 128],
                                       gw_sb[:, k8 * 16 : k8 * 16 + 10],
                                       start=(k8 == 0), stop=(k8 == 7))
                    inc(ins, "pe_m")
                    for k8 in range(KC):
                        ins = t.matmul(X[:, 1024:1152],
                                       hTs[tt % 2][:, k8 * 128 : (k8 + 1) * 128],
                                       aA_sb[:, k8 * 128 : (k8 + 1) * 128],
                                       start=(k8 == 0), stop=(k8 == 7))
                    inc(ins, "pe_m")
                    wge(t, "dve_m", B["dve_m"] + 9 * tt + 7)
                    ins = inc(tpose(t, stg[0][:, 0:128], zw_bf[:, :],
                                          id_bf[:, :]), "pe_m")
                    wge(t, "dve_m", B["dve_m"] + 9 * tt + 8)
                    if tt >= 2:
                        wge(t, "dve_m", B["dve_m"] + 9 * (tt - 2) + 9)
                    par = (X if tt % 2 == 0 else Y)[:, 0:1024]
                    for half in range(2):
                        sl = par[:, half * 512 : (half + 1) * 512]
                        for k8 in range(KC):
                            t.matmul(sl,
                                     hTs[tt % 2][:, k8 * 128 : (k8 + 1) * 128],
                                     bw_sb[:, k8 * 1024 + half * 512 :
                                           k8 * 1024 + (half + 1) * 512],
                                     start=(k8 == 0), stop=False)
                        t.matmul(sl, ones_bf[0:1, 0:128],
                                 bbr[0:1, half * 512 : (half + 1) * 512],
                                 start=False, stop=False)
                        ins = t.matmul(sl, zwT_sb[:, :],
                                       bB_sb[:, half * 512 : (half + 1) * 512],
                                       start=False, stop=True)
                    inc(ins, "pe_m")
                pdone(t, "pe_m")

            @block.sync
            def _(s):
                s.wait_ge(sems["phz"], bt())
                pdone(s)

        end_phase()
        s3.close()

    ffn_phase(x1_dram, out if upto == 2 else x2_dram, "f1")
    if upto == 2:
        tail_st.close()
        base_st.close()
        return nc

    s4 = contextlib.ExitStack()
    m4 = s4.enter_context
    w2_sb = m4(nc.sbuf_tensor("w2_sb", [128, 16 * 1024], BF16))
    k_sb = m4(nc.sbuf_tensor("k_sb", [128, 32 * 1024], BF16))
    v_sb = m4(nc.sbuf_tensor("v_sb", [128, 32 * 1024], BF16))
    kv_sb = m4(nc.sbuf_tensor("kv_sb", [128, 8 * 1024], BF16))
    qT_sb = m4(nc.sbuf_tensor("qT_sb", [128, 1024], BF16))

    with nc.Block() as block:
        B = snap()

        @block.gpsimd
        def _(g):
            g.wait_ge(sems["phz"], bt())
            for c8 in range(8):
                inc(g.dma_start(w2_sb[:, c8 * 1024 : (c8 + 1) * 1024],
                                w_k2[c8 * 128 : (c8 + 1) * 128, :]), "dma_w", 16)
                inc(g.dma_start(w2_sb[:, 8192 + c8 * 1024 : 8192 + (c8 + 1) * 1024],
                                w_v2[c8 * 128 : (c8 + 1) * 128, :]), "dma_w", 16)
            ins = None
            for tt in range(NT):
                if tt >= 2:
                    wge(g, "dve_m", B["dve_m"] + 5 * (tt - 2) + 2)
                ins = inc(g.dma_start(xbuf[tt % 2][:, :],
                                      x2_dram[tt * 128 : (tt + 1) * 128, :]),
                          "dma_x", 16)
                wge(g, "dve_m", B["dve_m"] + 5 * tt + 3)
                inc(g.dma_start(h2t_dram[tt * 128 : (tt + 1) * 128, :],
                                hTs[tt % 2][:, :]), "dma_st", 16)
            g.wait_ge(sems["dma_st"], B["dma_st"] + 16 * NT)
            pdone(g)

        @block.vector
        def _(v):
            v.wait_ge(sems["phz"], bt())
            ins = None
            for tt in range(NT):
                wge(v, "dma_x", B["dma_x"] + 16 * (tt + 1))
                inc(ln_dve_stats(v, xbuf[tt % 2][:, :]), "dve_m")
                wge(v, "act_m", B["act_m"] + tt + 1)
                if tt >= 2:
                    wge(v, "pe_m", B["pe_m"] + 3 * (tt - 2) + 1)
                inc(ln_dve_apply(v, xbuf[tt % 2][:, :], hbuf[tt % 2][:, :]),
                    "dve_m")
                wge(v, "pe_m", B["pe_m"] + 3 * tt + 1)
                if tt >= 2:
                    wge(v, "dma_st", B["dma_st"] + 16 * (tt - 1))
                for k8 in range(KC):
                    ins = v.tensor_copy(
                        hTs[tt % 2][:, k8 * 128 : (k8 + 1) * 128],
                        stg[k8 // 4][:, (k8 % 4) * 128 : (k8 % 4 + 1) * 128])
                inc(ins, "dve_m")
                wge(v, "pe_m", B["pe_m"] + 3 * tt + 2)
                par = (X if tt % 2 == 0 else Y)[:, 0:1024]
                v.tensor_copy(k_sb[:, tt * 1024 : tt * 1024 + 512], par[:, 0:512])
                ins = v.tensor_copy(k_sb[:, tt * 1024 + 512 : (tt + 1) * 1024],
                                    par[:, 512:1024])
                inc(ins, "dve_m")
                wge(v, "pe_m", B["pe_m"] + 3 * tt + 3)
                anti = (Y if tt % 2 == 0 else X)[:, 0:1024]
                v.tensor_copy(v_sb[:, tt * 1024 : tt * 1024 + 512], anti[:, 0:512])
                ins = v.tensor_copy(v_sb[:, tt * 1024 + 512 : (tt + 1) * 1024],
                                    anti[:, 512:1024])
                inc(ins, "dve_m")
            pdone(v, "dve_m")

        @block.scalar
        def _(sc):
            sc.wait_ge(sems["phz"], bt())
            ins = None
            for tt in range(NT):
                wge(sc, "dve_m", B["dve_m"] + 5 * tt + 1)
                ins = inc(ln_act_rstd(sc), "act_m")
            pdone(sc, "act_m")

        @block.tensor
        def _(t):
            t.wait_ge(sems["phz"], bt())
            wge(t, "dma_w", B["dma_w"] + 16 * 16)
            ins = None
            for tt in range(NT):
                wge(t, "dve_m", B["dve_m"] + 5 * tt + 2)
                if tt >= 1:
                    wge(t, "dve_m", B["dve_m"] + 5 * (tt - 1) + 3)
                for k8 in range(KC):
                    ins = tpose(t, 
                        stg[k8 // 4][:, (k8 % 4) * 128 : (k8 % 4 + 1) * 128],
                        hbuf[tt % 2][:, k8 * 128 : (k8 + 1) * 128], id_bf[:, :])
                inc(ins, "pe_m")
                wge(t, "dve_m", B["dve_m"] + 5 * tt + 3)
                if tt >= 1:
                    wge(t, "dve_m", B["dve_m"] + 5 * (tt - 1) + 5)
                par = (X if tt % 2 == 0 else Y)[:, 0:1024]
                for half in range(2):
                    for k8 in range(KC):
                        ins = t.matmul(par[:, half * 512 : (half + 1) * 512],
                                       hTs[tt % 2][:, k8 * 128 : (k8 + 1) * 128],
                                       w2_sb[:, k8 * 1024 + half * 512 :
                                             k8 * 1024 + (half + 1) * 512],
                                       start=(k8 == 0), stop=(k8 == 7))
                inc(ins, "pe_m")
                if tt >= 1:
                    wge(t, "dve_m", B["dve_m"] + 5 * (tt - 1) + 4)
                anti = (Y if tt % 2 == 0 else X)[:, 0:1024]
                for half in range(2):
                    for k8 in range(KC):
                        ins = t.matmul(anti[:, half * 512 : (half + 1) * 512],
                                       hTs[tt % 2][:, k8 * 128 : (k8 + 1) * 128],
                                       w2_sb[:, 8192 + k8 * 1024 + half * 512 :
                                             8192 + k8 * 1024 + (half + 1) * 512],
                                       start=(k8 == 0), stop=(k8 == 7))
                inc(ins, "pe_m")
            pdone(t, "pe_m")

        @block.sync
        def _(s):
            s.wait_ge(sems["phz"], bt())
            pdone(s)

    end_phase()

    with nc.Block() as block:
        B = snap()

        @block.tensor
        def _(t):
            t.wait_ge(sems["phz"], bt())
            ins = None
            for r in range(16):
                ac, bh = r // 2, r % 2
                if r >= 2:
                    wge(t, "dve_m", B["dve_m"] + r - 1)
                par = (X if r % 2 == 0 else Y)[:, 0:512]
                for tt in range(NT):
                    ins = t.matmul(par,
                                   k_sb[:, tt * 1024 + ac * 128 :
                                        tt * 1024 + (ac + 1) * 128],
                                   v_sb[:, tt * 1024 + bh * 512 :
                                        tt * 1024 + (bh + 1) * 512],
                                   start=(tt == 0), stop=(tt == NT - 1))
                inc(ins, "pe_m")
            pdone(t, "pe_m")

        @block.vector
        def _(v):
            v.wait_ge(sems["phz"], bt())
            ins = None
            for r in range(16):
                ac, bh = r // 2, r % 2
                wge(v, "pe_m", B["pe_m"] + r + 1)
                par = (X if r % 2 == 0 else Y)[:, 0:512]
                ins = inc(v.tensor_copy(
                    kv_sb[:, ac * 1024 + bh * 512 : ac * 1024 + (bh + 1) * 512],
                    par), "dve_m")
            pdone(v, "dve_m")

        @block.scalar
        def _(sc):
            sc.wait_ge(sems["phz"], bt())
            pdone(sc)

        @block.gpsimd
        def _(g):
            g.wait_ge(sems["phz"], bt())
            for c8 in range(8):
                inc(g.dma_start(w2_sb[:, c8 * 1024 : (c8 + 1) * 1024],
                                w_q2[c8 * 128 : (c8 + 1) * 128, :]), "dma_w", 16)
            g.wait_ge(sems["dma_w"], B["dma_w"] + 8 * 16)
            pdone(g)

        @block.sync
        def _(s):
            s.wait_ge(sems["phz"], bt())
            pdone(s)

    end_phase()

    dst3 = out if upto == 3 else x3_dram
    with nc.Block() as block:
        B = snap()

        @block.gpsimd
        def _(g):
            g.wait_ge(sems["phz"], bt())
            ins = None
            for tt in range(NT):
                if tt >= 2:
                    wge(g, "pe_m", B["pe_m"] + 2 * (tt - 2) + 1)
                inc(g.dma_start(hTs[tt % 2][:, :],
                                h2t_dram[tt * 128 : (tt + 1) * 128, :]),
                    "dma_x", 16)
                if tt >= 2:
                    wge(g, "dve_m", B["dve_m"] + 2 * (tt - 2) + 2)
                inc(g.dma_start(xbuf[tt % 2][:, :],
                                x2_dram[tt * 128 : (tt + 1) * 128, :]),
                    "dma_x", 16)
                if tt >= 1:
                    wge(g, "dve_m", B["dve_m"] + 2 * (tt - 1) + 2)
                    ins = inc(g.dma_start(dst3[(tt - 1) * 128 : tt * 128, :],
                                          xob[(tt - 1) % 2][:, :]), "dma_st", 16)
            wge(g, "dve_m", B["dve_m"] + 2 * (NT - 1) + 2)
            inc(g.dma_start(dst3[(NT - 1) * 128 : NT * 128, :],
                            xob[(NT - 1) % 2][:, :]), "dma_st", 16)
            g.wait_ge(sems["dma_st"], B["dma_st"] + 16 * NT)
            pdone(g)

        @block.vector
        def _(v):
            v.wait_ge(sems["phz"], bt())
            ins = None
            for tt in range(NT):
                wge(v, "pe_m", B["pe_m"] + 2 * tt + 1)
                if tt >= 1:
                    wge(v, "pe_m", B["pe_m"] + 2 * (tt - 1) + 2)
                par = (X if tt % 2 == 0 else Y)[:, 0:1024]
                v.tensor_copy(qT_sb[:, 0:512], par[:, 0:512])
                ins = v.tensor_copy(qT_sb[:, 512:1024], par[:, 512:1024])
                inc(ins, "dve_m")
                wge(v, "pe_m", B["pe_m"] + 2 * tt + 2)
                wge(v, "dma_x", B["dma_x"] + 32 * (tt + 1))
                if tt >= 2:
                    wge(v, "dma_st", B["dma_st"] + 16 * (tt - 1))
                anti = (Y if tt % 2 == 0 else X)[:, 0:1024]
                ins = inc(v.scalar_tensor_tensor(xob[tt % 2][:, :], anti, 32.0,
                                                 xbuf[tt % 2][:, :], ALU.mult,
                                                 ALU.add), "dve_m")
            pdone(v, "dve_m")

        @block.scalar
        def _(sc):
            sc.wait_ge(sems["phz"], bt())
            pdone(sc)

        @block.tensor
        def _(t):
            t.wait_ge(sems["phz"], bt())
            ins = None
            for tt in range(NT):
                wge(t, "dma_x", B["dma_x"] + 32 * tt + 16)
                if tt >= 2:
                    wge(t, "dve_m", B["dve_m"] + 2 * (tt - 2) + 1)
                if tt >= 1:
                    wge(t, "dve_m", B["dve_m"] + 2 * (tt - 1) + 2)
                par = (X if tt % 2 == 0 else Y)[:, 0:1024]
                for at in range(8):
                    for k8 in range(KC):
                        ins = t.matmul(par[:, at * 128 : (at + 1) * 128],
                                       w2_sb[:, k8 * 1024 + at * 128 :
                                             k8 * 1024 + (at + 1) * 128],
                                       hTs[tt % 2][:, k8 * 128 : (k8 + 1) * 128],
                                       start=(k8 == 0), stop=(k8 == 7))
                inc(ins, "pe_m")
                wge(t, "dve_m", B["dve_m"] + 2 * tt + 1)
                if tt >= 2:
                    wge(t, "dve_m", B["dve_m"] + 2 * (tt - 2) + 2)
                anti = (Y if tt % 2 == 0 else X)[:, 0:1024]
                for half in range(2):
                    for ac in range(8):
                        ins = t.matmul(anti[:, half * 512 : (half + 1) * 512],
                                       qT_sb[:, ac * 128 : (ac + 1) * 128],
                                       kv_sb[:, ac * 1024 + half * 512 :
                                             ac * 1024 + (half + 1) * 512],
                                       start=(ac == 0), stop=(ac == 7))
                inc(ins, "pe_m")
            pdone(t, "pe_m")

        @block.sync
        def _(s):
            s.wait_ge(sems["phz"], bt())
            pdone(s)

    end_phase()

    if upto == 3:
        s4.close()
        tail_st.close()
        base_st.close()
        return nc

    s4.close()

    ffn_phase(x3_dram, out, "f2")

    tail_st.close()
    base_st.close()
    return nc




def kernel(L0_attn_a_q, L0_attn_a_Wq, L0_attn_a_bq, L0_attn_a_Wk, L0_attn_a_bk,
           L0_attn_a_Wv, L0_attn_a_bv, L0_attn_a_Wo, L0_attn_a_bo,
           L0_attn_a_rel_table, L0_attn_b_w_q, L0_attn_b_w_k, L0_attn_b_w_v,
           L0_ffn_base_w, L0_ffn_base_b, L0_ffn_gate_w, L0_ffn_lora_A,
           L0_ffn_lora_B, upto=4, dbg=False):
    f = np.float32
    x = np.ascontiguousarray(np.asarray(L0_attn_a_q, dtype=f))
    Wq = np.ascontiguousarray(np.asarray(L0_attn_a_Wq, dtype=f))
    Wk = np.ascontiguousarray(np.asarray(L0_attn_a_Wk, dtype=f))
    Wv = np.ascontiguousarray(np.asarray(L0_attn_a_Wv, dtype=f))
    Wo = np.ascontiguousarray(np.asarray(L0_attn_a_Wo, dtype=f))
    bq = np.asarray(L0_attn_a_bq, dtype=f)
    bk = np.asarray(L0_attn_a_bk, dtype=f)
    bv = np.asarray(L0_attn_a_bv, dtype=f)
    bo = np.asarray(L0_attn_a_bo, dtype=f)
    rel = np.asarray(L0_attn_a_rel_table, dtype=f)
    wq2 = np.ascontiguousarray(np.asarray(L0_attn_b_w_q, dtype=f))
    wk2 = np.ascontiguousarray(np.asarray(L0_attn_b_w_k, dtype=f))
    wv2 = np.ascontiguousarray(np.asarray(L0_attn_b_w_v, dtype=f))
    base_w = np.asarray(L0_ffn_base_w, dtype=f)
    base_b = np.asarray(L0_ffn_base_b, dtype=f)
    gate_w = np.asarray(L0_ffn_gate_w, dtype=f)
    lora_A = np.asarray(L0_ffn_lora_A, dtype=f)
    lora_B = np.asarray(L0_ffn_lora_B, dtype=f)

    relT = np.ascontiguousarray(rel.T)
    relT2 = np.ascontiguousarray(np.concatenate([relT, relT], axis=0))
    common = {
        "Wq": Wq, "Wk": Wk, "Wv": Wv, "Wo": Wo,
        "bq_pp": np.ascontiguousarray(bq.reshape(8, 128).T),
        "bk_pp": np.ascontiguousarray(bk.reshape(8, 128).T),
        "bv_row": np.ascontiguousarray(bv[None, :]),
        "bo_row": np.ascontiguousarray(bo[None, :]),
        "relT2": relT2,
        "w_q2": wq2, "w_k2": wk2, "w_v2": wv2,
        "basewT": np.ascontiguousarray(base_w.T),
        "baseb_row": np.ascontiguousarray(base_b[None, :]),
        "gatewT": np.ascontiguousarray(gate_w.T),
        "AcatT": np.ascontiguousarray(lora_A.reshape(128, D).T),
        "BcatT": np.ascontiguousarray(lora_B.transpose(0, 2, 1).reshape(128, D)),
        "ident_in": np.eye(128, dtype=f),
    }
    nc = build(upto, dbg=dbg)
    in_maps = [dict(common, x0=np.ascontiguousarray(x[b])) for b in range(8)]
    res = run_bass_kernel_spmd(nc, in_maps, core_ids=list(range(8)))
    if dbg:
        return res.results
    return np.stack([res.results[b]["out"] for b in range(8)], axis=0)
